# version 13
# baseline (speedup 1.0000x reference)
import numpy as np
from contextlib import ExitStack

import concourse.bass as bass
import concourse.mybir as mybir
from concourse.bass_utils import run_bass_kernel_spmd
from concourse.alu_op_type import AluOpType

F32 = mybir.dt.float32
BF16 = mybir.dt.bfloat16
AF = mybir.ActivationFunctionType
AX = mybir.AxisListType.X

HID = 2048
NH = 16
HD = 128
NCORES = 8
HPC = NH // NCORES
NPAIR = 8
C = 128
MAX_CG = 30
LAMB_LB = 0.25
K_EXP = 8
R_EXP = 64
W = NPAIR * HD
import os
DBG_STAGE = int(os.environ.get("MESA_STAGE", "99"))
DBG_ITERS = int(os.environ.get("MESA_ITERS", str(MAX_CG)))
DBG_CSTAGE = int(os.environ.get("MESA_CSTAGE", "99"))
DBG_NCHUNK = int(os.environ.get("MESA_NCHUNK", "99"))

MU, AD, SU, MI, MX, BYP = (AluOpType.mult, AluOpType.add, AluOpType.subtract,
                           AluOpType.min, AluOpType.max, AluOpType.bypass)
GE = AluOpType.is_ge


class Prog:

    def __init__(self):
        self.ops = {e: [] for e in ("t", "v", "a", "g", "s")}
        self.cnt = {}
        self.sems = None

    def bump(self, sem, n):
        self.cnt[sem] = self.cnt.get(sem, 0) + n
        return self.cnt[sem]

    def emit(self, eng, fn, waits=(), inc=None, inc_n=1):
        waits = tuple((s, v) for (s, v) in waits if v > 0)
        val = None
        if inc is not None:
            self.cnt[inc] = self.cnt.get(inc, 0) + inc_n
            val = self.cnt[inc]
        self.ops[eng].append((waits, inc, inc_n, fn))
        return val

    def val(self, sem):
        return self.cnt.get(sem, 0)

    def realize(self, block, sems):
        self.sems = sems

        def runner(handle, eng):
            if eng in ("v", "a", "g"):
                handle = _AutoDrain(handle)
            for waits, inc, inc_n, fn in self.ops[eng]:
                for s, v in waits:
                    handle.wait_ge(sems[s], v)
                ins = fn(handle)
                if inc is not None:
                    ins.then_inc(sems[inc], inc_n)

        block.tensor(lambda h: runner(h, "t"))
        block.vector(lambda h: runner(h, "v"))
        block.scalar(lambda h: runner(h, "a"))
        block.gpsimd(lambda h: runner(h, "g"))
        block.sync(lambda h: runner(h, "s"))


_DRAIN_OPS = frozenset((
    "tensor_tensor", "tensor_scalar", "tensor_copy", "tensor_single_scalar",
    "scalar_tensor_tensor", "reciprocal", "memset", "memzero", "tensor_reduce",
    "reduce_sum", "reduce_max", "select", "iota", "affine_select", "activation",
    "tensor_add", "tensor_sub", "tensor_mul", "tensor_max", "square", "copy",
    "add", "mul", "sqrt", "sign",
))


class _AutoDrain:

    def __init__(self, h):
        self._h = h
        self._pend = False

    @property
    def raw(self):
        return self._h

    def skip(self):
        self._pend = False

    def __getattr__(self, name):
        real = getattr(self._h, name)
        if name in _DRAIN_OPS:
            def wrap(*a, **k):
                if self._pend:
                    self._h.drain()
                self._pend = True
                return real(*a, **k)
            return wrap
        return real


def pc(p):
    return slice(p * HD, (p + 1) * HD)


class Mesa:

    def __init__(self, nc, ctx, prog, T):
        self.nc = nc
        self.prog = prog
        self.T = T
        self.nchunk = T // C
        S = lambda nm, sh, dt=F32: ctx.enter_context(nc.sbuf_tensor(nm, sh, dt))
        P_ = lambda nm, sh, dt=F32: ctx.enter_context(nc.psum_tensor(nm, sh, dt))

        self.ident_b = S("mz_ident_b", [128, 128], BF16)
        self.ident_f = S("mz_ident_f", [128, 128], F32)
        self.utones_f = S("mz_utones_f", [128, 128], F32)
        self.utmask_b = S("mz_utmask_b", [128, 128], BF16)
        self.ones_col_f = S("mz_ones_col_f", [128, 1], F32)
        self.ones_col_b = S("mz_ones_col_b", [128, 1], BF16)
        self.ones_row_f = S("mz_ones_row_f", [1, 128], F32)
        self.ones_sq_b = S("mz_ones_sq_b", [128, 128], BF16)
        self.lamb_eps_arena = S("mz_lamb_eps", [128, W], F32)
        self.diag_lamb = S("mz_diag_lamb", [128, HPC * 128], BF16)
        self.onw_col = S("mz_onw_col", [128, 1], F32)

        self.Hkk_f = S("mz_Hkk_f", [128, W], F32)
        self.Hkk_b = S("mz_Hkk_b", [128, W], BF16)
        self.Hvk_f = S("mz_Hvk_f", [128, W], F32)
        self.Hvk_b = S("mz_Hvk_b", [128, W], BF16)
        self.Hd_row = S("mz_Hd_row", [1, W], F32)

        self.X = S("mz_X", [128, W], F32)
        self.R = S("mz_R", [128, W], F32)
        self.Pv = S("mz_Pv", [128, W], F32)
        self.Pv2 = S("mz_Pv2", [128, W], F32)
        self.Pd = S("mz_Pd", [128, W], BF16)
        self.Pd_e = S("mz_Pd_e", [128, W], BF16)
        self.Pb = S("mz_Pb", [128, W], BF16)
        self.diag_e = S("mz_diag_e", [128, W], BF16)
        self.Xt = S("mz_Xt", [128, W], BF16)

        self.aq = [S(f"mz_aq{z}", [128, W], F32) for z in range(2)]
        self.ak = [S(f"mz_ak{z}", [128, W], BF16) for z in range(2)]
        self.av = [S(f"mz_av{z}", [128, W], BF16) for z in range(2)]
        self.ag = [S(f"mz_ag{z}", [128, NPAIR], F32) for z in range(2)]
        self.ab = [S(f"mz_ab{z}", [128, NPAIR], F32) for z in range(2)]

        self.Kd = S("mz_Kd", [128, W], BF16)
        self.Vd = S("mz_Vd", [128, W], BF16)
        self.Ksq = S("mz_Ksq", [128, W], BF16)
        self.Ktil = S("mz_Ktil", [128, W], BF16)
        self.Dp = S("mz_Dp", [128, W], BF16)
        self.Mp = S("mz_Mp", [128, W], BF16)
        self.scrf = S("mz_scrf", [128, W], F32)
        self.scrf2 = S("mz_scrf2", [128, W], F32)
        self.scrb = S("mz_scrb", [128, W], BF16)

        self.Gc = S("mz_Gc", [128, NPAIR], F32)
        self.e_all = S("mz_e_all", [128, NPAIR], F32)
        self.Gcrow = S("mz_Gcrow", [1, W], F32)
        self.erow = S("mz_erow", [1, W], F32)
        self.w_all = S("mz_w_all", [128, NPAIR], F32)
        self.dec_bc = S("mz_dec_bc", [128, NPAIR], F32)
        self.decs_row = S("mz_decs_row", [1, NPAIR], F32)
        self.pq = S("mz_pq", [128, NPAIR], F32)
        self.dA = S("mz_dA", [128, NPAIR], F32)
        self.dB = S("mz_dB", [128, NPAIR], F32)
        self.nd = S("mz_nd", [128, NPAIR], F32)
        self.alpha = S("mz_alpha", [128, NPAIR], F32)
        self.alphan = S("mz_alphan", [128, NPAIR], F32)
        self.betac = S("mz_betac", [128, NPAIR], F32)
        self.c1 = S("mz_c1", [128, NPAIR], F32)
        self.lamrow = S("mz_lamrow", [1, HPC * 128], F32)
        self.sprow = S("mz_sprow", [1, HPC * 128], F32)
        self.sprow2 = S("mz_sprow2", [1, HPC * 128], F32)
        self.onwrow = S("mz_onwrow", [1, 128], F32)

        self.ps_T = P_("mz_ps_T", [128, W], F32)
        self.ps_Z = P_("mz_ps_Z", [128, W], F32)
        self.ps_B = P_("mz_ps_B", [128, W], F32)
        self.ps_S = P_("mz_ps_S", [128, 4 * HD], F32)
        self.ps_Tb = P_("mz_ps_Tb", [128, W], BF16)

        self.rd = {k: ("sv", 0) for k in ("T", "Z", "B", "S", "Tb")}
        self.wr = {k: 0 for k in ("T", "Z", "B", "S", "Tb")}
        self.hstate_v = 0
        self.pready = 0
        self.de_rd = 0
        self.de_a = 0
        self.pb_a = 0
        self.pb_a_h0 = 0
        self.x_done = 0
        self.hstate_a = 0
        self.kv_a = 0

    def w_waits(self, *banks):
        return [self.rd[b] for b in banks]

    def set_w(self, bank, tv):
        self.wr[bank] = tv

    def r_waits(self, *banks):
        return [("st", self.wr[b]) for b in banks]

    def set_r(self, bank, sem, vv):
        self.rd[bank] = (sem, vv)

    def emit_consts(self, lamp_ap, onw_ap):
        P, M = self.prog, self

        def g_init1(g):
            for t in (M.ident_b, M.ident_f, M.utones_f, M.utmask_b):
                ins = g.memset(t[:], 0.0)
            ins = g.memset(M.ones_col_f[:], 1.0)
            ins = g.memset(M.ones_col_b[:], 1.0)
            ins = g.memset(M.ones_row_f[:], 1.0)
            ins = g.memset(M.ones_sq_b[:], 1.0)
            ins = g.memset(M.Hkk_f[:], 0.0)
            ins = g.memset(M.Hkk_b[:], 0.0)
            ins = g.memset(M.Hvk_f[:], 0.0)
            ins = g.memset(M.Hvk_b[:], 0.0)
            return g.memset(M.Hd_row[:], 0.0)
        gv1 = P.emit("g", g_init1, inc="sg")

        def g_init2(g):
            for t, sel in ((M.ident_b, "ne"), (M.ident_f, "ne"), (M.utones_f, "gt"),
                           (M.utmask_b, "gt")):
                ins = g.affine_select(out=t[:], in_=t[:],
                                compare_op=AluOpType.not_equal if sel == "ne" else AluOpType.is_gt,
                                fill=1.0, base=0, pattern=[[-1, 128]], channel_multiplier=1)
            return ins
        gv = P.emit("g", g_init2, waits=[("sg", gv1)], inc="sg")
        self.consts_gv = gv

        P.emit("s", lambda s: s.dma_start(out=M.lamrow[:], in_=lamp_ap), inc="dlc", inc_n=16)
        P.emit("s", lambda s: s.dma_start(out=M.onwrow[:], in_=onw_ap), inc="dlc", inc_n=16)
        dl0 = P.val("dlc")

        if DBG_CSTAGE < 1:
            self.consts_done = P.emit("v", lambda v: v.tensor_copy(out=M.sprow[:], in_=M.lamrow[:]),
                                      waits=[("dlc", dl0), ("sg", gv)], inc="sv")
            return
        def v_sp1(v):
            ins = v.tensor_scalar(out=M.sprow[:], in0=M.lamrow[:], scalar1=-1.0, scalar2=None, op0=MU)
            ins = v.tensor_tensor(out=M.sprow[:], in0=M.lamrow[:], in1=M.sprow[:], op=MI)
            return v.tensor_scalar(out=M.sprow2[:], in0=M.lamrow[:], scalar1=0.0, scalar2=None, op0=MX)
        vv = P.emit("v", v_sp1, waits=[("dlc", dl0)], inc="sv")
        av = P.emit("a", lambda a: a.activation(out=M.sprow[:], in_=M.sprow[:], func=AF.Exp),
                    waits=[("sv", vv), ("sg", gv)], inc="sa")
        av = P.emit("a", lambda a: a.activation(out=M.sprow[:], in_=M.sprow[:], func=AF.Ln,
                                                bias=M.ones_col_f[0:1, :]),
                    waits=[("sa", av)], inc="sa")

        def v_sp4(v):
            ins = v.tensor_tensor(out=M.lamrow[:], in0=M.sprow2[:], in1=M.sprow[:], op=AD)
            return v.tensor_scalar(out=M.lamrow[:], in0=M.lamrow[:], scalar1=LAMB_LB, scalar2=None, op0=AD)
        vv = P.emit("v", v_sp4, waits=[("sa", av)], inc="sv")

        if DBG_CSTAGE < 2:
            self.consts_done = vv
            return
        def t_lam(t):
            for h in range(HPC):
                ins = t.matmul(out=M.ps_B[:, h * 128:(h + 1) * 128], lhsT=M.ones_row_f[:],
                         rhs=M.lamrow[0:1, h * 128:(h + 1) * 128], start=True, stop=True)
            return t.matmul(out=M.ps_B[:, 256:257], lhsT=M.onwrow[:],
                            rhs=M.ones_row_f[0:1, 0:1], start=True, stop=True)
        tv = P.emit("t", t_lam, waits=[("sv", vv), ("sg", gv)], inc="st")
        self.set_w("B", tv)

        def v_lam2(v):
            for p in range(NPAIR):
                h = p % HPC
                ins = v.tensor_scalar(out=M.lamb_eps_arena[:, pc(p)], in0=M.ps_B[:, h * 128:(h + 1) * 128],
                                scalar1=1e-8, scalar2=None, op0=AD)
            for h in range(HPC):
                ins = v.tensor_tensor(out=M.diag_lamb[:, h * 128:(h + 1) * 128],
                                in0=M.ps_B[:, h * 128:(h + 1) * 128], in1=M.ident_b[:], op=MU)
            return v.tensor_copy(out=M.onw_col[:], in_=M.ps_B[:, 256:257])
        vv = P.emit("v", v_lam2, waits=self.r_waits("B"), inc="sv")
        self.set_r("B", "sv", vv)
        self.consts_done = vv

    def emit_mv(self, Psrc, ak):
        P, M = self.prog, self

        tv = P.emit("t", lambda t: [t.transpose(out=M.ps_Tb[:, pc(p)], in_=M.Pb[:, pc(p)],
                                                identity=M.ident_b[:]) for p in range(NPAIR)][-1],
                    waits=self.w_waits("Tb") + [("sa", self.pb_a)], inc="st")
        self.set_w("Tb", tv)
        def t_pe(t):
            for p in range(NPAIR):
                ins = t.matmul(out=M.ps_B[:, pc(p)], lhsT=M.Pb[:, pc(p)], rhs=M.diag_e[:, pc(p)],
                               start=True, stop=True)
            return ins
        tve = P.emit("t", t_pe, waits=self.w_waits("B") + [("sa", self.pb_a), ("sa", self.de_a)],
                     inc="st")
        self.set_w("B", tve)
        av1 = P.emit("a", lambda a: a.activation(out=M.Pd[:], in_=M.ps_Tb[:], func=AF.Copy),
                     waits=self.r_waits("Tb"), inc="sa")
        self.set_r("Tb", "sa", av1)
        av2 = P.emit("a", lambda a: a.activation(out=M.Pd_e[:], in_=M.ps_B[:], func=AF.Copy),
                     waits=self.r_waits("B"), inc="sa")
        self.set_r("B", "sa", av2)

        def t_dl(t):
            for p in range(NPAIR):
                h = p % HPC
                ins = t.matmul(out=M.ps_Z[:, pc(p)], lhsT=M.Pd[:, pc(p)],
                               rhs=M.diag_lamb[:, h * 128:(h + 1) * 128],
                               start=(p % 4 == 0), stop=False)
            return ins
        P.emit("t", t_dl, waits=self.w_waits("Z") + [("sa", av1), ("sv", self.hstate_v)], inc="st")

        def t_s0(t):
            for p in range(4):
                ins = t.matmul(out=M.ps_S[:, p * HD:(p + 1) * HD], lhsT=M.Kd[:, pc(p)],
                         rhs=M.Pd[:, pc(p)], start=True, stop=True)
            return ins
        tv_s0 = P.emit("t", t_s0, waits=self.w_waits("S") + [("sa", self.kv_a)], inc="st")
        self.set_w("S", tv_s0)
        def t_s1(t):
            for p in range(4, 8):
                ins = t.matmul(out=M.ps_T[:, (p - 4) * HD:(p - 3) * HD], lhsT=M.Kd[:, pc(p)],
                         rhs=M.Pd[:, pc(p)], start=True, stop=True)
            return ins
        tv_s1 = P.emit("t", t_s1, waits=self.w_waits("T") + [("sa", av1)], inc="st")
        self.set_w("T", tv_s1)

        def t_hk(t):
            for p in range(NPAIR):
                ins = t.matmul(out=M.ps_Z[:, pc(p)], lhsT=M.Pd_e[:, pc(p)], rhs=M.Hkk_b[:, pc(p)],
                               start=False, stop=False)
            return ins
        P.emit("t", t_hk, waits=[("sa", av2), ("sa", self.hstate_a)], inc="st")

        def v_m0(v):
            v.skip()
            return v.tensor_tensor(out=M.Mp[:, 0:512], in0=M.Dp[:, 0:512],
                                   in1=M.ps_S[:], op=MU)
        vv_m0 = P.emit("v", v_m0, waits=self.r_waits("S"), inc="sv")
        self.set_r("S", "sv", vv_m0)
        def v_m1(v):
            v.skip()
            return v.tensor_tensor(out=M.Mp[:, 512:1024], in0=M.Dp[:, 512:1024],
                                   in1=M.ps_T[:, 0:512], op=MU)
        vv_m1 = P.emit("v", v_m1, waits=[("st", tv_s1)], inc="sv")
        self.set_r("T", "sv", vv_m1)

        def t_y20(t, ak=ak):
            for p in range(4):
                ins = t.matmul(out=M.ps_Z[:, pc(p)], lhsT=M.Mp[:, pc(p)], rhs=ak[:, pc(p)],
                         start=False, stop=(p == 3))
            return ins
        P.emit("t", t_y20, waits=[("sv", vv_m0)], inc="st")
        def t_y21(t, ak=ak):
            for p in range(4, 8):
                ins = t.matmul(out=M.ps_Z[:, pc(p)], lhsT=M.Mp[:, pc(p)], rhs=ak[:, pc(p)],
                         start=False, stop=(p == 7))
            return ins
        tvh = P.emit("t", t_y21, waits=[("sv", vv_m1)], inc="st")
        self.set_w("Z", tvh)
        return tvh

    def emit_chunk(self, z, c, o_dram, dl_sem, dl_val, b_stride):
        P, M = self.prog, self
        aq, ak, av, ag, ab = M.aq[z], M.ak[z], M.av[z], M.ag[z], M.ab[z]

        base_wait = [(dl_sem, dl_val), ("sv", self.consts_done)]

        def t_gc(t):
            for p in range(NPAIR):
                ins = t.matmul(out=M.ps_Z[:, p:p + 1], lhsT=M.utones_f[:], rhs=ag[:, p:p + 1],
                         start=True, stop=True)
            return ins
        tv = P.emit("t", t_gc, waits=base_wait + self.w_waits("Z"), inc="st")
        self.set_w("Z", tv)
        vv = P.emit("v", lambda v: v.tensor_copy(out=M.Gc[:], in_=M.ps_Z[:, 0:NPAIR]),
                    waits=self.r_waits("Z"), inc="sv")
        self.set_r("Z", "sv", vv)
        gc_v = vv
        e_a = P.emit("a", lambda a: a.activation(out=M.e_all[:], in_=M.Gc[:], func=AF.Exp),
                     waits=[("sv", vv)], inc="sa")
        def a_de(a):
            for p in range(NPAIR):
                ins = a.activation(out=M.diag_e[:, pc(p)], in_=M.ident_f[:], func=AF.Copy,
                                   scale=M.e_all[:, p:p + 1])
            return ins
        self.de_a = P.emit("a", a_de, waits=[("sa", e_a), ("sg", self.consts_gv),
                                             ("st", self.de_rd)], inc="sa")

        if DBG_CSTAGE < 3:
            self.hstate_v = gc_v
            return None
        def t_gr(t):
            for p in range(NPAIR):
                ins = t.transpose(out=M.ps_T[0:1, pc(p)], in_=M.Gc[:, p:p + 1], identity=M.ident_f[:])
            return ins
        tv = P.emit("t", t_gr, waits=[("sv", gc_v)] + self.w_waits("T"), inc="st")
        self.set_w("T", tv)
        vv = P.emit("v", lambda v: v.tensor_copy(out=M.Gcrow[:], in_=M.ps_T[0:1, :]),
                    waits=self.r_waits("T"), inc="sv")
        self.set_r("T", "sv", vv)
        gcrow_v = vv
        P.emit("a", lambda a: a.activation(out=M.erow[:], in_=M.Gcrow[:], func=AF.Exp),
               waits=[("sv", vv)], inc="sa")
        erow_a = P.val("sa")

        if DBG_CSTAGE < 4:
            self.hstate_v = P.val("sv")
            return None
        def t_gbc(t):
            for p in range(NPAIR):
                ins = t.matmul(out=M.ps_B[:, pc(p)], lhsT=M.ones_row_f[:], rhs=M.Gcrow[0:1, pc(p)],
                         start=True, stop=True)
            return ins
        tv = P.emit("t", t_gbc, waits=[("sv", gcrow_v)] + self.w_waits("B"), inc="st")
        self.set_w("B", tv)

        def v_dg(v):
            for p in range(NPAIR):
                h = v if p == 0 else v.raw
                ins = h.tensor_scalar(out=M.scrf[:, pc(p)], in0=M.ps_B[:, pc(p)],
                                scalar1=M.Gc[:, p:p + 1], scalar2=0.0, op0=SU, op1=MI)
            return ins
        vv = P.emit("v", v_dg, waits=self.r_waits("B"), inc="sv")
        self.set_r("B", "sv", vv)
        av_ = P.emit("a", lambda a: a.activation(out=M.scrf[:], in_=M.scrf[:], func=AF.Exp),
                     waits=[("sv", vv)], inc="sa")

        def v_w(v):
            ins = v.tensor_copy(out=M.w_all[:],
                          in_=M.scrf[:].rearrange("s (p d) -> s p d", p=NPAIR)[:, :, 127:128])
            return v.tensor_tensor(out=M.w_all[:], in0=M.w_all[:], in1=ab[:], op=MU)
        w_v = P.emit("v", v_w, waits=[("sa", av_)], inc="sv")

        def v_dp(v):
            for p in range(NPAIR):
                h = v if p == 0 else v.raw
                ins = h.scalar_tensor_tensor(out=M.Dp[:, pc(p)], in0=M.utmask_b[:],
                                       scalar=ab[:, p:p + 1], in1=M.scrf[:, pc(p)],
                                       op0=MU, op1=MU)
            return ins
        dp_v = P.emit("v", v_dp, waits=[("sa", av_)], inc="sv")

        if DBG_CSTAGE < 5:
            self.hstate_v = P.val("sv")
            return None
        ksq_a = P.emit("a", lambda a: a.activation(out=M.Ksq[:], in_=ak[:], func=AF.Square),
                       waits=base_wait, inc="sa")
        for (src, dst) in ((ak, M.Kd), (av, M.Vd)):
            def t_tb(t, src=src):
                for p in range(NPAIR):
                    ins = t.transpose(out=M.ps_Tb[:, pc(p)], in_=src[:, pc(p)], identity=M.ident_b[:])
                return ins
            tv = P.emit("t", t_tb, waits=base_wait + self.w_waits("Tb"), inc="st")
            self.set_w("Tb", tv)
            kv_a = P.emit("a", lambda a, dst=dst: a.activation(out=dst[:], in_=M.ps_Tb[:], func=AF.Copy),
                          waits=self.r_waits("Tb"), inc="sa")
            self.set_r("Tb", "sa", kv_a)
        self.kv_a = kv_a

        if DBG_CSTAGE < 6:
            self.hstate_v = P.val("sv")
            return None
        def t_dec(t):
            for p in range(NPAIR):
                ins = t.matmul(out=M.ps_Z[0:1, p:p + 1], lhsT=ag[:, p:p + 1], rhs=M.ones_col_f[:],
                         start=True, stop=True)
            return ins
        tv = P.emit("t", t_dec, waits=base_wait + self.w_waits("Z"), inc="st")
        self.set_w("Z", tv)
        vv = P.emit("v", lambda v: v.tensor_copy(out=M.decs_row[:], in_=M.ps_Z[0:1, 0:NPAIR]),
                    waits=self.r_waits("Z"), inc="sv")
        self.set_r("Z", "sv", vv)
        av_ = P.emit("a", lambda a: a.activation(out=M.decs_row[:], in_=M.decs_row[:], func=AF.Exp),
                     waits=[("sv", vv)], inc="sa")
        tv = P.emit("t", lambda t: t.matmul(out=M.ps_Z[:, 0:NPAIR], lhsT=M.ones_row_f[:],
                                            rhs=M.decs_row[:], start=True, stop=True),
                    waits=[("sa", av_)] + self.w_waits("Z"), inc="st")
        self.set_w("Z", tv)
        vv = P.emit("v", lambda v: v.tensor_copy(out=M.dec_bc[:], in_=M.ps_Z[:, 0:NPAIR]),
                    waits=self.r_waits("Z"), inc="sv")
        self.set_r("Z", "sv", vv)

        if DBG_STAGE < 2:
            self.hstate_v = self.prog.val("sv")
            return None
        def t_diag(t):
            for p in range(NPAIR):
                ins = t.matmul(out=M.ps_Z[:, pc(p)], lhsT=M.Dp[:, pc(p)], rhs=M.Ksq[:, pc(p)],
                         start=True, stop=False)
                ins = t.matmul(out=M.ps_Z[:, pc(p)], lhsT=M.erow[0:1, pc(p)],
                         rhs=M.Hd_row[0:1, pc(p)], start=False, stop=True)
            return ins
        tv = P.emit("t", t_diag, waits=[("sv", dp_v), ("sa", ksq_a), ("sa", erow_a),
                                        ("sv", self.hstate_v)] + self.w_waits("Z"), inc="st")
        self.set_w("Z", tv)

        def v_x0(v):
            ins = v.tensor_tensor(out=M.scrf2[:], in0=M.ps_Z[:], in1=M.lamb_eps_arena[:], op=AD)
            ins = v.reciprocal(out=M.scrf2[:], in_=M.scrf2[:])
            return v.tensor_tensor(out=M.X[:], in0=aq[:], in1=M.scrf2[:], op=MU)
        vv = P.emit("v", v_x0, waits=self.r_waits("Z"), inc="sv")
        self.set_r("Z", "sv", vv)

        if DBG_STAGE < 3:
            self.hstate_v = self.prog.val("sv")
            return None
        self.pready = vv
        self.pb_a = P.emit("a", lambda a: a.activation(out=M.Pb[:], in_=M.X[:], func=AF.Copy),
                           waits=[("sv", vv)], inc="sa")
        self.pb_a_h0 = self.pb_a
        q_tv = self.emit_mv(M.X, ak)

        def v_r0(v):
            v.tensor_tensor(out=M.R[:], in0=aq[:], in1=M.ps_Z[:], op=SU)
            return v.tensor_copy(out=M.Pv[:], in_=M.R[:])
        vv = P.emit("v", v_r0, waits=[("st", q_tv)], inc="sv")
        self.set_r("Z", "sv", vv)
        p_v = vv

        def a_d0(a):
            for p in range(NPAIR):
                h = a if p == 0 else a.raw
                ins = h.activation(out=M.scrb[:, pc(p)], in_=M.R[:, pc(p)], func=AF.Square,
                             accum_out=M.dA[:, p:p + 1])
            return ins
        d_a = P.emit("a", a_d0, waits=[("sv", vv)], inc="sa")
        self.pb_a = P.emit("a", lambda a: a.activation(out=M.Pb[:], in_=M.Pv[:], func=AF.Copy),
                           waits=[("sv", vv)], inc="sa")
        self.pb_a_h0 = self.pb_a
        nd0_v = P.emit("v", lambda v: v.tensor_single_scalar(out=M.nd[:], in_=M.dA[:],
                                                             scalar=1e-10, op=GE),
                       waits=[("sa", d_a)], inc="sv")

        dcur, dnxt = M.dA, M.dB
        Pcur, Pnxt = M.Pv, M.Pv2
        chain_v = nd0_v

        if DBG_STAGE < 4:
            self.hstate_v = self.prog.val("sv")
            return None
        niter = min(DBG_ITERS, MAX_CG)
        for j in range(niter):
            last = (j == niter - 1)
            self.pready = p_v
            q_tv = self.emit_mv(Pcur, ak)

            def v_pq(v, Pcur=Pcur):
                v.skip()
                for p in range(NPAIR):
                    h = v if p == 0 else v.raw
                    ins = h.scalar_tensor_tensor(out=M.scrb[:, pc(p)], in0=Pcur[:, pc(p)], scalar=0.0,
                                           in1=M.ps_Z[:, pc(p)], op0=BYP, op1=MU,
                                           accum_out=M.pq[:, p:p + 1])
                return ins
            pq_v = P.emit("v", v_pq, waits=[("st", q_tv)], inc="sv")

            def v_ca(v, dcur=dcur):
                ins = v.tensor_tensor(out=M.c1[:], in0=M.pq[:], in1=M.pq[:], op=MU)
                ins = v.tensor_single_scalar(out=M.c1[:], in_=M.c1[:], scalar=1e-20, op=GE)
                ins = v.tensor_tensor(out=M.nd[:], in0=M.nd[:], in1=M.c1[:], op=MU)
                ins = v.tensor_scalar(out=M.alphan[:], in0=M.nd[:], scalar1=-1.0, scalar2=1.0,
                                op0=MU, op1=AD)
                ins = v.tensor_tensor(out=M.c1[:], in0=M.pq[:], in1=M.nd[:], op=MU)
                ins = v.tensor_tensor(out=M.c1[:], in0=M.c1[:], in1=M.alphan[:], op=AD)
                ins = v.reciprocal(out=M.c1[:], in_=M.c1[:])
                ins = v.tensor_tensor(out=M.c1[:], in0=M.c1[:], in1=M.nd[:], op=MU)
                ins = v.tensor_tensor(out=M.alpha[:], in0=dcur[:], in1=M.c1[:], op=MU)
                return v.tensor_scalar(out=M.alphan[:], in0=M.alpha[:], scalar1=-1.0,
                                       scalar2=None, op0=MU)
            ca_v = P.emit("v", v_ca, inc="sv")

            if not last:
                rv = ca_v
                for p in range(NPAIR):
                    def v_r(v, p=p):
                        if p > 0:
                            v.skip()
                        return v.scalar_tensor_tensor(
                                    out=M.R[:, pc(p)], in0=M.ps_Z[:, pc(p)],
                                    scalar=M.alphan[:, p:p + 1], in1=M.R[:, pc(p)],
                                    op0=MU, op1=AD)
                    rv = P.emit("v", v_r, inc="sv")
                    if p <= 4:
                        def a_dn(a, p=p, dnxt=dnxt):
                            a.skip()
                            return a.activation(
                                        out=M.scrb[:, pc(p)], in_=M.R[:, pc(p)], func=AF.Square,
                                        accum_out=dnxt[:, p:p + 1])
                        dn_a = P.emit("a", a_dn, waits=[("sv", rv)], inc="sa")
                self.set_r("Z", "sv", rv)
                def v_dn2(v, dnxt=dnxt):
                    v.skip()
                    for p in range(5, 8):
                        h = v if p == 5 else v.raw
                        ins = h.scalar_tensor_tensor(out=M.scrb[:, pc(p)], in0=M.R[:, pc(p)],
                                               scalar=0.0, in1=M.R[:, pc(p)], op0=BYP, op1=MU,
                                               accum_out=dnxt[:, p:p + 1])
                    return ins
                dn_v = P.emit("v", v_dn2, inc="sv")

                def v_cb(v, dcur=dcur, dnxt=dnxt):
                    ins = v.tensor_scalar(out=M.c1[:], in0=M.nd[:], scalar1=-1.0, scalar2=1.0,
                                    op0=MU, op1=AD)
                    ins = v.tensor_tensor(out=M.betac[:], in0=dcur[:], in1=M.nd[:], op=MU)
                    ins = v.tensor_tensor(out=M.c1[:], in0=M.betac[:], in1=M.c1[:], op=AD)
                    ins = v.reciprocal(out=M.c1[:], in_=M.c1[:])
                    ins = v.tensor_tensor(out=M.c1[:], in0=M.c1[:], in1=M.nd[:], op=MU)
                    ins = v.tensor_tensor(out=M.betac[:], in0=dnxt[:], in1=M.c1[:], op=MU)
                    ins = v.tensor_single_scalar(out=M.c1[:], in_=dnxt[:], scalar=1e-10, op=GE)
                    return v.tensor_tensor(out=M.nd[:], in0=M.nd[:], in1=M.c1[:], op=MU)
                cb_v = P.emit("v", v_cb, waits=[("sa", dn_a)], inc="sv")
                chain_v = cb_v

                def v_p1(v, Pcur=Pcur, Pnxt=Pnxt):
                    for p in range(4):
                        h = v if p == 0 else v.raw
                        ins = h.scalar_tensor_tensor(out=Pnxt[:, pc(p)], in0=Pcur[:, pc(p)],
                                               scalar=M.betac[:, p:p + 1], in1=M.R[:, pc(p)],
                                               op0=MU, op1=AD)
                    return ins
                p_v1 = P.emit("v", v_p1, inc="sv")
                def v_p2(v, Pcur=Pcur, Pnxt=Pnxt):
                    v.skip()
                    for p in range(4, 8):
                        h = v if p == 4 else v.raw
                        ins = h.scalar_tensor_tensor(out=Pnxt[:, pc(p)], in0=Pcur[:, pc(p)],
                                               scalar=M.betac[:, p:p + 1], in1=M.R[:, pc(p)],
                                               op0=MU, op1=AD)
                    return ins
                p_v = P.emit("v", v_p2, inc="sv")
                pb1 = P.emit("a", lambda a, Pnxt=Pnxt: a.activation(out=M.Pb[:, 0:512],
                                                                    in_=Pnxt[:, 0:512], func=AF.Copy),
                             waits=[("sv", p_v1)], inc="sa")
                self.pb_a = P.emit("a", lambda a, Pnxt=Pnxt: a.activation(out=M.Pb[:, 512:1024],
                                                                          in_=Pnxt[:, 512:1024],
                                                                          func=AF.Copy),
                                   waits=[("sv", p_v)], inc="sa")
                self.pb_a_h0 = pb1
            else:
                self.set_r("Z", "sv", ca_v)

            def v_x(v, Pcur=Pcur, last=last):
                if not last:
                    v.skip()
                for p in range(NPAIR):
                    h = v if p == 0 else v.raw
                    ins = h.scalar_tensor_tensor(out=M.X[:, pc(p)], in0=Pcur[:, pc(p)],
                                           scalar=M.alpha[:, p:p + 1], in1=M.X[:, pc(p)],
                                           op0=MU, op1=AD)
                return ins
            x_v = P.emit("v", v_x, waits=[("sv", ca_v)] if last else (), inc="sv")
            self.x_done = x_v

            dcur, dnxt = dnxt, dcur
            Pcur, Pnxt = Pnxt, Pcur

        if os.environ.get("MESA_DBG_TAP") and c == 0:
            def g_tap(g):
                ins = None
                for i, tsr in enumerate([M.pq, M.dA, M.dB, M.nd, M.alpha, M.c1, M.betac, M.e_all]):
                    ins = g.dma_start(out=self.dbg_d[i * 128:(i + 1) * 128, 0:NPAIR], in_=tsr[:])
                    ins.then_inc(P.sems["do"], 16)
                ins = g.dma_start(out=self.dbg_d[8 * 128:9 * 128, :], in_=M.X[:, 0:128])
                ins.then_inc(P.sems["do"], 16)
                ins = g.dma_start(out=self.dbg_d[9 * 128:10 * 128, :], in_=M.R[:, 0:128])
                ins.then_inc(P.sems["do"], 16)
                ins = g.dma_start(out=self.dbg_d[10 * 128:11 * 128, :], in_=M.Pv[:, 0:128])
                ins.then_inc(P.sems["do"], 16)
                return ins
            P.emit("g", g_tap, waits=[("sv", p_v)])
            P.bump("do", 16 * 11)
        if DBG_STAGE < 5:
            self.hstate_v = self.prog.val("sv")
            return None
        a_xb = P.emit("a", lambda a: a.activation(out=M.Pb[:], in_=M.X[:], func=AF.Copy),
                      waits=[("sv", self.x_done)], inc="sa")
        def t_xe(t):
            for p in range(NPAIR):
                ins = t.matmul(out=M.ps_B[:, pc(p)], lhsT=M.Pb[:, pc(p)], rhs=M.diag_e[:, pc(p)],
                               start=True, stop=True)
            return ins
        tvx = P.emit("t", t_xe, waits=self.w_waits("B") + [("sa", a_xb), ("sa", self.de_a)],
                     inc="st")
        self.set_w("B", tvx)
        self.de_rd = tvx
        def t_tx(t):
            for p in range(NPAIR):
                ins = t.transpose(out=M.ps_Tb[:, pc(p)], in_=M.Pb[:, pc(p)], identity=M.ident_b[:])
            return ins
        tvp = P.emit("t", t_tx, waits=self.w_waits("Tb") + [("sa", a_xb)], inc="st")
        self.set_w("Tb", tvp)
        av_x = P.emit("a", lambda a: a.activation(out=M.Xt[:], in_=M.ps_B[:], func=AF.Copy),
                      waits=self.r_waits("B"), inc="sa")
        self.set_r("B", "sa", av_x)
        av_p = P.emit("a", lambda a: a.activation(out=M.Pd[:], in_=M.ps_Tb[:], func=AF.Copy),
                      waits=self.r_waits("Tb"), inc="sa")
        self.set_r("Tb", "sa", av_p)
        xt_a = av_p

        def t_o1(t):
            for p in range(NPAIR):
                ins = t.matmul(out=M.ps_Z[:, pc(p)], lhsT=M.Hvk_b[:, pc(p)], rhs=M.Xt[:, pc(p)],
                         start=(p % 4 == 0), stop=False)
            return ins
        tv = P.emit("t", t_o1, waits=self.w_waits("Z") + [("sa", xt_a)], inc="st")
        self.set_w("Z", tv)

        for half in range(2):
            ps = range(half * 4, half * 4 + 4)
            def t_s2(t, ps=ps):
                for p in ps:
                    ins = t.matmul(out=M.ps_S[:, (p % 4) * HD:(p % 4 + 1) * HD], lhsT=M.Vd[:, pc(p)],
                             rhs=M.Pd[:, pc(p)], start=True, stop=True)
                return ins
            tv_s = P.emit("t", t_s2, waits=self.w_waits("S") + [("sa", xt_a), ("sa", self.kv_a)], inc="st")
            self.set_w("S", tv_s)
            def v_m2(v, half=half):
                v.skip()
                return v.tensor_tensor(out=M.Mp[:, half * 512:(half + 1) * 512],
                                       in0=M.Dp[:, half * 512:(half + 1) * 512],
                                       in1=M.ps_S[:], op=MU)
            vv_m = P.emit("v", v_m2, waits=self.r_waits("S"), inc="sv")
            self.set_r("S", "sv", vv_m)
            def t_o2(t, ps=ps):
                for p in ps:
                    ins = t.matmul(out=M.ps_Z[:, pc(p)], lhsT=ak[:, pc(p)], rhs=M.Mp[:, pc(p)],
                             start=False, stop=(p % 4 == 3))
                return ins
            tv_o2 = P.emit("t", t_o2, waits=[("sv", vv_m)], inc="st")
        self.set_w("Z", tv_o2)

        av_ = P.emit("a", lambda a: a.activation(out=M.scrb[:], in_=M.ps_Z[:], func=AF.Square),
                     waits=self.r_waits("Z"), inc="sa")
        self.set_r("Z", "sa", av_)

        def t_rms(t):
            for p in range(NPAIR):
                ins = t.matmul(out=M.ps_B[:, pc(p)], lhsT=M.ones_sq_b[:], rhs=M.scrb[:, pc(p)],
                         start=True, stop=True)
            return ins
        tv = P.emit("t", t_rms, waits=self.w_waits("B") + [("sa", av_)], inc="st")
        self.set_w("B", tv)

        def v_rms(v):
            return v.tensor_scalar(out=M.scrf[:], in0=M.ps_B[:], scalar1=1.0 / HD,
                                   scalar2=1e-6, op0=MU, op1=AD)
        vv = P.emit("v", v_rms, waits=self.r_waits("B"), inc="sv")
        self.set_r("B", "sv", vv)
        av_ = P.emit("a", lambda a: a.activation(out=M.scrf[:], in_=M.scrf[:], func=AF.Sqrt),
                     waits=[("sv", vv)], inc="sa")
        vv = P.emit("v", lambda v: v.reciprocal(out=M.scrf[:], in_=M.scrf[:]),
                    waits=[("sa", av_)], inc="sv")

        def v_on(v):
            return v.scalar_tensor_tensor(out=M.scrb[:], in0=M.ps_Z[:], scalar=M.onw_col[:],
                                          in1=M.scrf[:], op0=MU, op1=MU)
        on_v = P.emit("v", v_on, waits=[("sv", vv)] + self.r_waits("Z"), inc="sv")
        self.set_r("Z", "sv", on_v)

        def g_o(g):
            ins = None
            for p in range(NPAIR):
                b, h = p // HPC, p % HPC
                ins = g.dma_start(out=o_dram[h * 128:(h + 1) * 128,
                                             b * b_stride + c * C: b * b_stride + (c + 1) * C],
                                  in_=M.scrb[:, pc(p)])
                ins.then_inc(P.sems["do"], 16)
            return ins
        P.emit("g", g_o, waits=[("sv", on_v)])
        o_done = P.bump("do", 16 * NPAIR)

        if DBG_STAGE < 6:
            self.hstate_v = self.prog.val("sv")
            return None
        def v_kt(v):
            for p in range(NPAIR):
                h = v if p == 0 else v.raw
                ins = h.tensor_scalar(out=M.Ktil[:, pc(p)], in0=ak[:, pc(p)],
                                scalar1=M.w_all[:, p:p + 1], scalar2=None, op0=MU)
            return ins
        kt_v = P.emit("v", v_kt, waits=[("sv", w_v), ("sv", on_v)], inc="sv")

        def t_kk(t):
            for p in range(NPAIR):
                ins = t.matmul(out=M.ps_Z[:, pc(p)], lhsT=M.Ktil[:, pc(p)], rhs=ak[:, pc(p)],
                         start=True, stop=True)
            return ins
        tv = P.emit("t", t_kk, waits=self.w_waits("Z") + [("sv", kt_v)], inc="st")
        self.set_w("Z", tv)

        def v_hkk(v):
            for p in range(NPAIR):
                h = v if p == 0 else v.raw
                ins = h.scalar_tensor_tensor(out=M.Hkk_f[:, pc(p)], in0=M.Hkk_f[:, pc(p)],
                                       scalar=M.dec_bc[:, p:p + 1], in1=M.ps_Z[:, pc(p)],
                                       op0=MU, op1=AD)
            return ins
        vv = P.emit("v", v_hkk, waits=self.r_waits("Z"), inc="sv")
        self.set_r("Z", "sv", vv)
        hkkb_a = P.emit("a", lambda a: a.activation(out=M.Hkk_b[:], in_=M.Hkk_f[:], func=AF.Copy),
                        waits=[("sv", vv)], inc="sa")

        def v_vt(v):
            for p in range(NPAIR):
                h = v if p == 0 else v.raw
                ins = h.tensor_scalar(out=M.Ktil[:, pc(p)], in0=av[:, pc(p)],
                                scalar1=M.w_all[:, p:p + 1], scalar2=None, op0=MU)
            return ins
        vt_v = P.emit("v", v_vt, waits=[("sv", vv)], inc="sv")

        def t_vk(t):
            for p in range(NPAIR):
                ins = t.matmul(out=M.ps_B[:, pc(p)], lhsT=M.Ktil[:, pc(p)], rhs=ak[:, pc(p)],
                         start=True, stop=True)
            return ins
        tv = P.emit("t", t_vk, waits=self.w_waits("B") + [("sv", vt_v)], inc="st")
        self.set_w("B", tv)

        def v_hvk(v):
            for p in range(NPAIR):
                h = v if p == 0 else v.raw
                ins = h.scalar_tensor_tensor(out=M.Hvk_f[:, pc(p)], in0=M.Hvk_f[:, pc(p)],
                                       scalar=M.dec_bc[:, p:p + 1], in1=M.ps_B[:, pc(p)],
                                       op0=MU, op1=AD)
            return ins
        vv2 = P.emit("v", v_hvk, waits=self.r_waits("B"), inc="sv")
        self.set_r("B", "sv", vv2)
        hvkb_a = P.emit("a", lambda a: a.activation(out=M.Hvk_b[:], in_=M.Hvk_f[:], func=AF.Copy),
                        waits=[("sv", vv2)], inc="sa")

        def v_wk(v):
            for p in range(NPAIR):
                h = v if p == 0 else v.raw
                ins = h.tensor_scalar(out=M.scrb[:, pc(p)], in0=M.Ksq[:, pc(p)],
                                scalar1=M.w_all[:, p:p + 1], scalar2=None, op0=MU)
            return ins
        wk_v = P.emit("v", v_wk, waits=[("sv", vv2), ("do", o_done)], inc="sv")

        def t_hd(t):
            for p in range(NPAIR):
                ins = t.matmul(out=M.ps_Z[0:1, pc(p)], lhsT=M.ones_col_b[:], rhs=M.scrb[:, pc(p)],
                         start=True, stop=True)
            return ins
        tv = P.emit("t", t_hd, waits=self.w_waits("Z") + [("sv", wk_v)], inc="st")
        self.set_w("Z", tv)

        def v_hd(v):
            for p in range(NPAIR):
                h = v if p == 0 else v.raw
                ins = h.scalar_tensor_tensor(out=M.Hd_row[0:1, pc(p)], in0=M.Hd_row[0:1, pc(p)],
                                       scalar=M.decs_row[0:1, p:p + 1], in1=M.ps_Z[0:1, pc(p)],
                                       op0=MU, op1=AD)
            return ins
        vv3 = P.emit("v", v_hd, waits=self.r_waits("Z"), inc="sv")
        self.set_r("Z", "sv", vv3)

        self.hstate_v = vv3
        self.hstate_a = max(hkkb_a, hvkb_a)
        return o_done



def build_mesa_test(T):
    nc = bass.Bass("TRN2", target_bir_lowering=False, debug=False, detect_race_conditions=False)
    nchunk = T // C
    qn_d = nc.dram_tensor("qn", [4 * T, 2 * HD], F32, kind="ExternalInput").ap()
    kn_d = nc.dram_tensor("kn", [4 * T, 2 * HD], BF16, kind="ExternalInput").ap()
    vn_d = nc.dram_tensor("vn", [4 * T, 2 * HD], BF16, kind="ExternalInput").ap()
    g_d = nc.dram_tensor("g", [4 * T, HPC], F32, kind="ExternalInput").ap()
    bt_d = nc.dram_tensor("bt", [4 * T, HPC], F32, kind="ExternalInput").ap()
    lamp_d = nc.dram_tensor("lamp", [1, HPC * 128], F32, kind="ExternalInput").ap()
    onw_d = nc.dram_tensor("onw", [1, 128], F32, kind="ExternalInput").ap()
    o_d = nc.dram_tensor("o", [HPC * 128, 4 * T], BF16, kind="ExternalOutput").ap()
    dbg_d = nc.dram_tensor("dbg", [11 * 128, 128], F32, kind="ExternalOutput").ap()

    prog = Prog()
    with ExitStack() as ctx:
        M = Mesa(nc, ctx, prog, T)
        M.dbg_d = dbg_d
        sem_names = ["st", "sv", "sa", "sg", "dlc", "dl0", "dl1", "do", "dsx", "dbb", "dst", "dr2", "ddy", "do0", "do1"]
        sems = {n: ctx.enter_context(nc.semaphore(n)) for n in sem_names}
        M.emit_consts(lamp_d, onw_d)

        for cidx in range(min(nchunk, DBG_NCHUNK)):
            z = cidx % 2
            dl_sem = f"dl{z}"
            def s_ld(s, z=z, cidx=cidx, dl_sem=dl_sem):
                ins = None
                for b in range(4):
                    r0 = b * T + cidx * C
                    for (dst, srcd) in ((M.aq[z], qn_d), (M.ak[z], kn_d), (M.av[z], vn_d)):
                        ins = s.dma_start(out=dst[:, b * 256:(b + 1) * 256],
                                          in_=srcd[r0:r0 + C, :])
                        ins.then_inc(prog.sems[dl_sem], 16)
                    ins = s.dma_start(out=M.ag[z][:, b * HPC:(b + 1) * HPC], in_=g_d[r0:r0 + C, :])
                    ins.then_inc(prog.sems[dl_sem], 16)
                    ins = s.dma_start(out=M.ab[z][:, b * HPC:(b + 1) * HPC], in_=bt_d[r0:r0 + C, :])
                    ins.then_inc(prog.sems[dl_sem], 16)
                return ins
            w = [("sv", M.hstate_v)] if cidx >= 2 else []
            prog.emit("s", s_ld, waits=w)
            dl_val = prog.bump(dl_sem, 16 * 20)
            M.emit_chunk(z, cidx, o_d, dl_sem, dl_val, T)

        prog.emit("g", lambda g: g.wait_ge(sems["do"], prog.val("do")) or g.nop())

        block = ctx.enter_context(nc.Block())
        prog.realize(block, sems)
    return nc


def mesa_test_run(qn, kn, vn, g, bt, lamp, onw):
    T = qn.shape[0] // 4
    nc = build_mesa_test(T)
    import ml_dtypes
    in_map = dict(qn=qn.astype(np.float32), kn=kn.astype(ml_dtypes.bfloat16),
                  vn=vn.astype(ml_dtypes.bfloat16), g=g.astype(np.float32),
                  bt=bt.astype(np.float32), lamp=lamp.astype(np.float32),
                  onw=onw.astype(np.float32))
    res = run_bass_kernel_spmd(nc, [in_map], core_ids=[0])
    return np.asarray(res.results[0]["o"].astype(np.float32))



ROWS = 16384
T_FULL = 4096
RPC = 2048


def emit_p12(nc, prog, M, ctx, xT, xTr, wcat, babb, st_in, st_all,
             q_d, k_d, v_d, g_d, bt_d):
    P = prog
    S = lambda nm, sh, dt=F32: ctx.enter_context(nc.sbuf_tensor(nm, sh, dt))
    stg = S("p2_stg", [128, 772])
    wcb = S("p2_wcb", [128, 16 * 772], BF16)
    sbx = [S(f"p2_sbx{z}", [128, 16 * 128], F32) for z in range(2)]
    sbxb = [S(f"p2_sbxb{z}", [128, 16 * 128], BF16) for z in range(2)]
    sbig = S("p2_sbig", [128, 512])
    sbig2 = S("p2_sbig2", [128, 512])
    negmu_stg = S("p2_negmu", [1, RPC])
    rs_stg = S("p2_rsstg", [1, RPC])
    rs_cols = S("p2_rscols", [128, 128])
    csw = S("p2_csw", [1, 772])
    qsil = [S(f"p2_qsil{z}", [128, 512]) for z in range(2)]
    sq = [S(f"p2_sq{z}", [128, 512]) for z in range(2)]
    rn4 = [S(f"p2_rn4{z}", [128, 4]) for z in range(2)]
    qout = [S(f"p2_qout{z}", [128, 256]) for z in range(2)]
    kout = [S(f"p2_kout{z}", [128, 256], BF16) for z in range(2)]
    vout = [S(f"p2_vout{z}", [128, 256], BF16) for z in range(2)]
    abt = [S(f"p2_abt{z}", [128, 4]) for z in range(2)]
    gout = [S(f"p2_gout{z}", [128, HPC]) for z in range(2)]
    btout = [S(f"p2_btout{z}", [128, HPC]) for z in range(2)]
    bias_bc = S("p2_biasbc", [128, 4])
    babb_r = S("p2_babbr", [1, 4])
    strow = S("p2_strow", [1, 2 * RPC])

    consts_v = M.consts_done

    for dt in range(16):
        def s_w(s, dt=dt):
            ins = s.dma_start(out=stg[:], in_=wcat[dt * 128:(dt + 1) * 128, :])
            ins.then_inc(P.sems["dlc"], 16)
            return ins
        P.emit("s", s_w, waits=[("sv", P.val("sv")), ("st", P.val("st"))])
        wv = P.bump("dlc", 16)
        vv = P.emit("v", lambda v, dt=dt: v.tensor_copy(out=wcb[:, dt * 772:(dt + 1) * 772], in_=stg[:]),
                    waits=[("dlc", wv)], inc="sv")
        def t_cs(t, dt=dt):
            t.matmul(out=M.ps_B[0:1, 0:512], lhsT=M.ones_col_f[:], rhs=stg[:, 0:512],
                     start=(dt == 0), stop=(dt == 15))
            return t.matmul(out=M.ps_B[0:1, 512:772], lhsT=M.ones_col_f[:], rhs=stg[:, 512:772],
                            start=(dt == 0), stop=(dt == 15))
        tv = P.emit("t", t_cs, waits=[("dlc", wv), ("sv", consts_v)] + (M.w_waits("B") if dt == 0 else []), inc="st")
    M.set_w("B", tv)
    vv = P.emit("v", lambda v: v.tensor_copy(out=csw[:], in_=M.ps_B[0:1, 0:772]),
                waits=M.r_waits("B"), inc="sv")
    M.set_r("B", "sv", vv)

    def s_bb(s):
        ins = s.dma_start(out=babb_r[:], in_=babb)
        ins.then_inc(P.sems["dbb"], 16)
        return ins
    P.emit("s", s_bb)
    bbv = P.bump("dbb", 16)
    tv = P.emit("t", lambda t: t.matmul(out=M.ps_T[:, 0:4], lhsT=M.ones_row_f[:], rhs=babb_r[:],
                                        start=True, stop=True),
                waits=[("dbb", bbv), ("sv", consts_v)] + M.w_waits("T"), inc="st")
    M.set_w("T", tv)
    vv = P.emit("v", lambda v: v.tensor_copy(out=bias_bc[:], in_=M.ps_T[:, 0:4]),
                waits=M.r_waits("T"), inc="sv")
    M.set_r("T", "sv", vv)

    for rb in range(4):
        for dt in range(16):
            def s_x(s, rb=rb, dt=dt):
                ins = s.dma_start(out=sbig[:], in_=xTr[dt * 128:(dt + 1) * 128,
                                                      rb * 512:(rb + 1) * 512])
                ins.then_inc(P.sems["dsx"], 16)
                return ins
            P.emit("s", s_x, waits=[("sa", P.val("sa")), ("st", P.val("st")),
                                    ("sv", P.val("sv"))])
            xv = P.bump("dsx", 16)
            av = P.emit("a", lambda a: a.activation(out=sbig2[:], in_=sbig[:], func=AF.Square),
                        waits=[("dsx", xv)], inc="sa")
            def t_st(t, rb=rb, dt=dt):
                t.matmul(out=M.ps_S[0:1, 0:512], lhsT=M.ones_col_f[:], rhs=sbig[:],
                         start=(dt == 0), stop=(dt == 15))
                return t.matmul(out=M.ps_B[0:1, 0:512], lhsT=M.ones_col_f[:], rhs=sbig2[:],
                                start=(dt == 0), stop=(dt == 15))
            tv = P.emit("t", t_st, waits=[("dsx", xv), ("sa", av)]
                        + (M.w_waits("S", "B") if dt == 0 else []), inc="st")
        M.set_w("S", tv)
        M.set_w("B", tv)

        def v_fin(v, rb=rb):
            v.tensor_scalar(out=strow[0:1, rb * 512:(rb + 1) * 512], in0=M.ps_S[0:1, 0:512],
                            scalar1=1.0 / HID, scalar2=None, op0=MU)
            v.tensor_scalar(out=sbig[0:1, 0:512], in0=M.ps_B[0:1, 0:512],
                            scalar1=1.0 / HID, scalar2=None, op0=MU)
            v.tensor_tensor(out=sbig2[0:1, 0:512], in0=strow[0:1, rb * 512:(rb + 1) * 512],
                            in1=strow[0:1, rb * 512:(rb + 1) * 512], op=MU)
            v.tensor_tensor(out=sbig[0:1, 0:512], in0=sbig[0:1, 0:512], in1=sbig2[0:1, 0:512], op=SU)
            return v.tensor_scalar(out=sbig[0:1, 0:512], in0=sbig[0:1, 0:512],
                                   scalar1=1e-5, scalar2=None, op0=AD)
        vv = P.emit("v", v_fin, waits=M.r_waits("S", "B"), inc="sv")
        M.set_r("S", "sv", vv)
        M.set_r("B", "sv", vv)
        av = P.emit("a", lambda a, rb=rb: a.activation(out=sbig[0:1, 0:512], in_=sbig[0:1, 0:512],
                                                       func=AF.Sqrt),
                    waits=[("sv", vv)], inc="sa")
        vv = P.emit("v", lambda v, rb=rb: v.reciprocal(out=strow[0:1, RPC + rb * 512:RPC + (rb + 1) * 512],
                                                       in_=sbig[0:1, 0:512]),
                    waits=[("sa", av)], inc="sv")
    stats_v = vv

    def g_st(g):
        ins = g.dma_start(out=st_in[:, :], in_=strow[0:1, :])
        ins.then_inc(P.sems["do"], 16)
        return ins
    P.emit("g", g_st, waits=[("sv", stats_v)])
    sv1 = P.bump("do", 16)
    def g_ag(g):
        return g.collective_compute("AllGather", mybir.AluOpType.bypass,
                                    replica_groups=[list(range(NCORES))],
                                    ins=[st_in[:, :].opt()], outs=[st_all[:, :].opt()])
    agv = P.emit("g", g_ag, waits=[("do", sv1)], inc="cc")
    tv = None
    for cb in range(8):
        def g_rd(g, cb=cb):
            ins = g.dma_start(out=rs_stg[:], in_=st_all[2 * cb + 1:2 * cb + 2, :])
            ins.then_inc(P.sems["dst"], 16)
            return ins
        P.emit("g", g_rd, waits=[("cc", agv), ("st", tv or 0)])
        stv = P.bump("dst", 16)
        def t_rsc(t, cb=cb):
            ins = None
            for j in range(16):
                ins = t.transpose(out=M.ps_T[:, cb * 16 + j:cb * 16 + j + 1],
                                  in_=rs_stg[0:1, j * 128:(j + 1) * 128],
                                  identity=M.ones_row_f[0:1, 0:1])
            return ins
        tv = P.emit("t", t_rsc, waits=[("dst", stv)] + (M.w_waits("T") if cb == 0 else []), inc="st")
    M.set_w("T", tv)
    vv = P.emit("v", lambda v: v.tensor_copy(out=rs_cols[:], in_=M.ps_T[:, 0:128]),
                waits=M.r_waits("T"), inc="sv")
    M.set_r("T", "sv", vv)
    rsc_v = vv

    wcast_v = P.val("sv")
    negmu_v = 0
    banks = ["Z", "B"]
    psq = [M.ps_Z, M.ps_B]
    cast_prev = [0, 0]
    xv_pre = [0, 0]
    tv_prev = [0, 0]
    sil_prev = [0, 0]
    ag_prev = [0, 0]
    g2_prev = [0, 0]
    do_prev = [0, 0]
    for rt in range(128):
        z = rt % 2
        dl_sem = f"dl{z}"
        bank = banks[z]
        ps = psq[z]
        if rt % 16 == 0:
            cb = rt // 16
            def g_mu(g, cb=cb):
                ins = g.dma_start(out=rs_stg[:], in_=st_all[2 * cb:2 * cb + 1, :])
                ins.then_inc(P.sems["dst"], 16)
                return ins
            P.emit("g", g_mu, waits=[("cc", agv), ("st", P.val("st"))])
            muv = P.bump("dst", 16)
            negmu_v = P.emit("v", lambda v: v.tensor_scalar(out=negmu_stg[:], in0=rs_stg[:],
                                                            scalar1=-1.0, scalar2=None, op0=MU),
                             waits=[("dst", muv), ("sv", rsc_v)], inc="sv")
        def s_xt(s, rt=rt, z=z, dl_sem=dl_sem):
            ins = None
            for dt in range(16):
                ins = s.dma_start(out=sbx[z][:, dt * 128:(dt + 1) * 128],
                                  in_=xT[dt * 128:(dt + 1) * 128, rt * 128:(rt + 1) * 128])
                ins.then_inc(P.sems[dl_sem], 16)
            return ins
        P.emit("s", s_xt, waits=[("sv", cast_prev[z])])
        xv_pre[z] = P.bump(dl_sem, 16 * 16)
        cast_prev[z] = P.emit("v", lambda v, z=z: v.tensor_copy(out=sbxb[z][:], in_=sbx[z][:]),
                              waits=[(dl_sem, xv_pre[z]), ("sv", wcast_v), ("st", tv_prev[z])],
                              inc="sv")
        def t_pj(t, rt=rt, z=z, ps=ps):
            for dt in range(16):
                t.matmul(out=ps[:, 0:512], lhsT=sbxb[z][:, dt * 128:(dt + 1) * 128],
                         rhs=wcb[:, dt * 772:dt * 772 + 512], start=(dt == 0), stop=False)
                t.matmul(out=ps[:, 512:772], lhsT=sbxb[z][:, dt * 128:(dt + 1) * 128],
                         rhs=wcb[:, dt * 772 + 512:(dt + 1) * 772], start=(dt == 0), stop=False)
            rtl = rt % 16
            t.matmul(out=ps[:, 0:512], lhsT=negmu_stg[0:1, rtl * 128:(rtl + 1) * 128],
                     rhs=csw[0:1, 0:512], start=False, stop=True)
            return t.matmul(out=ps[:, 512:772], lhsT=negmu_stg[0:1, rtl * 128:(rtl + 1) * 128],
                            rhs=csw[0:1, 512:772], start=False, stop=True)
        tv = P.emit("t", t_pj, waits=[("sv", cast_prev[z]), ("sv", negmu_v), ("sv", rsc_v),
                                      ("sa", sil_prev[z])] + M.w_waits(bank),
                    inc="st")
        M.set_w(bank, tv)
        tv_prev[z] = tv

        def a_sil(a, rt=rt, z=z, ps=ps):
            a.activation(out=qsil[z][:], in_=ps[:, 0:512], func=AF.Silu,
                         scale=rs_cols[:, rt:rt + 1])
            a.activation(out=vout[z][:], in_=ps[:, 512:768], func=AF.Silu,
                         scale=rs_cols[:, rt:rt + 1])
            return a.activation(out=sq[z][:], in_=qsil[z][:], func=AF.Square)
        av = P.emit("a", a_sil, waits=M.r_waits(bank) + [(f"do{z}", do_prev[z]), ("sv", g2_prev[z])],
                    inc="sa")
        sil_prev[z] = av
        def v_ab(v, rt=rt, z=z, ps=ps):
            return v.tensor_scalar(out=abt[z][:], in0=ps[:, 768:772],
                                   scalar1=rs_cols[:, rt:rt + 1], scalar2=None, op0=MU)
        abv = P.emit("v", v_ab, waits=M.r_waits(bank) + [("sa", ag_prev[z]), ("sv", g2_prev[z])],
                     inc="sv")
        M.set_r(bank, "sv", abv)

        def v_l2(v, z=z):
            return v.tensor_reduce(out=rn4[z][:], in_=sq[z][:].rearrange("p (s d) -> p s d", s=4),
                                   axis=AX, op=AD)
        lv = P.emit("v", v_l2, waits=[("sa", av)], inc="sv")
        av2 = P.emit("a", lambda a, z=z: a.activation(out=rn4[z][:], in_=rn4[z][:], func=AF.Sqrt),
                     waits=[("sv", lv)], inc="sa")
        def v_rn(v, z=z):
            v.tensor_scalar(out=rn4[z][:], in0=rn4[z][:], scalar1=1e-12, scalar2=None, op0=MX)
            return v.reciprocal(out=rn4[z][:], in_=rn4[z][:])
        rv = P.emit("v", v_rn, waits=[("sa", av2)], inc="sv")
        def v_sc(v, z=z):
            for sgi in range(2):
                v.tensor_scalar(out=qout[z][:, sgi * 128:(sgi + 1) * 128],
                                in0=qsil[z][:, sgi * 128:(sgi + 1) * 128],
                                scalar1=rn4[z][:, sgi:sgi + 1], scalar2=None, op0=MU)
            for sgi in range(2):
                v.tensor_scalar(out=kout[z][:, sgi * 128:(sgi + 1) * 128],
                                in0=qsil[z][:, 256 + sgi * 128:256 + (sgi + 1) * 128],
                                scalar1=rn4[z][:, 2 + sgi:3 + sgi], scalar2=None, op0=MU)
            v.tensor_tensor(out=abt[z][:], in0=abt[z][:], in1=bias_bc[:], op=AD)
            v.tensor_scalar(out=gout[z][:], in0=abt[z][:, 0:2], scalar1=0.0, scalar2=None, op0=MI)
            v.tensor_scalar(out=sq[z][0:128, 0:2], in0=abt[z][:, 0:2], scalar1=-1.0, scalar2=None, op0=MU)
            return v.tensor_tensor(out=sq[z][0:128, 2:4], in0=abt[z][:, 0:2], in1=sq[z][0:128, 0:2], op=MI)
        scv = P.emit("v", v_sc, waits=[("sv", rv), ("sa", ag_prev[z]), (f"do{z}", do_prev[z])], inc="sv")
        def a_g(a, z=z):
            a.activation(out=sq[z][0:128, 4:6], in_=sq[z][0:128, 2:4], func=AF.Exp)
            a.activation(out=sq[z][0:128, 4:6], in_=sq[z][0:128, 4:6], func=AF.Ln,
                         bias=M.ones_col_f[:])
            return a.activation(out=btout[z][:], in_=abt[z][:, 2:4], func=AF.Sigmoid)
        agv2 = P.emit("a", a_g, waits=[("sv", scv)], inc="sa")
        ag_prev[z] = agv2
        def v_g2(v, z=z):
            return v.tensor_tensor(out=gout[z][:], in0=gout[z][:], in1=sq[z][0:128, 4:6], op=SU)
        gv2 = P.emit("v", v_g2, waits=[("sa", agv2)], inc="sv")
        g2_prev[z] = gv2

        def g_out(g, rt=rt, z=z):
            ins = None
            for (dst, srct) in ((q_d, qout[z]), (k_d, kout[z]), (v_d, vout[z]),
                                (g_d, gout[z]), (bt_d, btout[z])):
                ins = g.dma_start(out=dst[rt * 128:(rt + 1) * 128, :], in_=srct[:])
                ins.then_inc(P.sems[f"do{z}"], 16)
            return ins
        P.emit("g", g_out, waits=[("sv", gv2), ("sa", agv2)])
        P.bump(f"do{z}", 16 * 5)
        do_prev[z] = P.val(f"do{z}")
    return (P.val("do0"), P.val("do1"))


def emit_p4a(nc, prog, ctx4, xTr, wo, o_all, x1_d, x1b_d, ln2st_d, bw):
    P = prog
    S = lambda nm, sh, dt=F32: ctx4.enter_context(nc.sbuf_tensor(nm, sh, dt))
    P_ = lambda nm, sh, dt=F32: ctx4.enter_context(nc.psum_tensor(nm, sh, dt))
    wob = S("p4_wob", [128, 16 * 2048], BF16)
    stg = S("p4_stg", [128, 2048])
    oT = S("p4_oT", [128, 16 * 512], BF16)
    xa = S("p4_xa", [128, 16 * 512])
    atn = S("p4_atn", [128, 16 * 512], BF16)
    sqt = S("p4_sqt", [128, 512], BF16)
    mu_bc = S("p4_mubc", [128, 512])
    rs_bc = S("p4_rsbc", [128, 512])
    srow = S("p4_srow", [1, 1024])
    t1 = S("p4_t1", [128, 512])
    x1t = S("p4_x1t", [128, 512])
    x1bt = S("p4_x1bt", [128, 512], BF16)
    ones_col_f = S("p4_onescf", [128, 1])
    ones_col_b = S("p4_onescb", [128, 1], BF16)
    ones_row_f = S("p4_onesrf", [1, 128])
    ps1 = P_("p4_ps1", [128, 512])
    ps2 = P_("p4_ps2", [128, 512])
    pss = P_("p4_pss", [1, 2048])
    rd = {"1": ("sv", 0), "2": ("sv", 0), "s": ("sv", 0)}
    wr = {"1": 0, "2": 0, "s": 0}
    bar = bw

    gv = P.emit("g", lambda g: (g.memset(ones_col_f[:], 1.0), g.memset(ones_col_b[:], 1.0),
                                g.memset(ones_row_f[:], 1.0))[-1], waits=bar, inc="sg")

    cast_v = 0
    for dt in range(16):
        def s_w(s, dt=dt):
            ins = s.dma_start(out=stg[:], in_=wo[dt * 128:(dt + 1) * 128, :])
            ins.then_inc(P.sems["dlc"], 16)
            return ins
        P.emit("s", s_w, waits=bar + [("sv", cast_v)])
        wv = P.bump("dlc", 16)
        cast_v = P.emit("v", lambda v, dt=dt: v.tensor_copy(out=wob[:, dt * 2048:(dt + 1) * 2048],
                                                            in_=stg[:]),
                        waits=bar + [("dlc", wv)], inc="sv")

    ln2_parts = []
    ep_v = 0
    prev_l2do = 0
    for r in range(4):
        def s_o(g, r=r):
            pid = g.partition_id()
            src3 = o_all.rearrange("(dt p) r -> p dt r", p=128)[:, :, bass.ds(pid * RPC + r * 512, 512)]
            ins = g.dma_start(out=oT[:].rearrange("p (dt c) -> p dt c", dt=16), in_=src3)
            ins.then_inc(P.sems["ddy"], 16)
            return ins
        P.emit("g", s_o, waits=bar + [("sv", ep_v)])
        ov = P.bump("ddy", 16)
        def s_xa(s, r=r):
            ins = None
            for dt in range(16):
                ins = s.dma_start(out=xa[:, dt * 512:(dt + 1) * 512],
                                  in_=xTr[dt * 128:(dt + 1) * 128, r * 512:(r + 1) * 512])
                ins.then_inc(P.sems["dl1"], 16)
            return ins
        P.emit("s", s_xa, waits=bar + [("sv", ep_v)])
        xv = P.bump("dl1", 16 * 16)

        st_tv = None
        for cot in range(16):
            def t_wo2(t, cot=cot):
                ins = None
                for dt in range(16):
                    ins = t.matmul(out=ps1[:], lhsT=wob[:, dt * 2048 + cot * 128:dt * 2048 + (cot + 1) * 128],
                                   rhs=oT[:, dt * 512:(dt + 1) * 512], start=(dt == 0), stop=(dt == 15))
                return ins
            tv = P.emit("t", t_wo2, waits=[("ddy", ov), ("sv", cast_v)] + [rd["1"]], inc="st")
            wr["1"] = tv
            vv = P.emit("v", lambda v, cot=cot: v.tensor_copy(out=atn[:, cot * 512:(cot + 1) * 512],
                                                              in_=ps1[:]),
                        waits=[("st", wr["1"])], inc="sv")
            rd["1"] = ("sv", vv)
            av = P.emit("a", lambda a, cot=cot: a.activation(out=sqt[:], in_=atn[:, cot * 512:(cot + 1) * 512],
                                                             func=AF.Square),
                        waits=[("sv", vv), ("sa", P.val("sa")), ("st", st_tv or wr["s"])], inc="sa")
            def t_st(t, cot=cot):
                t.matmul(out=pss[0:1, 0:512], lhsT=ones_col_b[:], rhs=atn[:, cot * 512:(cot + 1) * 512],
                         start=(cot == 0), stop=(cot == 15))
                return t.matmul(out=pss[0:1, 512:1024], lhsT=ones_col_b[:], rhs=sqt[:],
                                start=(cot == 0), stop=(cot == 15))
            st_tv = P.emit("t", t_st, waits=[("sv", vv), ("sa", av)] + ([rd["s"]] if cot == 0 else []),
                           inc="st")
        wr["s"] = st_tv

        def v_fin(v):
            v.tensor_scalar(out=srow[0:1, 0:512], in0=pss[0:1, 0:512], scalar1=1.0 / HID, scalar2=None, op0=MU)
            v.tensor_scalar(out=srow[0:1, 512:1024], in0=pss[0:1, 512:1024], scalar1=1.0 / HID, scalar2=None, op0=MU)
            v.tensor_tensor(out=t1[0:1, :], in0=srow[0:1, 0:512], in1=srow[0:1, 0:512], op=MU)
            v.tensor_tensor(out=srow[0:1, 512:1024], in0=srow[0:1, 512:1024], in1=t1[0:1, :], op=SU)
            return v.tensor_scalar(out=srow[0:1, 512:1024], in0=srow[0:1, 512:1024], scalar1=1e-5, scalar2=None, op0=AD)
        vv = P.emit("v", v_fin, waits=[("st", wr["s"]), ("do", prev_l2do)], inc="sv")
        rd["s"] = ("sv", vv)
        av = P.emit("a", lambda a: a.activation(out=srow[0:1, 512:1024], in_=srow[0:1, 512:1024], func=AF.Sqrt),
                    waits=[("sv", vv)], inc="sa")
        vv = P.emit("v", lambda v: v.reciprocal(out=srow[0:1, 512:1024], in_=srow[0:1, 512:1024]),
                    waits=[("sa", av)], inc="sv")
        def t_bc(t):
            t.matmul(out=ps1[:], lhsT=ones_row_f[:], rhs=srow[0:1, 0:512], start=True, stop=True)
            return t.matmul(out=ps2[:], lhsT=ones_row_f[:], rhs=srow[0:1, 512:1024], start=True, stop=True)
        tv = P.emit("t", t_bc, waits=[("sv", vv)] + [rd["1"], rd["2"]], inc="st")
        wr["1"] = wr["2"] = tv
        def v_bc(v):
            v.tensor_copy(out=mu_bc[:], in_=ps1[:])
            return v.tensor_copy(out=rs_bc[:], in_=ps2[:])
        vv = P.emit("v", v_bc, waits=[("st", tv)], inc="sv")
        rd["1"] = rd["2"] = ("sv", vv)
        bc_v = vv

        ln2_tv = None
        for cot in range(16):
            def v_x1(v, cot=cot):
                v.tensor_tensor(out=t1[:], in0=atn[:, cot * 512:(cot + 1) * 512], in1=mu_bc[:], op=SU)
                v.tensor_tensor(out=t1[:], in0=t1[:], in1=rs_bc[:], op=MU)
                v.tensor_tensor(out=x1t[:], in0=t1[:], in1=xa[:, cot * 512:(cot + 1) * 512], op=AD)
                return v.tensor_copy(out=x1bt[:], in_=x1t[:])
            vv = P.emit("v", v_x1, waits=[("sv", bc_v), ("dl1", xv), ("do", P.val("do"))], inc="sv")
            av = P.emit("a", lambda a: a.activation(out=sqt[:], in_=x1bt[:], func=AF.Square),
                        waits=[("sv", vv), ("st", ln2_tv or st_tv)], inc="sa")
            def t_l2(t, cot=cot):
                t.matmul(out=pss[0:1, 1024:1536], lhsT=ones_col_b[:], rhs=x1bt[:],
                         start=(cot == 0), stop=(cot == 15))
                return t.matmul(out=pss[0:1, 1536:2048], lhsT=ones_col_b[:], rhs=sqt[:],
                                start=(cot == 0), stop=(cot == 15))
            ln2_tv = P.emit("t", t_l2, waits=[("sv", vv), ("sa", av)], inc="st")
            def g_x1(g, cot=cot, r=r):
                ins = g.dma_start(out=x1_d[cot * 128:(cot + 1) * 128, r * 512:(r + 1) * 512], in_=x1t[:])
                ins.then_inc(P.sems["do"], 16)
                ins = g.dma_start(out=x1b_d[cot * 128:(cot + 1) * 128, r * 512:(r + 1) * 512], in_=x1bt[:])
                ins.then_inc(P.sems["do"], 16)
                return ins
            P.emit("g", g_x1, waits=[("st", ln2_tv)])
            P.bump("do", 32)

        def v_f2(v, r=r):
            v.tensor_scalar(out=srow[0:1, 0:512], in0=pss[0:1, 1024:1536], scalar1=1.0 / HID, scalar2=None, op0=MU)
            v.tensor_scalar(out=srow[0:1, 512:1024], in0=pss[0:1, 1536:2048], scalar1=1.0 / HID, scalar2=None, op0=MU)
            v.tensor_tensor(out=t1[0:1, :], in0=srow[0:1, 0:512], in1=srow[0:1, 0:512], op=MU)
            v.tensor_tensor(out=srow[0:1, 512:1024], in0=srow[0:1, 512:1024], in1=t1[0:1, :], op=SU)
            return v.tensor_scalar(out=srow[0:1, 512:1024], in0=srow[0:1, 512:1024], scalar1=1e-5, scalar2=None, op0=AD)
        vv = P.emit("v", v_f2, waits=[("st", ln2_tv)], inc="sv")
        av = P.emit("a", lambda a: a.activation(out=srow[0:1, 512:1024], in_=srow[0:1, 512:1024], func=AF.Sqrt),
                    waits=[("sv", vv)], inc="sa")
        vv = P.emit("v", lambda v: v.reciprocal(out=srow[0:1, 512:1024], in_=srow[0:1, 512:1024]),
                    waits=[("sa", av)], inc="sv")
        def g_l2(g, r=r):
            ins = g.dma_start(out=ln2st_d[0:1, r * 512:(r + 1) * 512], in_=srow[0:1, 0:512])
            ins.then_inc(P.sems["do"], 16)
            ins = g.dma_start(out=ln2st_d[0:1, 2048 + r * 512:2048 + (r + 1) * 512], in_=srow[0:1, 512:1024])
            ins.then_inc(P.sems["do"], 16)
            return ins
        P.emit("g", g_l2, waits=[("sv", vv)])
        P.bump("do", 32)
        prev_l2do = P.val("do")
        ep_v = vv
    return P.val("do")


def emit_p4b(nc, prog, ctx4, x1_d, x1b_d, ln2st_d, w1, b1, w2, b2, ucat, vcatT, y, bw):
    P = prog
    S = lambda nm, sh, dt=F32: ctx4.enter_context(nc.sbuf_tensor(nm, sh, dt))
    P_ = lambda nm, sh, dt=F32: ctx4.enter_context(nc.psum_tensor(nm, sh, dt))
    w1b = S("pb_w1b", [128, 16 * 2048], BF16)
    ucb = S("pb_ucb", [128, 4 * 2048], BF16)
    vcb = S("pb_vcb", [128, 16 * 512], BF16)
    w2b = S("pb_w2b", [128, 16 * 8], BF16)
    stg = S("pb_stg", [128, 2048])
    x1b = S("pb_x1b", [128, 16 * 512], BF16)
    h_bf = S("pb_hbf", [128, 16 * 512], BF16)
    vxt = S("pb_vxt", [128, 4 * 512], BF16)
    ffn_bf = S("pb_ffnbf", [128, 16 * 512], BF16)
    sqt = S("pb_sqt", [128, 512], BF16)
    t1 = S("pb_t1", [128, 512])
    x1t = S("pb_x1t", [128, 512])
    x2t = S("pb_x2t", [128, 512])
    mu_bc = S("pb_mubc", [128, 512])
    rs_bc = S("pb_rsbc", [128, 512])
    rs2_bc = S("pb_rs2bc", [128, 512])
    srow = S("pb_srow", [1, 1024])
    negmu2 = S("pb_negmu2", [1, 4 * 512])
    rs2stg = S("pb_rs2stg", [1, 512])
    csw1 = S("pb_csw1", [1, 2048])
    csv = S("pb_csv", [1, 512])
    b2r = S("pb_b2r", [1, 8])
    b1c = S("pb_b1c", [128, 16])
    E8 = S("pb_E8", [8, 512], BF16)
    alph = S("pb_alph", [128, 8])
    mx1 = S("pb_mx1", [128, 1])
    alT = S("pb_alT", [8, 512], BF16)
    ones_col_f = S("pb_onescf", [128, 1])
    ones_col_b = S("pb_onescb", [128, 1], BF16)
    ones_row_f = S("pb_onesrf", [1, 128])
    identf = S("pb_identf", [128, 128])
    ps1 = P_("pb_ps1", [128, 512])
    ps2 = P_("pb_ps2", [128, 512])
    pss = P_("pb_pss", [1, 2048])
    psa = P_("pb_psa", [128, 8])
    psT4 = P_("pb_psT4", [8, 512])
    rd = {"1": ("sv", 0), "2": ("sv", 0), "s": ("sv", 0), "a": ("sv", 0), "t4": ("sv", 0)}
    wr = {"1": 0, "2": 0, "s": 0, "a": 0, "t4": 0}
    bar = bw

    def g_c(g):
        g.memset(ones_col_f[:], 1.0)
        g.memset(ones_col_b[:], 1.0)
        g.memset(ones_row_f[:], 1.0)
        g.memset(E8[:], 1.0)
        g.memset(identf[:], 0.0)
        g.affine_select(out=identf[:], in_=identf[:], compare_op=mybir.AluOpType.not_equal,
                        fill=1.0, base=0, pattern=[[-1, 128]], channel_multiplier=1)
        g.affine_select(out=E8[:], in_=E8[:], compare_op=mybir.AluOpType.is_ge, fill=0.0,
                        base=0, pattern=[[1, 512]], channel_multiplier=-64)
        return g.affine_select(out=E8[:], in_=E8[:], compare_op=mybir.AluOpType.is_ge, fill=0.0,
                               base=63, pattern=[[-1, 512]], channel_multiplier=64)
    gv = P.emit("g", g_c, waits=bar, inc="sg")

    cast_v = 0
    csw_tv = None
    for dt in range(16):
        def s_w(s, dt=dt):
            ins = s.dma_start(out=stg[:], in_=w1[dt * 128:(dt + 1) * 128, :])
            ins.then_inc(P.sems["dlc"], 16)
            return ins
        P.emit("s", s_w, waits=bar + [("sv", cast_v), ("st", csw_tv or 0)])
        wv = P.bump("dlc", 16)
        cast_v = P.emit("v", lambda v, dt=dt: v.tensor_copy(out=w1b[:, dt * 2048:(dt + 1) * 2048],
                                                            in_=stg[:]),
                        waits=bar + [("dlc", wv)], inc="sv")
        def t_cs2(t, dt=dt):
            ins = None
            for nb in range(4):
                ins = t.matmul(out=pss[0:1, nb * 512:(nb + 1) * 512], lhsT=ones_col_f[:],
                               rhs=stg[:, nb * 512:(nb + 1) * 512],
                               start=(dt == 0), stop=(dt == 15))
            return ins
        csw_tv = P.emit("t", t_cs2, waits=[("dlc", wv)] + ([rd["s"], ("sg", gv)] if dt == 0 else []), inc="st")
    wr["s"] = csw_tv
    def v_cs(v):
        ins = None
        for nb in range(4):
            ins = v.tensor_copy(out=csw1[0:1, nb * 512:(nb + 1) * 512], in_=pss[0:1, nb * 512:(nb + 1) * 512])
        return ins
    vv = P.emit("v", v_cs, waits=[("st", csw_tv)], inc="sv")
    rd["s"] = ("sv", vv)

    csv_tv = csw_tv
    cast2_v = cast_v
    for dt in range(16):
        def s_w(s, dt=dt):
            ins = s.dma_start(out=stg[:, 0:512], in_=vcatT[dt * 128:(dt + 1) * 128, :])
            ins.then_inc(P.sems["dlc"], 16)
            return ins
        P.emit("s", s_w, waits=[("sv", cast2_v), ("st", csv_tv or 0)])
        wv = P.bump("dlc", 16)
        cast2_v = P.emit("v", lambda v, dt=dt: v.tensor_copy(out=vcb[:, dt * 512:(dt + 1) * 512],
                                                             in_=stg[:, 0:512]),
                         waits=[("dlc", wv)], inc="sv")
        csv_tv = P.emit("t", lambda t, dt=dt: t.matmul(out=ps2[0:1, 0:512],
                                                       lhsT=ones_col_f[:], rhs=stg[:, 0:512],
                                                       start=(dt == 0), stop=(dt == 15)),
                        waits=[("dlc", wv)] + ([rd["2"], ("sg", gv)] if dt == 0 else []), inc="st")
    wr["2"] = csv_tv
    vv = P.emit("v", lambda v: v.tensor_copy(out=csv[:], in_=ps2[0:1, 0:512]),
                waits=[("st", csv_tv)], inc="sv")
    rd["2"] = ("sv", vv)

    for kb in range(4):
        def s_w(s, kb=kb):
            ins = s.dma_start(out=stg[:], in_=ucat[kb * 128:(kb + 1) * 128, :])
            ins.then_inc(P.sems["dlc"], 16)
            return ins
        P.emit("s", s_w, waits=[("sv", P.val("sv")), ("st", csv_tv)])
        wv = P.bump("dlc", 16)
        P.emit("v", lambda v, kb=kb: v.tensor_copy(out=ucb[:, kb * 2048:(kb + 1) * 2048], in_=stg[:]),
               waits=[("dlc", wv)], inc="sv")
    def s_w2(s):
        ins = None
        for dt in range(16):
            ins = s.dma_start(out=stg[:, dt * 8:(dt + 1) * 8], in_=w2[dt * 128:(dt + 1) * 128, :])
            ins.then_inc(P.sems["dlc"], 16)
        ins = s.dma_start(out=b2r[:], in_=b2)
        ins.then_inc(P.sems["dlc"], 16)
        ins = s.dma_start(out=b1c[:], in_=b1.rearrange("(dt p) one -> p (dt one)", p=128), allow_slow_non_contiguous=True)
        ins.then_inc(P.sems["dlc"], 16)
        ins = s.dma_start(out=negmu2[:], in_=ln2st_d[0:1, 0:2048])
        ins.then_inc(P.sems["dlc"], 16)
        return ins
    P.emit("s", s_w2, waits=[("sv", P.val("sv")), ("do", P.val("do"))])
    mv_ = P.bump("dlc", 16 * 19)
    w2c_v = P.emit("v", lambda v: v.tensor_copy(out=w2b[:], in_=stg[:, 0:128]),
                   waits=[("dlc", mv_)], inc="sv")
    neg_v = P.emit("v", lambda v: v.tensor_scalar(out=negmu2[:], in0=negmu2[:],
                                                  scalar1=-1.0, scalar2=None, op0=MU),
                   waits=[("dlc", mv_)], inc="sv")

    ep_v = 0
    ldx_prev = 0
    for r in range(4):
        def s_x(s, r=r):
            ins = None
            for dt in range(16):
                ins = s.dma_start(out=x1b[:, dt * 512:(dt + 1) * 512],
                                  in_=x1b_d[dt * 128:(dt + 1) * 128, r * 512:(r + 1) * 512])
                ins.then_inc(P.sems["dl0"], 16)
            return ins
        P.emit("s", s_x, waits=[("sv", ep_v), ("do", P.val("do"))])
        xv = P.bump("dl0", 16 * 16)

        def s_r2(s, r=r):
            ins = s.dma_start(out=rs2stg[:], in_=ln2st_d[0:1, 2048 + r * 512:2048 + (r + 1) * 512])
            ins.then_inc(P.sems["dr2"], 16)
            return ins
        P.emit("s", s_r2, waits=[("st", P.val("st"))])
        r2v = P.bump("dr2", 16)
        def t_r2(t, r=r):
            return t.matmul(out=ps2[:], lhsT=ones_row_f[:], rhs=rs2stg[:],
                            start=True, stop=True)
        tv = P.emit("t", t_r2, waits=[("dr2", r2v)] + [rd["2"]], inc="st")
        wr["2"] = tv
        vv = P.emit("v", lambda v: v.tensor_copy(out=rs2_bc[:], in_=ps2[:]),
                    waits=[("st", tv)], inc="sv")
        rd["2"] = ("sv", vv)
        rs2_v = vv

        prev_hav = 0
        for et in range(16):
            def t_h(t, et=et, r=r):
                ins = None
                for dt in range(16):
                    ins = t.matmul(out=ps1[:], lhsT=w1b[:, dt * 2048 + et * 128:dt * 2048 + (et + 1) * 128],
                                   rhs=x1b[:, dt * 512:(dt + 1) * 512], start=(dt == 0), stop=False)
                return t.matmul(out=ps1[:], lhsT=csw1[0:1, et * 128:(et + 1) * 128],
                                rhs=negmu2[0:1, r * 512:(r + 1) * 512], start=False, stop=True)
            tv = P.emit("t", t_h, waits=[("dl0", xv), ("sv", rs2_v), ("sv", neg_v),
                                         ("sv", w2c_v)] + [rd["1"]], inc="st")
            wr["1"] = tv
            vv = P.emit("v", lambda v: v.tensor_tensor(out=t1[:], in0=ps1[:], in1=rs2_bc[:], op=MU),
                        waits=[("st", tv), ("sa", prev_hav)], inc="sv")
            rd["1"] = ("sv", vv)
            av = P.emit("a", lambda a, et=et: a.activation(out=h_bf[:, et * 512:(et + 1) * 512],
                                                           in_=t1[:], func=AF.Relu,
                                                           bias=b1c[:, et:et + 1]),
                        waits=[("sv", vv)], inc="sa")
            prev_hav = av
        h_v = av

        for rsub in range(4):
            def t_a(t, rsub=rsub):
                ins = None
                for et in range(16):
                    ins = t.matmul(out=psa[:], lhsT=h_bf[:, et * 512 + rsub * 128:et * 512 + (rsub + 1) * 128],
                                   rhs=w2b[:, et * 8:(et + 1) * 8], start=(et == 0), stop=False)
                return t.matmul(out=psa[:], lhsT=ones_row_f[:], rhs=b2r[:], start=False, stop=True)
            tv = P.emit("t", t_a, waits=[("sa", h_v)] + [rd["a"]], inc="st")
            wr["a"] = tv
            def v_sm(v):
                v.tensor_copy(out=alph[:], in_=psa[:])
                v.tensor_reduce(out=mx1[:], in_=alph[:], axis=AX, op=MX)
                return v.tensor_scalar(out=alph[:], in0=alph[:], scalar1=mx1[:, 0:1],
                                       scalar2=None, op0=SU)
            vv = P.emit("v", v_sm, waits=[("st", tv)], inc="sv")
            rd["a"] = ("sv", vv)
            av = P.emit("a", lambda a: a.activation(out=alph[:], in_=alph[:], func=AF.Exp),
                        waits=[("sv", vv)], inc="sa")
            def v_sm2(v):
                v.tensor_reduce(out=mx1[:], in_=alph[:], axis=AX, op=AD)
                v.reciprocal(out=mx1[:], in_=mx1[:])
                return v.tensor_scalar(out=alph[:], in0=alph[:], scalar1=mx1[:, 0:1],
                                       scalar2=None, op0=MU)
            vv = P.emit("v", v_sm2, waits=[("sa", av)], inc="sv")
            def t_tr(t, rsub=rsub):
                return t.transpose(out=psT4[0:8, rsub * 128:(rsub + 1) * 128], in_=alph[:],
                                   identity=identf[:])
            tv = P.emit("t", t_tr, waits=[("sv", vv)] + ([rd["t4"]] if rsub == 0 else []), inc="st")
        wr["t4"] = tv
        vv = P.emit("v", lambda v: v.tensor_copy(out=alT[:], in_=psT4[0:8, :]),
                    waits=[("st", tv)], inc="sv")
        rd["t4"] = ("sv", vv)
        al_v = vv

        for krt in range(4):
            def t_vx(t, krt=krt, r=r):
                ins = None
                for dt in range(16):
                    ins = t.matmul(out=ps1[:], lhsT=vcb[:, dt * 512 + krt * 128:dt * 512 + (krt + 1) * 128],
                                   rhs=x1b[:, dt * 512:(dt + 1) * 512], start=(dt == 0), stop=False)
                return t.matmul(out=ps1[:], lhsT=csv[0:1, krt * 128:(krt + 1) * 128],
                                rhs=negmu2[0:1, r * 512:(r + 1) * 512], start=False, stop=True)
            tv1 = P.emit("t", t_vx, waits=[("sa", h_v)] + [rd["1"]], inc="st")
            wr["1"] = tv1
            def t_ae(t, krt=krt):
                return t.matmul(out=ps2[:], lhsT=E8[0:8, krt * 128:(krt + 1) * 128], rhs=alT[:],
                                start=True, stop=True)
            tv2 = P.emit("t", t_ae, waits=[("sv", al_v)] + [rd["2"]], inc="st")
            wr["2"] = tv2
            def v_ax(v, krt=krt):
                v.tensor_tensor(out=t1[:], in0=ps2[:], in1=rs2_bc[:], op=MU)
                return v.tensor_tensor(out=x1t[:], in0=ps1[:], in1=t1[:], op=MU)
            vv = P.emit("v", v_ax, waits=[("st", tv1), ("st", tv2)], inc="sv")
            rd["1"] = rd["2"] = ("sv", vv)
            vv = P.emit("v", lambda v, krt=krt: v.tensor_copy(out=vxt[:, krt * 512:(krt + 1) * 512],
                                                              in_=x1t[:]),
                        waits=[("sv", vv)], inc="sv")
        vx_v = vv

        st_tv = None
        for dt in range(16):
            def t_f(t, dt=dt):
                ins = None
                for krt in range(4):
                    ins = t.matmul(out=ps1[:], lhsT=ucb[:, krt * 2048 + dt * 128:krt * 2048 + (dt + 1) * 128],
                                   rhs=vxt[:, krt * 512:(krt + 1) * 512], start=(krt == 0), stop=(krt == 3))
                return ins
            tv = P.emit("t", t_f, waits=[("sv", vx_v)] + [rd["1"]], inc="st")
            wr["1"] = tv
            vv = P.emit("v", lambda v, dt=dt: v.tensor_copy(out=ffn_bf[:, dt * 512:(dt + 1) * 512],
                                                            in_=ps1[:]),
                        waits=[("st", tv)], inc="sv")
            rd["1"] = ("sv", vv)
            av = P.emit("a", lambda a, dt=dt: a.activation(out=sqt[:], in_=ffn_bf[:, dt * 512:(dt + 1) * 512],
                                                           func=AF.Square),
                        waits=[("sv", vv), ("st", st_tv or wr["s"])], inc="sa")
            def t_s(t, dt=dt):
                t.matmul(out=pss[0:1, 0:512], lhsT=ones_col_b[:], rhs=ffn_bf[:, dt * 512:(dt + 1) * 512],
                         start=(dt == 0), stop=(dt == 15))
                return t.matmul(out=pss[0:1, 512:1024], lhsT=ones_col_b[:], rhs=sqt[:],
                                start=(dt == 0), stop=(dt == 15))
            st_tv = P.emit("t", t_s, waits=[("sv", vv), ("sa", av)] + ([rd["s"]] if dt == 0 else []),
                           inc="st")
        wr["s"] = st_tv

        def v_fin(v):
            v.tensor_scalar(out=srow[0:1, 0:512], in0=pss[0:1, 0:512], scalar1=1.0 / HID, scalar2=None, op0=MU)
            v.tensor_scalar(out=srow[0:1, 512:1024], in0=pss[0:1, 512:1024], scalar1=1.0 / HID, scalar2=None, op0=MU)
            v.tensor_tensor(out=t1[0:1, :], in0=srow[0:1, 0:512], in1=srow[0:1, 0:512], op=MU)
            v.tensor_tensor(out=srow[0:1, 512:1024], in0=srow[0:1, 512:1024], in1=t1[0:1, :], op=SU)
            return v.tensor_scalar(out=srow[0:1, 512:1024], in0=srow[0:1, 512:1024], scalar1=1e-5, scalar2=None, op0=AD)
        vv = P.emit("v", v_fin, waits=[("st", st_tv)], inc="sv")
        rd["s"] = ("sv", vv)
        av = P.emit("a", lambda a: a.activation(out=srow[0:1, 512:1024], in_=srow[0:1, 512:1024], func=AF.Sqrt),
                    waits=[("sv", vv)], inc="sa")
        vv = P.emit("v", lambda v: v.reciprocal(out=srow[0:1, 512:1024], in_=srow[0:1, 512:1024]),
                    waits=[("sa", av)], inc="sv")
        def t_bc(t):
            t.matmul(out=ps1[:], lhsT=ones_row_f[:], rhs=srow[0:1, 0:512], start=True, stop=True)
            return t.matmul(out=ps2[:], lhsT=ones_row_f[:], rhs=srow[0:1, 512:1024], start=True, stop=True)
        tv = P.emit("t", t_bc, waits=[("sv", vv)] + [rd["1"], rd["2"]], inc="st")
        wr["1"] = wr["2"] = tv
        def v_bc(v):
            v.tensor_copy(out=mu_bc[:], in_=ps1[:])
            return v.tensor_copy(out=rs_bc[:], in_=ps2[:])
        bc_v = P.emit("v", v_bc, waits=[("st", tv)], inc="sv")
        rd["1"] = rd["2"] = ("sv", bc_v)

        for dt in range(16):
            def s_x1(s, dt=dt, r=r):
                ins = s.dma_start(out=x1t[:], in_=x1_d[dt * 128:(dt + 1) * 128, r * 512:(r + 1) * 512])
                ins.then_inc(P.sems["dl1"], 16)
                return ins
            P.emit("s", s_x1, waits=[("sv", P.val("sv"))])
            x1v = P.bump("dl1", 16)
            def v_x2(v, dt=dt):
                v.tensor_tensor(out=t1[:], in0=ffn_bf[:, dt * 512:(dt + 1) * 512], in1=mu_bc[:], op=SU)
                v.tensor_tensor(out=t1[:], in0=t1[:], in1=rs_bc[:], op=MU)
                return v.tensor_tensor(out=x2t[:], in0=t1[:], in1=x1t[:], op=AD)
            vv = P.emit("v", v_x2, waits=[("sv", bc_v), ("dl1", x1v), ("do", P.val("do"))], inc="sv")
            def g_y(g, dt=dt, r=r):
                ins = g.dma_start(out=y[dt * 128:(dt + 1) * 128, r * 512:(r + 1) * 512], in_=x2t[:])
                ins.then_inc(P.sems["do"], 16)
                return ins
            P.emit("g", g_y, waits=[("sv", vv)])
            P.bump("do", 16)
        ep_v = P.val("sv")


def barrier_waits(prog):
    return [(s, prog.val(s)) for s in ("st", "sv", "sa", "sg", "dlc", "dl0", "dl1", "do", "cc", "dsx", "dbb", "dst", "dr2", "ddy", "do0", "do1")]


def build_full():
    PH = int(os.environ.get("FULL_PH", "9"))
    nc = bass.Bass("TRN2", target_bir_lowering=False, debug=False,
                   detect_race_conditions=bool(os.environ.get("KK_RACE")))
    DT = lambda nm, sh, dt=F32, kind="ExternalInput": nc.dram_tensor(nm, sh, dt, kind=kind).ap()
    xT = DT("xT", [HID, ROWS])
    xTr = DT("xTr", [HID, RPC])
    wcat = DT("wcat", [HID, 772])
    babb = DT("babb", [1, 4])
    lamp = DT("lamp", [1, 256])
    onw = DT("onw", [1, 128])
    wo = DT("wo", [HID, HID])
    w1 = DT("w1", [HID, HID])
    b1 = DT("b1", [HID, 1])
    w2 = DT("w2", [HID, 8])
    b2 = DT("b2", [1, 8])
    ucat = DT("ucat", [512, HID])
    vcatT = DT("vcatT", [HID, 512])
    y = DT("y", [HID, RPC], kind="ExternalOutput")

    _dbgk = "ExternalOutput" if os.environ.get("KK_DBG") else "Internal"
    q_d = DT("q_d", [ROWS, 256], kind=_dbgk)
    k_d = DT("k_d", [ROWS, 256], BF16, kind=_dbgk)
    v_d = DT("v_d", [ROWS, 256], BF16, kind=_dbgk)
    g_d = DT("g_d", [ROWS, HPC], kind=_dbgk)
    bt_d = DT("bt_d", [ROWS, HPC], kind=_dbgk)
    o_d = DT("o_d", [256, ROWS], BF16, kind="Internal")
    o_all = DT("o_all", [HID, ROWS], BF16, kind="Internal")
    st_in = DT("st_in", [2, RPC], kind="Internal")
    st_all = DT("st_all", [16, RPC], kind="Internal")
    x1_d = DT("x1_d", [HID, RPC], kind=_dbgk)
    x1b_d = DT("x1b_d", [HID, RPC], BF16, kind="Internal")
    ln2st_d = DT("ln2st_d", [1, 2 * RPC], kind="Internal")

    prog = Prog()
    sem_names = ["st", "sv", "sa", "sg", "dlc", "dl0", "dl1", "do", "cc", "dsx", "dbb", "dst", "dr2", "ddy", "do0", "do1"]
    with ExitStack() as ctx:
        sems = {n: ctx.enter_context(nc.semaphore(n)) for n in sem_names}
        prog.sems = sems

        def g_pid(g):
            prog.pid_g = g.partition_id()
            return g.nop()
        prog.emit("g", g_pid)

        with ExitStack() as c_mesa:
            M = Mesa(nc, c_mesa, prog, T_FULL)
            with ExitStack() as c_p2:
                M.emit_consts(lamp, onw)
                p2_d0, p2_d1 = emit_p12(nc, prog, M, c_p2, xT, xTr, wcat, babb, st_in, st_all,
                                        q_d, k_d, v_d, g_d, bt_d)

                _nck = min(int(os.environ.get("KK_NCHUNK", "32")), T_FULL // C)
                for cidx in range(0 if PH < 3 else _nck):
                    z = cidx % 2
                    dl_sem = f"dl{z}"
                    def s_ld(s, z=z, cidx=cidx, dl_sem=dl_sem):
                        ins = None
                        for b in range(4):
                            r0 = b * T_FULL + cidx * C
                            for (dst, srcd) in ((M.aq[z], q_d), (M.ak[z], k_d), (M.av[z], v_d)):
                                ins = s.dma_start(out=dst[:, b * 256:(b + 1) * 256],
                                                  in_=srcd[r0:r0 + C, :])
                                ins.then_inc(prog.sems[dl_sem], 16)
                            ins = s.dma_start(out=M.ag[z][:, b * HPC:(b + 1) * HPC],
                                              in_=g_d[r0:r0 + C, :])
                            ins.then_inc(prog.sems[dl_sem], 16)
                            ins = s.dma_start(out=M.ab[z][:, b * HPC:(b + 1) * HPC],
                                              in_=bt_d[r0:r0 + C, :])
                            ins.then_inc(prog.sems[dl_sem], 16)
                        return ins
                    w = ([("sv", M.hstate_v)] if cidx >= 2
                         else [("do0", p2_d0), ("do1", p2_d1)])
                    prog.emit("s", s_ld, waits=w)
                    dl_val = prog.bump(dl_sem, 16 * 20)
                    M.emit_chunk(z, cidx, o_d, dl_sem, dl_val, T_FULL)

                if PH >= 3:
                    def g_ago(g):
                        return g.collective_compute("AllGather", mybir.AluOpType.bypass,
                                                    replica_groups=[list(range(NCORES))],
                                                    ins=[o_d[:, :].opt()], outs=[o_all[:, :].opt()])
                    prog.emit("g", g_ago, waits=[("do", prog.val("do")), ("sv", M.hstate_v)], inc="cc")

        bw = barrier_waits(prog)
        if PH >= 4:
            with ExitStack() as c4a:
                emit_p4a(nc, prog, c4a, xTr, wo, o_all, x1_d, x1b_d, ln2st_d, bw)
        bw2 = barrier_waits(prog)
        if PH >= 5:
            with ExitStack() as c4b:
                emit_p4b(nc, prog, c4b, x1_d, x1b_d, ln2st_d, w1, b1, w2, b2, ucat, vcatT, y, bw2)

        prog.emit("g", lambda g: g.nop(), waits=[("do", prog.val("do"))])

        block = ctx.enter_context(nc.Block())
        prog.realize(block, sems)
    return nc


_NC_CACHE = {}


def kernel(L0_attn_x, L0_attn_Wq, L0_attn_Wk, L0_attn_Wv, L0_attn_Wa, L0_attn_ba,
           L0_attn_Wb, L0_attn_bb, L0_attn_lambda_params, L0_attn_o_norm_weight,
           L0_attn_Wo, L0_ffn_U, L0_ffn_V, L0_ffn_W1, L0_ffn_b1, L0_ffn_W2, L0_ffn_b2):
    f32 = np.float32
    x = np.ascontiguousarray(np.asarray(L0_attn_x, dtype=f32).reshape(ROWS, HID))
    xT = np.ascontiguousarray(x.T)
    Wq = np.asarray(L0_attn_Wq, f32)
    Wk = np.asarray(L0_attn_Wk, f32)
    Wv = np.asarray(L0_attn_Wv, f32)
    Wa = np.asarray(L0_attn_Wa, f32)
    Wb = np.asarray(L0_attn_Wb, f32)
    ba = np.asarray(L0_attn_ba, f32)
    bb = np.asarray(L0_attn_bb, f32)
    lam = np.asarray(L0_attn_lambda_params, f32).reshape(NH, HD)
    onw = np.asarray(L0_attn_o_norm_weight, f32).reshape(1, HD)
    wo = np.ascontiguousarray(np.asarray(L0_attn_Wo, f32))
    w1 = np.ascontiguousarray(np.asarray(L0_ffn_W1, f32))
    b1 = np.asarray(L0_ffn_b1, f32).reshape(HID, 1)
    w2 = np.ascontiguousarray(np.asarray(L0_ffn_W2, f32))
    b2 = np.asarray(L0_ffn_b2, f32).reshape(1, 8)
    U = np.asarray(L0_ffn_U, f32)
    V = np.asarray(L0_ffn_V, f32)
    ucat = np.ascontiguousarray(U.transpose(0, 2, 1).reshape(512, HID))
    vcatT = np.ascontiguousarray(V.reshape(512, HID).T)

    if "nc" not in _NC_CACHE:
        _NC_CACHE["nc"] = build_full()
    nc = _NC_CACHE["nc"]

    in_maps = []
    for c in range(NCORES):
        hs = slice(c * HPC * HD, (c + 1) * HPC * HD)
        hh = slice(c * HPC, (c + 1) * HPC)
        wcat_c = np.ascontiguousarray(np.concatenate(
            [Wq[:, hs], Wk[:, hs], Wv[:, hs], Wa[:, hh], Wb[:, hh]], axis=1))
        babb_c = np.ascontiguousarray(
            np.concatenate([ba[hh], bb[hh]]).reshape(1, 4))
        lamp_c = np.ascontiguousarray(lam[hh].reshape(1, HPC * HD))
        xTr_c = np.ascontiguousarray(xT[:, c * RPC:(c + 1) * RPC])
        in_maps.append(dict(xT=xT, xTr=xTr_c, wcat=wcat_c, babb=babb_c, lamp=lamp_c, onw=onw,
                            wo=wo, w1=w1, b1=b1, w2=w2, b2=b2, ucat=ucat, vcatT=vcatT))

    res = run_bass_kernel_spmd(nc, in_maps, core_ids=list(range(NCORES)))
    yT = np.concatenate([np.asarray(r["y"], dtype=f32) for r in res.results], axis=1)
    out = np.ascontiguousarray(yT.T).reshape(4, T_FULL, HID)
    return out


# version 14
# speedup vs baseline: 1.0015x; 1.0015x over previous
import numpy as np
from contextlib import ExitStack

import concourse.bass as bass
import concourse.mybir as mybir
from concourse.bass_utils import run_bass_kernel_spmd
from concourse.alu_op_type import AluOpType

F32 = mybir.dt.float32
BF16 = mybir.dt.bfloat16
AF = mybir.ActivationFunctionType
AX = mybir.AxisListType.X

HID = 2048
NH = 16
HD = 128
NCORES = 8
HPC = NH // NCORES
NPAIR = 8
C = 128
MAX_CG = 30
LAMB_LB = 0.25
K_EXP = 8
R_EXP = 64
W = NPAIR * HD
import os
DBG_STAGE = int(os.environ.get("MESA_STAGE", "99"))
DBG_ITERS = int(os.environ.get("MESA_ITERS", str(MAX_CG)))
DBG_CSTAGE = int(os.environ.get("MESA_CSTAGE", "99"))
DBG_NCHUNK = int(os.environ.get("MESA_NCHUNK", "99"))

MU, AD, SU, MI, MX, BYP = (AluOpType.mult, AluOpType.add, AluOpType.subtract,
                           AluOpType.min, AluOpType.max, AluOpType.bypass)
GE = AluOpType.is_ge


class Prog:

    def __init__(self):
        self.ops = {e: [] for e in ("t", "v", "a", "g", "s")}
        self.cnt = {}
        self.sems = None

    def bump(self, sem, n):
        self.cnt[sem] = self.cnt.get(sem, 0) + n
        return self.cnt[sem]

    def emit(self, eng, fn, waits=(), inc=None, inc_n=1):
        waits = tuple((s, v) for (s, v) in waits if v > 0)
        val = None
        if inc is not None:
            self.cnt[inc] = self.cnt.get(inc, 0) + inc_n
            val = self.cnt[inc]
        self.ops[eng].append((waits, inc, inc_n, fn))
        return val

    def val(self, sem):
        return self.cnt.get(sem, 0)

    def realize(self, block, sems):
        self.sems = sems

        def runner(handle, eng):
            if eng in ("v", "a", "g"):
                handle = _AutoDrain(handle)
            for waits, inc, inc_n, fn in self.ops[eng]:
                for s, v in waits:
                    handle.wait_ge(sems[s], v)
                ins = fn(handle)
                if inc is not None:
                    ins.then_inc(sems[inc], inc_n)

        block.tensor(lambda h: runner(h, "t"))
        block.vector(lambda h: runner(h, "v"))
        block.scalar(lambda h: runner(h, "a"))
        block.gpsimd(lambda h: runner(h, "g"))
        block.sync(lambda h: runner(h, "s"))


_DRAIN_OPS = frozenset((
    "tensor_tensor", "tensor_scalar", "tensor_copy", "tensor_single_scalar",
    "scalar_tensor_tensor", "reciprocal", "memset", "memzero", "tensor_reduce",
    "reduce_sum", "reduce_max", "select", "iota", "affine_select", "activation",
    "tensor_add", "tensor_sub", "tensor_mul", "tensor_max", "square", "copy",
    "add", "mul", "sqrt", "sign",
))


class _AutoDrain:

    def __init__(self, h):
        self._h = h
        self._pend = False

    @property
    def raw(self):
        return self._h

    def skip(self):
        self._pend = False

    def __getattr__(self, name):
        real = getattr(self._h, name)
        if name in _DRAIN_OPS:
            def wrap(*a, **k):
                if self._pend:
                    self._h.drain()
                self._pend = True
                return real(*a, **k)
            return wrap
        return real


def pc(p):
    return slice(p * HD, (p + 1) * HD)


class Mesa:

    def __init__(self, nc, ctx, prog, T):
        self.nc = nc
        self.prog = prog
        self.T = T
        self.nchunk = T // C
        S = lambda nm, sh, dt=F32: ctx.enter_context(nc.sbuf_tensor(nm, sh, dt))
        P_ = lambda nm, sh, dt=F32: ctx.enter_context(nc.psum_tensor(nm, sh, dt))

        self.ident_b = S("mz_ident_b", [128, 128], BF16)
        self.ident_f = S("mz_ident_f", [128, 128], F32)
        self.utones_f = S("mz_utones_f", [128, 128], F32)
        self.utmask_b = S("mz_utmask_b", [128, 128], BF16)
        self.ones_col_f = S("mz_ones_col_f", [128, 1], F32)
        self.ones_col_b = S("mz_ones_col_b", [128, 1], BF16)
        self.ones_row_f = S("mz_ones_row_f", [1, 128], F32)
        self.ones_sq_b = S("mz_ones_sq_b", [128, 128], BF16)
        self.lamb_eps_arena = S("mz_lamb_eps", [128, W], F32)
        self.diag_lamb = S("mz_diag_lamb", [128, HPC * 128], BF16)
        self.onw_col = S("mz_onw_col", [128, 1], F32)

        self.Hkk_f = S("mz_Hkk_f", [128, W], F32)
        self.Hkk_b = S("mz_Hkk_b", [128, W], BF16)
        self.Hvk_f = S("mz_Hvk_f", [128, W], F32)
        self.Hvk_b = S("mz_Hvk_b", [128, W], BF16)
        self.Hd_row = S("mz_Hd_row", [1, W], F32)

        self.X = S("mz_X", [128, W], F32)
        self.R = S("mz_R", [128, W], F32)
        self.Pv = S("mz_Pv", [128, W], F32)
        self.Pv2 = S("mz_Pv2", [128, W], F32)
        self.Pd = S("mz_Pd", [128, W], BF16)
        self.Pd_e = S("mz_Pd_e", [128, W], BF16)
        self.Pb = S("mz_Pb", [128, W], BF16)
        self.diag_e = S("mz_diag_e", [128, W], BF16)
        self.Xt = S("mz_Xt", [128, W], BF16)

        self.aq = [S(f"mz_aq{z}", [128, W], F32) for z in range(2)]
        self.ak = [S(f"mz_ak{z}", [128, W], BF16) for z in range(2)]
        self.av = [S(f"mz_av{z}", [128, W], BF16) for z in range(2)]
        self.ag = [S(f"mz_ag{z}", [128, NPAIR], F32) for z in range(2)]
        self.ab = [S(f"mz_ab{z}", [128, NPAIR], F32) for z in range(2)]

        self.Kd = S("mz_Kd", [128, W], BF16)
        self.Vd = S("mz_Vd", [128, W], BF16)
        self.Ksq = S("mz_Ksq", [128, W], BF16)
        self.Ktil = S("mz_Ktil", [128, W], BF16)
        self.Dp = S("mz_Dp", [128, W], BF16)
        self.Mp = S("mz_Mp", [128, W], BF16)
        self.scrf = S("mz_scrf", [128, W], F32)
        self.scrf2 = S("mz_scrf2", [128, W], F32)
        self.scrb = S("mz_scrb", [128, W], BF16)

        self.Gc = S("mz_Gc", [128, NPAIR], F32)
        self.e_all = S("mz_e_all", [128, NPAIR], F32)
        self.Gcrow = S("mz_Gcrow", [1, W], F32)
        self.erow = S("mz_erow", [1, W], F32)
        self.w_all = S("mz_w_all", [128, NPAIR], F32)
        self.dec_bc = S("mz_dec_bc", [128, NPAIR], F32)
        self.decs_row = S("mz_decs_row", [1, NPAIR], F32)
        self.pq = S("mz_pq", [128, NPAIR], F32)
        self.dA = S("mz_dA", [128, NPAIR], F32)
        self.dB = S("mz_dB", [128, NPAIR], F32)
        self.nd = S("mz_nd", [128, NPAIR], F32)
        self.alpha = S("mz_alpha", [128, NPAIR], F32)
        self.alphan = S("mz_alphan", [128, NPAIR], F32)
        self.betac = S("mz_betac", [128, NPAIR], F32)
        self.c1 = S("mz_c1", [128, NPAIR], F32)
        self.lamrow = S("mz_lamrow", [1, HPC * 128], F32)
        self.sprow = S("mz_sprow", [1, HPC * 128], F32)
        self.sprow2 = S("mz_sprow2", [1, HPC * 128], F32)
        self.onwrow = S("mz_onwrow", [1, 128], F32)

        self.ps_T = P_("mz_ps_T", [128, W], F32)
        self.ps_Z = P_("mz_ps_Z", [128, W], F32)
        self.ps_B = P_("mz_ps_B", [128, W], F32)
        self.ps_S = P_("mz_ps_S", [128, 4 * HD], F32)
        self.ps_Tb = P_("mz_ps_Tb", [128, W], BF16)

        self.rd = {k: ("sv", 0) for k in ("T", "Z", "B", "S", "Tb")}
        self.wr = {k: 0 for k in ("T", "Z", "B", "S", "Tb")}
        self.hstate_v = 0
        self.pready = 0
        self.de_rd = 0
        self.de_a = 0
        self.pb_a = 0
        self.pb_a_h0 = 0
        self.x_done = 0
        self.hstate_a = 0
        self.kv_a = 0

    def w_waits(self, *banks):
        return [self.rd[b] for b in banks]

    def set_w(self, bank, tv):
        self.wr[bank] = tv

    def r_waits(self, *banks):
        return [("st", self.wr[b]) for b in banks]

    def set_r(self, bank, sem, vv):
        self.rd[bank] = (sem, vv)

    def emit_consts(self, lamp_ap, onw_ap):
        P, M = self.prog, self

        def g_init1(g):
            for t in (M.ident_b, M.ident_f, M.utones_f, M.utmask_b):
                ins = g.memset(t[:], 0.0)
            ins = g.memset(M.ones_col_f[:], 1.0)
            ins = g.memset(M.ones_col_b[:], 1.0)
            ins = g.memset(M.ones_row_f[:], 1.0)
            ins = g.memset(M.ones_sq_b[:], 1.0)
            ins = g.memset(M.Hkk_f[:], 0.0)
            ins = g.memset(M.Hkk_b[:], 0.0)
            ins = g.memset(M.Hvk_f[:], 0.0)
            ins = g.memset(M.Hvk_b[:], 0.0)
            return g.memset(M.Hd_row[:], 0.0)
        gv1 = P.emit("g", g_init1, inc="sg")

        def g_init2(g):
            for t, sel in ((M.ident_b, "ne"), (M.ident_f, "ne"), (M.utones_f, "gt"),
                           (M.utmask_b, "gt")):
                ins = g.affine_select(out=t[:], in_=t[:],
                                compare_op=AluOpType.not_equal if sel == "ne" else AluOpType.is_gt,
                                fill=1.0, base=0, pattern=[[-1, 128]], channel_multiplier=1)
            return ins
        gv = P.emit("g", g_init2, waits=[("sg", gv1)], inc="sg")
        self.consts_gv = gv

        P.emit("s", lambda s: s.dma_start(out=M.lamrow[:], in_=lamp_ap), inc="dlc", inc_n=16)
        P.emit("s", lambda s: s.dma_start(out=M.onwrow[:], in_=onw_ap), inc="dlc", inc_n=16)
        dl0 = P.val("dlc")

        if DBG_CSTAGE < 1:
            self.consts_done = P.emit("v", lambda v: v.tensor_copy(out=M.sprow[:], in_=M.lamrow[:]),
                                      waits=[("dlc", dl0), ("sg", gv)], inc="sv")
            return
        def v_sp1(v):
            ins = v.tensor_scalar(out=M.sprow[:], in0=M.lamrow[:], scalar1=-1.0, scalar2=None, op0=MU)
            ins = v.tensor_tensor(out=M.sprow[:], in0=M.lamrow[:], in1=M.sprow[:], op=MI)
            return v.tensor_scalar(out=M.sprow2[:], in0=M.lamrow[:], scalar1=0.0, scalar2=None, op0=MX)
        vv = P.emit("v", v_sp1, waits=[("dlc", dl0)], inc="sv")
        av = P.emit("a", lambda a: a.activation(out=M.sprow[:], in_=M.sprow[:], func=AF.Exp),
                    waits=[("sv", vv), ("sg", gv)], inc="sa")
        av = P.emit("a", lambda a: a.activation(out=M.sprow[:], in_=M.sprow[:], func=AF.Ln,
                                                bias=M.ones_col_f[0:1, :]),
                    waits=[("sa", av)], inc="sa")

        def v_sp4(v):
            ins = v.tensor_tensor(out=M.lamrow[:], in0=M.sprow2[:], in1=M.sprow[:], op=AD)
            return v.tensor_scalar(out=M.lamrow[:], in0=M.lamrow[:], scalar1=LAMB_LB, scalar2=None, op0=AD)
        vv = P.emit("v", v_sp4, waits=[("sa", av)], inc="sv")

        if DBG_CSTAGE < 2:
            self.consts_done = vv
            return
        def t_lam(t):
            for h in range(HPC):
                ins = t.matmul(out=M.ps_B[:, h * 128:(h + 1) * 128], lhsT=M.ones_row_f[:],
                         rhs=M.lamrow[0:1, h * 128:(h + 1) * 128], start=True, stop=True)
            return t.matmul(out=M.ps_B[:, 256:257], lhsT=M.onwrow[:],
                            rhs=M.ones_row_f[0:1, 0:1], start=True, stop=True)
        tv = P.emit("t", t_lam, waits=[("sv", vv), ("sg", gv)], inc="st")
        self.set_w("B", tv)

        def v_lam2(v):
            for p in range(NPAIR):
                h = p % HPC
                ins = v.tensor_scalar(out=M.lamb_eps_arena[:, pc(p)], in0=M.ps_B[:, h * 128:(h + 1) * 128],
                                scalar1=1e-8, scalar2=None, op0=AD)
            for h in range(HPC):
                ins = v.tensor_tensor(out=M.diag_lamb[:, h * 128:(h + 1) * 128],
                                in0=M.ps_B[:, h * 128:(h + 1) * 128], in1=M.ident_b[:], op=MU)
            return v.tensor_copy(out=M.onw_col[:], in_=M.ps_B[:, 256:257])
        vv = P.emit("v", v_lam2, waits=self.r_waits("B"), inc="sv")
        self.set_r("B", "sv", vv)
        self.consts_done = vv

    def emit_mv(self, Psrc, ak):
        P, M = self.prog, self

        tv = P.emit("t", lambda t: [t.transpose(out=M.ps_Tb[:, pc(p)], in_=M.Pb[:, pc(p)],
                                                identity=M.ident_b[:]) for p in range(NPAIR)][-1],
                    waits=self.w_waits("Tb") + [("sa", self.pb_a)], inc="st")
        self.set_w("Tb", tv)
        def t_pe(t):
            for p in range(NPAIR):
                ins = t.matmul(out=M.ps_B[:, pc(p)], lhsT=M.Pb[:, pc(p)], rhs=M.diag_e[:, pc(p)],
                               start=True, stop=True)
            return ins
        tve = P.emit("t", t_pe, waits=self.w_waits("B") + [("sa", self.pb_a), ("sa", self.de_a)],
                     inc="st")
        self.set_w("B", tve)
        av1 = P.emit("a", lambda a: a.activation(out=M.Pd[:], in_=M.ps_Tb[:], func=AF.Copy),
                     waits=self.r_waits("Tb"), inc="sa")
        self.set_r("Tb", "sa", av1)
        av2 = P.emit("a", lambda a: a.activation(out=M.Pd_e[:], in_=M.ps_B[:], func=AF.Copy),
                     waits=self.r_waits("B"), inc="sa")
        self.set_r("B", "sa", av2)

        def t_dl(t):
            for p in range(NPAIR):
                h = p % HPC
                ins = t.matmul(out=M.ps_Z[:, pc(p)], lhsT=M.Pd[:, pc(p)],
                               rhs=M.diag_lamb[:, h * 128:(h + 1) * 128],
                               start=(p % 4 == 0), stop=False)
            return ins
        P.emit("t", t_dl, waits=self.w_waits("Z") + [("sa", av1), ("sv", self.hstate_v)], inc="st")

        def t_s0(t):
            for p in range(4):
                ins = t.matmul(out=M.ps_S[:, p * HD:(p + 1) * HD], lhsT=M.Kd[:, pc(p)],
                         rhs=M.Pd[:, pc(p)], start=True, stop=True)
            return ins
        tv_s0 = P.emit("t", t_s0, waits=self.w_waits("S") + [("sa", self.kv_a)], inc="st")
        self.set_w("S", tv_s0)
        def t_s1(t):
            for p in range(4, 8):
                ins = t.matmul(out=M.ps_T[:, (p - 4) * HD:(p - 3) * HD], lhsT=M.Kd[:, pc(p)],
                         rhs=M.Pd[:, pc(p)], start=True, stop=True)
            return ins
        tv_s1 = P.emit("t", t_s1, waits=self.w_waits("T") + [("sa", av1)], inc="st")
        self.set_w("T", tv_s1)

        def t_hk(t):
            for p in range(NPAIR):
                ins = t.matmul(out=M.ps_Z[:, pc(p)], lhsT=M.Pd_e[:, pc(p)], rhs=M.Hkk_b[:, pc(p)],
                               start=False, stop=False)
            return ins
        P.emit("t", t_hk, waits=[("sa", av2), ("sa", self.hstate_a)], inc="st")

        def v_m0(v):
            v.skip()
            return v.tensor_tensor(out=M.Mp[:, 0:512], in0=M.Dp[:, 0:512],
                                   in1=M.ps_S[:], op=MU)
        vv_m0 = P.emit("v", v_m0, waits=self.r_waits("S"), inc="sv")
        self.set_r("S", "sv", vv_m0)
        def v_m1(v):
            v.skip()
            return v.tensor_tensor(out=M.Mp[:, 512:1024], in0=M.Dp[:, 512:1024],
                                   in1=M.ps_T[:, 0:512], op=MU)
        vv_m1 = P.emit("v", v_m1, waits=[("st", tv_s1)], inc="sv")
        self.set_r("T", "sv", vv_m1)

        def t_y20(t, ak=ak):
            for p in range(4):
                ins = t.matmul(out=M.ps_Z[:, pc(p)], lhsT=M.Mp[:, pc(p)], rhs=ak[:, pc(p)],
                         start=False, stop=(p == 3))
            return ins
        P.emit("t", t_y20, waits=[("sv", vv_m0)], inc="st")
        def t_y21(t, ak=ak):
            for p in range(4, 8):
                ins = t.matmul(out=M.ps_Z[:, pc(p)], lhsT=M.Mp[:, pc(p)], rhs=ak[:, pc(p)],
                         start=False, stop=(p == 7))
            return ins
        tvh = P.emit("t", t_y21, waits=[("sv", vv_m1)], inc="st")
        self.set_w("Z", tvh)
        return tvh

    def emit_chunk(self, z, c, o_dram, dl_sem, dl_val, b_stride):
        P, M = self.prog, self
        aq, ak, av, ag, ab = M.aq[z], M.ak[z], M.av[z], M.ag[z], M.ab[z]

        base_wait = [(dl_sem, dl_val), ("sv", self.consts_done)]

        def t_gc(t):
            for p in range(NPAIR):
                ins = t.matmul(out=M.ps_Z[:, p:p + 1], lhsT=M.utones_f[:], rhs=ag[:, p:p + 1],
                         start=True, stop=True)
            return ins
        tv = P.emit("t", t_gc, waits=base_wait + self.w_waits("Z"), inc="st")
        self.set_w("Z", tv)
        vv = P.emit("v", lambda v: v.tensor_copy(out=M.Gc[:], in_=M.ps_Z[:, 0:NPAIR]),
                    waits=self.r_waits("Z"), inc="sv")
        self.set_r("Z", "sv", vv)
        gc_v = vv
        e_a = P.emit("a", lambda a: a.activation(out=M.e_all[:], in_=M.Gc[:], func=AF.Exp),
                     waits=[("sv", vv)], inc="sa")
        def a_de(a):
            for p in range(NPAIR):
                ins = a.activation(out=M.diag_e[:, pc(p)], in_=M.ident_f[:], func=AF.Copy,
                                   scale=M.e_all[:, p:p + 1])
            return ins
        self.de_a = P.emit("a", a_de, waits=[("sa", e_a), ("sg", self.consts_gv),
                                             ("st", self.de_rd)], inc="sa")

        if DBG_CSTAGE < 3:
            self.hstate_v = gc_v
            return None
        def t_gr(t):
            for p in range(NPAIR):
                ins = t.transpose(out=M.ps_T[0:1, pc(p)], in_=M.Gc[:, p:p + 1], identity=M.ident_f[:])
            return ins
        tv = P.emit("t", t_gr, waits=[("sv", gc_v)] + self.w_waits("T"), inc="st")
        self.set_w("T", tv)
        vv = P.emit("v", lambda v: v.tensor_copy(out=M.Gcrow[:], in_=M.ps_T[0:1, :]),
                    waits=self.r_waits("T"), inc="sv")
        self.set_r("T", "sv", vv)
        gcrow_v = vv
        P.emit("a", lambda a: a.activation(out=M.erow[:], in_=M.Gcrow[:], func=AF.Exp),
               waits=[("sv", vv)], inc="sa")
        erow_a = P.val("sa")

        if DBG_CSTAGE < 4:
            self.hstate_v = P.val("sv")
            return None
        def t_gbc(t):
            for p in range(NPAIR):
                ins = t.matmul(out=M.ps_B[:, pc(p)], lhsT=M.ones_row_f[:], rhs=M.Gcrow[0:1, pc(p)],
                         start=True, stop=True)
            return ins
        tv = P.emit("t", t_gbc, waits=[("sv", gcrow_v)] + self.w_waits("B"), inc="st")
        self.set_w("B", tv)

        def v_dg(v):
            for p in range(NPAIR):
                h = v if p == 0 else v.raw
                ins = h.tensor_scalar(out=M.scrf[:, pc(p)], in0=M.ps_B[:, pc(p)],
                                scalar1=M.Gc[:, p:p + 1], scalar2=0.0, op0=SU, op1=MI)
            return ins
        vv = P.emit("v", v_dg, waits=self.r_waits("B"), inc="sv")
        self.set_r("B", "sv", vv)
        av_ = P.emit("a", lambda a: a.activation(out=M.scrf[:], in_=M.scrf[:], func=AF.Exp),
                     waits=[("sv", vv)], inc="sa")

        def v_w(v):
            ins = v.tensor_copy(out=M.w_all[:],
                          in_=M.scrf[:].rearrange("s (p d) -> s p d", p=NPAIR)[:, :, 127:128])
            return v.tensor_tensor(out=M.w_all[:], in0=M.w_all[:], in1=ab[:], op=MU)
        w_v = P.emit("v", v_w, waits=[("sa", av_)], inc="sv")

        def v_dp(v):
            for p in range(NPAIR):
                h = v if p == 0 else v.raw
                ins = h.scalar_tensor_tensor(out=M.Dp[:, pc(p)], in0=M.utmask_b[:],
                                       scalar=ab[:, p:p + 1], in1=M.scrf[:, pc(p)],
                                       op0=MU, op1=MU)
            return ins
        dp_v = P.emit("v", v_dp, waits=[("sa", av_)], inc="sv")

        if DBG_CSTAGE < 5:
            self.hstate_v = P.val("sv")
            return None
        ksq_a = P.emit("a", lambda a: a.activation(out=M.Ksq[:], in_=ak[:], func=AF.Square),
                       waits=base_wait, inc="sa")
        for (src, dst) in ((ak, M.Kd), (av, M.Vd)):
            def t_tb(t, src=src):
                for p in range(NPAIR):
                    ins = t.transpose(out=M.ps_Tb[:, pc(p)], in_=src[:, pc(p)], identity=M.ident_b[:])
                return ins
            tv = P.emit("t", t_tb, waits=base_wait + self.w_waits("Tb"), inc="st")
            self.set_w("Tb", tv)
            kv_a = P.emit("a", lambda a, dst=dst: a.activation(out=dst[:], in_=M.ps_Tb[:], func=AF.Copy),
                          waits=self.r_waits("Tb"), inc="sa")
            self.set_r("Tb", "sa", kv_a)
        self.kv_a = kv_a

        if DBG_CSTAGE < 6:
            self.hstate_v = P.val("sv")
            return None
        def t_dec(t):
            for p in range(NPAIR):
                ins = t.matmul(out=M.ps_Z[0:1, p:p + 1], lhsT=ag[:, p:p + 1], rhs=M.ones_col_f[:],
                         start=True, stop=True)
            return ins
        tv = P.emit("t", t_dec, waits=base_wait + self.w_waits("Z"), inc="st")
        self.set_w("Z", tv)
        vv = P.emit("v", lambda v: v.tensor_copy(out=M.decs_row[:], in_=M.ps_Z[0:1, 0:NPAIR]),
                    waits=self.r_waits("Z"), inc="sv")
        self.set_r("Z", "sv", vv)
        av_ = P.emit("a", lambda a: a.activation(out=M.decs_row[:], in_=M.decs_row[:], func=AF.Exp),
                     waits=[("sv", vv)], inc="sa")
        tv = P.emit("t", lambda t: t.matmul(out=M.ps_Z[:, 0:NPAIR], lhsT=M.ones_row_f[:],
                                            rhs=M.decs_row[:], start=True, stop=True),
                    waits=[("sa", av_)] + self.w_waits("Z"), inc="st")
        self.set_w("Z", tv)
        vv = P.emit("v", lambda v: v.tensor_copy(out=M.dec_bc[:], in_=M.ps_Z[:, 0:NPAIR]),
                    waits=self.r_waits("Z"), inc="sv")
        self.set_r("Z", "sv", vv)

        if DBG_STAGE < 2:
            self.hstate_v = self.prog.val("sv")
            return None
        def t_diag(t):
            for p in range(NPAIR):
                ins = t.matmul(out=M.ps_Z[:, pc(p)], lhsT=M.Dp[:, pc(p)], rhs=M.Ksq[:, pc(p)],
                         start=True, stop=False)
                ins = t.matmul(out=M.ps_Z[:, pc(p)], lhsT=M.erow[0:1, pc(p)],
                         rhs=M.Hd_row[0:1, pc(p)], start=False, stop=True)
            return ins
        tv = P.emit("t", t_diag, waits=[("sv", dp_v), ("sa", ksq_a), ("sa", erow_a),
                                        ("sv", self.hstate_v)] + self.w_waits("Z"), inc="st")
        self.set_w("Z", tv)

        def v_x0(v):
            ins = v.tensor_tensor(out=M.scrf2[:], in0=M.ps_Z[:], in1=M.lamb_eps_arena[:], op=AD)
            ins = v.reciprocal(out=M.scrf2[:], in_=M.scrf2[:])
            return v.tensor_tensor(out=M.X[:], in0=aq[:], in1=M.scrf2[:], op=MU)
        vv = P.emit("v", v_x0, waits=self.r_waits("Z"), inc="sv")
        self.set_r("Z", "sv", vv)

        if DBG_STAGE < 3:
            self.hstate_v = self.prog.val("sv")
            return None
        self.pready = vv
        self.pb_a = P.emit("a", lambda a: a.activation(out=M.Pb[:], in_=M.X[:], func=AF.Copy),
                           waits=[("sv", vv)], inc="sa")
        self.pb_a_h0 = self.pb_a
        q_tv = self.emit_mv(M.X, ak)

        def v_r0(v):
            v.tensor_tensor(out=M.R[:], in0=aq[:], in1=M.ps_Z[:], op=SU)
            return v.tensor_copy(out=M.Pv[:], in_=M.R[:])
        vv = P.emit("v", v_r0, waits=[("st", q_tv)], inc="sv")
        self.set_r("Z", "sv", vv)
        p_v = vv

        def a_d0(a):
            for p in range(NPAIR):
                h = a if p == 0 else a.raw
                ins = h.activation(out=M.scrb[:, pc(p)], in_=M.R[:, pc(p)], func=AF.Square,
                             accum_out=M.dA[:, p:p + 1])
            return ins
        d_a = P.emit("a", a_d0, waits=[("sv", vv)], inc="sa")
        self.pb_a = P.emit("a", lambda a: a.activation(out=M.Pb[:], in_=M.Pv[:], func=AF.Copy),
                           waits=[("sv", vv)], inc="sa")
        self.pb_a_h0 = self.pb_a
        nd0_v = P.emit("v", lambda v: v.tensor_single_scalar(out=M.nd[:], in_=M.dA[:],
                                                             scalar=1e-10, op=GE),
                       waits=[("sa", d_a)], inc="sv")

        dcur, dnxt = M.dA, M.dB
        Pcur, Pnxt = M.Pv, M.Pv2
        chain_v = nd0_v

        if DBG_STAGE < 4:
            self.hstate_v = self.prog.val("sv")
            return None
        niter = min(DBG_ITERS, MAX_CG)
        for j in range(niter):
            last = (j == niter - 1)
            self.pready = p_v
            q_tv = self.emit_mv(Pcur, ak)

            def v_pq(v, Pcur=Pcur):
                v.skip()
                for p in range(NPAIR):
                    h = v if p == 0 else v.raw
                    ins = h.scalar_tensor_tensor(out=M.scrb[:, pc(p)], in0=Pcur[:, pc(p)], scalar=0.0,
                                           in1=M.ps_Z[:, pc(p)], op0=BYP, op1=MU,
                                           accum_out=M.pq[:, p:p + 1])
                return ins
            pq_v = P.emit("v", v_pq, waits=[("st", q_tv)], inc="sv")

            def v_ca(v, dcur=dcur):
                ins = v.tensor_tensor(out=M.c1[:], in0=M.pq[:], in1=M.pq[:], op=MU)
                ins = v.tensor_single_scalar(out=M.c1[:], in_=M.c1[:], scalar=1e-20, op=GE)
                ins = v.tensor_tensor(out=M.nd[:], in0=M.nd[:], in1=M.c1[:], op=MU)
                ins = v.tensor_scalar(out=M.alphan[:], in0=M.nd[:], scalar1=-1.0, scalar2=1.0,
                                op0=MU, op1=AD)
                ins = v.tensor_tensor(out=M.c1[:], in0=M.pq[:], in1=M.nd[:], op=MU)
                ins = v.tensor_tensor(out=M.c1[:], in0=M.c1[:], in1=M.alphan[:], op=AD)
                ins = v.reciprocal(out=M.c1[:], in_=M.c1[:])
                ins = v.tensor_tensor(out=M.c1[:], in0=M.c1[:], in1=M.nd[:], op=MU)
                ins = v.tensor_tensor(out=M.alpha[:], in0=dcur[:], in1=M.c1[:], op=MU)
                return v.tensor_scalar(out=M.alphan[:], in0=M.alpha[:], scalar1=-1.0,
                                       scalar2=None, op0=MU)
            ca_v = P.emit("v", v_ca, inc="sv")

            if not last:
                rv = ca_v
                for p in range(NPAIR):
                    def v_r(v, p=p):
                        if p > 0:
                            v.skip()
                        return v.scalar_tensor_tensor(
                                    out=M.R[:, pc(p)], in0=M.ps_Z[:, pc(p)],
                                    scalar=M.alphan[:, p:p + 1], in1=M.R[:, pc(p)],
                                    op0=MU, op1=AD)
                    rv = P.emit("v", v_r, inc="sv")
                    if p <= 4:
                        def a_dn(a, p=p, dnxt=dnxt):
                            a.skip()
                            return a.activation(
                                        out=M.scrb[:, pc(p)], in_=M.R[:, pc(p)], func=AF.Square,
                                        accum_out=dnxt[:, p:p + 1])
                        dn_a = P.emit("a", a_dn, waits=[("sv", rv)], inc="sa")
                self.set_r("Z", "sv", rv)
                def v_dn2(v, dnxt=dnxt):
                    v.skip()
                    for p in range(5, 8):
                        h = v if p == 5 else v.raw
                        ins = h.scalar_tensor_tensor(out=M.scrb[:, pc(p)], in0=M.R[:, pc(p)],
                                               scalar=0.0, in1=M.R[:, pc(p)], op0=BYP, op1=MU,
                                               accum_out=dnxt[:, p:p + 1])
                    return ins
                dn_v = P.emit("v", v_dn2, inc="sv")

                def v_cb(v, dcur=dcur, dnxt=dnxt):
                    ins = v.tensor_scalar(out=M.c1[:], in0=M.nd[:], scalar1=-1.0, scalar2=1.0,
                                    op0=MU, op1=AD)
                    ins = v.tensor_tensor(out=M.betac[:], in0=dcur[:], in1=M.nd[:], op=MU)
                    ins = v.tensor_tensor(out=M.c1[:], in0=M.betac[:], in1=M.c1[:], op=AD)
                    ins = v.reciprocal(out=M.c1[:], in_=M.c1[:])
                    ins = v.tensor_tensor(out=M.c1[:], in0=M.c1[:], in1=M.nd[:], op=MU)
                    ins = v.tensor_tensor(out=M.betac[:], in0=dnxt[:], in1=M.c1[:], op=MU)
                    ins = v.tensor_single_scalar(out=M.c1[:], in_=dnxt[:], scalar=1e-10, op=GE)
                    return v.tensor_tensor(out=M.nd[:], in0=M.nd[:], in1=M.c1[:], op=MU)
                cb_v = P.emit("v", v_cb, waits=[("sa", dn_a)], inc="sv")
                chain_v = cb_v

                def v_p1(v, Pcur=Pcur, Pnxt=Pnxt):
                    for p in range(4):
                        h = v if p == 0 else v.raw
                        ins = h.scalar_tensor_tensor(out=Pnxt[:, pc(p)], in0=Pcur[:, pc(p)],
                                               scalar=M.betac[:, p:p + 1], in1=M.R[:, pc(p)],
                                               op0=MU, op1=AD)
                    return ins
                p_v1 = P.emit("v", v_p1, inc="sv")
                def v_p2(v, Pcur=Pcur, Pnxt=Pnxt):
                    v.skip()
                    for p in range(4, 8):
                        h = v if p == 4 else v.raw
                        ins = h.scalar_tensor_tensor(out=Pnxt[:, pc(p)], in0=Pcur[:, pc(p)],
                                               scalar=M.betac[:, p:p + 1], in1=M.R[:, pc(p)],
                                               op0=MU, op1=AD)
                    return ins
                p_v = P.emit("v", v_p2, inc="sv")
                pb1 = P.emit("a", lambda a, Pnxt=Pnxt: a.activation(out=M.Pb[:, 0:512],
                                                                    in_=Pnxt[:, 0:512], func=AF.Copy),
                             waits=[("sv", p_v1)], inc="sa")
                self.pb_a = P.emit("a", lambda a, Pnxt=Pnxt: a.activation(out=M.Pb[:, 512:1024],
                                                                          in_=Pnxt[:, 512:1024],
                                                                          func=AF.Copy),
                                   waits=[("sv", p_v)], inc="sa")
                self.pb_a_h0 = pb1
            else:
                self.set_r("Z", "sv", ca_v)

            def v_x(v, Pcur=Pcur, last=last):
                if not last:
                    v.skip()
                for p in range(NPAIR):
                    h = v if p == 0 else v.raw
                    ins = h.scalar_tensor_tensor(out=M.X[:, pc(p)], in0=Pcur[:, pc(p)],
                                           scalar=M.alpha[:, p:p + 1], in1=M.X[:, pc(p)],
                                           op0=MU, op1=AD)
                return ins
            x_v = P.emit("v", v_x, waits=[("sv", ca_v)] if last else (), inc="sv")
            self.x_done = x_v

            dcur, dnxt = dnxt, dcur
            Pcur, Pnxt = Pnxt, Pcur

        if os.environ.get("MESA_DBG_TAP") and c == 0:
            def g_tap(g):
                ins = None
                for i, tsr in enumerate([M.pq, M.dA, M.dB, M.nd, M.alpha, M.c1, M.betac, M.e_all]):
                    ins = g.dma_start(out=self.dbg_d[i * 128:(i + 1) * 128, 0:NPAIR], in_=tsr[:])
                    ins.then_inc(P.sems["do"], 16)
                ins = g.dma_start(out=self.dbg_d[8 * 128:9 * 128, :], in_=M.X[:, 0:128])
                ins.then_inc(P.sems["do"], 16)
                ins = g.dma_start(out=self.dbg_d[9 * 128:10 * 128, :], in_=M.R[:, 0:128])
                ins.then_inc(P.sems["do"], 16)
                ins = g.dma_start(out=self.dbg_d[10 * 128:11 * 128, :], in_=M.Pv[:, 0:128])
                ins.then_inc(P.sems["do"], 16)
                return ins
            P.emit("g", g_tap, waits=[("sv", p_v)])
            P.bump("do", 16 * 11)
        if DBG_STAGE < 5:
            self.hstate_v = self.prog.val("sv")
            return None
        a_xb = P.emit("a", lambda a: a.activation(out=M.Pb[:], in_=M.X[:], func=AF.Copy),
                      waits=[("sv", self.x_done)], inc="sa")
        def t_xe(t):
            for p in range(NPAIR):
                ins = t.matmul(out=M.ps_B[:, pc(p)], lhsT=M.Pb[:, pc(p)], rhs=M.diag_e[:, pc(p)],
                               start=True, stop=True)
            return ins
        tvx = P.emit("t", t_xe, waits=self.w_waits("B") + [("sa", a_xb), ("sa", self.de_a)],
                     inc="st")
        self.set_w("B", tvx)
        self.de_rd = tvx
        def t_tx(t):
            for p in range(NPAIR):
                ins = t.transpose(out=M.ps_Tb[:, pc(p)], in_=M.Pb[:, pc(p)], identity=M.ident_b[:])
            return ins
        tvp = P.emit("t", t_tx, waits=self.w_waits("Tb") + [("sa", a_xb)], inc="st")
        self.set_w("Tb", tvp)
        av_x = P.emit("a", lambda a: a.activation(out=M.Xt[:], in_=M.ps_B[:], func=AF.Copy),
                      waits=self.r_waits("B"), inc="sa")
        self.set_r("B", "sa", av_x)
        av_p = P.emit("a", lambda a: a.activation(out=M.Pd[:], in_=M.ps_Tb[:], func=AF.Copy),
                      waits=self.r_waits("Tb"), inc="sa")
        self.set_r("Tb", "sa", av_p)
        xt_a = av_p

        def t_o1(t):
            for p in range(NPAIR):
                ins = t.matmul(out=M.ps_Z[:, pc(p)], lhsT=M.Hvk_b[:, pc(p)], rhs=M.Xt[:, pc(p)],
                         start=(p % 4 == 0), stop=False)
            return ins
        tv = P.emit("t", t_o1, waits=self.w_waits("Z") + [("sa", xt_a)], inc="st")
        self.set_w("Z", tv)

        for half in range(2):
            ps = range(half * 4, half * 4 + 4)
            def t_s2(t, ps=ps):
                for p in ps:
                    ins = t.matmul(out=M.ps_S[:, (p % 4) * HD:(p % 4 + 1) * HD], lhsT=M.Vd[:, pc(p)],
                             rhs=M.Pd[:, pc(p)], start=True, stop=True)
                return ins
            tv_s = P.emit("t", t_s2, waits=self.w_waits("S") + [("sa", xt_a), ("sa", self.kv_a)], inc="st")
            self.set_w("S", tv_s)
            def v_m2(v, half=half):
                v.skip()
                return v.tensor_tensor(out=M.Mp[:, half * 512:(half + 1) * 512],
                                       in0=M.Dp[:, half * 512:(half + 1) * 512],
                                       in1=M.ps_S[:], op=MU)
            vv_m = P.emit("v", v_m2, waits=self.r_waits("S"), inc="sv")
            self.set_r("S", "sv", vv_m)
            def t_o2(t, ps=ps):
                for p in ps:
                    ins = t.matmul(out=M.ps_Z[:, pc(p)], lhsT=ak[:, pc(p)], rhs=M.Mp[:, pc(p)],
                             start=False, stop=(p % 4 == 3))
                return ins
            tv_o2 = P.emit("t", t_o2, waits=[("sv", vv_m)], inc="st")
        self.set_w("Z", tv_o2)

        av_ = P.emit("a", lambda a: a.activation(out=M.scrb[:], in_=M.ps_Z[:], func=AF.Square),
                     waits=self.r_waits("Z"), inc="sa")
        self.set_r("Z", "sa", av_)

        def t_rms(t):
            for p in range(NPAIR):
                ins = t.matmul(out=M.ps_B[:, pc(p)], lhsT=M.ones_sq_b[:], rhs=M.scrb[:, pc(p)],
                         start=True, stop=True)
            return ins
        tv = P.emit("t", t_rms, waits=self.w_waits("B") + [("sa", av_)], inc="st")
        self.set_w("B", tv)

        def v_rms(v):
            return v.tensor_scalar(out=M.scrf[:], in0=M.ps_B[:], scalar1=1.0 / HD,
                                   scalar2=1e-6, op0=MU, op1=AD)
        vv = P.emit("v", v_rms, waits=self.r_waits("B"), inc="sv")
        self.set_r("B", "sv", vv)
        av_ = P.emit("a", lambda a: a.activation(out=M.scrf[:], in_=M.scrf[:], func=AF.Sqrt),
                     waits=[("sv", vv)], inc="sa")
        vv = P.emit("v", lambda v: v.reciprocal(out=M.scrf[:], in_=M.scrf[:]),
                    waits=[("sa", av_)], inc="sv")

        def v_on(v):
            return v.scalar_tensor_tensor(out=M.scrb[:], in0=M.ps_Z[:], scalar=M.onw_col[:],
                                          in1=M.scrf[:], op0=MU, op1=MU)
        on_v = P.emit("v", v_on, waits=[("sv", vv)] + self.r_waits("Z"), inc="sv")
        self.set_r("Z", "sv", on_v)

        def g_o(g):
            ins = None
            for p in range(NPAIR):
                b, h = p // HPC, p % HPC
                ins = g.dma_start(out=o_dram[h * 128:(h + 1) * 128,
                                             b * b_stride + c * C: b * b_stride + (c + 1) * C],
                                  in_=M.scrb[:, pc(p)])
                ins.then_inc(P.sems["do"], 16)
            return ins
        P.emit("g", g_o, waits=[("sv", on_v)])
        o_done = P.bump("do", 16 * NPAIR)

        if DBG_STAGE < 6:
            self.hstate_v = self.prog.val("sv")
            return None
        def v_kt(v):
            for p in range(NPAIR):
                h = v if p == 0 else v.raw
                ins = h.tensor_scalar(out=M.Ktil[:, pc(p)], in0=ak[:, pc(p)],
                                scalar1=M.w_all[:, p:p + 1], scalar2=None, op0=MU)
            return ins
        kt_v = P.emit("v", v_kt, waits=[("sv", w_v), ("sv", on_v)], inc="sv")

        def t_kk(t):
            for p in range(NPAIR):
                ins = t.matmul(out=M.ps_Z[:, pc(p)], lhsT=M.Ktil[:, pc(p)], rhs=ak[:, pc(p)],
                         start=True, stop=True)
            return ins
        tv = P.emit("t", t_kk, waits=self.w_waits("Z") + [("sv", kt_v)], inc="st")
        self.set_w("Z", tv)

        def v_hkk(v):
            for p in range(NPAIR):
                h = v if p == 0 else v.raw
                ins = h.scalar_tensor_tensor(out=M.Hkk_f[:, pc(p)], in0=M.Hkk_f[:, pc(p)],
                                       scalar=M.dec_bc[:, p:p + 1], in1=M.ps_Z[:, pc(p)],
                                       op0=MU, op1=AD)
            return ins
        vv = P.emit("v", v_hkk, waits=self.r_waits("Z"), inc="sv")
        self.set_r("Z", "sv", vv)
        hkkb_a = P.emit("a", lambda a: a.activation(out=M.Hkk_b[:], in_=M.Hkk_f[:], func=AF.Copy),
                        waits=[("sv", vv)], inc="sa")

        def v_vt(v):
            for p in range(NPAIR):
                h = v if p == 0 else v.raw
                ins = h.tensor_scalar(out=M.Ktil[:, pc(p)], in0=av[:, pc(p)],
                                scalar1=M.w_all[:, p:p + 1], scalar2=None, op0=MU)
            return ins
        vt_v = P.emit("v", v_vt, waits=[("sv", vv)], inc="sv")

        def t_vk(t):
            for p in range(NPAIR):
                ins = t.matmul(out=M.ps_B[:, pc(p)], lhsT=M.Ktil[:, pc(p)], rhs=ak[:, pc(p)],
                         start=True, stop=True)
            return ins
        tv = P.emit("t", t_vk, waits=self.w_waits("B") + [("sv", vt_v)], inc="st")
        self.set_w("B", tv)

        def v_hvk(v):
            for p in range(NPAIR):
                h = v if p == 0 else v.raw
                ins = h.scalar_tensor_tensor(out=M.Hvk_f[:, pc(p)], in0=M.Hvk_f[:, pc(p)],
                                       scalar=M.dec_bc[:, p:p + 1], in1=M.ps_B[:, pc(p)],
                                       op0=MU, op1=AD)
            return ins
        vv2 = P.emit("v", v_hvk, waits=self.r_waits("B"), inc="sv")
        self.set_r("B", "sv", vv2)
        hvkb_a = P.emit("a", lambda a: a.activation(out=M.Hvk_b[:], in_=M.Hvk_f[:], func=AF.Copy),
                        waits=[("sv", vv2)], inc="sa")

        def v_wk(v):
            for p in range(NPAIR):
                h = v if p == 0 else v.raw
                ins = h.tensor_scalar(out=M.scrb[:, pc(p)], in0=M.Ksq[:, pc(p)],
                                scalar1=M.w_all[:, p:p + 1], scalar2=None, op0=MU)
            return ins
        wk_v = P.emit("v", v_wk, waits=[("sv", vv2), ("do", o_done)], inc="sv")

        def t_hd(t):
            for p in range(NPAIR):
                ins = t.matmul(out=M.ps_Z[0:1, pc(p)], lhsT=M.ones_col_b[:], rhs=M.scrb[:, pc(p)],
                         start=True, stop=True)
            return ins
        tv = P.emit("t", t_hd, waits=self.w_waits("Z") + [("sv", wk_v)], inc="st")
        self.set_w("Z", tv)

        def v_hd(v):
            for p in range(NPAIR):
                h = v if p == 0 else v.raw
                ins = h.scalar_tensor_tensor(out=M.Hd_row[0:1, pc(p)], in0=M.Hd_row[0:1, pc(p)],
                                       scalar=M.decs_row[0:1, p:p + 1], in1=M.ps_Z[0:1, pc(p)],
                                       op0=MU, op1=AD)
            return ins
        vv3 = P.emit("v", v_hd, waits=self.r_waits("Z"), inc="sv")
        self.set_r("Z", "sv", vv3)

        self.hstate_v = vv3
        self.hstate_a = max(hkkb_a, hvkb_a)
        return o_done



def build_mesa_test(T):
    nc = bass.Bass("TRN2", target_bir_lowering=False, debug=False, detect_race_conditions=False)
    nchunk = T // C
    qn_d = nc.dram_tensor("qn", [4 * T, 2 * HD], F32, kind="ExternalInput").ap()
    kn_d = nc.dram_tensor("kn", [4 * T, 2 * HD], BF16, kind="ExternalInput").ap()
    vn_d = nc.dram_tensor("vn", [4 * T, 2 * HD], BF16, kind="ExternalInput").ap()
    g_d = nc.dram_tensor("g", [4 * T, HPC], F32, kind="ExternalInput").ap()
    bt_d = nc.dram_tensor("bt", [4 * T, HPC], F32, kind="ExternalInput").ap()
    lamp_d = nc.dram_tensor("lamp", [1, HPC * 128], F32, kind="ExternalInput").ap()
    onw_d = nc.dram_tensor("onw", [1, 128], F32, kind="ExternalInput").ap()
    o_d = nc.dram_tensor("o", [HPC * 128, 4 * T], BF16, kind="ExternalOutput").ap()
    dbg_d = nc.dram_tensor("dbg", [11 * 128, 128], F32, kind="ExternalOutput").ap()

    prog = Prog()
    with ExitStack() as ctx:
        M = Mesa(nc, ctx, prog, T)
        M.dbg_d = dbg_d
        sem_names = ["st", "sv", "sa", "sg", "dlc", "dl0", "dl1", "do", "dsx", "dbb", "dst", "dr2", "ddy", "do0", "do1"]
        sems = {n: ctx.enter_context(nc.semaphore(n)) for n in sem_names}
        M.emit_consts(lamp_d, onw_d)

        for cidx in range(min(nchunk, DBG_NCHUNK)):
            z = cidx % 2
            dl_sem = f"dl{z}"
            def s_ld(s, z=z, cidx=cidx, dl_sem=dl_sem):
                ins = None
                for b in range(4):
                    r0 = b * T + cidx * C
                    for (dst, srcd) in ((M.aq[z], qn_d), (M.ak[z], kn_d), (M.av[z], vn_d)):
                        ins = s.dma_start(out=dst[:, b * 256:(b + 1) * 256],
                                          in_=srcd[r0:r0 + C, :])
                        ins.then_inc(prog.sems[dl_sem], 16)
                    ins = s.dma_start(out=M.ag[z][:, b * HPC:(b + 1) * HPC], in_=g_d[r0:r0 + C, :])
                    ins.then_inc(prog.sems[dl_sem], 16)
                    ins = s.dma_start(out=M.ab[z][:, b * HPC:(b + 1) * HPC], in_=bt_d[r0:r0 + C, :])
                    ins.then_inc(prog.sems[dl_sem], 16)
                return ins
            w = [("sv", M.hstate_v)] if cidx >= 2 else []
            prog.emit("s", s_ld, waits=w)
            dl_val = prog.bump(dl_sem, 16 * 20)
            M.emit_chunk(z, cidx, o_d, dl_sem, dl_val, T)

        prog.emit("g", lambda g: g.wait_ge(sems["do"], prog.val("do")) or g.nop())

        block = ctx.enter_context(nc.Block())
        prog.realize(block, sems)
    return nc


def mesa_test_run(qn, kn, vn, g, bt, lamp, onw):
    T = qn.shape[0] // 4
    nc = build_mesa_test(T)
    import ml_dtypes
    in_map = dict(qn=qn.astype(np.float32), kn=kn.astype(ml_dtypes.bfloat16),
                  vn=vn.astype(ml_dtypes.bfloat16), g=g.astype(np.float32),
                  bt=bt.astype(np.float32), lamp=lamp.astype(np.float32),
                  onw=onw.astype(np.float32))
    res = run_bass_kernel_spmd(nc, [in_map], core_ids=[0])
    return np.asarray(res.results[0]["o"].astype(np.float32))



ROWS = 16384
T_FULL = 4096
RPC = 2048


def emit_p12(nc, prog, M, ctx, xT, xTr, wcat, babb, st_in, st_all,
             q_d, k_d, v_d, g_d, bt_d):
    P = prog
    S = lambda nm, sh, dt=F32: ctx.enter_context(nc.sbuf_tensor(nm, sh, dt))
    stg = S("p2_stg", [128, 772])
    wcb = S("p2_wcb", [128, 16 * 772], BF16)
    sbx = [S(f"p2_sbx{z}", [128, 16 * 128], F32) for z in range(2)]
    sbxb = [S(f"p2_sbxb{z}", [128, 16 * 128], BF16) for z in range(2)]
    sbig = S("p2_sbig", [128, 512])
    sbig2 = S("p2_sbig2", [128, 512])
    negmu_stg = S("p2_negmu", [1, RPC])
    rs_stg = S("p2_rsstg", [1, RPC])
    rs_cols = S("p2_rscols", [128, 128])
    csw = S("p2_csw", [1, 772])
    qsil = [S(f"p2_qsil{z}", [128, 512]) for z in range(2)]
    sq = [S(f"p2_sq{z}", [128, 512]) for z in range(2)]
    rn4 = [S(f"p2_rn4{z}", [128, 4]) for z in range(2)]
    qout = [S(f"p2_qout{z}", [128, 256]) for z in range(2)]
    kout = [S(f"p2_kout{z}", [128, 256], BF16) for z in range(2)]
    vout = [S(f"p2_vout{z}", [128, 256], BF16) for z in range(2)]
    abt = [S(f"p2_abt{z}", [128, 4]) for z in range(2)]
    gout = [S(f"p2_gout{z}", [128, HPC]) for z in range(2)]
    btout = [S(f"p2_btout{z}", [128, HPC]) for z in range(2)]
    bias_bc = S("p2_biasbc", [128, 4])
    babb_r = S("p2_babbr", [1, 4])
    strow = S("p2_strow", [1, 2 * RPC])

    consts_v = M.consts_done

    for dt in range(16):
        def s_w(s, dt=dt):
            ins = s.dma_start(out=stg[:], in_=wcat[dt * 128:(dt + 1) * 128, :])
            ins.then_inc(P.sems["dlc"], 16)
            return ins
        P.emit("s", s_w, waits=[("sv", P.val("sv")), ("st", P.val("st"))])
        wv = P.bump("dlc", 16)
        vv = P.emit("v", lambda v, dt=dt: v.tensor_copy(out=wcb[:, dt * 772:(dt + 1) * 772], in_=stg[:]),
                    waits=[("dlc", wv)], inc="sv")
        def t_cs(t, dt=dt):
            t.matmul(out=M.ps_B[0:1, 0:512], lhsT=M.ones_col_f[:], rhs=stg[:, 0:512],
                     start=(dt == 0), stop=(dt == 15))
            return t.matmul(out=M.ps_B[0:1, 512:772], lhsT=M.ones_col_f[:], rhs=stg[:, 512:772],
                            start=(dt == 0), stop=(dt == 15))
        tv = P.emit("t", t_cs, waits=[("dlc", wv), ("sv", consts_v)] + (M.w_waits("B") if dt == 0 else []), inc="st")
    M.set_w("B", tv)
    vv = P.emit("v", lambda v: v.tensor_copy(out=csw[:], in_=M.ps_B[0:1, 0:772]),
                waits=M.r_waits("B"), inc="sv")
    M.set_r("B", "sv", vv)

    def s_bb(s):
        ins = s.dma_start(out=babb_r[:], in_=babb)
        ins.then_inc(P.sems["dbb"], 16)
        return ins
    P.emit("s", s_bb)
    bbv = P.bump("dbb", 16)
    tv = P.emit("t", lambda t: t.matmul(out=M.ps_T[:, 0:4], lhsT=M.ones_row_f[:], rhs=babb_r[:],
                                        start=True, stop=True),
                waits=[("dbb", bbv), ("sv", consts_v)] + M.w_waits("T"), inc="st")
    M.set_w("T", tv)
    vv = P.emit("v", lambda v: v.tensor_copy(out=bias_bc[:], in_=M.ps_T[:, 0:4]),
                waits=M.r_waits("T"), inc="sv")
    M.set_r("T", "sv", vv)

    for rb in range(4):
        for dt in range(16):
            def s_x(s, rb=rb, dt=dt):
                ins = s.dma_start(out=sbig[:], in_=xTr[dt * 128:(dt + 1) * 128,
                                                      rb * 512:(rb + 1) * 512])
                ins.then_inc(P.sems["dsx"], 16)
                return ins
            P.emit("s", s_x, waits=[("sa", P.val("sa")), ("st", P.val("st")),
                                    ("sv", P.val("sv"))])
            xv = P.bump("dsx", 16)
            av = P.emit("a", lambda a: a.activation(out=sbig2[:], in_=sbig[:], func=AF.Square),
                        waits=[("dsx", xv)], inc="sa")
            def t_st(t, rb=rb, dt=dt):
                t.matmul(out=M.ps_S[0:1, 0:512], lhsT=M.ones_col_f[:], rhs=sbig[:],
                         start=(dt == 0), stop=(dt == 15))
                return t.matmul(out=M.ps_B[0:1, 0:512], lhsT=M.ones_col_f[:], rhs=sbig2[:],
                                start=(dt == 0), stop=(dt == 15))
            tv = P.emit("t", t_st, waits=[("dsx", xv), ("sa", av)]
                        + (M.w_waits("S", "B") if dt == 0 else []), inc="st")
        M.set_w("S", tv)
        M.set_w("B", tv)

        def v_fin(v, rb=rb):
            v.tensor_scalar(out=strow[0:1, rb * 512:(rb + 1) * 512], in0=M.ps_S[0:1, 0:512],
                            scalar1=1.0 / HID, scalar2=None, op0=MU)
            v.tensor_scalar(out=sbig[0:1, 0:512], in0=M.ps_B[0:1, 0:512],
                            scalar1=1.0 / HID, scalar2=None, op0=MU)
            v.tensor_tensor(out=sbig2[0:1, 0:512], in0=strow[0:1, rb * 512:(rb + 1) * 512],
                            in1=strow[0:1, rb * 512:(rb + 1) * 512], op=MU)
            v.tensor_tensor(out=sbig[0:1, 0:512], in0=sbig[0:1, 0:512], in1=sbig2[0:1, 0:512], op=SU)
            return v.tensor_scalar(out=sbig[0:1, 0:512], in0=sbig[0:1, 0:512],
                                   scalar1=1e-5, scalar2=None, op0=AD)
        vv = P.emit("v", v_fin, waits=M.r_waits("S", "B"), inc="sv")
        M.set_r("S", "sv", vv)
        M.set_r("B", "sv", vv)
        av = P.emit("a", lambda a, rb=rb: a.activation(out=sbig[0:1, 0:512], in_=sbig[0:1, 0:512],
                                                       func=AF.Sqrt),
                    waits=[("sv", vv)], inc="sa")
        vv = P.emit("v", lambda v, rb=rb: v.reciprocal(out=strow[0:1, RPC + rb * 512:RPC + (rb + 1) * 512],
                                                       in_=sbig[0:1, 0:512]),
                    waits=[("sa", av)], inc="sv")
    stats_v = vv

    def g_st(g):
        ins = g.dma_start(out=st_in[:, :], in_=strow[0:1, :])
        ins.then_inc(P.sems["do"], 16)
        return ins
    P.emit("g", g_st, waits=[("sv", stats_v)])
    sv1 = P.bump("do", 16)
    def g_ag(g):
        return g.collective_compute("AllGather", mybir.AluOpType.bypass,
                                    replica_groups=[list(range(NCORES))],
                                    ins=[st_in[:, :].opt()], outs=[st_all[:, :].opt()])
    agv = P.emit("g", g_ag, waits=[("do", sv1)], inc="cc")
    tv = None
    for cb in range(8):
        def g_rd(g, cb=cb):
            ins = g.dma_start(out=rs_stg[:], in_=st_all[2 * cb + 1:2 * cb + 2, :])
            ins.then_inc(P.sems["dst"], 16)
            return ins
        P.emit("g", g_rd, waits=[("cc", agv), ("st", tv or 0)])
        stv = P.bump("dst", 16)
        def t_rsc(t, cb=cb):
            ins = None
            for j in range(16):
                ins = t.transpose(out=M.ps_T[:, cb * 16 + j:cb * 16 + j + 1],
                                  in_=rs_stg[0:1, j * 128:(j + 1) * 128],
                                  identity=M.ones_row_f[0:1, 0:1])
            return ins
        tv = P.emit("t", t_rsc, waits=[("dst", stv)] + (M.w_waits("T") if cb == 0 else []), inc="st")
    M.set_w("T", tv)
    vv = P.emit("v", lambda v: v.tensor_copy(out=rs_cols[:], in_=M.ps_T[:, 0:128]),
                waits=M.r_waits("T"), inc="sv")
    M.set_r("T", "sv", vv)
    rsc_v = vv

    wcast_v = P.val("sv")
    negmu_v = 0
    banks = ["Z", "B"]
    psq = [M.ps_Z, M.ps_B]
    cast_prev = [0, 0]
    xv_pre = [0, 0]
    tv_prev = [0, 0]
    sil_prev = [0, 0]
    ag_prev = [0, 0]
    g2_prev = [0, 0]
    do_prev = [0, 0]
    for rt in range(128):
        z = rt % 2
        dl_sem = f"dl{z}"
        bank = banks[z]
        ps = psq[z]
        if rt % 16 == 0:
            cb = rt // 16
            def g_mu(g, cb=cb):
                ins = g.dma_start(out=rs_stg[:], in_=st_all[2 * cb:2 * cb + 1, :])
                ins.then_inc(P.sems["dst"], 16)
                return ins
            P.emit("g", g_mu, waits=[("cc", agv), ("st", P.val("st"))])
            muv = P.bump("dst", 16)
            negmu_v = P.emit("v", lambda v: v.tensor_scalar(out=negmu_stg[:], in0=rs_stg[:],
                                                            scalar1=-1.0, scalar2=None, op0=MU),
                             waits=[("dst", muv), ("sv", rsc_v)], inc="sv")
        def s_xt(s, rt=rt, z=z, dl_sem=dl_sem):
            ins = None
            for dt in range(16):
                ins = s.dma_start(out=sbx[z][:, dt * 128:(dt + 1) * 128],
                                  in_=xT[dt * 128:(dt + 1) * 128, rt * 128:(rt + 1) * 128])
                ins.then_inc(P.sems[dl_sem], 16)
            return ins
        P.emit("s", s_xt, waits=[("sv", cast_prev[z])])
        xv_pre[z] = P.bump(dl_sem, 16 * 16)
        cast_prev[z] = P.emit("v", lambda v, z=z: v.tensor_copy(out=sbxb[z][:], in_=sbx[z][:]),
                              waits=[(dl_sem, xv_pre[z]), ("sv", wcast_v), ("st", tv_prev[z])],
                              inc="sv")
        def t_pj(t, rt=rt, z=z, ps=ps):
            for dt in range(16):
                t.matmul(out=ps[:, 0:512], lhsT=sbxb[z][:, dt * 128:(dt + 1) * 128],
                         rhs=wcb[:, dt * 772:dt * 772 + 512], start=(dt == 0), stop=False)
                t.matmul(out=ps[:, 512:772], lhsT=sbxb[z][:, dt * 128:(dt + 1) * 128],
                         rhs=wcb[:, dt * 772 + 512:(dt + 1) * 772], start=(dt == 0), stop=False)
            rtl = rt % 16
            t.matmul(out=ps[:, 0:512], lhsT=negmu_stg[0:1, rtl * 128:(rtl + 1) * 128],
                     rhs=csw[0:1, 0:512], start=False, stop=True)
            return t.matmul(out=ps[:, 512:772], lhsT=negmu_stg[0:1, rtl * 128:(rtl + 1) * 128],
                            rhs=csw[0:1, 512:772], start=False, stop=True)
        tv = P.emit("t", t_pj, waits=[("sv", cast_prev[z]), ("sv", negmu_v), ("sv", rsc_v),
                                      ("sa", sil_prev[z])] + M.w_waits(bank),
                    inc="st")
        M.set_w(bank, tv)
        tv_prev[z] = tv

        def a_sil(a, rt=rt, z=z, ps=ps):
            a.activation(out=qsil[z][:], in_=ps[:, 0:512], func=AF.Silu,
                         scale=rs_cols[:, rt:rt + 1])
            a.activation(out=vout[z][:], in_=ps[:, 512:768], func=AF.Silu,
                         scale=rs_cols[:, rt:rt + 1])
            return a.activation(out=sq[z][:], in_=qsil[z][:], func=AF.Square)
        av = P.emit("a", a_sil, waits=M.r_waits(bank) + [(f"do{z}", do_prev[z]), ("sv", g2_prev[z])],
                    inc="sa")
        sil_prev[z] = av
        def v_ab(v, rt=rt, z=z, ps=ps):
            return v.tensor_scalar(out=abt[z][:], in0=ps[:, 768:772],
                                   scalar1=rs_cols[:, rt:rt + 1], scalar2=None, op0=MU)
        abv = P.emit("v", v_ab, waits=M.r_waits(bank) + [("sa", ag_prev[z]), ("sv", g2_prev[z])],
                     inc="sv")
        M.set_r(bank, "sv", abv)

        def v_l2(v, z=z):
            return v.tensor_reduce(out=rn4[z][:], in_=sq[z][:].rearrange("p (s d) -> p s d", s=4),
                                   axis=AX, op=AD)
        lv = P.emit("v", v_l2, waits=[("sa", av)], inc="sv")
        av2 = P.emit("a", lambda a, z=z: a.activation(out=rn4[z][:], in_=rn4[z][:], func=AF.Sqrt),
                     waits=[("sv", lv)], inc="sa")
        def v_rn(v, z=z):
            v.tensor_scalar(out=rn4[z][:], in0=rn4[z][:], scalar1=1e-12, scalar2=None, op0=MX)
            return v.reciprocal(out=rn4[z][:], in_=rn4[z][:])
        rv = P.emit("v", v_rn, waits=[("sa", av2)], inc="sv")
        def v_sc(v, z=z):
            for sgi in range(2):
                v.tensor_scalar(out=qout[z][:, sgi * 128:(sgi + 1) * 128],
                                in0=qsil[z][:, sgi * 128:(sgi + 1) * 128],
                                scalar1=rn4[z][:, sgi:sgi + 1], scalar2=None, op0=MU)
            for sgi in range(2):
                v.tensor_scalar(out=kout[z][:, sgi * 128:(sgi + 1) * 128],
                                in0=qsil[z][:, 256 + sgi * 128:256 + (sgi + 1) * 128],
                                scalar1=rn4[z][:, 2 + sgi:3 + sgi], scalar2=None, op0=MU)
            v.tensor_tensor(out=abt[z][:], in0=abt[z][:], in1=bias_bc[:], op=AD)
            v.tensor_scalar(out=gout[z][:], in0=abt[z][:, 0:2], scalar1=0.0, scalar2=None, op0=MI)
            v.tensor_scalar(out=sq[z][0:128, 0:2], in0=abt[z][:, 0:2], scalar1=-1.0, scalar2=None, op0=MU)
            return v.tensor_tensor(out=sq[z][0:128, 2:4], in0=abt[z][:, 0:2], in1=sq[z][0:128, 0:2], op=MI)
        scv = P.emit("v", v_sc, waits=[("sv", rv), ("sa", ag_prev[z]), (f"do{z}", do_prev[z])], inc="sv")
        def a_g(a, z=z):
            a.activation(out=sq[z][0:128, 4:6], in_=sq[z][0:128, 2:4], func=AF.Exp)
            a.activation(out=sq[z][0:128, 4:6], in_=sq[z][0:128, 4:6], func=AF.Ln,
                         bias=M.ones_col_f[:])
            return a.activation(out=btout[z][:], in_=abt[z][:, 2:4], func=AF.Sigmoid)
        agv2 = P.emit("a", a_g, waits=[("sv", scv)], inc="sa")
        ag_prev[z] = agv2
        def v_g2(v, z=z):
            return v.tensor_tensor(out=gout[z][:], in0=gout[z][:], in1=sq[z][0:128, 4:6], op=SU)
        gv2 = P.emit("v", v_g2, waits=[("sa", agv2)], inc="sv")
        g2_prev[z] = gv2

        def g_out(g, rt=rt, z=z):
            ins = None
            for (dst, srct) in ((q_d, qout[z]), (k_d, kout[z]), (v_d, vout[z]),
                                (g_d, gout[z]), (bt_d, btout[z])):
                ins = g.dma_start(out=dst[rt * 128:(rt + 1) * 128, :], in_=srct[:])
                ins.then_inc(P.sems[f"do{z}"], 16)
            return ins
        P.emit("g", g_out, waits=[("sv", gv2), ("sa", agv2)])
        P.bump(f"do{z}", 16 * 5)
        do_prev[z] = P.val(f"do{z}")
    return (P.val("do0"), P.val("do1"))


def emit_p4a(nc, prog, ctx4, xTr, wo, o_all, x1_d, x1b_d, ln2st_d, bw):
    P = prog
    S = lambda nm, sh, dt=F32: ctx4.enter_context(nc.sbuf_tensor(nm, sh, dt))
    P_ = lambda nm, sh, dt=F32: ctx4.enter_context(nc.psum_tensor(nm, sh, dt))
    wob = S("p4_wob", [128, 16 * 2048], BF16)
    stg = S("p4_stg", [128, 2048])
    oT = S("p4_oT", [128, 16 * 512], BF16)
    xa = S("p4_xa", [128, 16 * 512])
    atn = S("p4_atn", [128, 16 * 512], BF16)
    sqt = S("p4_sqt", [128, 512], BF16)
    mu_bc = S("p4_mubc", [128, 512])
    rs_bc = S("p4_rsbc", [128, 512])
    srow = S("p4_srow", [1, 1024])
    t1 = S("p4_t1", [128, 512])
    x1t = S("p4_x1t", [128, 512])
    x1bt = S("p4_x1bt", [128, 512], BF16)
    ones_col_f = S("p4_onescf", [128, 1])
    ones_col_b = S("p4_onescb", [128, 1], BF16)
    ones_row_f = S("p4_onesrf", [1, 128])
    ps1 = P_("p4_ps1", [128, 512])
    ps2 = P_("p4_ps2", [128, 512])
    pss = P_("p4_pss", [1, 2048])
    rd = {"1": ("sv", 0), "2": ("sv", 0), "s": ("sv", 0)}
    wr = {"1": 0, "2": 0, "s": 0}
    bar = bw

    gv = P.emit("g", lambda g: (g.memset(ones_col_f[:], 1.0), g.memset(ones_col_b[:], 1.0),
                                g.memset(ones_row_f[:], 1.0))[-1], waits=bar, inc="sg")

    cast_v = 0
    for dt in range(16):
        def s_w(s, dt=dt):
            ins = s.dma_start(out=stg[:], in_=wo[dt * 128:(dt + 1) * 128, :])
            ins.then_inc(P.sems["dlc"], 16)
            return ins
        P.emit("s", s_w, waits=bar + [("sv", cast_v)])
        wv = P.bump("dlc", 16)
        cast_v = P.emit("v", lambda v, dt=dt: v.tensor_copy(out=wob[:, dt * 2048:(dt + 1) * 2048],
                                                            in_=stg[:]),
                        waits=bar + [("dlc", wv)], inc="sv")

    ln2_parts = []
    ep_v = 0
    prev_l2do = 0
    for r in range(4):
        def s_o(g, r=r):
            pid = g.partition_id()
            src3 = o_all.rearrange("(dt p) r -> p dt r", p=128)[:, :, bass.ds(pid * RPC + r * 512, 512)]
            ins = g.dma_start(out=oT[:].rearrange("p (dt c) -> p dt c", dt=16), in_=src3)
            ins.then_inc(P.sems["ddy"], 16)
            return ins
        P.emit("g", s_o, waits=bar + [("sv", ep_v)])
        ov = P.bump("ddy", 16)
        def s_xa(s, r=r):
            ins = None
            for dt in range(16):
                ins = s.dma_start(out=xa[:, dt * 512:(dt + 1) * 512],
                                  in_=xTr[dt * 128:(dt + 1) * 128, r * 512:(r + 1) * 512])
                ins.then_inc(P.sems["dl1"], 16)
            return ins
        P.emit("s", s_xa, waits=bar + [("sv", ep_v)])
        xv = P.bump("dl1", 16 * 16)

        st_tv = None
        for cot in range(16):
            key = "1" if cot % 2 == 0 else "2"
            psx = ps1 if cot % 2 == 0 else ps2
            def t_wo2(t, cot=cot, psx=psx):
                ins = None
                for dt in range(16):
                    ins = t.matmul(out=psx[:], lhsT=wob[:, dt * 2048 + cot * 128:dt * 2048 + (cot + 1) * 128],
                                   rhs=oT[:, dt * 512:(dt + 1) * 512], start=(dt == 0), stop=(dt == 15))
                return ins
            tv = P.emit("t", t_wo2, waits=[("ddy", ov), ("sv", cast_v)] + [rd[key]], inc="st")
            wr[key] = tv
            vv = P.emit("v", lambda v, cot=cot, psx=psx: v.tensor_copy(out=atn[:, cot * 512:(cot + 1) * 512],
                                                                       in_=psx[:]),
                        waits=[("st", wr[key])], inc="sv")
            rd[key] = ("sv", vv)
            av = P.emit("a", lambda a, cot=cot: a.activation(out=sqt[:], in_=atn[:, cot * 512:(cot + 1) * 512],
                                                             func=AF.Square),
                        waits=[("sv", vv), ("sa", P.val("sa")), ("st", st_tv or wr["s"])], inc="sa")
            def t_st(t, cot=cot):
                t.matmul(out=pss[0:1, 0:512], lhsT=ones_col_b[:], rhs=atn[:, cot * 512:(cot + 1) * 512],
                         start=(cot == 0), stop=(cot == 15))
                return t.matmul(out=pss[0:1, 512:1024], lhsT=ones_col_b[:], rhs=sqt[:],
                                start=(cot == 0), stop=(cot == 15))
            st_tv = P.emit("t", t_st, waits=[("sv", vv), ("sa", av)] + ([rd["s"]] if cot == 0 else []),
                           inc="st")
        wr["s"] = st_tv

        def v_fin(v):
            v.tensor_scalar(out=srow[0:1, 0:512], in0=pss[0:1, 0:512], scalar1=1.0 / HID, scalar2=None, op0=MU)
            v.tensor_scalar(out=srow[0:1, 512:1024], in0=pss[0:1, 512:1024], scalar1=1.0 / HID, scalar2=None, op0=MU)
            v.tensor_tensor(out=t1[0:1, :], in0=srow[0:1, 0:512], in1=srow[0:1, 0:512], op=MU)
            v.tensor_tensor(out=srow[0:1, 512:1024], in0=srow[0:1, 512:1024], in1=t1[0:1, :], op=SU)
            return v.tensor_scalar(out=srow[0:1, 512:1024], in0=srow[0:1, 512:1024], scalar1=1e-5, scalar2=None, op0=AD)
        vv = P.emit("v", v_fin, waits=[("st", wr["s"]), ("do", prev_l2do)], inc="sv")
        rd["s"] = ("sv", vv)
        av = P.emit("a", lambda a: a.activation(out=srow[0:1, 512:1024], in_=srow[0:1, 512:1024], func=AF.Sqrt),
                    waits=[("sv", vv)], inc="sa")
        vv = P.emit("v", lambda v: v.reciprocal(out=srow[0:1, 512:1024], in_=srow[0:1, 512:1024]),
                    waits=[("sa", av)], inc="sv")
        def t_bc(t):
            t.matmul(out=ps1[:], lhsT=ones_row_f[:], rhs=srow[0:1, 0:512], start=True, stop=True)
            return t.matmul(out=ps2[:], lhsT=ones_row_f[:], rhs=srow[0:1, 512:1024], start=True, stop=True)
        tv = P.emit("t", t_bc, waits=[("sv", vv)] + [rd["1"], rd["2"]], inc="st")
        wr["1"] = wr["2"] = tv
        def v_bc(v):
            v.tensor_copy(out=mu_bc[:], in_=ps1[:])
            return v.tensor_copy(out=rs_bc[:], in_=ps2[:])
        vv = P.emit("v", v_bc, waits=[("st", tv)], inc="sv")
        rd["1"] = rd["2"] = ("sv", vv)
        bc_v = vv

        ln2_tv = None
        for cot in range(16):
            def v_x1(v, cot=cot):
                v.tensor_tensor(out=t1[:], in0=atn[:, cot * 512:(cot + 1) * 512], in1=mu_bc[:], op=SU)
                v.tensor_tensor(out=t1[:], in0=t1[:], in1=rs_bc[:], op=MU)
                v.tensor_tensor(out=x1t[:], in0=t1[:], in1=xa[:, cot * 512:(cot + 1) * 512], op=AD)
                return v.tensor_copy(out=x1bt[:], in_=x1t[:])
            vv = P.emit("v", v_x1, waits=[("sv", bc_v), ("dl1", xv), ("do", P.val("do"))], inc="sv")
            av = P.emit("a", lambda a: a.activation(out=sqt[:], in_=x1bt[:], func=AF.Square),
                        waits=[("sv", vv), ("st", ln2_tv or st_tv)], inc="sa")
            def t_l2(t, cot=cot):
                t.matmul(out=pss[0:1, 1024:1536], lhsT=ones_col_b[:], rhs=x1bt[:],
                         start=(cot == 0), stop=(cot == 15))
                return t.matmul(out=pss[0:1, 1536:2048], lhsT=ones_col_b[:], rhs=sqt[:],
                                start=(cot == 0), stop=(cot == 15))
            ln2_tv = P.emit("t", t_l2, waits=[("sv", vv), ("sa", av)], inc="st")
            def g_x1(g, cot=cot, r=r):
                ins = g.dma_start(out=x1_d[cot * 128:(cot + 1) * 128, r * 512:(r + 1) * 512], in_=x1t[:])
                ins.then_inc(P.sems["do"], 16)
                ins = g.dma_start(out=x1b_d[cot * 128:(cot + 1) * 128, r * 512:(r + 1) * 512], in_=x1bt[:])
                ins.then_inc(P.sems["do"], 16)
                return ins
            P.emit("g", g_x1, waits=[("st", ln2_tv)])
            P.bump("do", 32)

        def v_f2(v, r=r):
            v.tensor_scalar(out=srow[0:1, 0:512], in0=pss[0:1, 1024:1536], scalar1=1.0 / HID, scalar2=None, op0=MU)
            v.tensor_scalar(out=srow[0:1, 512:1024], in0=pss[0:1, 1536:2048], scalar1=1.0 / HID, scalar2=None, op0=MU)
            v.tensor_tensor(out=t1[0:1, :], in0=srow[0:1, 0:512], in1=srow[0:1, 0:512], op=MU)
            v.tensor_tensor(out=srow[0:1, 512:1024], in0=srow[0:1, 512:1024], in1=t1[0:1, :], op=SU)
            return v.tensor_scalar(out=srow[0:1, 512:1024], in0=srow[0:1, 512:1024], scalar1=1e-5, scalar2=None, op0=AD)
        vv = P.emit("v", v_f2, waits=[("st", ln2_tv)], inc="sv")
        av = P.emit("a", lambda a: a.activation(out=srow[0:1, 512:1024], in_=srow[0:1, 512:1024], func=AF.Sqrt),
                    waits=[("sv", vv)], inc="sa")
        vv = P.emit("v", lambda v: v.reciprocal(out=srow[0:1, 512:1024], in_=srow[0:1, 512:1024]),
                    waits=[("sa", av)], inc="sv")
        def g_l2(g, r=r):
            ins = g.dma_start(out=ln2st_d[0:1, r * 512:(r + 1) * 512], in_=srow[0:1, 0:512])
            ins.then_inc(P.sems["do"], 16)
            ins = g.dma_start(out=ln2st_d[0:1, 2048 + r * 512:2048 + (r + 1) * 512], in_=srow[0:1, 512:1024])
            ins.then_inc(P.sems["do"], 16)
            return ins
        P.emit("g", g_l2, waits=[("sv", vv)])
        P.bump("do", 32)
        prev_l2do = P.val("do")
        ep_v = vv
    return P.val("do")


def emit_p4b(nc, prog, ctx4, x1_d, x1b_d, ln2st_d, w1, b1, w2, b2, ucat, vcatT, y, bw):
    P = prog
    S = lambda nm, sh, dt=F32: ctx4.enter_context(nc.sbuf_tensor(nm, sh, dt))
    P_ = lambda nm, sh, dt=F32: ctx4.enter_context(nc.psum_tensor(nm, sh, dt))
    w1b = S("pb_w1b", [128, 16 * 2048], BF16)
    ucb = S("pb_ucb", [128, 4 * 2048], BF16)
    vcb = S("pb_vcb", [128, 16 * 512], BF16)
    w2b = S("pb_w2b", [128, 16 * 8], BF16)
    stg = S("pb_stg", [128, 2048])
    x1b = S("pb_x1b", [128, 16 * 512], BF16)
    h_bf = S("pb_hbf", [128, 16 * 512], BF16)
    vxt = S("pb_vxt", [128, 4 * 512], BF16)
    ffn_bf = S("pb_ffnbf", [128, 16 * 512], BF16)
    sqt = S("pb_sqt", [128, 512], BF16)
    t1 = S("pb_t1", [128, 512])
    x1t = S("pb_x1t", [128, 512])
    x2t = S("pb_x2t", [128, 512])
    mu_bc = S("pb_mubc", [128, 512])
    rs_bc = S("pb_rsbc", [128, 512])
    rs2_bc = S("pb_rs2bc", [128, 512])
    srow = S("pb_srow", [1, 1024])
    negmu2 = S("pb_negmu2", [1, 4 * 512])
    rs2stg = S("pb_rs2stg", [1, 512])
    csw1 = S("pb_csw1", [1, 2048])
    csv = S("pb_csv", [1, 512])
    b2r = S("pb_b2r", [1, 8])
    b1c = S("pb_b1c", [128, 16])
    E8 = S("pb_E8", [8, 512], BF16)
    alph = S("pb_alph", [128, 8])
    mx1 = S("pb_mx1", [128, 1])
    alT = S("pb_alT", [8, 512], BF16)
    ones_col_f = S("pb_onescf", [128, 1])
    ones_col_b = S("pb_onescb", [128, 1], BF16)
    ones_row_f = S("pb_onesrf", [1, 128])
    identf = S("pb_identf", [128, 128])
    ps1 = P_("pb_ps1", [128, 512])
    ps2 = P_("pb_ps2", [128, 512])
    pss = P_("pb_pss", [1, 2048])
    psa = P_("pb_psa", [128, 8])
    psT4 = P_("pb_psT4", [8, 512])
    rd = {"1": ("sv", 0), "2": ("sv", 0), "s": ("sv", 0), "a": ("sv", 0), "t4": ("sv", 0)}
    wr = {"1": 0, "2": 0, "s": 0, "a": 0, "t4": 0}
    bar = bw

    def g_c(g):
        g.memset(ones_col_f[:], 1.0)
        g.memset(ones_col_b[:], 1.0)
        g.memset(ones_row_f[:], 1.0)
        g.memset(E8[:], 1.0)
        g.memset(identf[:], 0.0)
        g.affine_select(out=identf[:], in_=identf[:], compare_op=mybir.AluOpType.not_equal,
                        fill=1.0, base=0, pattern=[[-1, 128]], channel_multiplier=1)
        g.affine_select(out=E8[:], in_=E8[:], compare_op=mybir.AluOpType.is_ge, fill=0.0,
                        base=0, pattern=[[1, 512]], channel_multiplier=-64)
        return g.affine_select(out=E8[:], in_=E8[:], compare_op=mybir.AluOpType.is_ge, fill=0.0,
                               base=63, pattern=[[-1, 512]], channel_multiplier=64)
    gv = P.emit("g", g_c, waits=bar, inc="sg")

    cast_v = 0
    csw_tv = None
    for dt in range(16):
        def s_w(s, dt=dt):
            ins = s.dma_start(out=stg[:], in_=w1[dt * 128:(dt + 1) * 128, :])
            ins.then_inc(P.sems["dlc"], 16)
            return ins
        P.emit("s", s_w, waits=bar + [("sv", cast_v), ("st", csw_tv or 0)])
        wv = P.bump("dlc", 16)
        cast_v = P.emit("v", lambda v, dt=dt: v.tensor_copy(out=w1b[:, dt * 2048:(dt + 1) * 2048],
                                                            in_=stg[:]),
                        waits=bar + [("dlc", wv)], inc="sv")
        def t_cs2(t, dt=dt):
            ins = None
            for nb in range(4):
                ins = t.matmul(out=pss[0:1, nb * 512:(nb + 1) * 512], lhsT=ones_col_f[:],
                               rhs=stg[:, nb * 512:(nb + 1) * 512],
                               start=(dt == 0), stop=(dt == 15))
            return ins
        csw_tv = P.emit("t", t_cs2, waits=[("dlc", wv)] + ([rd["s"], ("sg", gv)] if dt == 0 else []), inc="st")
    wr["s"] = csw_tv
    def v_cs(v):
        ins = None
        for nb in range(4):
            ins = v.tensor_copy(out=csw1[0:1, nb * 512:(nb + 1) * 512], in_=pss[0:1, nb * 512:(nb + 1) * 512])
        return ins
    vv = P.emit("v", v_cs, waits=[("st", csw_tv)], inc="sv")
    rd["s"] = ("sv", vv)

    csv_tv = csw_tv
    cast2_v = cast_v
    for dt in range(16):
        def s_w(s, dt=dt):
            ins = s.dma_start(out=stg[:, 0:512], in_=vcatT[dt * 128:(dt + 1) * 128, :])
            ins.then_inc(P.sems["dlc"], 16)
            return ins
        P.emit("s", s_w, waits=[("sv", cast2_v), ("st", csv_tv or 0)])
        wv = P.bump("dlc", 16)
        cast2_v = P.emit("v", lambda v, dt=dt: v.tensor_copy(out=vcb[:, dt * 512:(dt + 1) * 512],
                                                             in_=stg[:, 0:512]),
                         waits=[("dlc", wv)], inc="sv")
        csv_tv = P.emit("t", lambda t, dt=dt: t.matmul(out=ps2[0:1, 0:512],
                                                       lhsT=ones_col_f[:], rhs=stg[:, 0:512],
                                                       start=(dt == 0), stop=(dt == 15)),
                        waits=[("dlc", wv)] + ([rd["2"], ("sg", gv)] if dt == 0 else []), inc="st")
    wr["2"] = csv_tv
    vv = P.emit("v", lambda v: v.tensor_copy(out=csv[:], in_=ps2[0:1, 0:512]),
                waits=[("st", csv_tv)], inc="sv")
    rd["2"] = ("sv", vv)

    for kb in range(4):
        def s_w(s, kb=kb):
            ins = s.dma_start(out=stg[:], in_=ucat[kb * 128:(kb + 1) * 128, :])
            ins.then_inc(P.sems["dlc"], 16)
            return ins
        P.emit("s", s_w, waits=[("sv", P.val("sv")), ("st", csv_tv)])
        wv = P.bump("dlc", 16)
        P.emit("v", lambda v, kb=kb: v.tensor_copy(out=ucb[:, kb * 2048:(kb + 1) * 2048], in_=stg[:]),
               waits=[("dlc", wv)], inc="sv")
    def s_w2(s):
        ins = None
        for dt in range(16):
            ins = s.dma_start(out=stg[:, dt * 8:(dt + 1) * 8], in_=w2[dt * 128:(dt + 1) * 128, :])
            ins.then_inc(P.sems["dlc"], 16)
        ins = s.dma_start(out=b2r[:], in_=b2)
        ins.then_inc(P.sems["dlc"], 16)
        ins = s.dma_start(out=b1c[:], in_=b1.rearrange("(dt p) one -> p (dt one)", p=128), allow_slow_non_contiguous=True)
        ins.then_inc(P.sems["dlc"], 16)
        ins = s.dma_start(out=negmu2[:], in_=ln2st_d[0:1, 0:2048])
        ins.then_inc(P.sems["dlc"], 16)
        return ins
    P.emit("s", s_w2, waits=[("sv", P.val("sv")), ("do", P.val("do"))])
    mv_ = P.bump("dlc", 16 * 19)
    w2c_v = P.emit("v", lambda v: v.tensor_copy(out=w2b[:], in_=stg[:, 0:128]),
                   waits=[("dlc", mv_)], inc="sv")
    neg_v = P.emit("v", lambda v: v.tensor_scalar(out=negmu2[:], in0=negmu2[:],
                                                  scalar1=-1.0, scalar2=None, op0=MU),
                   waits=[("dlc", mv_)], inc="sv")

    ep_v = 0
    ldx_prev = 0
    for r in range(4):
        def s_x(s, r=r):
            ins = None
            for dt in range(16):
                ins = s.dma_start(out=x1b[:, dt * 512:(dt + 1) * 512],
                                  in_=x1b_d[dt * 128:(dt + 1) * 128, r * 512:(r + 1) * 512])
                ins.then_inc(P.sems["dl0"], 16)
            return ins
        P.emit("s", s_x, waits=[("sv", ep_v), ("do", P.val("do"))])
        xv = P.bump("dl0", 16 * 16)

        def s_r2(s, r=r):
            ins = s.dma_start(out=rs2stg[:], in_=ln2st_d[0:1, 2048 + r * 512:2048 + (r + 1) * 512])
            ins.then_inc(P.sems["dr2"], 16)
            return ins
        P.emit("s", s_r2, waits=[("st", P.val("st"))])
        r2v = P.bump("dr2", 16)
        def t_r2(t, r=r):
            return t.matmul(out=ps2[:], lhsT=ones_row_f[:], rhs=rs2stg[:],
                            start=True, stop=True)
        tv = P.emit("t", t_r2, waits=[("dr2", r2v)] + [rd["2"]], inc="st")
        wr["2"] = tv
        vv = P.emit("v", lambda v: v.tensor_copy(out=rs2_bc[:], in_=ps2[:]),
                    waits=[("st", tv)], inc="sv")
        rd["2"] = ("sv", vv)
        rs2_v = vv

        prev_hav = 0
        for et in range(16):
            key = "1" if et % 2 == 0 else "2"
            psx = ps1 if et % 2 == 0 else ps2
            def t_h(t, et=et, r=r, psx=psx):
                ins = None
                for dt in range(16):
                    ins = t.matmul(out=psx[:], lhsT=w1b[:, dt * 2048 + et * 128:dt * 2048 + (et + 1) * 128],
                                   rhs=x1b[:, dt * 512:(dt + 1) * 512], start=(dt == 0), stop=False)
                return t.matmul(out=psx[:], lhsT=csw1[0:1, et * 128:(et + 1) * 128],
                                rhs=negmu2[0:1, r * 512:(r + 1) * 512], start=False, stop=True)
            tv = P.emit("t", t_h, waits=[("dl0", xv), ("sv", rs2_v), ("sv", neg_v),
                                         ("sv", w2c_v)] + [rd[key]], inc="st")
            wr[key] = tv
            vv = P.emit("v", lambda v, psx=psx: v.tensor_tensor(out=t1[:], in0=psx[:], in1=rs2_bc[:], op=MU),
                        waits=[("st", tv), ("sa", prev_hav)], inc="sv")
            rd[key] = ("sv", vv)
            av = P.emit("a", lambda a, et=et: a.activation(out=h_bf[:, et * 512:(et + 1) * 512],
                                                           in_=t1[:], func=AF.Relu,
                                                           bias=b1c[:, et:et + 1]),
                        waits=[("sv", vv)], inc="sa")
            prev_hav = av
        h_v = av

        for rsub in range(4):
            def t_a(t, rsub=rsub):
                ins = None
                for et in range(16):
                    ins = t.matmul(out=psa[:], lhsT=h_bf[:, et * 512 + rsub * 128:et * 512 + (rsub + 1) * 128],
                                   rhs=w2b[:, et * 8:(et + 1) * 8], start=(et == 0), stop=False)
                return t.matmul(out=psa[:], lhsT=ones_row_f[:], rhs=b2r[:], start=False, stop=True)
            tv = P.emit("t", t_a, waits=[("sa", h_v)] + [rd["a"]], inc="st")
            wr["a"] = tv
            def v_sm(v):
                v.tensor_copy(out=alph[:], in_=psa[:])
                v.tensor_reduce(out=mx1[:], in_=alph[:], axis=AX, op=MX)
                return v.tensor_scalar(out=alph[:], in0=alph[:], scalar1=mx1[:, 0:1],
                                       scalar2=None, op0=SU)
            vv = P.emit("v", v_sm, waits=[("st", tv)], inc="sv")
            rd["a"] = ("sv", vv)
            av = P.emit("a", lambda a: a.activation(out=alph[:], in_=alph[:], func=AF.Exp),
                        waits=[("sv", vv)], inc="sa")
            def v_sm2(v):
                v.tensor_reduce(out=mx1[:], in_=alph[:], axis=AX, op=AD)
                v.reciprocal(out=mx1[:], in_=mx1[:])
                return v.tensor_scalar(out=alph[:], in0=alph[:], scalar1=mx1[:, 0:1],
                                       scalar2=None, op0=MU)
            vv = P.emit("v", v_sm2, waits=[("sa", av)], inc="sv")
            def t_tr(t, rsub=rsub):
                return t.transpose(out=psT4[0:8, rsub * 128:(rsub + 1) * 128], in_=alph[:],
                                   identity=identf[:])
            tv = P.emit("t", t_tr, waits=[("sv", vv)] + ([rd["t4"]] if rsub == 0 else []), inc="st")
        wr["t4"] = tv
        vv = P.emit("v", lambda v: v.tensor_copy(out=alT[:], in_=psT4[0:8, :]),
                    waits=[("st", tv)], inc="sv")
        rd["t4"] = ("sv", vv)
        al_v = vv

        for krt in range(4):
            def t_vx(t, krt=krt, r=r):
                ins = None
                for dt in range(16):
                    ins = t.matmul(out=ps1[:], lhsT=vcb[:, dt * 512 + krt * 128:dt * 512 + (krt + 1) * 128],
                                   rhs=x1b[:, dt * 512:(dt + 1) * 512], start=(dt == 0), stop=False)
                return t.matmul(out=ps1[:], lhsT=csv[0:1, krt * 128:(krt + 1) * 128],
                                rhs=negmu2[0:1, r * 512:(r + 1) * 512], start=False, stop=True)
            tv1 = P.emit("t", t_vx, waits=[("sa", h_v)] + [rd["1"]], inc="st")
            wr["1"] = tv1
            def t_ae(t, krt=krt):
                return t.matmul(out=ps2[:], lhsT=E8[0:8, krt * 128:(krt + 1) * 128], rhs=alT[:],
                                start=True, stop=True)
            tv2 = P.emit("t", t_ae, waits=[("sv", al_v)] + [rd["2"]], inc="st")
            wr["2"] = tv2
            def v_ax(v, krt=krt):
                v.tensor_tensor(out=t1[:], in0=ps2[:], in1=rs2_bc[:], op=MU)
                return v.tensor_tensor(out=x1t[:], in0=ps1[:], in1=t1[:], op=MU)
            vv = P.emit("v", v_ax, waits=[("st", tv1), ("st", tv2)], inc="sv")
            rd["1"] = rd["2"] = ("sv", vv)
            vv = P.emit("v", lambda v, krt=krt: v.tensor_copy(out=vxt[:, krt * 512:(krt + 1) * 512],
                                                              in_=x1t[:]),
                        waits=[("sv", vv)], inc="sv")
        vx_v = vv

        st_tv = None
        for dt in range(16):
            key = "1" if dt % 2 == 0 else "2"
            psx = ps1 if dt % 2 == 0 else ps2
            def t_f(t, dt=dt, psx=psx):
                ins = None
                for krt in range(4):
                    ins = t.matmul(out=psx[:], lhsT=ucb[:, krt * 2048 + dt * 128:krt * 2048 + (dt + 1) * 128],
                                   rhs=vxt[:, krt * 512:(krt + 1) * 512], start=(krt == 0), stop=(krt == 3))
                return ins
            tv = P.emit("t", t_f, waits=[("sv", vx_v)] + [rd[key]], inc="st")
            wr[key] = tv
            vv = P.emit("v", lambda v, dt=dt, psx=psx: v.tensor_copy(out=ffn_bf[:, dt * 512:(dt + 1) * 512],
                                                                     in_=psx[:]),
                        waits=[("st", tv)], inc="sv")
            rd[key] = ("sv", vv)
            av = P.emit("a", lambda a, dt=dt: a.activation(out=sqt[:], in_=ffn_bf[:, dt * 512:(dt + 1) * 512],
                                                           func=AF.Square),
                        waits=[("sv", vv), ("st", st_tv or wr["s"])], inc="sa")
            def t_s(t, dt=dt):
                t.matmul(out=pss[0:1, 0:512], lhsT=ones_col_b[:], rhs=ffn_bf[:, dt * 512:(dt + 1) * 512],
                         start=(dt == 0), stop=(dt == 15))
                return t.matmul(out=pss[0:1, 512:1024], lhsT=ones_col_b[:], rhs=sqt[:],
                                start=(dt == 0), stop=(dt == 15))
            st_tv = P.emit("t", t_s, waits=[("sv", vv), ("sa", av)] + ([rd["s"]] if dt == 0 else []),
                           inc="st")
        wr["s"] = st_tv

        def v_fin(v):
            v.tensor_scalar(out=srow[0:1, 0:512], in0=pss[0:1, 0:512], scalar1=1.0 / HID, scalar2=None, op0=MU)
            v.tensor_scalar(out=srow[0:1, 512:1024], in0=pss[0:1, 512:1024], scalar1=1.0 / HID, scalar2=None, op0=MU)
            v.tensor_tensor(out=t1[0:1, :], in0=srow[0:1, 0:512], in1=srow[0:1, 0:512], op=MU)
            v.tensor_tensor(out=srow[0:1, 512:1024], in0=srow[0:1, 512:1024], in1=t1[0:1, :], op=SU)
            return v.tensor_scalar(out=srow[0:1, 512:1024], in0=srow[0:1, 512:1024], scalar1=1e-5, scalar2=None, op0=AD)
        vv = P.emit("v", v_fin, waits=[("st", st_tv)], inc="sv")
        rd["s"] = ("sv", vv)
        av = P.emit("a", lambda a: a.activation(out=srow[0:1, 512:1024], in_=srow[0:1, 512:1024], func=AF.Sqrt),
                    waits=[("sv", vv)], inc="sa")
        vv = P.emit("v", lambda v: v.reciprocal(out=srow[0:1, 512:1024], in_=srow[0:1, 512:1024]),
                    waits=[("sa", av)], inc="sv")
        def t_bc(t):
            t.matmul(out=ps1[:], lhsT=ones_row_f[:], rhs=srow[0:1, 0:512], start=True, stop=True)
            return t.matmul(out=ps2[:], lhsT=ones_row_f[:], rhs=srow[0:1, 512:1024], start=True, stop=True)
        tv = P.emit("t", t_bc, waits=[("sv", vv)] + [rd["1"], rd["2"]], inc="st")
        wr["1"] = wr["2"] = tv
        def v_bc(v):
            v.tensor_copy(out=mu_bc[:], in_=ps1[:])
            return v.tensor_copy(out=rs_bc[:], in_=ps2[:])
        bc_v = P.emit("v", v_bc, waits=[("st", tv)], inc="sv")
        rd["1"] = rd["2"] = ("sv", bc_v)

        for dt in range(16):
            def s_x1(s, dt=dt, r=r):
                ins = s.dma_start(out=x1t[:], in_=x1_d[dt * 128:(dt + 1) * 128, r * 512:(r + 1) * 512])
                ins.then_inc(P.sems["dl1"], 16)
                return ins
            P.emit("s", s_x1, waits=[("sv", P.val("sv"))])
            x1v = P.bump("dl1", 16)
            def v_x2(v, dt=dt):
                v.tensor_tensor(out=t1[:], in0=ffn_bf[:, dt * 512:(dt + 1) * 512], in1=mu_bc[:], op=SU)
                v.tensor_tensor(out=t1[:], in0=t1[:], in1=rs_bc[:], op=MU)
                return v.tensor_tensor(out=x2t[:], in0=t1[:], in1=x1t[:], op=AD)
            vv = P.emit("v", v_x2, waits=[("sv", bc_v), ("dl1", x1v), ("do", P.val("do"))], inc="sv")
            def g_y(g, dt=dt, r=r):
                ins = g.dma_start(out=y[dt * 128:(dt + 1) * 128, r * 512:(r + 1) * 512], in_=x2t[:])
                ins.then_inc(P.sems["do"], 16)
                return ins
            P.emit("g", g_y, waits=[("sv", vv)])
            P.bump("do", 16)
        ep_v = P.val("sv")


def barrier_waits(prog):
    return [(s, prog.val(s)) for s in ("st", "sv", "sa", "sg", "dlc", "dl0", "dl1", "do", "cc", "dsx", "dbb", "dst", "dr2", "ddy", "do0", "do1")]


def build_full():
    PH = int(os.environ.get("FULL_PH", "9"))
    nc = bass.Bass("TRN2", target_bir_lowering=False, debug=False,
                   detect_race_conditions=bool(os.environ.get("KK_RACE")))
    DT = lambda nm, sh, dt=F32, kind="ExternalInput": nc.dram_tensor(nm, sh, dt, kind=kind).ap()
    xT = DT("xT", [HID, ROWS])
    xTr = DT("xTr", [HID, RPC])
    wcat = DT("wcat", [HID, 772])
    babb = DT("babb", [1, 4])
    lamp = DT("lamp", [1, 256])
    onw = DT("onw", [1, 128])
    wo = DT("wo", [HID, HID])
    w1 = DT("w1", [HID, HID])
    b1 = DT("b1", [HID, 1])
    w2 = DT("w2", [HID, 8])
    b2 = DT("b2", [1, 8])
    ucat = DT("ucat", [512, HID])
    vcatT = DT("vcatT", [HID, 512])
    y = DT("y", [HID, RPC], kind="ExternalOutput")

    _dbgk = "ExternalOutput" if os.environ.get("KK_DBG") else "Internal"
    q_d = DT("q_d", [ROWS, 256], kind=_dbgk)
    k_d = DT("k_d", [ROWS, 256], BF16, kind=_dbgk)
    v_d = DT("v_d", [ROWS, 256], BF16, kind=_dbgk)
    g_d = DT("g_d", [ROWS, HPC], kind=_dbgk)
    bt_d = DT("bt_d", [ROWS, HPC], kind=_dbgk)
    o_d = DT("o_d", [256, ROWS], BF16, kind="Internal")
    o_all = DT("o_all", [HID, ROWS], BF16, kind="Internal")
    st_in = DT("st_in", [2, RPC], kind="Internal")
    st_all = DT("st_all", [16, RPC], kind="Internal")
    x1_d = DT("x1_d", [HID, RPC], kind=_dbgk)
    x1b_d = DT("x1b_d", [HID, RPC], BF16, kind="Internal")
    ln2st_d = DT("ln2st_d", [1, 2 * RPC], kind="Internal")

    prog = Prog()
    sem_names = ["st", "sv", "sa", "sg", "dlc", "dl0", "dl1", "do", "cc", "dsx", "dbb", "dst", "dr2", "ddy", "do0", "do1"]
    with ExitStack() as ctx:
        sems = {n: ctx.enter_context(nc.semaphore(n)) for n in sem_names}
        prog.sems = sems

        def g_pid(g):
            prog.pid_g = g.partition_id()
            return g.nop()
        prog.emit("g", g_pid)

        with ExitStack() as c_mesa:
            M = Mesa(nc, c_mesa, prog, T_FULL)
            with ExitStack() as c_p2:
                M.emit_consts(lamp, onw)
                p2_d0, p2_d1 = emit_p12(nc, prog, M, c_p2, xT, xTr, wcat, babb, st_in, st_all,
                                        q_d, k_d, v_d, g_d, bt_d)

                _nck = min(int(os.environ.get("KK_NCHUNK", "32")), T_FULL // C)
                for cidx in range(0 if PH < 3 else _nck):
                    z = cidx % 2
                    dl_sem = f"dl{z}"
                    def s_ld(s, z=z, cidx=cidx, dl_sem=dl_sem):
                        ins = None
                        for b in range(4):
                            r0 = b * T_FULL + cidx * C
                            for (dst, srcd) in ((M.aq[z], q_d), (M.ak[z], k_d), (M.av[z], v_d)):
                                ins = s.dma_start(out=dst[:, b * 256:(b + 1) * 256],
                                                  in_=srcd[r0:r0 + C, :])
                                ins.then_inc(prog.sems[dl_sem], 16)
                            ins = s.dma_start(out=M.ag[z][:, b * HPC:(b + 1) * HPC],
                                              in_=g_d[r0:r0 + C, :])
                            ins.then_inc(prog.sems[dl_sem], 16)
                            ins = s.dma_start(out=M.ab[z][:, b * HPC:(b + 1) * HPC],
                                              in_=bt_d[r0:r0 + C, :])
                            ins.then_inc(prog.sems[dl_sem], 16)
                        return ins
                    w = ([("sv", M.hstate_v)] if cidx >= 2
                         else [("do0", p2_d0), ("do1", p2_d1)])
                    prog.emit("s", s_ld, waits=w)
                    dl_val = prog.bump(dl_sem, 16 * 20)
                    M.emit_chunk(z, cidx, o_d, dl_sem, dl_val, T_FULL)

                if PH >= 3:
                    def g_ago(g):
                        return g.collective_compute("AllGather", mybir.AluOpType.bypass,
                                                    replica_groups=[list(range(NCORES))],
                                                    ins=[o_d[:, :].opt()], outs=[o_all[:, :].opt()])
                    prog.emit("g", g_ago, waits=[("do", prog.val("do")), ("sv", M.hstate_v)], inc="cc")

        bw = barrier_waits(prog)
        if PH >= 4:
            with ExitStack() as c4a:
                emit_p4a(nc, prog, c4a, xTr, wo, o_all, x1_d, x1b_d, ln2st_d, bw)
        bw2 = barrier_waits(prog)
        if PH >= 5:
            with ExitStack() as c4b:
                emit_p4b(nc, prog, c4b, x1_d, x1b_d, ln2st_d, w1, b1, w2, b2, ucat, vcatT, y, bw2)

        prog.emit("g", lambda g: g.nop(), waits=[("do", prog.val("do"))])

        block = ctx.enter_context(nc.Block())
        prog.realize(block, sems)
    return nc


_NC_CACHE = {}


def kernel(L0_attn_x, L0_attn_Wq, L0_attn_Wk, L0_attn_Wv, L0_attn_Wa, L0_attn_ba,
           L0_attn_Wb, L0_attn_bb, L0_attn_lambda_params, L0_attn_o_norm_weight,
           L0_attn_Wo, L0_ffn_U, L0_ffn_V, L0_ffn_W1, L0_ffn_b1, L0_ffn_W2, L0_ffn_b2):
    f32 = np.float32
    x = np.ascontiguousarray(np.asarray(L0_attn_x, dtype=f32).reshape(ROWS, HID))
    xT = np.ascontiguousarray(x.T)
    Wq = np.asarray(L0_attn_Wq, f32)
    Wk = np.asarray(L0_attn_Wk, f32)
    Wv = np.asarray(L0_attn_Wv, f32)
    Wa = np.asarray(L0_attn_Wa, f32)
    Wb = np.asarray(L0_attn_Wb, f32)
    ba = np.asarray(L0_attn_ba, f32)
    bb = np.asarray(L0_attn_bb, f32)
    lam = np.asarray(L0_attn_lambda_params, f32).reshape(NH, HD)
    onw = np.asarray(L0_attn_o_norm_weight, f32).reshape(1, HD)
    wo = np.ascontiguousarray(np.asarray(L0_attn_Wo, f32))
    w1 = np.ascontiguousarray(np.asarray(L0_ffn_W1, f32))
    b1 = np.asarray(L0_ffn_b1, f32).reshape(HID, 1)
    w2 = np.ascontiguousarray(np.asarray(L0_ffn_W2, f32))
    b2 = np.asarray(L0_ffn_b2, f32).reshape(1, 8)
    U = np.asarray(L0_ffn_U, f32)
    V = np.asarray(L0_ffn_V, f32)
    ucat = np.ascontiguousarray(U.transpose(0, 2, 1).reshape(512, HID))
    vcatT = np.ascontiguousarray(V.reshape(512, HID).T)

    if "nc" not in _NC_CACHE:
        _NC_CACHE["nc"] = build_full()
    nc = _NC_CACHE["nc"]

    in_maps = []
    for c in range(NCORES):
        hs = slice(c * HPC * HD, (c + 1) * HPC * HD)
        hh = slice(c * HPC, (c + 1) * HPC)
        wcat_c = np.ascontiguousarray(np.concatenate(
            [Wq[:, hs], Wk[:, hs], Wv[:, hs], Wa[:, hh], Wb[:, hh]], axis=1))
        babb_c = np.ascontiguousarray(
            np.concatenate([ba[hh], bb[hh]]).reshape(1, 4))
        lamp_c = np.ascontiguousarray(lam[hh].reshape(1, HPC * HD))
        xTr_c = np.ascontiguousarray(xT[:, c * RPC:(c + 1) * RPC])
        in_maps.append(dict(xT=xT, xTr=xTr_c, wcat=wcat_c, babb=babb_c, lamp=lamp_c, onw=onw,
                            wo=wo, w1=w1, b1=b1, w2=w2, b2=b2, ucat=ucat, vcatT=vcatT))

    res = run_bass_kernel_spmd(nc, in_maps, core_ids=list(range(NCORES)))
    yT = np.concatenate([np.asarray(r["y"], dtype=f32) for r in res.results], axis=1)
    out = np.ascontiguousarray(yT.T).reshape(4, T_FULL, HID)
    return out


# version 15
# speedup vs baseline: 1.0100x; 1.0085x over previous
import numpy as np
from contextlib import ExitStack

import concourse.bass as bass
import concourse.mybir as mybir
from concourse.bass_utils import run_bass_kernel_spmd
from concourse.alu_op_type import AluOpType

F32 = mybir.dt.float32
BF16 = mybir.dt.bfloat16
AF = mybir.ActivationFunctionType
AX = mybir.AxisListType.X

HID = 2048
NH = 16
HD = 128
NCORES = 8
HPC = NH // NCORES
NPAIR = 8
C = 128
MAX_CG = 30
LAMB_LB = 0.25
K_EXP = 8
R_EXP = 64
W = NPAIR * HD
import os
DBG_STAGE = int(os.environ.get("MESA_STAGE", "99"))
DBG_ITERS = int(os.environ.get("MESA_ITERS", str(MAX_CG)))
DBG_CSTAGE = int(os.environ.get("MESA_CSTAGE", "99"))
DBG_NCHUNK = int(os.environ.get("MESA_NCHUNK", "99"))

MU, AD, SU, MI, MX, BYP = (AluOpType.mult, AluOpType.add, AluOpType.subtract,
                           AluOpType.min, AluOpType.max, AluOpType.bypass)
GE = AluOpType.is_ge


class Prog:

    def __init__(self):
        self.ops = {e: [] for e in ("t", "v", "a", "g", "s")}
        self.cnt = {}
        self.sems = None

    def bump(self, sem, n):
        self.cnt[sem] = self.cnt.get(sem, 0) + n
        return self.cnt[sem]

    def emit(self, eng, fn, waits=(), inc=None, inc_n=1):
        waits = tuple((s, v) for (s, v) in waits if v > 0)
        val = None
        if inc is not None:
            self.cnt[inc] = self.cnt.get(inc, 0) + inc_n
            val = self.cnt[inc]
        self.ops[eng].append((waits, inc, inc_n, fn))
        return val

    def val(self, sem):
        return self.cnt.get(sem, 0)

    def realize(self, block, sems):
        self.sems = sems

        def runner(handle, eng):
            if eng in ("v", "a", "g"):
                handle = _AutoDrain(handle)
            for waits, inc, inc_n, fn in self.ops[eng]:
                for s, v in waits:
                    handle.wait_ge(sems[s], v)
                ins = fn(handle)
                if inc is not None:
                    ins.then_inc(sems[inc], inc_n)

        block.tensor(lambda h: runner(h, "t"))
        block.vector(lambda h: runner(h, "v"))
        block.scalar(lambda h: runner(h, "a"))
        block.gpsimd(lambda h: runner(h, "g"))
        block.sync(lambda h: runner(h, "s"))


_DRAIN_OPS = frozenset((
    "tensor_tensor", "tensor_scalar", "tensor_copy", "tensor_single_scalar",
    "scalar_tensor_tensor", "reciprocal", "memset", "memzero", "tensor_reduce",
    "reduce_sum", "reduce_max", "select", "iota", "affine_select", "activation",
    "tensor_add", "tensor_sub", "tensor_mul", "tensor_max", "square", "copy",
    "add", "mul", "sqrt", "sign",
))


class _AutoDrain:

    def __init__(self, h):
        self._h = h
        self._pend = False

    @property
    def raw(self):
        return self._h

    def skip(self):
        self._pend = False

    def __getattr__(self, name):
        real = getattr(self._h, name)
        if name in _DRAIN_OPS:
            def wrap(*a, **k):
                if self._pend:
                    self._h.drain()
                self._pend = True
                return real(*a, **k)
            return wrap
        return real


def pc(p):
    return slice(p * HD, (p + 1) * HD)


class Mesa:

    def __init__(self, nc, ctx, prog, T):
        self.nc = nc
        self.prog = prog
        self.T = T
        self.nchunk = T // C
        S = lambda nm, sh, dt=F32: ctx.enter_context(nc.sbuf_tensor(nm, sh, dt))
        P_ = lambda nm, sh, dt=F32: ctx.enter_context(nc.psum_tensor(nm, sh, dt))

        self.ident_b = S("mz_ident_b", [128, 128], BF16)
        self.ident_f = S("mz_ident_f", [128, 128], F32)
        self.utones_f = S("mz_utones_f", [128, 128], F32)
        self.utmask_b = S("mz_utmask_b", [128, 128], BF16)
        self.ones_col_f = S("mz_ones_col_f", [128, 1], F32)
        self.ones_col_b = S("mz_ones_col_b", [128, 1], BF16)
        self.ones_row_f = S("mz_ones_row_f", [1, 128], F32)
        self.ones_sq_b = S("mz_ones_sq_b", [128, 128], BF16)
        self.lamb_eps_arena = S("mz_lamb_eps", [128, W], F32)
        self.diag_lamb = S("mz_diag_lamb", [128, HPC * 128], BF16)
        self.onw_col = S("mz_onw_col", [128, 1], F32)

        self.Hkk_f = S("mz_Hkk_f", [128, W], F32)
        self.Hkk_b = S("mz_Hkk_b", [128, W], BF16)
        self.Hvk_f = S("mz_Hvk_f", [128, W], F32)
        self.Hvk_b = S("mz_Hvk_b", [128, W], BF16)
        self.Hd_row = S("mz_Hd_row", [1, W], F32)

        self.X = S("mz_X", [128, W], F32)
        self.R = S("mz_R", [128, W], F32)
        self.Pv = S("mz_Pv", [128, W], F32)
        self.Pv2 = S("mz_Pv2", [128, W], F32)
        self.Pd = S("mz_Pd", [128, W], BF16)
        self.Pd_e = S("mz_Pd_e", [128, W], BF16)
        self.Pb = S("mz_Pb", [128, W], BF16)
        self.diag_e = S("mz_diag_e", [128, W], BF16)
        self.Xt = S("mz_Xt", [128, W], BF16)

        self.aq = [S(f"mz_aq{z}", [128, W], F32) for z in range(2)]
        self.ak = [S(f"mz_ak{z}", [128, W], BF16) for z in range(2)]
        self.av = [S(f"mz_av{z}", [128, W], BF16) for z in range(2)]
        self.ag = [S(f"mz_ag{z}", [128, NPAIR], F32) for z in range(2)]
        self.ab = [S(f"mz_ab{z}", [128, NPAIR], F32) for z in range(2)]

        self.Kd = S("mz_Kd", [128, W], BF16)
        self.Vd = S("mz_Vd", [128, W], BF16)
        self.Ksq = S("mz_Ksq", [128, W], BF16)
        self.Ktil = S("mz_Ktil", [128, W], BF16)
        self.Dp = S("mz_Dp", [128, W], BF16)
        self.Mp = S("mz_Mp", [128, W], BF16)
        self.scrf = S("mz_scrf", [128, W], F32)
        self.scrf2 = S("mz_scrf2", [128, W], F32)
        self.scrb = S("mz_scrb", [128, W], BF16)

        self.Gc = S("mz_Gc", [128, NPAIR], F32)
        self.e_all = S("mz_e_all", [128, NPAIR], F32)
        self.Gcrow = S("mz_Gcrow", [1, W], F32)
        self.erow = S("mz_erow", [1, W], F32)
        self.w_all = S("mz_w_all", [128, NPAIR], F32)
        self.dec_bc = S("mz_dec_bc", [128, NPAIR], F32)
        self.decs_row = S("mz_decs_row", [1, NPAIR], F32)
        self.pq = S("mz_pq", [128, NPAIR], F32)
        self.dA = S("mz_dA", [128, NPAIR], F32)
        self.dB = S("mz_dB", [128, NPAIR], F32)
        self.nd = S("mz_nd", [128, NPAIR], F32)
        self.alpha = S("mz_alpha", [128, NPAIR], F32)
        self.alphan = S("mz_alphan", [128, NPAIR], F32)
        self.betac = S("mz_betac", [128, NPAIR], F32)
        self.c1 = S("mz_c1", [128, NPAIR], F32)
        self.lamrow = S("mz_lamrow", [1, HPC * 128], F32)
        self.sprow = S("mz_sprow", [1, HPC * 128], F32)
        self.sprow2 = S("mz_sprow2", [1, HPC * 128], F32)
        self.onwrow = S("mz_onwrow", [1, 128], F32)

        self.ps_T = P_("mz_ps_T", [128, W], F32)
        self.ps_Z = P_("mz_ps_Z", [128, W], F32)
        self.ps_B = P_("mz_ps_B", [128, W], F32)
        self.ps_S = P_("mz_ps_S", [128, 4 * HD], F32)
        self.ps_Tb = P_("mz_ps_Tb", [128, W], BF16)

        self.rd = {k: ("sv", 0) for k in ("T", "Z", "B", "S", "Tb")}
        self.wr = {k: 0 for k in ("T", "Z", "B", "S", "Tb")}
        self.hstate_v = 0
        self.pready = 0
        self.de_rd = 0
        self.de_a = 0
        self.pb_a = 0
        self.pb_a_h0 = 0
        self.x_done = 0
        self.hstate_a = 0
        self.kv_a = 0

    def w_waits(self, *banks):
        return [self.rd[b] for b in banks]

    def set_w(self, bank, tv):
        self.wr[bank] = tv

    def r_waits(self, *banks):
        return [("st", self.wr[b]) for b in banks]

    def set_r(self, bank, sem, vv):
        self.rd[bank] = (sem, vv)

    def emit_consts(self, lamp_ap, onw_ap):
        P, M = self.prog, self

        def g_init1(g):
            for t in (M.ident_b, M.ident_f, M.utones_f, M.utmask_b):
                ins = g.memset(t[:], 0.0)
            ins = g.memset(M.ones_col_f[:], 1.0)
            ins = g.memset(M.ones_col_b[:], 1.0)
            ins = g.memset(M.ones_row_f[:], 1.0)
            ins = g.memset(M.ones_sq_b[:], 1.0)
            ins = g.memset(M.Hkk_f[:], 0.0)
            ins = g.memset(M.Hkk_b[:], 0.0)
            ins = g.memset(M.Hvk_f[:], 0.0)
            ins = g.memset(M.Hvk_b[:], 0.0)
            return g.memset(M.Hd_row[:], 0.0)
        gv1 = P.emit("g", g_init1, inc="sg")

        def g_init2(g):
            for t, sel in ((M.ident_b, "ne"), (M.ident_f, "ne"), (M.utones_f, "gt"),
                           (M.utmask_b, "gt")):
                ins = g.affine_select(out=t[:], in_=t[:],
                                compare_op=AluOpType.not_equal if sel == "ne" else AluOpType.is_gt,
                                fill=1.0, base=0, pattern=[[-1, 128]], channel_multiplier=1)
            return ins
        gv = P.emit("g", g_init2, waits=[("sg", gv1)], inc="sg")
        self.consts_gv = gv

        P.emit("s", lambda s: s.dma_start(out=M.lamrow[:], in_=lamp_ap), inc="dlc", inc_n=16)
        P.emit("s", lambda s: s.dma_start(out=M.onwrow[:], in_=onw_ap), inc="dlc", inc_n=16)
        dl0 = P.val("dlc")

        if DBG_CSTAGE < 1:
            self.consts_done = P.emit("v", lambda v: v.tensor_copy(out=M.sprow[:], in_=M.lamrow[:]),
                                      waits=[("dlc", dl0), ("sg", gv)], inc="sv")
            return
        def v_sp1(v):
            ins = v.tensor_scalar(out=M.sprow[:], in0=M.lamrow[:], scalar1=-1.0, scalar2=None, op0=MU)
            ins = v.tensor_tensor(out=M.sprow[:], in0=M.lamrow[:], in1=M.sprow[:], op=MI)
            return v.tensor_scalar(out=M.sprow2[:], in0=M.lamrow[:], scalar1=0.0, scalar2=None, op0=MX)
        vv = P.emit("v", v_sp1, waits=[("dlc", dl0)], inc="sv")
        av = P.emit("a", lambda a: a.activation(out=M.sprow[:], in_=M.sprow[:], func=AF.Exp),
                    waits=[("sv", vv), ("sg", gv)], inc="sa")
        av = P.emit("a", lambda a: a.activation(out=M.sprow[:], in_=M.sprow[:], func=AF.Ln,
                                                bias=M.ones_col_f[0:1, :]),
                    waits=[("sa", av)], inc="sa")

        def v_sp4(v):
            ins = v.tensor_tensor(out=M.lamrow[:], in0=M.sprow2[:], in1=M.sprow[:], op=AD)
            return v.tensor_scalar(out=M.lamrow[:], in0=M.lamrow[:], scalar1=LAMB_LB, scalar2=None, op0=AD)
        vv = P.emit("v", v_sp4, waits=[("sa", av)], inc="sv")

        if DBG_CSTAGE < 2:
            self.consts_done = vv
            return
        def t_lam(t):
            for h in range(HPC):
                ins = t.matmul(out=M.ps_B[:, h * 128:(h + 1) * 128], lhsT=M.ones_row_f[:],
                         rhs=M.lamrow[0:1, h * 128:(h + 1) * 128], start=True, stop=True)
            return t.matmul(out=M.ps_B[:, 256:257], lhsT=M.onwrow[:],
                            rhs=M.ones_row_f[0:1, 0:1], start=True, stop=True)
        tv = P.emit("t", t_lam, waits=[("sv", vv), ("sg", gv)], inc="st")
        self.set_w("B", tv)

        def v_lam2(v):
            for p in range(NPAIR):
                h = p % HPC
                ins = v.tensor_scalar(out=M.lamb_eps_arena[:, pc(p)], in0=M.ps_B[:, h * 128:(h + 1) * 128],
                                scalar1=1e-8, scalar2=None, op0=AD)
            for h in range(HPC):
                ins = v.tensor_tensor(out=M.diag_lamb[:, h * 128:(h + 1) * 128],
                                in0=M.ps_B[:, h * 128:(h + 1) * 128], in1=M.ident_b[:], op=MU)
            return v.tensor_copy(out=M.onw_col[:], in_=M.ps_B[:, 256:257])
        vv = P.emit("v", v_lam2, waits=self.r_waits("B"), inc="sv")
        self.set_r("B", "sv", vv)
        self.consts_done = vv

    def emit_mv(self, Psrc, ak):
        P, M = self.prog, self

        tv = P.emit("t", lambda t: [t.transpose(out=M.ps_Tb[:, pc(p)], in_=M.Pb[:, pc(p)],
                                                identity=M.ident_b[:]) for p in range(NPAIR)][-1],
                    waits=self.w_waits("Tb") + [("sa", self.pb_a)], inc="st")
        self.set_w("Tb", tv)
        def t_pe(t):
            for p in range(NPAIR):
                ins = t.matmul(out=M.ps_B[:, pc(p)], lhsT=M.Pb[:, pc(p)], rhs=M.diag_e[:, pc(p)],
                               start=True, stop=True)
            return ins
        tve = P.emit("t", t_pe, waits=self.w_waits("B") + [("sa", self.pb_a), ("sa", self.de_a)],
                     inc="st")
        self.set_w("B", tve)
        av1 = P.emit("a", lambda a: a.activation(out=M.Pd[:], in_=M.ps_Tb[:], func=AF.Copy),
                     waits=self.r_waits("Tb"), inc="sa")
        self.set_r("Tb", "sa", av1)
        av2 = P.emit("a", lambda a: a.activation(out=M.Pd_e[:], in_=M.ps_B[:], func=AF.Copy),
                     waits=self.r_waits("B"), inc="sa")
        self.set_r("B", "sa", av2)

        def t_dl(t):
            for p in range(NPAIR):
                h = p % HPC
                ins = t.matmul(out=M.ps_Z[:, pc(p)], lhsT=M.Pd[:, pc(p)],
                               rhs=M.diag_lamb[:, h * 128:(h + 1) * 128],
                               start=(p % 4 == 0), stop=False)
            return ins
        P.emit("t", t_dl, waits=self.w_waits("Z") + [("sa", av1), ("sv", self.hstate_v)], inc="st")

        def t_s0(t):
            for p in range(4):
                ins = t.matmul(out=M.ps_S[:, p * HD:(p + 1) * HD], lhsT=M.Kd[:, pc(p)],
                         rhs=M.Pd[:, pc(p)], start=True, stop=True)
            return ins
        tv_s0 = P.emit("t", t_s0, waits=self.w_waits("S") + [("sa", self.kv_a)], inc="st")
        self.set_w("S", tv_s0)
        def t_s1(t):
            for p in range(4, 8):
                ins = t.matmul(out=M.ps_T[:, (p - 4) * HD:(p - 3) * HD], lhsT=M.Kd[:, pc(p)],
                         rhs=M.Pd[:, pc(p)], start=True, stop=True)
            return ins
        tv_s1 = P.emit("t", t_s1, waits=self.w_waits("T") + [("sa", av1)], inc="st")
        self.set_w("T", tv_s1)

        def t_hk(t):
            for p in range(NPAIR):
                ins = t.matmul(out=M.ps_Z[:, pc(p)], lhsT=M.Pd_e[:, pc(p)], rhs=M.Hkk_b[:, pc(p)],
                               start=False, stop=False)
            return ins
        P.emit("t", t_hk, waits=[("sa", av2), ("sa", self.hstate_a)], inc="st")

        def v_m0(v):
            v.skip()
            return v.tensor_tensor(out=M.Mp[:, 0:512], in0=M.Dp[:, 0:512],
                                   in1=M.ps_S[:], op=MU)
        vv_m0 = P.emit("v", v_m0, waits=self.r_waits("S"), inc="sv")
        self.set_r("S", "sv", vv_m0)
        def v_m1(v):
            v.skip()
            return v.tensor_tensor(out=M.Mp[:, 512:1024], in0=M.Dp[:, 512:1024],
                                   in1=M.ps_T[:, 0:512], op=MU)
        vv_m1 = P.emit("v", v_m1, waits=[("st", tv_s1)], inc="sv")
        self.set_r("T", "sv", vv_m1)

        def t_y20(t, ak=ak):
            for p in range(4):
                ins = t.matmul(out=M.ps_Z[:, pc(p)], lhsT=M.Mp[:, pc(p)], rhs=ak[:, pc(p)],
                         start=False, stop=(p == 3))
            return ins
        P.emit("t", t_y20, waits=[("sv", vv_m0)], inc="st")
        def t_y21(t, ak=ak):
            for p in range(4, 8):
                ins = t.matmul(out=M.ps_Z[:, pc(p)], lhsT=M.Mp[:, pc(p)], rhs=ak[:, pc(p)],
                         start=False, stop=(p == 7))
            return ins
        tvh = P.emit("t", t_y21, waits=[("sv", vv_m1)], inc="st")
        self.set_w("Z", tvh)
        return tvh

    def emit_chunk(self, z, c, o_dram, dl_sem, dl_val, b_stride):
        P, M = self.prog, self
        aq, ak, av, ag, ab = M.aq[z], M.ak[z], M.av[z], M.ag[z], M.ab[z]

        base_wait = [(dl_sem, dl_val), ("sv", self.consts_done)]

        def t_gc(t):
            for p in range(NPAIR):
                ins = t.matmul(out=M.ps_Z[:, p:p + 1], lhsT=M.utones_f[:], rhs=ag[:, p:p + 1],
                         start=True, stop=True)
            return ins
        tv = P.emit("t", t_gc, waits=base_wait + self.w_waits("Z"), inc="st")
        self.set_w("Z", tv)
        vv = P.emit("v", lambda v: v.tensor_copy(out=M.Gc[:], in_=M.ps_Z[:, 0:NPAIR]),
                    waits=self.r_waits("Z"), inc="sv")
        self.set_r("Z", "sv", vv)
        gc_v = vv
        e_a = P.emit("a", lambda a: a.activation(out=M.e_all[:], in_=M.Gc[:], func=AF.Exp),
                     waits=[("sv", vv)], inc="sa")
        def a_de(a):
            for p in range(NPAIR):
                ins = a.activation(out=M.diag_e[:, pc(p)], in_=M.ident_f[:], func=AF.Copy,
                                   scale=M.e_all[:, p:p + 1])
            return ins
        self.de_a = P.emit("a", a_de, waits=[("sa", e_a), ("sg", self.consts_gv),
                                             ("st", self.de_rd)], inc="sa")

        if DBG_CSTAGE < 3:
            self.hstate_v = gc_v
            return None
        def t_gr(t):
            for p in range(NPAIR):
                ins = t.transpose(out=M.ps_T[0:1, pc(p)], in_=M.Gc[:, p:p + 1], identity=M.ident_f[:])
            return ins
        tv = P.emit("t", t_gr, waits=[("sv", gc_v)] + self.w_waits("T"), inc="st")
        self.set_w("T", tv)
        vv = P.emit("v", lambda v: v.tensor_copy(out=M.Gcrow[:], in_=M.ps_T[0:1, :]),
                    waits=self.r_waits("T"), inc="sv")
        self.set_r("T", "sv", vv)
        gcrow_v = vv
        P.emit("a", lambda a: a.activation(out=M.erow[:], in_=M.Gcrow[:], func=AF.Exp),
               waits=[("sv", vv)], inc="sa")
        erow_a = P.val("sa")

        if DBG_CSTAGE < 4:
            self.hstate_v = P.val("sv")
            return None
        def t_gbc(t):
            for p in range(NPAIR):
                ins = t.matmul(out=M.ps_B[:, pc(p)], lhsT=M.ones_row_f[:], rhs=M.Gcrow[0:1, pc(p)],
                         start=True, stop=True)
            return ins
        tv = P.emit("t", t_gbc, waits=[("sv", gcrow_v)] + self.w_waits("B"), inc="st")
        self.set_w("B", tv)

        def v_dg(v):
            for p in range(NPAIR):
                h = v if p == 0 else v.raw
                ins = h.tensor_scalar(out=M.scrf[:, pc(p)], in0=M.ps_B[:, pc(p)],
                                scalar1=M.Gc[:, p:p + 1], scalar2=0.0, op0=SU, op1=MI)
            return ins
        vv = P.emit("v", v_dg, waits=self.r_waits("B"), inc="sv")
        self.set_r("B", "sv", vv)
        av_ = P.emit("a", lambda a: a.activation(out=M.scrf[:], in_=M.scrf[:], func=AF.Exp),
                     waits=[("sv", vv)], inc="sa")

        def v_w(v):
            ins = v.tensor_copy(out=M.w_all[:],
                          in_=M.scrf[:].rearrange("s (p d) -> s p d", p=NPAIR)[:, :, 127:128])
            return v.tensor_tensor(out=M.w_all[:], in0=M.w_all[:], in1=ab[:], op=MU)
        w_v = P.emit("v", v_w, waits=[("sa", av_)], inc="sv")

        def v_dp(v):
            for p in range(NPAIR):
                h = v if p == 0 else v.raw
                ins = h.scalar_tensor_tensor(out=M.Dp[:, pc(p)], in0=M.utmask_b[:],
                                       scalar=ab[:, p:p + 1], in1=M.scrf[:, pc(p)],
                                       op0=MU, op1=MU)
            return ins
        dp_v = P.emit("v", v_dp, waits=[("sa", av_)], inc="sv")

        if DBG_CSTAGE < 5:
            self.hstate_v = P.val("sv")
            return None
        ksq_a = P.emit("a", lambda a: a.activation(out=M.Ksq[:], in_=ak[:], func=AF.Square),
                       waits=base_wait, inc="sa")
        for (src, dst) in ((ak, M.Kd), (av, M.Vd)):
            def t_tb(t, src=src):
                for p in range(NPAIR):
                    ins = t.transpose(out=M.ps_Tb[:, pc(p)], in_=src[:, pc(p)], identity=M.ident_b[:])
                return ins
            tv = P.emit("t", t_tb, waits=base_wait + self.w_waits("Tb"), inc="st")
            self.set_w("Tb", tv)
            kv_a = P.emit("a", lambda a, dst=dst: a.activation(out=dst[:], in_=M.ps_Tb[:], func=AF.Copy),
                          waits=self.r_waits("Tb"), inc="sa")
            self.set_r("Tb", "sa", kv_a)
        self.kv_a = kv_a

        if DBG_CSTAGE < 6:
            self.hstate_v = P.val("sv")
            return None
        def t_dec(t):
            for p in range(NPAIR):
                ins = t.matmul(out=M.ps_Z[0:1, p:p + 1], lhsT=ag[:, p:p + 1], rhs=M.ones_col_f[:],
                         start=True, stop=True)
            return ins
        tv = P.emit("t", t_dec, waits=base_wait + self.w_waits("Z"), inc="st")
        self.set_w("Z", tv)
        vv = P.emit("v", lambda v: v.tensor_copy(out=M.decs_row[:], in_=M.ps_Z[0:1, 0:NPAIR]),
                    waits=self.r_waits("Z"), inc="sv")
        self.set_r("Z", "sv", vv)
        av_ = P.emit("a", lambda a: a.activation(out=M.decs_row[:], in_=M.decs_row[:], func=AF.Exp),
                     waits=[("sv", vv)], inc="sa")
        tv = P.emit("t", lambda t: t.matmul(out=M.ps_Z[:, 0:NPAIR], lhsT=M.ones_row_f[:],
                                            rhs=M.decs_row[:], start=True, stop=True),
                    waits=[("sa", av_)] + self.w_waits("Z"), inc="st")
        self.set_w("Z", tv)
        vv = P.emit("v", lambda v: v.tensor_copy(out=M.dec_bc[:], in_=M.ps_Z[:, 0:NPAIR]),
                    waits=self.r_waits("Z"), inc="sv")
        self.set_r("Z", "sv", vv)

        if DBG_STAGE < 2:
            self.hstate_v = self.prog.val("sv")
            return None
        def t_diag(t):
            for p in range(NPAIR):
                ins = t.matmul(out=M.ps_Z[:, pc(p)], lhsT=M.Dp[:, pc(p)], rhs=M.Ksq[:, pc(p)],
                         start=True, stop=False)
                ins = t.matmul(out=M.ps_Z[:, pc(p)], lhsT=M.erow[0:1, pc(p)],
                         rhs=M.Hd_row[0:1, pc(p)], start=False, stop=True)
            return ins
        tv = P.emit("t", t_diag, waits=[("sv", dp_v), ("sa", ksq_a), ("sa", erow_a),
                                        ("sv", self.hstate_v)] + self.w_waits("Z"), inc="st")
        self.set_w("Z", tv)

        def v_x0(v):
            ins = v.tensor_tensor(out=M.scrf2[:], in0=M.ps_Z[:], in1=M.lamb_eps_arena[:], op=AD)
            ins = v.reciprocal(out=M.scrf2[:], in_=M.scrf2[:])
            return v.tensor_tensor(out=M.X[:], in0=aq[:], in1=M.scrf2[:], op=MU)
        vv = P.emit("v", v_x0, waits=self.r_waits("Z"), inc="sv")
        self.set_r("Z", "sv", vv)

        if DBG_STAGE < 3:
            self.hstate_v = self.prog.val("sv")
            return None
        self.pready = vv
        self.pb_a = P.emit("a", lambda a: a.activation(out=M.Pb[:], in_=M.X[:], func=AF.Copy),
                           waits=[("sv", vv)], inc="sa")
        self.pb_a_h0 = self.pb_a
        q_tv = self.emit_mv(M.X, ak)

        def v_r0(v):
            v.tensor_tensor(out=M.R[:], in0=aq[:], in1=M.ps_Z[:], op=SU)
            return v.tensor_copy(out=M.Pv[:], in_=M.R[:])
        vv = P.emit("v", v_r0, waits=[("st", q_tv)], inc="sv")
        self.set_r("Z", "sv", vv)
        p_v = vv

        def a_d0(a):
            for p in range(NPAIR):
                h = a if p == 0 else a.raw
                ins = h.activation(out=M.scrb[:, pc(p)], in_=M.R[:, pc(p)], func=AF.Square,
                             accum_out=M.dA[:, p:p + 1])
            return ins
        d_a = P.emit("a", a_d0, waits=[("sv", vv)], inc="sa")
        self.pb_a = P.emit("a", lambda a: a.activation(out=M.Pb[:], in_=M.Pv[:], func=AF.Copy),
                           waits=[("sv", vv)], inc="sa")
        self.pb_a_h0 = self.pb_a
        nd0_v = P.emit("v", lambda v: v.tensor_single_scalar(out=M.nd[:], in_=M.dA[:],
                                                             scalar=1e-10, op=GE),
                       waits=[("sa", d_a)], inc="sv")

        dcur, dnxt = M.dA, M.dB
        Pcur, Pnxt = M.Pv, M.Pv2
        chain_v = nd0_v

        if DBG_STAGE < 4:
            self.hstate_v = self.prog.val("sv")
            return None
        niter = min(DBG_ITERS, MAX_CG)
        for j in range(niter):
            last = (j == niter - 1)
            self.pready = p_v
            q_tv = self.emit_mv(Pcur, ak)

            def v_pq(v, Pcur=Pcur):
                v.skip()
                for p in range(NPAIR):
                    h = v if p == 0 else v.raw
                    ins = h.scalar_tensor_tensor(out=M.scrb[:, pc(p)], in0=Pcur[:, pc(p)], scalar=0.0,
                                           in1=M.ps_Z[:, pc(p)], op0=BYP, op1=MU,
                                           accum_out=M.pq[:, p:p + 1])
                return ins
            pq_v = P.emit("v", v_pq, waits=[("st", q_tv)], inc="sv")

            def v_ca(v, dcur=dcur):
                ins = v.tensor_tensor(out=M.c1[:], in0=M.pq[:], in1=M.pq[:], op=MU)
                ins = v.tensor_single_scalar(out=M.c1[:], in_=M.c1[:], scalar=1e-20, op=GE)
                ins = v.tensor_tensor(out=M.nd[:], in0=M.nd[:], in1=M.c1[:], op=MU)
                ins = v.tensor_scalar(out=M.alphan[:], in0=M.nd[:], scalar1=-1.0, scalar2=1.0,
                                op0=MU, op1=AD)
                ins = v.tensor_tensor(out=M.c1[:], in0=M.pq[:], in1=M.nd[:], op=MU)
                ins = v.tensor_tensor(out=M.c1[:], in0=M.c1[:], in1=M.alphan[:], op=AD)
                ins = v.reciprocal(out=M.c1[:], in_=M.c1[:])
                ins = v.tensor_tensor(out=M.c1[:], in0=M.c1[:], in1=M.nd[:], op=MU)
                ins = v.tensor_tensor(out=M.alpha[:], in0=dcur[:], in1=M.c1[:], op=MU)
                return v.tensor_scalar(out=M.alphan[:], in0=M.alpha[:], scalar1=-1.0,
                                       scalar2=None, op0=MU)
            ca_v = P.emit("v", v_ca, inc="sv")

            if not last:
                rv = ca_v
                for p in range(NPAIR):
                    def v_r(v, p=p):
                        if p > 0:
                            v.skip()
                        return v.scalar_tensor_tensor(
                                    out=M.R[:, pc(p)], in0=M.ps_Z[:, pc(p)],
                                    scalar=M.alphan[:, p:p + 1], in1=M.R[:, pc(p)],
                                    op0=MU, op1=AD)
                    rv = P.emit("v", v_r, inc="sv")
                    if p <= 4:
                        def a_dn(a, p=p, dnxt=dnxt):
                            a.skip()
                            return a.activation(
                                        out=M.scrb[:, pc(p)], in_=M.R[:, pc(p)], func=AF.Square,
                                        accum_out=dnxt[:, p:p + 1])
                        dn_a = P.emit("a", a_dn, waits=[("sv", rv)], inc="sa")
                self.set_r("Z", "sv", rv)
                def v_dn2(v, dnxt=dnxt):
                    v.skip()
                    for p in range(5, 8):
                        h = v if p == 5 else v.raw
                        ins = h.scalar_tensor_tensor(out=M.scrb[:, pc(p)], in0=M.R[:, pc(p)],
                                               scalar=0.0, in1=M.R[:, pc(p)], op0=BYP, op1=MU,
                                               accum_out=dnxt[:, p:p + 1])
                    return ins
                dn_v = P.emit("v", v_dn2, inc="sv")

                def v_cb(v, dcur=dcur, dnxt=dnxt):
                    ins = v.tensor_scalar(out=M.c1[:], in0=M.nd[:], scalar1=-1.0, scalar2=1.0,
                                    op0=MU, op1=AD)
                    ins = v.tensor_tensor(out=M.betac[:], in0=dcur[:], in1=M.nd[:], op=MU)
                    ins = v.tensor_tensor(out=M.c1[:], in0=M.betac[:], in1=M.c1[:], op=AD)
                    ins = v.reciprocal(out=M.c1[:], in_=M.c1[:])
                    ins = v.tensor_tensor(out=M.c1[:], in0=M.c1[:], in1=M.nd[:], op=MU)
                    ins = v.tensor_tensor(out=M.betac[:], in0=dnxt[:], in1=M.c1[:], op=MU)
                    ins = v.tensor_single_scalar(out=M.c1[:], in_=dnxt[:], scalar=1e-10, op=GE)
                    return v.tensor_tensor(out=M.nd[:], in0=M.nd[:], in1=M.c1[:], op=MU)
                cb_v = P.emit("v", v_cb, waits=[("sa", dn_a)], inc="sv")
                chain_v = cb_v

                def v_p1(v, Pcur=Pcur, Pnxt=Pnxt):
                    for p in range(4):
                        h = v if p == 0 else v.raw
                        ins = h.scalar_tensor_tensor(out=Pnxt[:, pc(p)], in0=Pcur[:, pc(p)],
                                               scalar=M.betac[:, p:p + 1], in1=M.R[:, pc(p)],
                                               op0=MU, op1=AD)
                    return ins
                p_v1 = P.emit("v", v_p1, inc="sv")
                def v_p2(v, Pcur=Pcur, Pnxt=Pnxt):
                    v.skip()
                    for p in range(4, 8):
                        h = v if p == 4 else v.raw
                        ins = h.scalar_tensor_tensor(out=Pnxt[:, pc(p)], in0=Pcur[:, pc(p)],
                                               scalar=M.betac[:, p:p + 1], in1=M.R[:, pc(p)],
                                               op0=MU, op1=AD)
                    return ins
                p_v = P.emit("v", v_p2, inc="sv")
                pb1 = P.emit("a", lambda a, Pnxt=Pnxt: a.activation(out=M.Pb[:, 0:512],
                                                                    in_=Pnxt[:, 0:512], func=AF.Copy),
                             waits=[("sv", p_v1)], inc="sa")
                self.pb_a = P.emit("a", lambda a, Pnxt=Pnxt: a.activation(out=M.Pb[:, 512:1024],
                                                                          in_=Pnxt[:, 512:1024],
                                                                          func=AF.Copy),
                                   waits=[("sv", p_v)], inc="sa")
                self.pb_a_h0 = pb1
            else:
                self.set_r("Z", "sv", ca_v)

            def v_x(v, Pcur=Pcur, last=last):
                if not last:
                    v.skip()
                for p in range(NPAIR):
                    h = v if p == 0 else v.raw
                    ins = h.scalar_tensor_tensor(out=M.X[:, pc(p)], in0=Pcur[:, pc(p)],
                                           scalar=M.alpha[:, p:p + 1], in1=M.X[:, pc(p)],
                                           op0=MU, op1=AD)
                return ins
            x_v = P.emit("v", v_x, waits=[("sv", ca_v)] if last else (), inc="sv")
            self.x_done = x_v

            dcur, dnxt = dnxt, dcur
            Pcur, Pnxt = Pnxt, Pcur

        if os.environ.get("MESA_DBG_TAP") and c == 0:
            def g_tap(g):
                ins = None
                for i, tsr in enumerate([M.pq, M.dA, M.dB, M.nd, M.alpha, M.c1, M.betac, M.e_all]):
                    ins = g.dma_start(out=self.dbg_d[i * 128:(i + 1) * 128, 0:NPAIR], in_=tsr[:])
                    ins.then_inc(P.sems["do"], 16)
                ins = g.dma_start(out=self.dbg_d[8 * 128:9 * 128, :], in_=M.X[:, 0:128])
                ins.then_inc(P.sems["do"], 16)
                ins = g.dma_start(out=self.dbg_d[9 * 128:10 * 128, :], in_=M.R[:, 0:128])
                ins.then_inc(P.sems["do"], 16)
                ins = g.dma_start(out=self.dbg_d[10 * 128:11 * 128, :], in_=M.Pv[:, 0:128])
                ins.then_inc(P.sems["do"], 16)
                return ins
            P.emit("g", g_tap, waits=[("sv", p_v)])
            P.bump("do", 16 * 11)
        if DBG_STAGE < 5:
            self.hstate_v = self.prog.val("sv")
            return None
        a_xb = P.emit("a", lambda a: a.activation(out=M.Pb[:], in_=M.X[:], func=AF.Copy),
                      waits=[("sv", self.x_done)], inc="sa")
        def t_xe(t):
            for p in range(NPAIR):
                ins = t.matmul(out=M.ps_B[:, pc(p)], lhsT=M.Pb[:, pc(p)], rhs=M.diag_e[:, pc(p)],
                               start=True, stop=True)
            return ins
        tvx = P.emit("t", t_xe, waits=self.w_waits("B") + [("sa", a_xb), ("sa", self.de_a)],
                     inc="st")
        self.set_w("B", tvx)
        self.de_rd = tvx
        def t_tx(t):
            for p in range(NPAIR):
                ins = t.transpose(out=M.ps_Tb[:, pc(p)], in_=M.Pb[:, pc(p)], identity=M.ident_b[:])
            return ins
        tvp = P.emit("t", t_tx, waits=self.w_waits("Tb") + [("sa", a_xb)], inc="st")
        self.set_w("Tb", tvp)
        av_x = P.emit("a", lambda a: a.activation(out=M.Xt[:], in_=M.ps_B[:], func=AF.Copy),
                      waits=self.r_waits("B"), inc="sa")
        self.set_r("B", "sa", av_x)
        av_p = P.emit("a", lambda a: a.activation(out=M.Pd[:], in_=M.ps_Tb[:], func=AF.Copy),
                      waits=self.r_waits("Tb"), inc="sa")
        self.set_r("Tb", "sa", av_p)
        xt_a = av_p

        def t_o1(t):
            for p in range(NPAIR):
                ins = t.matmul(out=M.ps_Z[:, pc(p)], lhsT=M.Hvk_b[:, pc(p)], rhs=M.Xt[:, pc(p)],
                         start=(p % 4 == 0), stop=False)
            return ins
        tv = P.emit("t", t_o1, waits=self.w_waits("Z") + [("sa", xt_a)], inc="st")
        self.set_w("Z", tv)

        for half in range(2):
            ps = range(half * 4, half * 4 + 4)
            def t_s2(t, ps=ps):
                for p in ps:
                    ins = t.matmul(out=M.ps_S[:, (p % 4) * HD:(p % 4 + 1) * HD], lhsT=M.Vd[:, pc(p)],
                             rhs=M.Pd[:, pc(p)], start=True, stop=True)
                return ins
            tv_s = P.emit("t", t_s2, waits=self.w_waits("S") + [("sa", xt_a), ("sa", self.kv_a)], inc="st")
            self.set_w("S", tv_s)
            def v_m2(v, half=half):
                v.skip()
                return v.tensor_tensor(out=M.Mp[:, half * 512:(half + 1) * 512],
                                       in0=M.Dp[:, half * 512:(half + 1) * 512],
                                       in1=M.ps_S[:], op=MU)
            vv_m = P.emit("v", v_m2, waits=self.r_waits("S"), inc="sv")
            self.set_r("S", "sv", vv_m)
            def t_o2(t, ps=ps):
                for p in ps:
                    ins = t.matmul(out=M.ps_Z[:, pc(p)], lhsT=ak[:, pc(p)], rhs=M.Mp[:, pc(p)],
                             start=False, stop=(p % 4 == 3))
                return ins
            tv_o2 = P.emit("t", t_o2, waits=[("sv", vv_m)], inc="st")
        self.set_w("Z", tv_o2)

        av_ = P.emit("a", lambda a: a.activation(out=M.scrb[:], in_=M.ps_Z[:], func=AF.Square),
                     waits=self.r_waits("Z"), inc="sa")
        self.set_r("Z", "sa", av_)

        def t_rms(t):
            for p in range(NPAIR):
                ins = t.matmul(out=M.ps_B[:, pc(p)], lhsT=M.ones_sq_b[:], rhs=M.scrb[:, pc(p)],
                         start=True, stop=True)
            return ins
        tv = P.emit("t", t_rms, waits=self.w_waits("B") + [("sa", av_)], inc="st")
        self.set_w("B", tv)

        def v_rms(v):
            return v.tensor_scalar(out=M.scrf[:], in0=M.ps_B[:], scalar1=1.0 / HD,
                                   scalar2=1e-6, op0=MU, op1=AD)
        vv = P.emit("v", v_rms, waits=self.r_waits("B"), inc="sv")
        self.set_r("B", "sv", vv)
        av_ = P.emit("a", lambda a: a.activation(out=M.scrf[:], in_=M.scrf[:], func=AF.Sqrt),
                     waits=[("sv", vv)], inc="sa")
        vv = P.emit("v", lambda v: v.reciprocal(out=M.scrf[:], in_=M.scrf[:]),
                    waits=[("sa", av_)], inc="sv")

        def v_on(v):
            return v.scalar_tensor_tensor(out=M.scrb[:], in0=M.ps_Z[:], scalar=M.onw_col[:],
                                          in1=M.scrf[:], op0=MU, op1=MU)
        on_v = P.emit("v", v_on, waits=[("sv", vv)] + self.r_waits("Z"), inc="sv")
        self.set_r("Z", "sv", on_v)

        def g_o(g):
            ins = None
            for p in range(NPAIR):
                b, h = p // HPC, p % HPC
                ins = g.dma_start(out=o_dram[h * 128:(h + 1) * 128,
                                             b * b_stride + c * C: b * b_stride + (c + 1) * C],
                                  in_=M.scrb[:, pc(p)])
                ins.then_inc(P.sems["do"], 16)
            return ins
        P.emit("g", g_o, waits=[("sv", on_v)])
        o_done = P.bump("do", 16 * NPAIR)

        if DBG_STAGE < 6:
            self.hstate_v = self.prog.val("sv")
            return None
        def v_kt(v):
            for p in range(NPAIR):
                h = v if p == 0 else v.raw
                ins = h.tensor_scalar(out=M.Ktil[:, pc(p)], in0=ak[:, pc(p)],
                                scalar1=M.w_all[:, p:p + 1], scalar2=None, op0=MU)
            return ins
        kt_v = P.emit("v", v_kt, waits=[("sv", w_v), ("sv", on_v)], inc="sv")

        def t_kk(t):
            for p in range(NPAIR):
                ins = t.matmul(out=M.ps_Z[:, pc(p)], lhsT=M.Ktil[:, pc(p)], rhs=ak[:, pc(p)],
                         start=True, stop=True)
            return ins
        tv = P.emit("t", t_kk, waits=self.w_waits("Z") + [("sv", kt_v)], inc="st")
        self.set_w("Z", tv)

        def v_hkk(v):
            for p in range(NPAIR):
                h = v if p == 0 else v.raw
                ins = h.scalar_tensor_tensor(out=M.Hkk_f[:, pc(p)], in0=M.Hkk_f[:, pc(p)],
                                       scalar=M.dec_bc[:, p:p + 1], in1=M.ps_Z[:, pc(p)],
                                       op0=MU, op1=AD)
            return ins
        vv = P.emit("v", v_hkk, waits=self.r_waits("Z"), inc="sv")
        self.set_r("Z", "sv", vv)
        hkkb_a = P.emit("a", lambda a: a.activation(out=M.Hkk_b[:], in_=M.Hkk_f[:], func=AF.Copy),
                        waits=[("sv", vv)], inc="sa")

        def v_vt(v):
            for p in range(NPAIR):
                h = v if p == 0 else v.raw
                ins = h.tensor_scalar(out=M.Ktil[:, pc(p)], in0=av[:, pc(p)],
                                scalar1=M.w_all[:, p:p + 1], scalar2=None, op0=MU)
            return ins
        vt_v = P.emit("v", v_vt, waits=[("sv", vv)], inc="sv")

        def t_vk(t):
            for p in range(NPAIR):
                ins = t.matmul(out=M.ps_B[:, pc(p)], lhsT=M.Ktil[:, pc(p)], rhs=ak[:, pc(p)],
                         start=True, stop=True)
            return ins
        tv = P.emit("t", t_vk, waits=self.w_waits("B") + [("sv", vt_v)], inc="st")
        self.set_w("B", tv)

        def v_hvk(v):
            for p in range(NPAIR):
                h = v if p == 0 else v.raw
                ins = h.scalar_tensor_tensor(out=M.Hvk_f[:, pc(p)], in0=M.Hvk_f[:, pc(p)],
                                       scalar=M.dec_bc[:, p:p + 1], in1=M.ps_B[:, pc(p)],
                                       op0=MU, op1=AD)
            return ins
        vv2 = P.emit("v", v_hvk, waits=self.r_waits("B"), inc="sv")
        self.set_r("B", "sv", vv2)
        hvkb_a = P.emit("a", lambda a: a.activation(out=M.Hvk_b[:], in_=M.Hvk_f[:], func=AF.Copy),
                        waits=[("sv", vv2)], inc="sa")

        def v_wk(v):
            for p in range(NPAIR):
                h = v if p == 0 else v.raw
                ins = h.tensor_scalar(out=M.scrb[:, pc(p)], in0=M.Ksq[:, pc(p)],
                                scalar1=M.w_all[:, p:p + 1], scalar2=None, op0=MU)
            return ins
        wk_v = P.emit("v", v_wk, waits=[("sv", vv2), ("do", o_done)], inc="sv")

        def t_hd(t):
            for p in range(NPAIR):
                ins = t.matmul(out=M.ps_Z[0:1, pc(p)], lhsT=M.ones_col_b[:], rhs=M.scrb[:, pc(p)],
                         start=True, stop=True)
            return ins
        tv = P.emit("t", t_hd, waits=self.w_waits("Z") + [("sv", wk_v)], inc="st")
        self.set_w("Z", tv)

        def v_hd(v):
            for p in range(NPAIR):
                h = v if p == 0 else v.raw
                ins = h.scalar_tensor_tensor(out=M.Hd_row[0:1, pc(p)], in0=M.Hd_row[0:1, pc(p)],
                                       scalar=M.decs_row[0:1, p:p + 1], in1=M.ps_Z[0:1, pc(p)],
                                       op0=MU, op1=AD)
            return ins
        vv3 = P.emit("v", v_hd, waits=self.r_waits("Z"), inc="sv")
        self.set_r("Z", "sv", vv3)

        self.hstate_v = vv3
        self.hstate_a = max(hkkb_a, hvkb_a)
        return o_done



def build_mesa_test(T):
    nc = bass.Bass("TRN2", target_bir_lowering=False, debug=False, detect_race_conditions=False)
    nchunk = T // C
    qn_d = nc.dram_tensor("qn", [4 * T, 2 * HD], F32, kind="ExternalInput").ap()
    kn_d = nc.dram_tensor("kn", [4 * T, 2 * HD], BF16, kind="ExternalInput").ap()
    vn_d = nc.dram_tensor("vn", [4 * T, 2 * HD], BF16, kind="ExternalInput").ap()
    g_d = nc.dram_tensor("g", [4 * T, HPC], F32, kind="ExternalInput").ap()
    bt_d = nc.dram_tensor("bt", [4 * T, HPC], F32, kind="ExternalInput").ap()
    lamp_d = nc.dram_tensor("lamp", [1, HPC * 128], F32, kind="ExternalInput").ap()
    onw_d = nc.dram_tensor("onw", [1, 128], F32, kind="ExternalInput").ap()
    o_d = nc.dram_tensor("o", [HPC * 128, 4 * T], BF16, kind="ExternalOutput").ap()
    dbg_d = nc.dram_tensor("dbg", [11 * 128, 128], F32, kind="ExternalOutput").ap()

    prog = Prog()
    with ExitStack() as ctx:
        M = Mesa(nc, ctx, prog, T)
        M.dbg_d = dbg_d
        sem_names = ["st", "sv", "sa", "sg", "dlc", "dl0", "dl1", "do", "dsx", "dbb", "dst", "dr2", "ddy", "do0", "do1"]
        sems = {n: ctx.enter_context(nc.semaphore(n)) for n in sem_names}
        M.emit_consts(lamp_d, onw_d)

        for cidx in range(min(nchunk, DBG_NCHUNK)):
            z = cidx % 2
            dl_sem = f"dl{z}"
            def s_ld(s, z=z, cidx=cidx, dl_sem=dl_sem):
                ins = None
                for b in range(4):
                    r0 = b * T + cidx * C
                    for (dst, srcd) in ((M.aq[z], qn_d), (M.ak[z], kn_d), (M.av[z], vn_d)):
                        ins = s.dma_start(out=dst[:, b * 256:(b + 1) * 256],
                                          in_=srcd[r0:r0 + C, :])
                        ins.then_inc(prog.sems[dl_sem], 16)
                    ins = s.dma_start(out=M.ag[z][:, b * HPC:(b + 1) * HPC], in_=g_d[r0:r0 + C, :])
                    ins.then_inc(prog.sems[dl_sem], 16)
                    ins = s.dma_start(out=M.ab[z][:, b * HPC:(b + 1) * HPC], in_=bt_d[r0:r0 + C, :])
                    ins.then_inc(prog.sems[dl_sem], 16)
                return ins
            w = [("sv", M.hstate_v)] if cidx >= 2 else []
            prog.emit("s", s_ld, waits=w)
            dl_val = prog.bump(dl_sem, 16 * 20)
            M.emit_chunk(z, cidx, o_d, dl_sem, dl_val, T)

        prog.emit("g", lambda g: g.wait_ge(sems["do"], prog.val("do")) or g.nop())

        block = ctx.enter_context(nc.Block())
        prog.realize(block, sems)
    return nc


def mesa_test_run(qn, kn, vn, g, bt, lamp, onw):
    T = qn.shape[0] // 4
    nc = build_mesa_test(T)
    import ml_dtypes
    in_map = dict(qn=qn.astype(np.float32), kn=kn.astype(ml_dtypes.bfloat16),
                  vn=vn.astype(ml_dtypes.bfloat16), g=g.astype(np.float32),
                  bt=bt.astype(np.float32), lamp=lamp.astype(np.float32),
                  onw=onw.astype(np.float32))
    res = run_bass_kernel_spmd(nc, [in_map], core_ids=[0])
    return np.asarray(res.results[0]["o"].astype(np.float32))



ROWS = 16384
T_FULL = 4096
RPC = 2048


def emit_p12(nc, prog, M, ctx, xT, xTr, wcat, babb, st_in, st_all,
             q_d, k_d, v_d, g_d, bt_d):
    P = prog
    S = lambda nm, sh, dt=F32: ctx.enter_context(nc.sbuf_tensor(nm, sh, dt))
    stg = S("p2_stg", [128, 772])
    wcb = S("p2_wcb", [128, 16 * 772], BF16)
    sbx = [S(f"p2_sbx{z}", [128, 16 * 128], F32) for z in range(2)]
    sbxb = [S(f"p2_sbxb{z}", [128, 16 * 128], BF16) for z in range(2)]
    sbig = S("p2_sbig", [128, 512])
    sbig2 = S("p2_sbig2", [128, 512])

    negmu_stg = S("p2_negmu", [1, RPC])
    rs_stg = S("p2_rsstg", [1, RPC])
    rs_cols = S("p2_rscols", [128, 128])
    csw = S("p2_csw", [1, 772])
    qsil = [S(f"p2_qsil{z}", [128, 512]) for z in range(2)]
    sq = [S(f"p2_sq{z}", [128, 512]) for z in range(2)]
    rn4 = [S(f"p2_rn4{z}", [128, 4]) for z in range(2)]
    qout = [S(f"p2_qout{z}", [128, 256]) for z in range(2)]
    kout = [S(f"p2_kout{z}", [128, 256], BF16) for z in range(2)]
    vout = [S(f"p2_vout{z}", [128, 256], BF16) for z in range(2)]
    abt = [S(f"p2_abt{z}", [128, 4]) for z in range(2)]
    gout = [S(f"p2_gout{z}", [128, HPC]) for z in range(2)]
    btout = [S(f"p2_btout{z}", [128, HPC]) for z in range(2)]
    bias_bc = S("p2_biasbc", [128, 4])
    babb_r = S("p2_babbr", [1, 4])
    strow = S("p2_strow", [1, 2 * RPC])

    consts_v = M.consts_done

    for dt in range(16):
        def s_w(s, dt=dt):
            ins = s.dma_start(out=stg[:], in_=wcat[dt * 128:(dt + 1) * 128, :])
            ins.then_inc(P.sems["dlc"], 16)
            return ins
        P.emit("s", s_w, waits=[("sv", P.val("sv")), ("st", P.val("st"))])
        wv = P.bump("dlc", 16)
        vv = P.emit("v", lambda v, dt=dt: v.tensor_copy(out=wcb[:, dt * 772:(dt + 1) * 772], in_=stg[:]),
                    waits=[("dlc", wv)], inc="sv")
        def t_cs(t, dt=dt):
            t.matmul(out=M.ps_B[0:1, 0:512], lhsT=M.ones_col_f[:], rhs=stg[:, 0:512],
                     start=(dt == 0), stop=(dt == 15))
            return t.matmul(out=M.ps_B[0:1, 512:772], lhsT=M.ones_col_f[:], rhs=stg[:, 512:772],
                            start=(dt == 0), stop=(dt == 15))
        tv = P.emit("t", t_cs, waits=[("dlc", wv), ("sv", consts_v)] + (M.w_waits("B") if dt == 0 else []), inc="st")
    M.set_w("B", tv)
    vv = P.emit("v", lambda v: v.tensor_copy(out=csw[:], in_=M.ps_B[0:1, 0:772]),
                waits=M.r_waits("B"), inc="sv")
    M.set_r("B", "sv", vv)

    def s_bb(s):
        ins = s.dma_start(out=babb_r[:], in_=babb)
        ins.then_inc(P.sems["dbb"], 16)
        return ins
    P.emit("s", s_bb)
    bbv = P.bump("dbb", 16)
    tv = P.emit("t", lambda t: t.matmul(out=M.ps_T[:, 0:4], lhsT=M.ones_row_f[:], rhs=babb_r[:],
                                        start=True, stop=True),
                waits=[("dbb", bbv), ("sv", consts_v)] + M.w_waits("T"), inc="st")
    M.set_w("T", tv)
    vv = P.emit("v", lambda v: v.tensor_copy(out=bias_bc[:], in_=M.ps_T[:, 0:4]),
                waits=M.r_waits("T"), inc="sv")
    M.set_r("T", "sv", vv)

    xTr3 = xTr.rearrange("(dt p) c -> p dt c", p=128)
    sbw, sqw = sbx[0], sbx[1]
    for rb in range(4):
        for qb in range(4):
            def s_x(s, rb=rb, qb=qb):
                ins = s.dma_start(out=sbw[:].rearrange("p (dt c) -> p dt c", dt=4),
                                  in_=xTr3[:, qb * 4:(qb + 1) * 4, rb * 512:(rb + 1) * 512])
                ins.then_inc(P.sems["dsx"], 16)
                return ins
            P.emit("s", s_x, waits=[("sa", P.val("sa")), ("st", P.val("st")),
                                    ("sv", P.val("sv"))])
            xv = P.bump("dsx", 16)
            av = P.emit("a", lambda a: a.activation(out=sqw[:], in_=sbw[:], func=AF.Square),
                        waits=[("dsx", xv)], inc="sa")
            def t_st(t, rb=rb, qb=qb):
                for k in range(4):
                    t.matmul(out=M.ps_S[0:1, 0:512], lhsT=M.ones_col_f[:],
                             rhs=sbw[:, k * 512:(k + 1) * 512],
                             start=(qb == 0 and k == 0), stop=(qb == 3 and k == 3))
                    ins = t.matmul(out=M.ps_B[0:1, 0:512], lhsT=M.ones_col_f[:],
                                   rhs=sqw[:, k * 512:(k + 1) * 512],
                                   start=(qb == 0 and k == 0), stop=(qb == 3 and k == 3))
                return ins
            tv = P.emit("t", t_st, waits=[("dsx", xv), ("sa", av)]
                        + (M.w_waits("S", "B") if qb == 0 else []), inc="st")
        M.set_w("S", tv)
        M.set_w("B", tv)

        def v_fin(v, rb=rb):
            v.tensor_scalar(out=strow[0:1, rb * 512:(rb + 1) * 512], in0=M.ps_S[0:1, 0:512],
                            scalar1=1.0 / HID, scalar2=None, op0=MU)
            v.tensor_scalar(out=sbig[0:1, 0:512], in0=M.ps_B[0:1, 0:512],
                            scalar1=1.0 / HID, scalar2=None, op0=MU)
            v.tensor_tensor(out=sbig2[0:1, 0:512], in0=strow[0:1, rb * 512:(rb + 1) * 512],
                            in1=strow[0:1, rb * 512:(rb + 1) * 512], op=MU)
            v.tensor_tensor(out=sbig[0:1, 0:512], in0=sbig[0:1, 0:512], in1=sbig2[0:1, 0:512], op=SU)
            return v.tensor_scalar(out=sbig[0:1, 0:512], in0=sbig[0:1, 0:512],
                                   scalar1=1e-5, scalar2=None, op0=AD)
        vv = P.emit("v", v_fin, waits=M.r_waits("S", "B"), inc="sv")
        M.set_r("S", "sv", vv)
        M.set_r("B", "sv", vv)
        av = P.emit("a", lambda a, rb=rb: a.activation(out=sbig[0:1, 0:512], in_=sbig[0:1, 0:512],
                                                       func=AF.Sqrt),
                    waits=[("sv", vv)], inc="sa")
        vv = P.emit("v", lambda v, rb=rb: v.reciprocal(out=strow[0:1, RPC + rb * 512:RPC + (rb + 1) * 512],
                                                       in_=sbig[0:1, 0:512]),
                    waits=[("sa", av)], inc="sv")
    stats_v = vv
    stats_sa, stats_st = P.val("sa"), P.val("st")

    def g_st(g):
        ins = g.dma_start(out=st_in[:, :], in_=strow[0:1, :])
        ins.then_inc(P.sems["do"], 16)
        return ins
    P.emit("g", g_st, waits=[("sv", stats_v)])
    sv1 = P.bump("do", 16)
    def g_ag(g):
        return g.collective_compute("AllGather", mybir.AluOpType.bypass,
                                    replica_groups=[list(range(NCORES))],
                                    ins=[st_in[:, :].opt()], outs=[st_all[:, :].opt()])
    agv = P.emit("g", g_ag, waits=[("do", sv1)], inc="cc")
    tv = None
    for cb in range(8):
        def g_rd(g, cb=cb):
            ins = g.dma_start(out=rs_stg[:], in_=st_all[2 * cb + 1:2 * cb + 2, :])
            ins.then_inc(P.sems["dst"], 16)
            return ins
        P.emit("g", g_rd, waits=[("cc", agv), ("st", tv or 0)])
        stv = P.bump("dst", 16)
        def t_rsc(t, cb=cb):
            ins = None
            for j in range(16):
                ins = t.transpose(out=M.ps_T[:, cb * 16 + j:cb * 16 + j + 1],
                                  in_=rs_stg[0:1, j * 128:(j + 1) * 128],
                                  identity=M.ones_row_f[0:1, 0:1])
            return ins
        tv = P.emit("t", t_rsc, waits=[("dst", stv)] + (M.w_waits("T") if cb == 0 else []), inc="st")
    M.set_w("T", tv)
    vv = P.emit("v", lambda v: v.tensor_copy(out=rs_cols[:], in_=M.ps_T[:, 0:128]),
                waits=M.r_waits("T"), inc="sv")
    M.set_r("T", "sv", vv)
    rsc_v = vv

    wcast_v = P.val("sv")
    negmu_v = 0
    banks = ["Z", "B"]
    psq = [M.ps_Z, M.ps_B]
    cast_prev = [0, 0]
    xv_pre = [0, 0]
    tv_prev = [0, 0]
    sil_prev = [0, 0]
    ag_prev = [0, 0]
    g2_prev = [0, 0]
    do_prev = [0, 0]
    for rt in range(128):
        z = rt % 2
        dl_sem = f"dl{z}"
        bank = banks[z]
        ps = psq[z]
        if rt % 16 == 0:
            cb = rt // 16
            def g_mu(g, cb=cb):
                ins = g.dma_start(out=rs_stg[:], in_=st_all[2 * cb:2 * cb + 1, :])
                ins.then_inc(P.sems["dst"], 16)
                return ins
            P.emit("g", g_mu, waits=[("cc", agv), ("st", P.val("st"))])
            muv = P.bump("dst", 16)
            negmu_v = P.emit("v", lambda v: v.tensor_scalar(out=negmu_stg[:], in0=rs_stg[:],
                                                            scalar1=-1.0, scalar2=None, op0=MU),
                             waits=[("dst", muv), ("sv", rsc_v)], inc="sv")
        def s_xt(s, rt=rt, z=z, dl_sem=dl_sem):
            ins = None
            for dt in range(16):
                ins = s.dma_start(out=sbx[z][:, dt * 128:(dt + 1) * 128],
                                  in_=xT[dt * 128:(dt + 1) * 128, rt * 128:(rt + 1) * 128])
                ins.then_inc(P.sems[dl_sem], 16)
            return ins
        exw = [("sa", stats_sa), ("st", stats_st)] if rt < 2 else []
        P.emit("s", s_xt, waits=[("sv", cast_prev[z])] + exw)
        xv_pre[z] = P.bump(dl_sem, 16 * 16)
        cast_prev[z] = P.emit("v", lambda v, z=z: v.tensor_copy(out=sbxb[z][:], in_=sbx[z][:]),
                              waits=[(dl_sem, xv_pre[z]), ("sv", wcast_v), ("st", tv_prev[z])],
                              inc="sv")
        def t_pj(t, rt=rt, z=z, ps=ps):
            for dt in range(16):
                t.matmul(out=ps[:, 0:512], lhsT=sbxb[z][:, dt * 128:(dt + 1) * 128],
                         rhs=wcb[:, dt * 772:dt * 772 + 512], start=(dt == 0), stop=False)
                t.matmul(out=ps[:, 512:772], lhsT=sbxb[z][:, dt * 128:(dt + 1) * 128],
                         rhs=wcb[:, dt * 772 + 512:(dt + 1) * 772], start=(dt == 0), stop=False)
            rtl = rt % 16
            t.matmul(out=ps[:, 0:512], lhsT=negmu_stg[0:1, rtl * 128:(rtl + 1) * 128],
                     rhs=csw[0:1, 0:512], start=False, stop=True)
            return t.matmul(out=ps[:, 512:772], lhsT=negmu_stg[0:1, rtl * 128:(rtl + 1) * 128],
                            rhs=csw[0:1, 512:772], start=False, stop=True)
        tv = P.emit("t", t_pj, waits=[("sv", cast_prev[z]), ("sv", negmu_v), ("sv", rsc_v),
                                      ("sa", sil_prev[z])] + M.w_waits(bank),
                    inc="st")
        M.set_w(bank, tv)
        tv_prev[z] = tv

        def a_sil(a, rt=rt, z=z, ps=ps):
            a.activation(out=qsil[z][:], in_=ps[:, 0:512], func=AF.Silu,
                         scale=rs_cols[:, rt:rt + 1])
            a.activation(out=vout[z][:], in_=ps[:, 512:768], func=AF.Silu,
                         scale=rs_cols[:, rt:rt + 1])
            return a.activation(out=sq[z][:], in_=qsil[z][:], func=AF.Square)
        av = P.emit("a", a_sil, waits=M.r_waits(bank) + [(f"do{z}", do_prev[z]), ("sv", g2_prev[z])],
                    inc="sa")
        sil_prev[z] = av
        def v_ab(v, rt=rt, z=z, ps=ps):
            return v.tensor_scalar(out=abt[z][:], in0=ps[:, 768:772],
                                   scalar1=rs_cols[:, rt:rt + 1], scalar2=None, op0=MU)
        abv = P.emit("v", v_ab, waits=M.r_waits(bank) + [("sa", ag_prev[z]), ("sv", g2_prev[z])],
                     inc="sv")
        M.set_r(bank, "sv", abv)

        def v_l2(v, z=z):
            return v.tensor_reduce(out=rn4[z][:], in_=sq[z][:].rearrange("p (s d) -> p s d", s=4),
                                   axis=AX, op=AD)
        lv = P.emit("v", v_l2, waits=[("sa", av)], inc="sv")
        av2 = P.emit("a", lambda a, z=z: a.activation(out=rn4[z][:], in_=rn4[z][:], func=AF.Sqrt),
                     waits=[("sv", lv)], inc="sa")
        def v_rn(v, z=z):
            v.tensor_scalar(out=rn4[z][:], in0=rn4[z][:], scalar1=1e-12, scalar2=None, op0=MX)
            return v.reciprocal(out=rn4[z][:], in_=rn4[z][:])
        rv = P.emit("v", v_rn, waits=[("sa", av2)], inc="sv")
        def v_sc(v, z=z):
            for sgi in range(2):
                v.tensor_scalar(out=qout[z][:, sgi * 128:(sgi + 1) * 128],
                                in0=qsil[z][:, sgi * 128:(sgi + 1) * 128],
                                scalar1=rn4[z][:, sgi:sgi + 1], scalar2=None, op0=MU)
            for sgi in range(2):
                v.tensor_scalar(out=kout[z][:, sgi * 128:(sgi + 1) * 128],
                                in0=qsil[z][:, 256 + sgi * 128:256 + (sgi + 1) * 128],
                                scalar1=rn4[z][:, 2 + sgi:3 + sgi], scalar2=None, op0=MU)
            v.tensor_tensor(out=abt[z][:], in0=abt[z][:], in1=bias_bc[:], op=AD)
            v.tensor_scalar(out=gout[z][:], in0=abt[z][:, 0:2], scalar1=0.0, scalar2=None, op0=MI)
            v.tensor_scalar(out=sq[z][0:128, 0:2], in0=abt[z][:, 0:2], scalar1=-1.0, scalar2=None, op0=MU)
            return v.tensor_tensor(out=sq[z][0:128, 2:4], in0=abt[z][:, 0:2], in1=sq[z][0:128, 0:2], op=MI)
        scv = P.emit("v", v_sc, waits=[("sv", rv), ("sa", ag_prev[z]), (f"do{z}", do_prev[z])], inc="sv")
        def a_g(a, z=z):
            a.activation(out=sq[z][0:128, 4:6], in_=sq[z][0:128, 2:4], func=AF.Exp)
            a.activation(out=sq[z][0:128, 4:6], in_=sq[z][0:128, 4:6], func=AF.Ln,
                         bias=M.ones_col_f[:])
            return a.activation(out=btout[z][:], in_=abt[z][:, 2:4], func=AF.Sigmoid)
        agv2 = P.emit("a", a_g, waits=[("sv", scv)], inc="sa")
        ag_prev[z] = agv2
        def v_g2(v, z=z):
            return v.tensor_tensor(out=gout[z][:], in0=gout[z][:], in1=sq[z][0:128, 4:6], op=SU)
        gv2 = P.emit("v", v_g2, waits=[("sa", agv2)], inc="sv")
        g2_prev[z] = gv2

        def g_out(g, rt=rt, z=z):
            ins = None
            for (dst, srct) in ((q_d, qout[z]), (k_d, kout[z]), (v_d, vout[z]),
                                (g_d, gout[z]), (bt_d, btout[z])):
                ins = g.dma_start(out=dst[rt * 128:(rt + 1) * 128, :], in_=srct[:])
                ins.then_inc(P.sems[f"do{z}"], 16)
            return ins
        P.emit("g", g_out, waits=[("sv", gv2), ("sa", agv2)])
        P.bump(f"do{z}", 16 * 5)
        do_prev[z] = P.val(f"do{z}")
    return (P.val("do0"), P.val("do1"))


def emit_p4a(nc, prog, ctx4, xTr, wo, o_all, x1_d, x1b_d, ln2st_d, bw):
    P = prog
    S = lambda nm, sh, dt=F32: ctx4.enter_context(nc.sbuf_tensor(nm, sh, dt))
    P_ = lambda nm, sh, dt=F32: ctx4.enter_context(nc.psum_tensor(nm, sh, dt))
    wob = S("p4_wob", [128, 16 * 2048], BF16)
    stg = S("p4_stg", [128, 2048])
    oT = S("p4_oT", [128, 16 * 512], BF16)
    xa = S("p4_xa", [128, 16 * 512])
    atn = S("p4_atn", [128, 16 * 512], BF16)
    sqt = S("p4_sqt", [128, 512], BF16)
    mu_bc = S("p4_mubc", [128, 512])
    rs_bc = S("p4_rsbc", [128, 512])
    srow = S("p4_srow", [1, 1024])
    t1 = S("p4_t1", [128, 512])
    x1t = S("p4_x1t", [128, 512])
    x1bt = S("p4_x1bt", [128, 512], BF16)
    ones_col_f = S("p4_onescf", [128, 1])
    ones_col_b = S("p4_onescb", [128, 1], BF16)
    ones_row_f = S("p4_onesrf", [1, 128])
    ps1 = P_("p4_ps1", [128, 512])
    ps2 = P_("p4_ps2", [128, 512])
    pss = P_("p4_pss", [1, 2048])
    rd = {"1": ("sv", 0), "2": ("sv", 0), "s": ("sv", 0)}
    wr = {"1": 0, "2": 0, "s": 0}
    bar = bw

    gv = P.emit("g", lambda g: (g.memset(ones_col_f[:], 1.0), g.memset(ones_col_b[:], 1.0),
                                g.memset(ones_row_f[:], 1.0))[-1], waits=bar, inc="sg")

    cast_v = 0
    for dt in range(16):
        def s_w(s, dt=dt):
            ins = s.dma_start(out=stg[:], in_=wo[dt * 128:(dt + 1) * 128, :])
            ins.then_inc(P.sems["dlc"], 16)
            return ins
        P.emit("s", s_w, waits=bar + [("sv", cast_v)])
        wv = P.bump("dlc", 16)
        cast_v = P.emit("v", lambda v, dt=dt: v.tensor_copy(out=wob[:, dt * 2048:(dt + 1) * 2048],
                                                            in_=stg[:]),
                        waits=bar + [("dlc", wv)], inc="sv")

    ln2_parts = []
    ep_v = 0
    prev_l2do = 0
    for r in range(4):
        def s_o(g, r=r):
            pid = g.partition_id()
            src3 = o_all.rearrange("(dt p) r -> p dt r", p=128)[:, :, bass.ds(pid * RPC + r * 512, 512)]
            ins = g.dma_start(out=oT[:].rearrange("p (dt c) -> p dt c", dt=16), in_=src3)
            ins.then_inc(P.sems["ddy"], 16)
            return ins
        P.emit("g", s_o, waits=bar + [("sv", ep_v)])
        ov = P.bump("ddy", 16)
        def s_xa(s, r=r):
            ins = None
            for dt in range(16):
                ins = s.dma_start(out=xa[:, dt * 512:(dt + 1) * 512],
                                  in_=xTr[dt * 128:(dt + 1) * 128, r * 512:(r + 1) * 512])
                ins.then_inc(P.sems["dl1"], 16)
            return ins
        P.emit("s", s_xa, waits=bar + [("sv", ep_v)])
        xv = P.bump("dl1", 16 * 16)

        st_tv = None
        for cot in range(16):
            key = "1" if cot % 2 == 0 else "2"
            psx = ps1 if cot % 2 == 0 else ps2
            def t_wo2(t, cot=cot, psx=psx):
                ins = None
                for dt in range(16):
                    ins = t.matmul(out=psx[:], lhsT=wob[:, dt * 2048 + cot * 128:dt * 2048 + (cot + 1) * 128],
                                   rhs=oT[:, dt * 512:(dt + 1) * 512], start=(dt == 0), stop=(dt == 15))
                return ins
            tv = P.emit("t", t_wo2, waits=[("ddy", ov), ("sv", cast_v)] + [rd[key]], inc="st")
            wr[key] = tv
            vv = P.emit("v", lambda v, cot=cot, psx=psx: v.tensor_copy(out=atn[:, cot * 512:(cot + 1) * 512],
                                                                       in_=psx[:]),
                        waits=[("st", wr[key])], inc="sv")
            rd[key] = ("sv", vv)
            av = P.emit("a", lambda a, cot=cot: a.activation(out=sqt[:], in_=atn[:, cot * 512:(cot + 1) * 512],
                                                             func=AF.Square),
                        waits=[("sv", vv), ("sa", P.val("sa")), ("st", st_tv or wr["s"])], inc="sa")
            def t_st(t, cot=cot):
                t.matmul(out=pss[0:1, 0:512], lhsT=ones_col_b[:], rhs=atn[:, cot * 512:(cot + 1) * 512],
                         start=(cot == 0), stop=(cot == 15))
                return t.matmul(out=pss[0:1, 512:1024], lhsT=ones_col_b[:], rhs=sqt[:],
                                start=(cot == 0), stop=(cot == 15))
            st_tv = P.emit("t", t_st, waits=[("sv", vv), ("sa", av)] + ([rd["s"]] if cot == 0 else []),
                           inc="st")
        wr["s"] = st_tv

        def v_fin(v):
            v.tensor_scalar(out=srow[0:1, 0:512], in0=pss[0:1, 0:512], scalar1=1.0 / HID, scalar2=None, op0=MU)
            v.tensor_scalar(out=srow[0:1, 512:1024], in0=pss[0:1, 512:1024], scalar1=1.0 / HID, scalar2=None, op0=MU)
            v.tensor_tensor(out=t1[0:1, :], in0=srow[0:1, 0:512], in1=srow[0:1, 0:512], op=MU)
            v.tensor_tensor(out=srow[0:1, 512:1024], in0=srow[0:1, 512:1024], in1=t1[0:1, :], op=SU)
            return v.tensor_scalar(out=srow[0:1, 512:1024], in0=srow[0:1, 512:1024], scalar1=1e-5, scalar2=None, op0=AD)
        vv = P.emit("v", v_fin, waits=[("st", wr["s"]), ("do", prev_l2do)], inc="sv")
        rd["s"] = ("sv", vv)
        av = P.emit("a", lambda a: a.activation(out=srow[0:1, 512:1024], in_=srow[0:1, 512:1024], func=AF.Sqrt),
                    waits=[("sv", vv)], inc="sa")
        vv = P.emit("v", lambda v: v.reciprocal(out=srow[0:1, 512:1024], in_=srow[0:1, 512:1024]),
                    waits=[("sa", av)], inc="sv")
        def t_bc(t):
            t.matmul(out=ps1[:], lhsT=ones_row_f[:], rhs=srow[0:1, 0:512], start=True, stop=True)
            return t.matmul(out=ps2[:], lhsT=ones_row_f[:], rhs=srow[0:1, 512:1024], start=True, stop=True)
        tv = P.emit("t", t_bc, waits=[("sv", vv)] + [rd["1"], rd["2"]], inc="st")
        wr["1"] = wr["2"] = tv
        def v_bc(v):
            v.tensor_copy(out=mu_bc[:], in_=ps1[:])
            return v.tensor_copy(out=rs_bc[:], in_=ps2[:])
        vv = P.emit("v", v_bc, waits=[("st", tv)], inc="sv")
        rd["1"] = rd["2"] = ("sv", vv)
        bc_v = vv

        ln2_tv = None
        for cot in range(16):
            def v_x1(v, cot=cot):
                v.tensor_tensor(out=t1[:], in0=atn[:, cot * 512:(cot + 1) * 512], in1=mu_bc[:], op=SU)
                v.tensor_tensor(out=t1[:], in0=t1[:], in1=rs_bc[:], op=MU)
                v.tensor_tensor(out=x1t[:], in0=t1[:], in1=xa[:, cot * 512:(cot + 1) * 512], op=AD)
                return v.tensor_copy(out=x1bt[:], in_=x1t[:])
            vv = P.emit("v", v_x1, waits=[("sv", bc_v), ("dl1", xv), ("do", P.val("do"))], inc="sv")
            av = P.emit("a", lambda a: a.activation(out=sqt[:], in_=x1bt[:], func=AF.Square),
                        waits=[("sv", vv), ("st", ln2_tv or st_tv)], inc="sa")
            def t_l2(t, cot=cot):
                t.matmul(out=pss[0:1, 1024:1536], lhsT=ones_col_b[:], rhs=x1bt[:],
                         start=(cot == 0), stop=(cot == 15))
                return t.matmul(out=pss[0:1, 1536:2048], lhsT=ones_col_b[:], rhs=sqt[:],
                                start=(cot == 0), stop=(cot == 15))
            ln2_tv = P.emit("t", t_l2, waits=[("sv", vv), ("sa", av)], inc="st")
            def g_x1(g, cot=cot, r=r):
                ins = g.dma_start(out=x1_d[cot * 128:(cot + 1) * 128, r * 512:(r + 1) * 512], in_=x1t[:])
                ins.then_inc(P.sems["do"], 16)
                ins = g.dma_start(out=x1b_d[cot * 128:(cot + 1) * 128, r * 512:(r + 1) * 512], in_=x1bt[:])
                ins.then_inc(P.sems["do"], 16)
                return ins
            P.emit("g", g_x1, waits=[("st", ln2_tv)])
            P.bump("do", 32)

        def v_f2(v, r=r):
            v.tensor_scalar(out=srow[0:1, 0:512], in0=pss[0:1, 1024:1536], scalar1=1.0 / HID, scalar2=None, op0=MU)
            v.tensor_scalar(out=srow[0:1, 512:1024], in0=pss[0:1, 1536:2048], scalar1=1.0 / HID, scalar2=None, op0=MU)
            v.tensor_tensor(out=t1[0:1, :], in0=srow[0:1, 0:512], in1=srow[0:1, 0:512], op=MU)
            v.tensor_tensor(out=srow[0:1, 512:1024], in0=srow[0:1, 512:1024], in1=t1[0:1, :], op=SU)
            return v.tensor_scalar(out=srow[0:1, 512:1024], in0=srow[0:1, 512:1024], scalar1=1e-5, scalar2=None, op0=AD)
        vv = P.emit("v", v_f2, waits=[("st", ln2_tv)], inc="sv")
        av = P.emit("a", lambda a: a.activation(out=srow[0:1, 512:1024], in_=srow[0:1, 512:1024], func=AF.Sqrt),
                    waits=[("sv", vv)], inc="sa")
        vv = P.emit("v", lambda v: v.reciprocal(out=srow[0:1, 512:1024], in_=srow[0:1, 512:1024]),
                    waits=[("sa", av)], inc="sv")
        def g_l2(g, r=r):
            ins = g.dma_start(out=ln2st_d[0:1, r * 512:(r + 1) * 512], in_=srow[0:1, 0:512])
            ins.then_inc(P.sems["do"], 16)
            ins = g.dma_start(out=ln2st_d[0:1, 2048 + r * 512:2048 + (r + 1) * 512], in_=srow[0:1, 512:1024])
            ins.then_inc(P.sems["do"], 16)
            return ins
        P.emit("g", g_l2, waits=[("sv", vv)])
        P.bump("do", 32)
        prev_l2do = P.val("do")
        ep_v = vv
    return P.val("do")


def emit_p4b(nc, prog, ctx4, x1_d, x1b_d, ln2st_d, w1, b1, w2, b2, ucat, vcatT, y, bw):
    P = prog
    S = lambda nm, sh, dt=F32: ctx4.enter_context(nc.sbuf_tensor(nm, sh, dt))
    P_ = lambda nm, sh, dt=F32: ctx4.enter_context(nc.psum_tensor(nm, sh, dt))
    w1b = S("pb_w1b", [128, 16 * 2048], BF16)
    ucb = S("pb_ucb", [128, 4 * 2048], BF16)
    vcb = S("pb_vcb", [128, 16 * 512], BF16)
    w2b = S("pb_w2b", [128, 16 * 8], BF16)
    stg = S("pb_stg", [128, 2048])
    x1b = S("pb_x1b", [128, 16 * 512], BF16)
    h_bf = S("pb_hbf", [128, 16 * 512], BF16)
    vxt = S("pb_vxt", [128, 4 * 512], BF16)
    ffn_bf = S("pb_ffnbf", [128, 16 * 512], BF16)
    sqt = S("pb_sqt", [128, 512], BF16)
    t1 = S("pb_t1", [128, 512])
    x1t = S("pb_x1t", [128, 512])
    x2t = S("pb_x2t", [128, 512])
    mu_bc = S("pb_mubc", [128, 512])
    rs_bc = S("pb_rsbc", [128, 512])
    rs2_bc = S("pb_rs2bc", [128, 512])
    srow = S("pb_srow", [1, 1024])
    negmu2 = S("pb_negmu2", [1, 4 * 512])
    rs2stg = S("pb_rs2stg", [1, 512])
    csw1 = S("pb_csw1", [1, 2048])
    csv = S("pb_csv", [1, 512])
    b2r = S("pb_b2r", [1, 8])
    b1c = S("pb_b1c", [128, 16])
    E8 = S("pb_E8", [8, 512], BF16)
    alph = S("pb_alph", [128, 8])
    mx1 = S("pb_mx1", [128, 1])
    alT = S("pb_alT", [8, 512], BF16)
    ones_col_f = S("pb_onescf", [128, 1])
    ones_col_b = S("pb_onescb", [128, 1], BF16)
    ones_row_f = S("pb_onesrf", [1, 128])
    identf = S("pb_identf", [128, 128])
    ps1 = P_("pb_ps1", [128, 512])
    ps2 = P_("pb_ps2", [128, 512])
    pss = P_("pb_pss", [1, 2048])
    psa = P_("pb_psa", [128, 8])
    psT4 = P_("pb_psT4", [8, 512])
    rd = {"1": ("sv", 0), "2": ("sv", 0), "s": ("sv", 0), "a": ("sv", 0), "t4": ("sv", 0)}
    wr = {"1": 0, "2": 0, "s": 0, "a": 0, "t4": 0}
    bar = bw

    def g_c(g):
        g.memset(ones_col_f[:], 1.0)
        g.memset(ones_col_b[:], 1.0)
        g.memset(ones_row_f[:], 1.0)
        g.memset(E8[:], 1.0)
        g.memset(identf[:], 0.0)
        g.affine_select(out=identf[:], in_=identf[:], compare_op=mybir.AluOpType.not_equal,
                        fill=1.0, base=0, pattern=[[-1, 128]], channel_multiplier=1)
        g.affine_select(out=E8[:], in_=E8[:], compare_op=mybir.AluOpType.is_ge, fill=0.0,
                        base=0, pattern=[[1, 512]], channel_multiplier=-64)
        return g.affine_select(out=E8[:], in_=E8[:], compare_op=mybir.AluOpType.is_ge, fill=0.0,
                               base=63, pattern=[[-1, 512]], channel_multiplier=64)
    gv = P.emit("g", g_c, waits=bar, inc="sg")

    cast_v = 0
    csw_tv = None
    for dt in range(16):
        def s_w(s, dt=dt):
            ins = s.dma_start(out=stg[:], in_=w1[dt * 128:(dt + 1) * 128, :])
            ins.then_inc(P.sems["dlc"], 16)
            return ins
        P.emit("s", s_w, waits=bar + [("sv", cast_v), ("st", csw_tv or 0)])
        wv = P.bump("dlc", 16)
        cast_v = P.emit("v", lambda v, dt=dt: v.tensor_copy(out=w1b[:, dt * 2048:(dt + 1) * 2048],
                                                            in_=stg[:]),
                        waits=bar + [("dlc", wv)], inc="sv")
        def t_cs2(t, dt=dt):
            ins = None
            for nb in range(4):
                ins = t.matmul(out=pss[0:1, nb * 512:(nb + 1) * 512], lhsT=ones_col_f[:],
                               rhs=stg[:, nb * 512:(nb + 1) * 512],
                               start=(dt == 0), stop=(dt == 15))
            return ins
        csw_tv = P.emit("t", t_cs2, waits=[("dlc", wv)] + ([rd["s"], ("sg", gv)] if dt == 0 else []), inc="st")
    wr["s"] = csw_tv
    def v_cs(v):
        ins = None
        for nb in range(4):
            ins = v.tensor_copy(out=csw1[0:1, nb * 512:(nb + 1) * 512], in_=pss[0:1, nb * 512:(nb + 1) * 512])
        return ins
    vv = P.emit("v", v_cs, waits=[("st", csw_tv)], inc="sv")
    rd["s"] = ("sv", vv)

    csv_tv = csw_tv
    cast2_v = cast_v
    for dt in range(16):
        def s_w(s, dt=dt):
            ins = s.dma_start(out=stg[:, 0:512], in_=vcatT[dt * 128:(dt + 1) * 128, :])
            ins.then_inc(P.sems["dlc"], 16)
            return ins
        P.emit("s", s_w, waits=[("sv", cast2_v), ("st", csv_tv or 0)])
        wv = P.bump("dlc", 16)
        cast2_v = P.emit("v", lambda v, dt=dt: v.tensor_copy(out=vcb[:, dt * 512:(dt + 1) * 512],
                                                             in_=stg[:, 0:512]),
                         waits=[("dlc", wv)], inc="sv")
        csv_tv = P.emit("t", lambda t, dt=dt: t.matmul(out=ps2[0:1, 0:512],
                                                       lhsT=ones_col_f[:], rhs=stg[:, 0:512],
                                                       start=(dt == 0), stop=(dt == 15)),
                        waits=[("dlc", wv)] + ([rd["2"], ("sg", gv)] if dt == 0 else []), inc="st")
    wr["2"] = csv_tv
    vv = P.emit("v", lambda v: v.tensor_copy(out=csv[:], in_=ps2[0:1, 0:512]),
                waits=[("st", csv_tv)], inc="sv")
    rd["2"] = ("sv", vv)

    for kb in range(4):
        def s_w(s, kb=kb):
            ins = s.dma_start(out=stg[:], in_=ucat[kb * 128:(kb + 1) * 128, :])
            ins.then_inc(P.sems["dlc"], 16)
            return ins
        P.emit("s", s_w, waits=[("sv", P.val("sv")), ("st", csv_tv)])
        wv = P.bump("dlc", 16)
        P.emit("v", lambda v, kb=kb: v.tensor_copy(out=ucb[:, kb * 2048:(kb + 1) * 2048], in_=stg[:]),
               waits=[("dlc", wv)], inc="sv")
    def s_w2(s):
        ins = None
        for dt in range(16):
            ins = s.dma_start(out=stg[:, dt * 8:(dt + 1) * 8], in_=w2[dt * 128:(dt + 1) * 128, :])
            ins.then_inc(P.sems["dlc"], 16)
        ins = s.dma_start(out=b2r[:], in_=b2)
        ins.then_inc(P.sems["dlc"], 16)
        ins = s.dma_start(out=b1c[:], in_=b1.rearrange("(dt p) one -> p (dt one)", p=128), allow_slow_non_contiguous=True)
        ins.then_inc(P.sems["dlc"], 16)
        ins = s.dma_start(out=negmu2[:], in_=ln2st_d[0:1, 0:2048])
        ins.then_inc(P.sems["dlc"], 16)
        return ins
    P.emit("s", s_w2, waits=[("sv", P.val("sv")), ("do", P.val("do"))])
    mv_ = P.bump("dlc", 16 * 19)
    w2c_v = P.emit("v", lambda v: v.tensor_copy(out=w2b[:], in_=stg[:, 0:128]),
                   waits=[("dlc", mv_)], inc="sv")
    neg_v = P.emit("v", lambda v: v.tensor_scalar(out=negmu2[:], in0=negmu2[:],
                                                  scalar1=-1.0, scalar2=None, op0=MU),
                   waits=[("dlc", mv_)], inc="sv")

    ep_v = 0
    ldx_prev = 0
    for r in range(4):
        def s_x(s, r=r):
            ins = None
            for dt in range(16):
                ins = s.dma_start(out=x1b[:, dt * 512:(dt + 1) * 512],
                                  in_=x1b_d[dt * 128:(dt + 1) * 128, r * 512:(r + 1) * 512])
                ins.then_inc(P.sems["dl0"], 16)
            return ins
        P.emit("s", s_x, waits=[("sv", ep_v), ("do", P.val("do"))])
        xv = P.bump("dl0", 16 * 16)

        def s_r2(s, r=r):
            ins = s.dma_start(out=rs2stg[:], in_=ln2st_d[0:1, 2048 + r * 512:2048 + (r + 1) * 512])
            ins.then_inc(P.sems["dr2"], 16)
            return ins
        P.emit("s", s_r2, waits=[("st", P.val("st"))])
        r2v = P.bump("dr2", 16)
        def t_r2(t, r=r):
            return t.matmul(out=ps2[:], lhsT=ones_row_f[:], rhs=rs2stg[:],
                            start=True, stop=True)
        tv = P.emit("t", t_r2, waits=[("dr2", r2v)] + [rd["2"]], inc="st")
        wr["2"] = tv
        vv = P.emit("v", lambda v: v.tensor_copy(out=rs2_bc[:], in_=ps2[:]),
                    waits=[("st", tv)], inc="sv")
        rd["2"] = ("sv", vv)
        rs2_v = vv

        prev_hav = 0
        for et in range(16):
            key = "1" if et % 2 == 0 else "2"
            psx = ps1 if et % 2 == 0 else ps2
            def t_h(t, et=et, r=r, psx=psx):
                ins = None
                for dt in range(16):
                    ins = t.matmul(out=psx[:], lhsT=w1b[:, dt * 2048 + et * 128:dt * 2048 + (et + 1) * 128],
                                   rhs=x1b[:, dt * 512:(dt + 1) * 512], start=(dt == 0), stop=False)
                return t.matmul(out=psx[:], lhsT=csw1[0:1, et * 128:(et + 1) * 128],
                                rhs=negmu2[0:1, r * 512:(r + 1) * 512], start=False, stop=True)
            tv = P.emit("t", t_h, waits=[("dl0", xv), ("sv", rs2_v), ("sv", neg_v),
                                         ("sv", w2c_v)] + [rd[key]], inc="st")
            wr[key] = tv
            vv = P.emit("v", lambda v, psx=psx: v.tensor_tensor(out=t1[:], in0=psx[:], in1=rs2_bc[:], op=MU),
                        waits=[("st", tv), ("sa", prev_hav)], inc="sv")
            rd[key] = ("sv", vv)
            av = P.emit("a", lambda a, et=et: a.activation(out=h_bf[:, et * 512:(et + 1) * 512],
                                                           in_=t1[:], func=AF.Relu,
                                                           bias=b1c[:, et:et + 1]),
                        waits=[("sv", vv)], inc="sa")
            prev_hav = av
        h_v = av

        for rsub in range(4):
            def t_a(t, rsub=rsub):
                ins = None
                for et in range(16):
                    ins = t.matmul(out=psa[:], lhsT=h_bf[:, et * 512 + rsub * 128:et * 512 + (rsub + 1) * 128],
                                   rhs=w2b[:, et * 8:(et + 1) * 8], start=(et == 0), stop=False)
                return t.matmul(out=psa[:], lhsT=ones_row_f[:], rhs=b2r[:], start=False, stop=True)
            tv = P.emit("t", t_a, waits=[("sa", h_v)] + [rd["a"]], inc="st")
            wr["a"] = tv
            def v_sm(v):
                v.tensor_copy(out=alph[:], in_=psa[:])
                v.tensor_reduce(out=mx1[:], in_=alph[:], axis=AX, op=MX)
                return v.tensor_scalar(out=alph[:], in0=alph[:], scalar1=mx1[:, 0:1],
                                       scalar2=None, op0=SU)
            vv = P.emit("v", v_sm, waits=[("st", tv)], inc="sv")
            rd["a"] = ("sv", vv)
            av = P.emit("a", lambda a: a.activation(out=alph[:], in_=alph[:], func=AF.Exp),
                        waits=[("sv", vv)], inc="sa")
            def v_sm2(v):
                v.tensor_reduce(out=mx1[:], in_=alph[:], axis=AX, op=AD)
                v.reciprocal(out=mx1[:], in_=mx1[:])
                return v.tensor_scalar(out=alph[:], in0=alph[:], scalar1=mx1[:, 0:1],
                                       scalar2=None, op0=MU)
            vv = P.emit("v", v_sm2, waits=[("sa", av)], inc="sv")
            def t_tr(t, rsub=rsub):
                return t.transpose(out=psT4[0:8, rsub * 128:(rsub + 1) * 128], in_=alph[:],
                                   identity=identf[:])
            tv = P.emit("t", t_tr, waits=[("sv", vv)] + ([rd["t4"]] if rsub == 0 else []), inc="st")
        wr["t4"] = tv
        vv = P.emit("v", lambda v: v.tensor_copy(out=alT[:], in_=psT4[0:8, :]),
                    waits=[("st", tv)], inc="sv")
        rd["t4"] = ("sv", vv)
        al_v = vv

        for krt in range(4):
            def t_vx(t, krt=krt, r=r):
                ins = None
                for dt in range(16):
                    ins = t.matmul(out=ps1[:], lhsT=vcb[:, dt * 512 + krt * 128:dt * 512 + (krt + 1) * 128],
                                   rhs=x1b[:, dt * 512:(dt + 1) * 512], start=(dt == 0), stop=False)
                return t.matmul(out=ps1[:], lhsT=csv[0:1, krt * 128:(krt + 1) * 128],
                                rhs=negmu2[0:1, r * 512:(r + 1) * 512], start=False, stop=True)
            tv1 = P.emit("t", t_vx, waits=[("sa", h_v)] + [rd["1"]], inc="st")
            wr["1"] = tv1
            def t_ae(t, krt=krt):
                return t.matmul(out=ps2[:], lhsT=E8[0:8, krt * 128:(krt + 1) * 128], rhs=alT[:],
                                start=True, stop=True)
            tv2 = P.emit("t", t_ae, waits=[("sv", al_v)] + [rd["2"]], inc="st")
            wr["2"] = tv2
            def v_ax(v, krt=krt):
                v.tensor_tensor(out=t1[:], in0=ps2[:], in1=rs2_bc[:], op=MU)
                return v.tensor_tensor(out=x1t[:], in0=ps1[:], in1=t1[:], op=MU)
            vv = P.emit("v", v_ax, waits=[("st", tv1), ("st", tv2)], inc="sv")
            rd["1"] = rd["2"] = ("sv", vv)
            vv = P.emit("v", lambda v, krt=krt: v.tensor_copy(out=vxt[:, krt * 512:(krt + 1) * 512],
                                                              in_=x1t[:]),
                        waits=[("sv", vv)], inc="sv")
        vx_v = vv

        st_tv = None
        for dt in range(16):
            key = "1" if dt % 2 == 0 else "2"
            psx = ps1 if dt % 2 == 0 else ps2
            def t_f(t, dt=dt, psx=psx):
                ins = None
                for krt in range(4):
                    ins = t.matmul(out=psx[:], lhsT=ucb[:, krt * 2048 + dt * 128:krt * 2048 + (dt + 1) * 128],
                                   rhs=vxt[:, krt * 512:(krt + 1) * 512], start=(krt == 0), stop=(krt == 3))
                return ins
            tv = P.emit("t", t_f, waits=[("sv", vx_v)] + [rd[key]], inc="st")
            wr[key] = tv
            vv = P.emit("v", lambda v, dt=dt, psx=psx: v.tensor_copy(out=ffn_bf[:, dt * 512:(dt + 1) * 512],
                                                                     in_=psx[:]),
                        waits=[("st", tv)], inc="sv")
            rd[key] = ("sv", vv)
            av = P.emit("a", lambda a, dt=dt: a.activation(out=sqt[:], in_=ffn_bf[:, dt * 512:(dt + 1) * 512],
                                                           func=AF.Square),
                        waits=[("sv", vv), ("st", st_tv or wr["s"])], inc="sa")
            def t_s(t, dt=dt):
                t.matmul(out=pss[0:1, 0:512], lhsT=ones_col_b[:], rhs=ffn_bf[:, dt * 512:(dt + 1) * 512],
                         start=(dt == 0), stop=(dt == 15))
                return t.matmul(out=pss[0:1, 512:1024], lhsT=ones_col_b[:], rhs=sqt[:],
                                start=(dt == 0), stop=(dt == 15))
            st_tv = P.emit("t", t_s, waits=[("sv", vv), ("sa", av)] + ([rd["s"]] if dt == 0 else []),
                           inc="st")
        wr["s"] = st_tv

        def v_fin(v):
            v.tensor_scalar(out=srow[0:1, 0:512], in0=pss[0:1, 0:512], scalar1=1.0 / HID, scalar2=None, op0=MU)
            v.tensor_scalar(out=srow[0:1, 512:1024], in0=pss[0:1, 512:1024], scalar1=1.0 / HID, scalar2=None, op0=MU)
            v.tensor_tensor(out=t1[0:1, :], in0=srow[0:1, 0:512], in1=srow[0:1, 0:512], op=MU)
            v.tensor_tensor(out=srow[0:1, 512:1024], in0=srow[0:1, 512:1024], in1=t1[0:1, :], op=SU)
            return v.tensor_scalar(out=srow[0:1, 512:1024], in0=srow[0:1, 512:1024], scalar1=1e-5, scalar2=None, op0=AD)
        vv = P.emit("v", v_fin, waits=[("st", st_tv)], inc="sv")
        rd["s"] = ("sv", vv)
        av = P.emit("a", lambda a: a.activation(out=srow[0:1, 512:1024], in_=srow[0:1, 512:1024], func=AF.Sqrt),
                    waits=[("sv", vv)], inc="sa")
        vv = P.emit("v", lambda v: v.reciprocal(out=srow[0:1, 512:1024], in_=srow[0:1, 512:1024]),
                    waits=[("sa", av)], inc="sv")
        def t_bc(t):
            t.matmul(out=ps1[:], lhsT=ones_row_f[:], rhs=srow[0:1, 0:512], start=True, stop=True)
            return t.matmul(out=ps2[:], lhsT=ones_row_f[:], rhs=srow[0:1, 512:1024], start=True, stop=True)
        tv = P.emit("t", t_bc, waits=[("sv", vv)] + [rd["1"], rd["2"]], inc="st")
        wr["1"] = wr["2"] = tv
        def v_bc(v):
            v.tensor_copy(out=mu_bc[:], in_=ps1[:])
            return v.tensor_copy(out=rs_bc[:], in_=ps2[:])
        bc_v = P.emit("v", v_bc, waits=[("st", tv)], inc="sv")
        rd["1"] = rd["2"] = ("sv", bc_v)

        for dt in range(16):
            def s_x1(s, dt=dt, r=r):
                ins = s.dma_start(out=x1t[:], in_=x1_d[dt * 128:(dt + 1) * 128, r * 512:(r + 1) * 512])
                ins.then_inc(P.sems["dl1"], 16)
                return ins
            P.emit("s", s_x1, waits=[("sv", P.val("sv"))])
            x1v = P.bump("dl1", 16)
            def v_x2(v, dt=dt):
                v.tensor_tensor(out=t1[:], in0=ffn_bf[:, dt * 512:(dt + 1) * 512], in1=mu_bc[:], op=SU)
                v.tensor_tensor(out=t1[:], in0=t1[:], in1=rs_bc[:], op=MU)
                return v.tensor_tensor(out=x2t[:], in0=t1[:], in1=x1t[:], op=AD)
            vv = P.emit("v", v_x2, waits=[("sv", bc_v), ("dl1", x1v), ("do", P.val("do"))], inc="sv")
            def g_y(g, dt=dt, r=r):
                ins = g.dma_start(out=y[dt * 128:(dt + 1) * 128, r * 512:(r + 1) * 512], in_=x2t[:])
                ins.then_inc(P.sems["do"], 16)
                return ins
            P.emit("g", g_y, waits=[("sv", vv)])
            P.bump("do", 16)
        ep_v = P.val("sv")


def barrier_waits(prog):
    return [(s, prog.val(s)) for s in ("st", "sv", "sa", "sg", "dlc", "dl0", "dl1", "do", "cc", "dsx", "dbb", "dst", "dr2", "ddy", "do0", "do1")]


def build_full():
    PH = int(os.environ.get("FULL_PH", "9"))
    nc = bass.Bass("TRN2", target_bir_lowering=False, debug=False,
                   detect_race_conditions=bool(os.environ.get("KK_RACE")))
    DT = lambda nm, sh, dt=F32, kind="ExternalInput": nc.dram_tensor(nm, sh, dt, kind=kind).ap()
    xT = DT("xT", [HID, ROWS])
    xTr = DT("xTr", [HID, RPC])
    wcat = DT("wcat", [HID, 772])
    babb = DT("babb", [1, 4])
    lamp = DT("lamp", [1, 256])
    onw = DT("onw", [1, 128])
    wo = DT("wo", [HID, HID])
    w1 = DT("w1", [HID, HID])
    b1 = DT("b1", [HID, 1])
    w2 = DT("w2", [HID, 8])
    b2 = DT("b2", [1, 8])
    ucat = DT("ucat", [512, HID])
    vcatT = DT("vcatT", [HID, 512])
    y = DT("y", [HID, RPC], kind="ExternalOutput")

    _dbgk = "ExternalOutput" if os.environ.get("KK_DBG") else "Internal"
    q_d = DT("q_d", [ROWS, 256], kind=_dbgk)
    k_d = DT("k_d", [ROWS, 256], BF16, kind=_dbgk)
    v_d = DT("v_d", [ROWS, 256], BF16, kind=_dbgk)
    g_d = DT("g_d", [ROWS, HPC], kind=_dbgk)
    bt_d = DT("bt_d", [ROWS, HPC], kind=_dbgk)
    o_d = DT("o_d", [256, ROWS], BF16, kind="Internal")
    o_all = DT("o_all", [HID, ROWS], BF16, kind="Internal")
    st_in = DT("st_in", [2, RPC], kind="Internal")
    st_all = DT("st_all", [16, RPC], kind="Internal")
    x1_d = DT("x1_d", [HID, RPC], kind=_dbgk)
    x1b_d = DT("x1b_d", [HID, RPC], BF16, kind="Internal")
    ln2st_d = DT("ln2st_d", [1, 2 * RPC], kind="Internal")

    prog = Prog()
    sem_names = ["st", "sv", "sa", "sg", "dlc", "dl0", "dl1", "do", "cc", "dsx", "dbb", "dst", "dr2", "ddy", "do0", "do1"]
    with ExitStack() as ctx:
        sems = {n: ctx.enter_context(nc.semaphore(n)) for n in sem_names}
        prog.sems = sems

        def g_pid(g):
            prog.pid_g = g.partition_id()
            return g.nop()
        prog.emit("g", g_pid)

        with ExitStack() as c_mesa:
            M = Mesa(nc, c_mesa, prog, T_FULL)
            with ExitStack() as c_p2:
                M.emit_consts(lamp, onw)
                p2_d0, p2_d1 = emit_p12(nc, prog, M, c_p2, xT, xTr, wcat, babb, st_in, st_all,
                                        q_d, k_d, v_d, g_d, bt_d)

                _nck = min(int(os.environ.get("KK_NCHUNK", "32")), T_FULL // C)
                for cidx in range(0 if PH < 3 else _nck):
                    z = cidx % 2
                    dl_sem = f"dl{z}"
                    def s_ld(s, z=z, cidx=cidx, dl_sem=dl_sem):
                        ins = None
                        for b in range(4):
                            r0 = b * T_FULL + cidx * C
                            for (dst, srcd) in ((M.aq[z], q_d), (M.ak[z], k_d), (M.av[z], v_d)):
                                ins = s.dma_start(out=dst[:, b * 256:(b + 1) * 256],
                                                  in_=srcd[r0:r0 + C, :])
                                ins.then_inc(prog.sems[dl_sem], 16)
                            ins = s.dma_start(out=M.ag[z][:, b * HPC:(b + 1) * HPC],
                                              in_=g_d[r0:r0 + C, :])
                            ins.then_inc(prog.sems[dl_sem], 16)
                            ins = s.dma_start(out=M.ab[z][:, b * HPC:(b + 1) * HPC],
                                              in_=bt_d[r0:r0 + C, :])
                            ins.then_inc(prog.sems[dl_sem], 16)
                        return ins
                    w = ([("sv", M.hstate_v)] if cidx >= 2
                         else [("do0", p2_d0), ("do1", p2_d1)])
                    prog.emit("s", s_ld, waits=w)
                    dl_val = prog.bump(dl_sem, 16 * 20)
                    M.emit_chunk(z, cidx, o_d, dl_sem, dl_val, T_FULL)

                if PH >= 3:
                    def g_ago(g):
                        return g.collective_compute("AllGather", mybir.AluOpType.bypass,
                                                    replica_groups=[list(range(NCORES))],
                                                    ins=[o_d[:, :].opt()], outs=[o_all[:, :].opt()])
                    prog.emit("g", g_ago, waits=[("do", prog.val("do")), ("sv", M.hstate_v)], inc="cc")

        bw = barrier_waits(prog)
        if PH >= 4:
            with ExitStack() as c4a:
                emit_p4a(nc, prog, c4a, xTr, wo, o_all, x1_d, x1b_d, ln2st_d, bw)
        bw2 = barrier_waits(prog)
        if PH >= 5:
            with ExitStack() as c4b:
                emit_p4b(nc, prog, c4b, x1_d, x1b_d, ln2st_d, w1, b1, w2, b2, ucat, vcatT, y, bw2)

        prog.emit("g", lambda g: g.nop(), waits=[("do", prog.val("do"))])

        block = ctx.enter_context(nc.Block())
        prog.realize(block, sems)
    return nc


_NC_CACHE = {}


def kernel(L0_attn_x, L0_attn_Wq, L0_attn_Wk, L0_attn_Wv, L0_attn_Wa, L0_attn_ba,
           L0_attn_Wb, L0_attn_bb, L0_attn_lambda_params, L0_attn_o_norm_weight,
           L0_attn_Wo, L0_ffn_U, L0_ffn_V, L0_ffn_W1, L0_ffn_b1, L0_ffn_W2, L0_ffn_b2):
    f32 = np.float32
    x = np.ascontiguousarray(np.asarray(L0_attn_x, dtype=f32).reshape(ROWS, HID))
    xT = np.ascontiguousarray(x.T)
    Wq = np.asarray(L0_attn_Wq, f32)
    Wk = np.asarray(L0_attn_Wk, f32)
    Wv = np.asarray(L0_attn_Wv, f32)
    Wa = np.asarray(L0_attn_Wa, f32)
    Wb = np.asarray(L0_attn_Wb, f32)
    ba = np.asarray(L0_attn_ba, f32)
    bb = np.asarray(L0_attn_bb, f32)
    lam = np.asarray(L0_attn_lambda_params, f32).reshape(NH, HD)
    onw = np.asarray(L0_attn_o_norm_weight, f32).reshape(1, HD)
    wo = np.ascontiguousarray(np.asarray(L0_attn_Wo, f32))
    w1 = np.ascontiguousarray(np.asarray(L0_ffn_W1, f32))
    b1 = np.asarray(L0_ffn_b1, f32).reshape(HID, 1)
    w2 = np.ascontiguousarray(np.asarray(L0_ffn_W2, f32))
    b2 = np.asarray(L0_ffn_b2, f32).reshape(1, 8)
    U = np.asarray(L0_ffn_U, f32)
    V = np.asarray(L0_ffn_V, f32)
    ucat = np.ascontiguousarray(U.transpose(0, 2, 1).reshape(512, HID))
    vcatT = np.ascontiguousarray(V.reshape(512, HID).T)

    if "nc" not in _NC_CACHE:
        _NC_CACHE["nc"] = build_full()
    nc = _NC_CACHE["nc"]

    in_maps = []
    for c in range(NCORES):
        hs = slice(c * HPC * HD, (c + 1) * HPC * HD)
        hh = slice(c * HPC, (c + 1) * HPC)
        wcat_c = np.ascontiguousarray(np.concatenate(
            [Wq[:, hs], Wk[:, hs], Wv[:, hs], Wa[:, hh], Wb[:, hh]], axis=1))
        babb_c = np.ascontiguousarray(
            np.concatenate([ba[hh], bb[hh]]).reshape(1, 4))
        lamp_c = np.ascontiguousarray(lam[hh].reshape(1, HPC * HD))
        xTr_c = np.ascontiguousarray(xT[:, c * RPC:(c + 1) * RPC])
        in_maps.append(dict(xT=xT, xTr=xTr_c, wcat=wcat_c, babb=babb_c, lamp=lamp_c, onw=onw,
                            wo=wo, w1=w1, b1=b1, w2=w2, b2=b2, ucat=ucat, vcatT=vcatT))

    res = run_bass_kernel_spmd(nc, in_maps, core_ids=list(range(NCORES)))
    yT = np.concatenate([np.asarray(r["y"], dtype=f32) for r in res.results], axis=1)
    out = np.ascontiguousarray(yT.T).reshape(4, T_FULL, HID)
    return out


# version 16 (final)
# speedup vs baseline: 1.0184x; 1.0083x over previous
import numpy as np
from contextlib import ExitStack

import concourse.bass as bass
import concourse.mybir as mybir
from concourse.bass_utils import run_bass_kernel_spmd
from concourse.alu_op_type import AluOpType

F32 = mybir.dt.float32
BF16 = mybir.dt.bfloat16
AF = mybir.ActivationFunctionType
AX = mybir.AxisListType.X

HID = 2048
NH = 16
HD = 128
NCORES = 8
HPC = NH // NCORES
NPAIR = 8
C = 128
MAX_CG = 30
LAMB_LB = 0.25
K_EXP = 8
R_EXP = 64
W = NPAIR * HD
import os
DBG_STAGE = int(os.environ.get("MESA_STAGE", "99"))
DBG_ITERS = int(os.environ.get("MESA_ITERS", str(MAX_CG)))
DBG_CSTAGE = int(os.environ.get("MESA_CSTAGE", "99"))
DBG_NCHUNK = int(os.environ.get("MESA_NCHUNK", "99"))

MU, AD, SU, MI, MX, BYP = (AluOpType.mult, AluOpType.add, AluOpType.subtract,
                           AluOpType.min, AluOpType.max, AluOpType.bypass)
GE = AluOpType.is_ge


class Prog:

    def __init__(self):
        self.ops = {e: [] for e in ("t", "v", "a", "g", "s")}
        self.cnt = {}
        self.sems = None

    def bump(self, sem, n):
        self.cnt[sem] = self.cnt.get(sem, 0) + n
        return self.cnt[sem]

    def emit(self, eng, fn, waits=(), inc=None, inc_n=1):
        waits = tuple((s, v) for (s, v) in waits if v > 0)
        val = None
        if inc is not None:
            self.cnt[inc] = self.cnt.get(inc, 0) + inc_n
            val = self.cnt[inc]
        self.ops[eng].append((waits, inc, inc_n, fn))
        return val

    def val(self, sem):
        return self.cnt.get(sem, 0)

    def realize(self, block, sems):
        self.sems = sems

        def runner(handle, eng):
            if eng in ("v", "a", "g"):
                handle = _AutoDrain(handle)
            for waits, inc, inc_n, fn in self.ops[eng]:
                for s, v in waits:
                    handle.wait_ge(sems[s], v)
                ins = fn(handle)
                if inc is not None:
                    ins.then_inc(sems[inc], inc_n)

        block.tensor(lambda h: runner(h, "t"))
        block.vector(lambda h: runner(h, "v"))
        block.scalar(lambda h: runner(h, "a"))
        block.gpsimd(lambda h: runner(h, "g"))
        block.sync(lambda h: runner(h, "s"))


_DRAIN_OPS = frozenset((
    "tensor_tensor", "tensor_scalar", "tensor_copy", "tensor_single_scalar",
    "scalar_tensor_tensor", "reciprocal", "memset", "memzero", "tensor_reduce",
    "reduce_sum", "reduce_max", "select", "iota", "affine_select", "activation",
    "tensor_add", "tensor_sub", "tensor_mul", "tensor_max", "square", "copy",
    "add", "mul", "sqrt", "sign",
))


class _AutoDrain:

    def __init__(self, h):
        self._h = h
        self._pend = False

    @property
    def raw(self):
        return self._h

    def skip(self):
        self._pend = False

    def __getattr__(self, name):
        real = getattr(self._h, name)
        if name in _DRAIN_OPS:
            def wrap(*a, **k):
                if self._pend:
                    self._h.drain()
                self._pend = True
                return real(*a, **k)
            return wrap
        return real


def pc(p):
    return slice(p * HD, (p + 1) * HD)


class Mesa:

    def __init__(self, nc, ctx, prog, T):
        self.nc = nc
        self.prog = prog
        self.T = T
        self.nchunk = T // C
        S = lambda nm, sh, dt=F32: ctx.enter_context(nc.sbuf_tensor(nm, sh, dt))
        P_ = lambda nm, sh, dt=F32: ctx.enter_context(nc.psum_tensor(nm, sh, dt))

        self.ident_b = S("mz_ident_b", [128, 128], BF16)
        self.ident_f = S("mz_ident_f", [128, 128], F32)
        self.utones_f = S("mz_utones_f", [128, 128], F32)
        self.utmask_b = S("mz_utmask_b", [128, 128], BF16)
        self.ones_col_f = S("mz_ones_col_f", [128, 1], F32)
        self.ones_col_b = S("mz_ones_col_b", [128, 1], BF16)
        self.ones_row_f = S("mz_ones_row_f", [1, 128], F32)
        self.ones_sq_b = S("mz_ones_sq_b", [128, 128], BF16)
        self.lamb_eps_arena = S("mz_lamb_eps", [128, W], F32)
        self.diag_lamb = S("mz_diag_lamb", [128, HPC * 128], BF16)
        self.onw_col = S("mz_onw_col", [128, 1], F32)

        self.Hkk_f = S("mz_Hkk_f", [128, W], F32)
        self.Hkk_b = S("mz_Hkk_b", [128, W], BF16)
        self.Hvk_f = S("mz_Hvk_f", [128, W], F32)
        self.Hvk_b = S("mz_Hvk_b", [128, W], BF16)
        self.Hd_row = S("mz_Hd_row", [1, W], F32)

        self.X = S("mz_X", [128, W], F32)
        self.R = S("mz_R", [128, W], F32)
        self.Pv = S("mz_Pv", [128, W], F32)
        self.Pv2 = S("mz_Pv2", [128, W], F32)
        self.Pd = S("mz_Pd", [128, W], BF16)
        self.Pd_e = S("mz_Pd_e", [128, W], BF16)
        self.Pb = S("mz_Pb", [128, W], BF16)
        self.diag_e = S("mz_diag_e", [128, W], BF16)
        self.Xt = S("mz_Xt", [128, W], BF16)

        self.aq = [S(f"mz_aq{z}", [128, W], F32) for z in range(2)]
        self.ak = [S(f"mz_ak{z}", [128, W], BF16) for z in range(2)]
        self.av = [S(f"mz_av{z}", [128, W], BF16) for z in range(2)]
        self.ag = [S(f"mz_ag{z}", [128, NPAIR], F32) for z in range(2)]
        self.ab = [S(f"mz_ab{z}", [128, NPAIR], F32) for z in range(2)]

        self.Kd = S("mz_Kd", [128, W], BF16)
        self.Vd = S("mz_Vd", [128, W], BF16)
        self.Ksq = S("mz_Ksq", [128, W], BF16)
        self.Ktil = S("mz_Ktil", [128, W], BF16)
        self.Dp = S("mz_Dp", [128, W], BF16)
        self.Mp = S("mz_Mp", [128, W], BF16)
        self.scrf = S("mz_scrf", [128, W], F32)
        self.scrf2 = S("mz_scrf2", [128, W], F32)
        self.scrb = S("mz_scrb", [128, W], BF16)

        self.Gc = S("mz_Gc", [128, NPAIR], F32)
        self.e_all = S("mz_e_all", [128, NPAIR], F32)
        self.Gcrow = S("mz_Gcrow", [1, W], F32)
        self.erow = S("mz_erow", [1, W], F32)
        self.w_all = S("mz_w_all", [128, NPAIR], F32)
        self.dec_bc = S("mz_dec_bc", [128, NPAIR], F32)
        self.decs_row = S("mz_decs_row", [1, NPAIR], F32)
        self.pq = S("mz_pq", [128, NPAIR], F32)
        self.dA = S("mz_dA", [128, NPAIR], F32)
        self.dB = S("mz_dB", [128, NPAIR], F32)
        self.nd = S("mz_nd", [128, NPAIR], F32)
        self.alpha = S("mz_alpha", [128, NPAIR], F32)
        self.alphan = S("mz_alphan", [128, NPAIR], F32)
        self.betac = S("mz_betac", [128, NPAIR], F32)
        self.c1 = S("mz_c1", [128, NPAIR], F32)
        self.lamrow = S("mz_lamrow", [1, HPC * 128], F32)
        self.sprow = S("mz_sprow", [1, HPC * 128], F32)
        self.sprow2 = S("mz_sprow2", [1, HPC * 128], F32)
        self.onwrow = S("mz_onwrow", [1, 128], F32)

        self.ps_T = P_("mz_ps_T", [128, W], F32)
        self.ps_Z = P_("mz_ps_Z", [128, W], F32)
        self.ps_B = P_("mz_ps_B", [128, W], F32)
        self.ps_S = P_("mz_ps_S", [128, 4 * HD], F32)
        self.ps_Tb = P_("mz_ps_Tb", [128, W], BF16)

        self.rd = {k: ("sv", 0) for k in ("T", "Z", "B", "S", "Tb")}
        self.wr = {k: 0 for k in ("T", "Z", "B", "S", "Tb")}
        self.hstate_v = 0
        self.pready = 0
        self.de_rd = 0
        self.de_a = 0
        self.pb_a = 0
        self.pb_a_h0 = 0
        self.x_done = 0
        self.hstate_a = 0
        self.kv_a = 0

    def w_waits(self, *banks):
        return [self.rd[b] for b in banks]

    def set_w(self, bank, tv):
        self.wr[bank] = tv

    def r_waits(self, *banks):
        return [("st", self.wr[b]) for b in banks]

    def set_r(self, bank, sem, vv):
        self.rd[bank] = (sem, vv)

    def emit_consts(self, lamp_ap, onw_ap):
        P, M = self.prog, self

        def g_init1(g):
            for t in (M.ident_b, M.ident_f, M.utones_f, M.utmask_b):
                ins = g.memset(t[:], 0.0)
            ins = g.memset(M.ones_col_f[:], 1.0)
            ins = g.memset(M.ones_col_b[:], 1.0)
            ins = g.memset(M.ones_row_f[:], 1.0)
            ins = g.memset(M.ones_sq_b[:], 1.0)
            ins = g.memset(M.Hkk_f[:], 0.0)
            ins = g.memset(M.Hkk_b[:], 0.0)
            ins = g.memset(M.Hvk_f[:], 0.0)
            ins = g.memset(M.Hvk_b[:], 0.0)
            return g.memset(M.Hd_row[:], 0.0)
        gv1 = P.emit("g", g_init1, inc="sg")

        def g_init2(g):
            for t, sel in ((M.ident_b, "ne"), (M.ident_f, "ne"), (M.utones_f, "gt"),
                           (M.utmask_b, "gt")):
                ins = g.affine_select(out=t[:], in_=t[:],
                                compare_op=AluOpType.not_equal if sel == "ne" else AluOpType.is_gt,
                                fill=1.0, base=0, pattern=[[-1, 128]], channel_multiplier=1)
            return ins
        gv = P.emit("g", g_init2, waits=[("sg", gv1)], inc="sg")
        self.consts_gv = gv

        P.emit("s", lambda s: s.dma_start(out=M.lamrow[:], in_=lamp_ap), inc="dlc", inc_n=16)
        P.emit("s", lambda s: s.dma_start(out=M.onwrow[:], in_=onw_ap), inc="dlc", inc_n=16)
        dl0 = P.val("dlc")

        if DBG_CSTAGE < 1:
            self.consts_done = P.emit("v", lambda v: v.tensor_copy(out=M.sprow[:], in_=M.lamrow[:]),
                                      waits=[("dlc", dl0), ("sg", gv)], inc="sv")
            return
        def v_sp1(v):
            ins = v.tensor_scalar(out=M.sprow[:], in0=M.lamrow[:], scalar1=-1.0, scalar2=None, op0=MU)
            ins = v.tensor_tensor(out=M.sprow[:], in0=M.lamrow[:], in1=M.sprow[:], op=MI)
            return v.tensor_scalar(out=M.sprow2[:], in0=M.lamrow[:], scalar1=0.0, scalar2=None, op0=MX)
        vv = P.emit("v", v_sp1, waits=[("dlc", dl0)], inc="sv")
        av = P.emit("a", lambda a: a.activation(out=M.sprow[:], in_=M.sprow[:], func=AF.Exp),
                    waits=[("sv", vv), ("sg", gv)], inc="sa")
        av = P.emit("a", lambda a: a.activation(out=M.sprow[:], in_=M.sprow[:], func=AF.Ln,
                                                bias=M.ones_col_f[0:1, :]),
                    waits=[("sa", av)], inc="sa")

        def v_sp4(v):
            ins = v.tensor_tensor(out=M.lamrow[:], in0=M.sprow2[:], in1=M.sprow[:], op=AD)
            return v.tensor_scalar(out=M.lamrow[:], in0=M.lamrow[:], scalar1=LAMB_LB, scalar2=None, op0=AD)
        vv = P.emit("v", v_sp4, waits=[("sa", av)], inc="sv")

        if DBG_CSTAGE < 2:
            self.consts_done = vv
            return
        def t_lam(t):
            for h in range(HPC):
                ins = t.matmul(out=M.ps_B[:, h * 128:(h + 1) * 128], lhsT=M.ones_row_f[:],
                         rhs=M.lamrow[0:1, h * 128:(h + 1) * 128], start=True, stop=True)
            return t.matmul(out=M.ps_B[:, 256:257], lhsT=M.onwrow[:],
                            rhs=M.ones_row_f[0:1, 0:1], start=True, stop=True)
        tv = P.emit("t", t_lam, waits=[("sv", vv), ("sg", gv)], inc="st")
        self.set_w("B", tv)

        def v_lam2(v):
            for p in range(NPAIR):
                h = p % HPC
                ins = v.tensor_scalar(out=M.lamb_eps_arena[:, pc(p)], in0=M.ps_B[:, h * 128:(h + 1) * 128],
                                scalar1=1e-8, scalar2=None, op0=AD)
            for h in range(HPC):
                ins = v.tensor_tensor(out=M.diag_lamb[:, h * 128:(h + 1) * 128],
                                in0=M.ps_B[:, h * 128:(h + 1) * 128], in1=M.ident_b[:], op=MU)
            return v.tensor_copy(out=M.onw_col[:], in_=M.ps_B[:, 256:257])
        vv = P.emit("v", v_lam2, waits=self.r_waits("B"), inc="sv")
        self.set_r("B", "sv", vv)
        self.consts_done = vv

    def emit_mv(self, Psrc, ak):
        P, M = self.prog, self

        tv = P.emit("t", lambda t: [t.transpose(out=M.ps_Tb[:, pc(p)], in_=M.Pb[:, pc(p)],
                                                identity=M.ident_b[:]) for p in range(NPAIR)][-1],
                    waits=self.w_waits("Tb") + [("sa", self.pb_a)], inc="st")
        self.set_w("Tb", tv)
        def t_pe(t):
            for p in range(NPAIR):
                ins = t.matmul(out=M.ps_B[:, pc(p)], lhsT=M.Pb[:, pc(p)], rhs=M.diag_e[:, pc(p)],
                               start=True, stop=True)
            return ins
        tve = P.emit("t", t_pe, waits=self.w_waits("B") + [("sa", self.pb_a), ("sa", self.de_a)],
                     inc="st")
        self.set_w("B", tve)
        av1 = P.emit("a", lambda a: a.activation(out=M.Pd[:], in_=M.ps_Tb[:], func=AF.Copy),
                     waits=self.r_waits("Tb"), inc="sa")
        self.set_r("Tb", "sa", av1)
        av2 = P.emit("a", lambda a: a.activation(out=M.Pd_e[:], in_=M.ps_B[:], func=AF.Copy),
                     waits=self.r_waits("B"), inc="sa")
        self.set_r("B", "sa", av2)

        def t_dl(t):
            for p in range(NPAIR):
                h = p % HPC
                ins = t.matmul(out=M.ps_Z[:, pc(p)], lhsT=M.Pd[:, pc(p)],
                               rhs=M.diag_lamb[:, h * 128:(h + 1) * 128],
                               start=(p % 4 == 0), stop=False)
            return ins
        P.emit("t", t_dl, waits=self.w_waits("Z") + [("sa", av1), ("sv", self.hstate_v)], inc="st")

        def t_s0(t):
            for p in range(4):
                ins = t.matmul(out=M.ps_S[:, p * HD:(p + 1) * HD], lhsT=M.Kd[:, pc(p)],
                         rhs=M.Pd[:, pc(p)], start=True, stop=True)
            return ins
        tv_s0 = P.emit("t", t_s0, waits=self.w_waits("S") + [("sa", self.kv_a)], inc="st")
        self.set_w("S", tv_s0)
        def t_s1(t):
            for p in range(4, 8):
                ins = t.matmul(out=M.ps_T[:, (p - 4) * HD:(p - 3) * HD], lhsT=M.Kd[:, pc(p)],
                         rhs=M.Pd[:, pc(p)], start=True, stop=True)
            return ins
        tv_s1 = P.emit("t", t_s1, waits=self.w_waits("T") + [("sa", av1)], inc="st")
        self.set_w("T", tv_s1)

        def t_hk(t):
            for p in range(NPAIR):
                ins = t.matmul(out=M.ps_Z[:, pc(p)], lhsT=M.Pd_e[:, pc(p)], rhs=M.Hkk_b[:, pc(p)],
                               start=False, stop=False)
            return ins
        P.emit("t", t_hk, waits=[("sa", av2), ("sa", self.hstate_a)], inc="st")

        def v_m0(v):
            v.skip()
            return v.tensor_tensor(out=M.Mp[:, 0:512], in0=M.Dp[:, 0:512],
                                   in1=M.ps_S[:], op=MU)
        vv_m0 = P.emit("v", v_m0, waits=self.r_waits("S"), inc="sv")
        self.set_r("S", "sv", vv_m0)
        def v_m1(v):
            v.skip()
            return v.tensor_tensor(out=M.Mp[:, 512:1024], in0=M.Dp[:, 512:1024],
                                   in1=M.ps_T[:, 0:512], op=MU)
        vv_m1 = P.emit("v", v_m1, waits=[("st", tv_s1)], inc="sv")
        self.set_r("T", "sv", vv_m1)

        def t_y20(t, ak=ak):
            for p in range(4):
                ins = t.matmul(out=M.ps_Z[:, pc(p)], lhsT=M.Mp[:, pc(p)], rhs=ak[:, pc(p)],
                         start=False, stop=(p == 3))
            return ins
        P.emit("t", t_y20, waits=[("sv", vv_m0)], inc="st")
        def t_y21(t, ak=ak):
            for p in range(4, 8):
                ins = t.matmul(out=M.ps_Z[:, pc(p)], lhsT=M.Mp[:, pc(p)], rhs=ak[:, pc(p)],
                         start=False, stop=(p == 7))
            return ins
        tvh = P.emit("t", t_y21, waits=[("sv", vv_m1)], inc="st")
        self.set_w("Z", tvh)
        return tvh

    def emit_chunk(self, z, c, o_dram, dl_sem, dl_val, b_stride):
        P, M = self.prog, self
        aq, ak, av, ag, ab = M.aq[z], M.ak[z], M.av[z], M.ag[z], M.ab[z]

        base_wait = [(dl_sem, dl_val), ("sv", self.consts_done)]

        def t_gc(t):
            for p in range(NPAIR):
                ins = t.matmul(out=M.ps_Z[:, p:p + 1], lhsT=M.utones_f[:], rhs=ag[:, p:p + 1],
                         start=True, stop=True)
            return ins
        tv = P.emit("t", t_gc, waits=base_wait + self.w_waits("Z"), inc="st")
        self.set_w("Z", tv)
        vv = P.emit("v", lambda v: v.tensor_copy(out=M.Gc[:], in_=M.ps_Z[:, 0:NPAIR]),
                    waits=self.r_waits("Z"), inc="sv")
        self.set_r("Z", "sv", vv)
        gc_v = vv
        e_a = P.emit("a", lambda a: a.activation(out=M.e_all[:], in_=M.Gc[:], func=AF.Exp),
                     waits=[("sv", vv)], inc="sa")
        def a_de(a):
            for p in range(NPAIR):
                ins = a.activation(out=M.diag_e[:, pc(p)], in_=M.ident_f[:], func=AF.Copy,
                                   scale=M.e_all[:, p:p + 1])
            return ins
        self.de_a = P.emit("a", a_de, waits=[("sa", e_a), ("sg", self.consts_gv),
                                             ("st", self.de_rd)], inc="sa")

        if DBG_CSTAGE < 3:
            self.hstate_v = gc_v
            return None
        def t_gr(t):
            for p in range(NPAIR):
                ins = t.transpose(out=M.ps_T[0:1, pc(p)], in_=M.Gc[:, p:p + 1], identity=M.ident_f[:])
            return ins
        tv = P.emit("t", t_gr, waits=[("sv", gc_v)] + self.w_waits("T"), inc="st")
        self.set_w("T", tv)
        vv = P.emit("v", lambda v: v.tensor_copy(out=M.Gcrow[:], in_=M.ps_T[0:1, :]),
                    waits=self.r_waits("T"), inc="sv")
        self.set_r("T", "sv", vv)
        gcrow_v = vv
        P.emit("a", lambda a: a.activation(out=M.erow[:], in_=M.Gcrow[:], func=AF.Exp),
               waits=[("sv", vv)], inc="sa")
        erow_a = P.val("sa")

        if DBG_CSTAGE < 4:
            self.hstate_v = P.val("sv")
            return None
        def t_gbc(t):
            for p in range(NPAIR):
                ins = t.matmul(out=M.ps_B[:, pc(p)], lhsT=M.ones_row_f[:], rhs=M.Gcrow[0:1, pc(p)],
                         start=True, stop=True)
            return ins
        tv = P.emit("t", t_gbc, waits=[("sv", gcrow_v)] + self.w_waits("B"), inc="st")
        self.set_w("B", tv)

        def v_dg(v):
            for p in range(NPAIR):
                h = v if p == 0 else v.raw
                ins = h.tensor_scalar(out=M.scrf[:, pc(p)], in0=M.ps_B[:, pc(p)],
                                scalar1=M.Gc[:, p:p + 1], scalar2=0.0, op0=SU, op1=MI)
            return ins
        vv = P.emit("v", v_dg, waits=self.r_waits("B"), inc="sv")
        self.set_r("B", "sv", vv)
        av_ = P.emit("a", lambda a: a.activation(out=M.scrf[:], in_=M.scrf[:], func=AF.Exp),
                     waits=[("sv", vv)], inc="sa")

        def v_w(v):
            ins = v.tensor_copy(out=M.w_all[:],
                          in_=M.scrf[:].rearrange("s (p d) -> s p d", p=NPAIR)[:, :, 127:128])
            return v.tensor_tensor(out=M.w_all[:], in0=M.w_all[:], in1=ab[:], op=MU)
        w_v = P.emit("v", v_w, waits=[("sa", av_)], inc="sv")

        def v_dp(v):
            for p in range(NPAIR):
                h = v if p == 0 else v.raw
                ins = h.scalar_tensor_tensor(out=M.Dp[:, pc(p)], in0=M.utmask_b[:],
                                       scalar=ab[:, p:p + 1], in1=M.scrf[:, pc(p)],
                                       op0=MU, op1=MU)
            return ins
        dp_v = P.emit("v", v_dp, waits=[("sa", av_)], inc="sv")

        if DBG_CSTAGE < 5:
            self.hstate_v = P.val("sv")
            return None
        ksq_a = P.emit("a", lambda a: a.activation(out=M.Ksq[:], in_=ak[:], func=AF.Square),
                       waits=base_wait, inc="sa")
        for (src, dst) in ((ak, M.Kd), (av, M.Vd)):
            def t_tb(t, src=src):
                for p in range(NPAIR):
                    ins = t.transpose(out=M.ps_Tb[:, pc(p)], in_=src[:, pc(p)], identity=M.ident_b[:])
                return ins
            tv = P.emit("t", t_tb, waits=base_wait + self.w_waits("Tb"), inc="st")
            self.set_w("Tb", tv)
            kv_a = P.emit("a", lambda a, dst=dst: a.activation(out=dst[:], in_=M.ps_Tb[:], func=AF.Copy),
                          waits=self.r_waits("Tb"), inc="sa")
            self.set_r("Tb", "sa", kv_a)
        self.kv_a = kv_a

        if DBG_CSTAGE < 6:
            self.hstate_v = P.val("sv")
            return None
        def t_dec(t):
            for p in range(NPAIR):
                ins = t.matmul(out=M.ps_Z[0:1, p:p + 1], lhsT=ag[:, p:p + 1], rhs=M.ones_col_f[:],
                         start=True, stop=True)
            return ins
        tv = P.emit("t", t_dec, waits=base_wait + self.w_waits("Z"), inc="st")
        self.set_w("Z", tv)
        vv = P.emit("v", lambda v: v.tensor_copy(out=M.decs_row[:], in_=M.ps_Z[0:1, 0:NPAIR]),
                    waits=self.r_waits("Z"), inc="sv")
        self.set_r("Z", "sv", vv)
        av_ = P.emit("a", lambda a: a.activation(out=M.decs_row[:], in_=M.decs_row[:], func=AF.Exp),
                     waits=[("sv", vv)], inc="sa")
        tv = P.emit("t", lambda t: t.matmul(out=M.ps_Z[:, 0:NPAIR], lhsT=M.ones_row_f[:],
                                            rhs=M.decs_row[:], start=True, stop=True),
                    waits=[("sa", av_)] + self.w_waits("Z"), inc="st")
        self.set_w("Z", tv)
        vv = P.emit("v", lambda v: v.tensor_copy(out=M.dec_bc[:], in_=M.ps_Z[:, 0:NPAIR]),
                    waits=self.r_waits("Z"), inc="sv")
        self.set_r("Z", "sv", vv)

        if DBG_STAGE < 2:
            self.hstate_v = self.prog.val("sv")
            return None
        def t_diag(t):
            for p in range(NPAIR):
                ins = t.matmul(out=M.ps_Z[:, pc(p)], lhsT=M.Dp[:, pc(p)], rhs=M.Ksq[:, pc(p)],
                         start=True, stop=False)
                ins = t.matmul(out=M.ps_Z[:, pc(p)], lhsT=M.erow[0:1, pc(p)],
                         rhs=M.Hd_row[0:1, pc(p)], start=False, stop=True)
            return ins
        tv = P.emit("t", t_diag, waits=[("sv", dp_v), ("sa", ksq_a), ("sa", erow_a),
                                        ("sv", self.hstate_v)] + self.w_waits("Z"), inc="st")
        self.set_w("Z", tv)

        def v_x0(v):
            ins = v.tensor_tensor(out=M.scrf2[:], in0=M.ps_Z[:], in1=M.lamb_eps_arena[:], op=AD)
            ins = v.reciprocal(out=M.scrf2[:], in_=M.scrf2[:])
            return v.tensor_tensor(out=M.X[:], in0=aq[:], in1=M.scrf2[:], op=MU)
        vv = P.emit("v", v_x0, waits=self.r_waits("Z"), inc="sv")
        self.set_r("Z", "sv", vv)

        if DBG_STAGE < 3:
            self.hstate_v = self.prog.val("sv")
            return None
        self.pready = vv
        self.pb_a = P.emit("a", lambda a: a.activation(out=M.Pb[:], in_=M.X[:], func=AF.Copy),
                           waits=[("sv", vv)], inc="sa")
        self.pb_a_h0 = self.pb_a
        q_tv = self.emit_mv(M.X, ak)

        def v_r0(v):
            v.tensor_tensor(out=M.R[:], in0=aq[:], in1=M.ps_Z[:], op=SU)
            return v.tensor_copy(out=M.Pv[:], in_=M.R[:])
        vv = P.emit("v", v_r0, waits=[("st", q_tv)], inc="sv")
        self.set_r("Z", "sv", vv)
        p_v = vv

        def a_d0(a):
            for p in range(NPAIR):
                h = a if p == 0 else a.raw
                ins = h.activation(out=M.scrb[:, pc(p)], in_=M.R[:, pc(p)], func=AF.Square,
                             accum_out=M.dA[:, p:p + 1])
            return ins
        d_a = P.emit("a", a_d0, waits=[("sv", vv)], inc="sa")
        self.pb_a = P.emit("a", lambda a: a.activation(out=M.Pb[:], in_=M.Pv[:], func=AF.Copy),
                           waits=[("sv", vv)], inc="sa")
        self.pb_a_h0 = self.pb_a
        nd0_v = P.emit("v", lambda v: v.tensor_single_scalar(out=M.nd[:], in_=M.dA[:],
                                                             scalar=1e-10, op=GE),
                       waits=[("sa", d_a)], inc="sv")

        dcur, dnxt = M.dA, M.dB
        Pcur, Pnxt = M.Pv, M.Pv2
        chain_v = nd0_v

        if DBG_STAGE < 4:
            self.hstate_v = self.prog.val("sv")
            return None
        niter = min(DBG_ITERS, MAX_CG)
        for j in range(niter):
            last = (j == niter - 1)
            self.pready = p_v
            q_tv = self.emit_mv(Pcur, ak)

            def v_pq(v, Pcur=Pcur):
                v.skip()
                for p in range(NPAIR):
                    h = v if p == 0 else v.raw
                    ins = h.scalar_tensor_tensor(out=M.scrb[:, pc(p)], in0=Pcur[:, pc(p)], scalar=0.0,
                                           in1=M.ps_Z[:, pc(p)], op0=BYP, op1=MU,
                                           accum_out=M.pq[:, p:p + 1])
                return ins
            pq_v = P.emit("v", v_pq, waits=[("st", q_tv)], inc="sv")

            def v_ca(v, dcur=dcur):
                ins = v.tensor_tensor(out=M.c1[:], in0=M.pq[:], in1=M.pq[:], op=MU)
                ins = v.tensor_single_scalar(out=M.c1[:], in_=M.c1[:], scalar=1e-20, op=GE)
                ins = v.tensor_tensor(out=M.nd[:], in0=M.nd[:], in1=M.c1[:], op=MU)
                ins = v.tensor_scalar(out=M.alphan[:], in0=M.nd[:], scalar1=-1.0, scalar2=1.0,
                                op0=MU, op1=AD)
                ins = v.tensor_tensor(out=M.c1[:], in0=M.pq[:], in1=M.nd[:], op=MU)
                ins = v.tensor_tensor(out=M.c1[:], in0=M.c1[:], in1=M.alphan[:], op=AD)
                ins = v.reciprocal(out=M.c1[:], in_=M.c1[:])
                ins = v.tensor_tensor(out=M.c1[:], in0=M.c1[:], in1=M.nd[:], op=MU)
                ins = v.tensor_tensor(out=M.alpha[:], in0=dcur[:], in1=M.c1[:], op=MU)
                return v.tensor_scalar(out=M.alphan[:], in0=M.alpha[:], scalar1=-1.0,
                                       scalar2=None, op0=MU)
            ca_v = P.emit("v", v_ca, inc="sv")

            if not last:
                rv = ca_v
                for p in range(NPAIR):
                    def v_r(v, p=p):
                        if p > 0:
                            v.skip()
                        return v.scalar_tensor_tensor(
                                    out=M.R[:, pc(p)], in0=M.ps_Z[:, pc(p)],
                                    scalar=M.alphan[:, p:p + 1], in1=M.R[:, pc(p)],
                                    op0=MU, op1=AD)
                    rv = P.emit("v", v_r, inc="sv")
                    if p <= 4:
                        def a_dn(a, p=p, dnxt=dnxt):
                            a.skip()
                            return a.activation(
                                        out=M.scrb[:, pc(p)], in_=M.R[:, pc(p)], func=AF.Square,
                                        accum_out=dnxt[:, p:p + 1])
                        dn_a = P.emit("a", a_dn, waits=[("sv", rv)], inc="sa")
                self.set_r("Z", "sv", rv)
                def v_dn2(v, dnxt=dnxt):
                    v.skip()
                    for p in range(5, 8):
                        h = v if p == 5 else v.raw
                        ins = h.scalar_tensor_tensor(out=M.scrb[:, pc(p)], in0=M.R[:, pc(p)],
                                               scalar=0.0, in1=M.R[:, pc(p)], op0=BYP, op1=MU,
                                               accum_out=dnxt[:, p:p + 1])
                    return ins
                dn_v = P.emit("v", v_dn2, inc="sv")

                def v_cb(v, dcur=dcur, dnxt=dnxt):
                    ins = v.tensor_scalar(out=M.c1[:], in0=M.nd[:], scalar1=-1.0, scalar2=1.0,
                                    op0=MU, op1=AD)
                    ins = v.tensor_tensor(out=M.betac[:], in0=dcur[:], in1=M.nd[:], op=MU)
                    ins = v.tensor_tensor(out=M.c1[:], in0=M.betac[:], in1=M.c1[:], op=AD)
                    ins = v.reciprocal(out=M.c1[:], in_=M.c1[:])
                    ins = v.tensor_tensor(out=M.c1[:], in0=M.c1[:], in1=M.nd[:], op=MU)
                    ins = v.tensor_tensor(out=M.betac[:], in0=dnxt[:], in1=M.c1[:], op=MU)
                    ins = v.tensor_single_scalar(out=M.c1[:], in_=dnxt[:], scalar=1e-10, op=GE)
                    return v.tensor_tensor(out=M.nd[:], in0=M.nd[:], in1=M.c1[:], op=MU)
                cb_v = P.emit("v", v_cb, waits=[("sa", dn_a)], inc="sv")
                chain_v = cb_v

                def v_p1(v, Pcur=Pcur, Pnxt=Pnxt):
                    for p in range(4):
                        h = v if p == 0 else v.raw
                        ins = h.scalar_tensor_tensor(out=Pnxt[:, pc(p)], in0=Pcur[:, pc(p)],
                                               scalar=M.betac[:, p:p + 1], in1=M.R[:, pc(p)],
                                               op0=MU, op1=AD)
                    return ins
                p_v1 = P.emit("v", v_p1, inc="sv")
                def v_p2(v, Pcur=Pcur, Pnxt=Pnxt):
                    v.skip()
                    for p in range(4, 8):
                        h = v if p == 4 else v.raw
                        ins = h.scalar_tensor_tensor(out=Pnxt[:, pc(p)], in0=Pcur[:, pc(p)],
                                               scalar=M.betac[:, p:p + 1], in1=M.R[:, pc(p)],
                                               op0=MU, op1=AD)
                    return ins
                p_v = P.emit("v", v_p2, inc="sv")
                pb1 = P.emit("a", lambda a, Pnxt=Pnxt: a.activation(out=M.Pb[:, 0:512],
                                                                    in_=Pnxt[:, 0:512], func=AF.Copy),
                             waits=[("sv", p_v1)], inc="sa")
                self.pb_a = P.emit("a", lambda a, Pnxt=Pnxt: a.activation(out=M.Pb[:, 512:1024],
                                                                          in_=Pnxt[:, 512:1024],
                                                                          func=AF.Copy),
                                   waits=[("sv", p_v)], inc="sa")
                self.pb_a_h0 = pb1
            else:
                self.set_r("Z", "sv", ca_v)

            def v_x(v, Pcur=Pcur, last=last):
                if not last:
                    v.skip()
                for p in range(NPAIR):
                    h = v if p == 0 else v.raw
                    ins = h.scalar_tensor_tensor(out=M.X[:, pc(p)], in0=Pcur[:, pc(p)],
                                           scalar=M.alpha[:, p:p + 1], in1=M.X[:, pc(p)],
                                           op0=MU, op1=AD)
                return ins
            x_v = P.emit("v", v_x, waits=[("sv", ca_v)] if last else (), inc="sv")
            self.x_done = x_v

            dcur, dnxt = dnxt, dcur
            Pcur, Pnxt = Pnxt, Pcur

        if os.environ.get("MESA_DBG_TAP") and c == 0:
            def g_tap(g):
                ins = None
                for i, tsr in enumerate([M.pq, M.dA, M.dB, M.nd, M.alpha, M.c1, M.betac, M.e_all]):
                    ins = g.dma_start(out=self.dbg_d[i * 128:(i + 1) * 128, 0:NPAIR], in_=tsr[:])
                    ins.then_inc(P.sems["do"], 16)
                ins = g.dma_start(out=self.dbg_d[8 * 128:9 * 128, :], in_=M.X[:, 0:128])
                ins.then_inc(P.sems["do"], 16)
                ins = g.dma_start(out=self.dbg_d[9 * 128:10 * 128, :], in_=M.R[:, 0:128])
                ins.then_inc(P.sems["do"], 16)
                ins = g.dma_start(out=self.dbg_d[10 * 128:11 * 128, :], in_=M.Pv[:, 0:128])
                ins.then_inc(P.sems["do"], 16)
                return ins
            P.emit("g", g_tap, waits=[("sv", p_v)])
            P.bump("do", 16 * 11)
        if DBG_STAGE < 5:
            self.hstate_v = self.prog.val("sv")
            return None
        a_xb = P.emit("a", lambda a: a.activation(out=M.Pb[:], in_=M.X[:], func=AF.Copy),
                      waits=[("sv", self.x_done)], inc="sa")
        def t_xe(t):
            for p in range(NPAIR):
                ins = t.matmul(out=M.ps_B[:, pc(p)], lhsT=M.Pb[:, pc(p)], rhs=M.diag_e[:, pc(p)],
                               start=True, stop=True)
            return ins
        tvx = P.emit("t", t_xe, waits=self.w_waits("B") + [("sa", a_xb), ("sa", self.de_a)],
                     inc="st")
        self.set_w("B", tvx)
        self.de_rd = tvx
        def t_tx(t):
            for p in range(NPAIR):
                ins = t.transpose(out=M.ps_Tb[:, pc(p)], in_=M.Pb[:, pc(p)], identity=M.ident_b[:])
            return ins
        tvp = P.emit("t", t_tx, waits=self.w_waits("Tb") + [("sa", a_xb)], inc="st")
        self.set_w("Tb", tvp)
        av_x = P.emit("a", lambda a: a.activation(out=M.Xt[:], in_=M.ps_B[:], func=AF.Copy),
                      waits=self.r_waits("B"), inc="sa")
        self.set_r("B", "sa", av_x)
        av_p = P.emit("a", lambda a: a.activation(out=M.Pd[:], in_=M.ps_Tb[:], func=AF.Copy),
                      waits=self.r_waits("Tb"), inc="sa")
        self.set_r("Tb", "sa", av_p)
        xt_a = av_p

        def t_o1(t):
            for p in range(NPAIR):
                ins = t.matmul(out=M.ps_Z[:, pc(p)], lhsT=M.Hvk_b[:, pc(p)], rhs=M.Xt[:, pc(p)],
                         start=(p % 4 == 0), stop=False)
            return ins
        tv = P.emit("t", t_o1, waits=self.w_waits("Z") + [("sa", xt_a)], inc="st")
        self.set_w("Z", tv)

        for half in range(2):
            ps = range(half * 4, half * 4 + 4)
            def t_s2(t, ps=ps):
                for p in ps:
                    ins = t.matmul(out=M.ps_S[:, (p % 4) * HD:(p % 4 + 1) * HD], lhsT=M.Vd[:, pc(p)],
                             rhs=M.Pd[:, pc(p)], start=True, stop=True)
                return ins
            tv_s = P.emit("t", t_s2, waits=self.w_waits("S") + [("sa", xt_a), ("sa", self.kv_a)], inc="st")
            self.set_w("S", tv_s)
            def v_m2(v, half=half):
                v.skip()
                return v.tensor_tensor(out=M.Mp[:, half * 512:(half + 1) * 512],
                                       in0=M.Dp[:, half * 512:(half + 1) * 512],
                                       in1=M.ps_S[:], op=MU)
            vv_m = P.emit("v", v_m2, waits=self.r_waits("S"), inc="sv")
            self.set_r("S", "sv", vv_m)
            def t_o2(t, ps=ps):
                for p in ps:
                    ins = t.matmul(out=M.ps_Z[:, pc(p)], lhsT=ak[:, pc(p)], rhs=M.Mp[:, pc(p)],
                             start=False, stop=(p % 4 == 3))
                return ins
            tv_o2 = P.emit("t", t_o2, waits=[("sv", vv_m)], inc="st")
        self.set_w("Z", tv_o2)

        av_ = P.emit("a", lambda a: a.activation(out=M.scrb[:], in_=M.ps_Z[:], func=AF.Square),
                     waits=self.r_waits("Z"), inc="sa")
        self.set_r("Z", "sa", av_)

        def t_rms(t):
            for p in range(NPAIR):
                ins = t.matmul(out=M.ps_B[:, pc(p)], lhsT=M.ones_sq_b[:], rhs=M.scrb[:, pc(p)],
                         start=True, stop=True)
            return ins
        tv = P.emit("t", t_rms, waits=self.w_waits("B") + [("sa", av_)], inc="st")
        self.set_w("B", tv)

        def v_rms(v):
            return v.tensor_scalar(out=M.scrf[:], in0=M.ps_B[:], scalar1=1.0 / HD,
                                   scalar2=1e-6, op0=MU, op1=AD)
        vv = P.emit("v", v_rms, waits=self.r_waits("B"), inc="sv")
        self.set_r("B", "sv", vv)
        av_ = P.emit("a", lambda a: a.activation(out=M.scrf[:], in_=M.scrf[:], func=AF.Sqrt),
                     waits=[("sv", vv)], inc="sa")
        vv = P.emit("v", lambda v: v.reciprocal(out=M.scrf[:], in_=M.scrf[:]),
                    waits=[("sa", av_)], inc="sv")

        def v_on(v):
            return v.scalar_tensor_tensor(out=M.scrb[:], in0=M.ps_Z[:], scalar=M.onw_col[:],
                                          in1=M.scrf[:], op0=MU, op1=MU)
        on_v = P.emit("v", v_on, waits=[("sv", vv)] + self.r_waits("Z"), inc="sv")
        self.set_r("Z", "sv", on_v)

        def g_o(g):
            ins = None
            for p in range(NPAIR):
                b, h = p // HPC, p % HPC
                ins = g.dma_start(out=o_dram[h * 128:(h + 1) * 128,
                                             b * b_stride + c * C: b * b_stride + (c + 1) * C],
                                  in_=M.scrb[:, pc(p)])
                ins.then_inc(P.sems["do"], 16)
            return ins
        P.emit("g", g_o, waits=[("sv", on_v)])
        o_done = P.bump("do", 16 * NPAIR)

        if DBG_STAGE < 6:
            self.hstate_v = self.prog.val("sv")
            return None
        def v_kt(v):
            for p in range(NPAIR):
                h = v if p == 0 else v.raw
                ins = h.tensor_scalar(out=M.Ktil[:, pc(p)], in0=ak[:, pc(p)],
                                scalar1=M.w_all[:, p:p + 1], scalar2=None, op0=MU)
            return ins
        kt_v = P.emit("v", v_kt, waits=[("sv", w_v), ("sv", on_v)], inc="sv")

        def t_kk(t):
            for p in range(NPAIR):
                ins = t.matmul(out=M.ps_Z[:, pc(p)], lhsT=M.Ktil[:, pc(p)], rhs=ak[:, pc(p)],
                         start=True, stop=True)
            return ins
        tv = P.emit("t", t_kk, waits=self.w_waits("Z") + [("sv", kt_v)], inc="st")
        self.set_w("Z", tv)

        def v_hkk(v):
            for p in range(NPAIR):
                h = v if p == 0 else v.raw
                ins = h.scalar_tensor_tensor(out=M.Hkk_f[:, pc(p)], in0=M.Hkk_f[:, pc(p)],
                                       scalar=M.dec_bc[:, p:p + 1], in1=M.ps_Z[:, pc(p)],
                                       op0=MU, op1=AD)
            return ins
        vv = P.emit("v", v_hkk, waits=self.r_waits("Z"), inc="sv")
        self.set_r("Z", "sv", vv)
        hkkb_a = P.emit("a", lambda a: a.activation(out=M.Hkk_b[:], in_=M.Hkk_f[:], func=AF.Copy),
                        waits=[("sv", vv)], inc="sa")

        def v_vt(v):
            for p in range(NPAIR):
                h = v if p == 0 else v.raw
                ins = h.tensor_scalar(out=M.Ktil[:, pc(p)], in0=av[:, pc(p)],
                                scalar1=M.w_all[:, p:p + 1], scalar2=None, op0=MU)
            return ins
        vt_v = P.emit("v", v_vt, waits=[("sv", vv)], inc="sv")

        def t_vk(t):
            for p in range(NPAIR):
                ins = t.matmul(out=M.ps_B[:, pc(p)], lhsT=M.Ktil[:, pc(p)], rhs=ak[:, pc(p)],
                         start=True, stop=True)
            return ins
        tv = P.emit("t", t_vk, waits=self.w_waits("B") + [("sv", vt_v)], inc="st")
        self.set_w("B", tv)

        def v_hvk(v):
            for p in range(NPAIR):
                h = v if p == 0 else v.raw
                ins = h.scalar_tensor_tensor(out=M.Hvk_f[:, pc(p)], in0=M.Hvk_f[:, pc(p)],
                                       scalar=M.dec_bc[:, p:p + 1], in1=M.ps_B[:, pc(p)],
                                       op0=MU, op1=AD)
            return ins
        vv2 = P.emit("v", v_hvk, waits=self.r_waits("B"), inc="sv")
        self.set_r("B", "sv", vv2)
        hvkb_a = P.emit("a", lambda a: a.activation(out=M.Hvk_b[:], in_=M.Hvk_f[:], func=AF.Copy),
                        waits=[("sv", vv2)], inc="sa")

        def v_wk(v):
            for p in range(NPAIR):
                h = v if p == 0 else v.raw
                ins = h.tensor_scalar(out=M.scrb[:, pc(p)], in0=M.Ksq[:, pc(p)],
                                scalar1=M.w_all[:, p:p + 1], scalar2=None, op0=MU)
            return ins
        wk_v = P.emit("v", v_wk, waits=[("sv", vv2), ("do", o_done)], inc="sv")

        def t_hd(t):
            for p in range(NPAIR):
                ins = t.matmul(out=M.ps_Z[0:1, pc(p)], lhsT=M.ones_col_b[:], rhs=M.scrb[:, pc(p)],
                         start=True, stop=True)
            return ins
        tv = P.emit("t", t_hd, waits=self.w_waits("Z") + [("sv", wk_v)], inc="st")
        self.set_w("Z", tv)

        def v_hd(v):
            for p in range(NPAIR):
                h = v if p == 0 else v.raw
                ins = h.scalar_tensor_tensor(out=M.Hd_row[0:1, pc(p)], in0=M.Hd_row[0:1, pc(p)],
                                       scalar=M.decs_row[0:1, p:p + 1], in1=M.ps_Z[0:1, pc(p)],
                                       op0=MU, op1=AD)
            return ins
        vv3 = P.emit("v", v_hd, waits=self.r_waits("Z"), inc="sv")
        self.set_r("Z", "sv", vv3)

        self.hstate_v = vv3
        self.hstate_a = max(hkkb_a, hvkb_a)
        return o_done



def build_mesa_test(T):
    nc = bass.Bass("TRN2", target_bir_lowering=False, debug=False, detect_race_conditions=False)
    nchunk = T // C
    qn_d = nc.dram_tensor("qn", [4 * T, 2 * HD], F32, kind="ExternalInput").ap()
    kn_d = nc.dram_tensor("kn", [4 * T, 2 * HD], BF16, kind="ExternalInput").ap()
    vn_d = nc.dram_tensor("vn", [4 * T, 2 * HD], BF16, kind="ExternalInput").ap()
    g_d = nc.dram_tensor("g", [4 * T, HPC], F32, kind="ExternalInput").ap()
    bt_d = nc.dram_tensor("bt", [4 * T, HPC], F32, kind="ExternalInput").ap()
    lamp_d = nc.dram_tensor("lamp", [1, HPC * 128], F32, kind="ExternalInput").ap()
    onw_d = nc.dram_tensor("onw", [1, 128], F32, kind="ExternalInput").ap()
    o_d = nc.dram_tensor("o", [HPC * 128, 4 * T], BF16, kind="ExternalOutput").ap()
    dbg_d = nc.dram_tensor("dbg", [11 * 128, 128], F32, kind="ExternalOutput").ap()

    prog = Prog()
    with ExitStack() as ctx:
        M = Mesa(nc, ctx, prog, T)
        M.dbg_d = dbg_d
        sem_names = ["st", "sv", "sa", "sg", "dlc", "dl0", "dl1", "do", "dsx", "dbb", "dst", "dr2", "ddy", "do0", "do1"]
        sems = {n: ctx.enter_context(nc.semaphore(n)) for n in sem_names}
        M.emit_consts(lamp_d, onw_d)

        for cidx in range(min(nchunk, DBG_NCHUNK)):
            z = cidx % 2
            dl_sem = f"dl{z}"
            def s_ld(s, z=z, cidx=cidx, dl_sem=dl_sem):
                ins = None
                for b in range(4):
                    r0 = b * T + cidx * C
                    for (dst, srcd) in ((M.aq[z], qn_d), (M.ak[z], kn_d), (M.av[z], vn_d)):
                        ins = s.dma_start(out=dst[:, b * 256:(b + 1) * 256],
                                          in_=srcd[r0:r0 + C, :])
                        ins.then_inc(prog.sems[dl_sem], 16)
                    ins = s.dma_start(out=M.ag[z][:, b * HPC:(b + 1) * HPC], in_=g_d[r0:r0 + C, :])
                    ins.then_inc(prog.sems[dl_sem], 16)
                    ins = s.dma_start(out=M.ab[z][:, b * HPC:(b + 1) * HPC], in_=bt_d[r0:r0 + C, :])
                    ins.then_inc(prog.sems[dl_sem], 16)
                return ins
            w = [("sv", M.hstate_v)] if cidx >= 2 else []
            prog.emit("s", s_ld, waits=w)
            dl_val = prog.bump(dl_sem, 16 * 20)
            M.emit_chunk(z, cidx, o_d, dl_sem, dl_val, T)

        prog.emit("g", lambda g: g.wait_ge(sems["do"], prog.val("do")) or g.nop())

        block = ctx.enter_context(nc.Block())
        prog.realize(block, sems)
    return nc


def mesa_test_run(qn, kn, vn, g, bt, lamp, onw):
    T = qn.shape[0] // 4
    nc = build_mesa_test(T)
    import ml_dtypes
    in_map = dict(qn=qn.astype(np.float32), kn=kn.astype(ml_dtypes.bfloat16),
                  vn=vn.astype(ml_dtypes.bfloat16), g=g.astype(np.float32),
                  bt=bt.astype(np.float32), lamp=lamp.astype(np.float32),
                  onw=onw.astype(np.float32))
    res = run_bass_kernel_spmd(nc, [in_map], core_ids=[0])
    return np.asarray(res.results[0]["o"].astype(np.float32))



ROWS = 16384
T_FULL = 4096
RPC = 2048


def emit_p12(nc, prog, M, ctx, xT, xTr, wcat, babb, st_in, st_all,
             q_d, k_d, v_d, g_d, bt_d):
    P = prog
    S = lambda nm, sh, dt=F32: ctx.enter_context(nc.sbuf_tensor(nm, sh, dt))
    stg = S("p2_stg", [128, 772])
    wcb = S("p2_wcb", [128, 16 * 772], BF16)
    sbx = [S(f"p2_sbx{z}", [128, 16 * 128], F32) for z in range(2)]
    sbxb = [S(f"p2_sbxb{z}", [128, 16 * 128], BF16) for z in range(2)]
    sbig = S("p2_sbig", [128, 512])
    sbig2 = S("p2_sbig2", [128, 512])

    negmu_stg = S("p2_negmu", [1, RPC])
    rs_stg = S("p2_rsstg", [1, RPC])
    rs_cols = S("p2_rscols", [128, 128])
    csw = S("p2_csw", [1, 772])
    qsil = [S(f"p2_qsil{z}", [128, 512]) for z in range(2)]
    sq = [S(f"p2_sq{z}", [128, 512]) for z in range(2)]
    rn4 = [S(f"p2_rn4{z}", [128, 4]) for z in range(2)]
    qout = [S(f"p2_qout{z}", [128, 256]) for z in range(2)]
    kout = [S(f"p2_kout{z}", [128, 256], BF16) for z in range(2)]
    vout = [S(f"p2_vout{z}", [128, 256], BF16) for z in range(2)]
    abt = [S(f"p2_abt{z}", [128, 4]) for z in range(2)]
    gout = [S(f"p2_gout{z}", [128, HPC]) for z in range(2)]
    btout = [S(f"p2_btout{z}", [128, HPC]) for z in range(2)]
    bias_bc = S("p2_biasbc", [128, 4])
    babb_r = S("p2_babbr", [1, 4])
    strow = S("p2_strow", [1, 2 * RPC])

    consts_v = M.consts_done

    for dt in range(16):
        def s_w(s, dt=dt):
            ins = s.dma_start(out=stg[:], in_=wcat[dt * 128:(dt + 1) * 128, :])
            ins.then_inc(P.sems["dlc"], 16)
            return ins
        P.emit("s", s_w, waits=[("sv", P.val("sv")), ("st", P.val("st"))])
        wv = P.bump("dlc", 16)
        vv = P.emit("v", lambda v, dt=dt: v.tensor_copy(out=wcb[:, dt * 772:(dt + 1) * 772], in_=stg[:]),
                    waits=[("dlc", wv)], inc="sv")
        def t_cs(t, dt=dt):
            t.matmul(out=M.ps_B[0:1, 0:512], lhsT=M.ones_col_f[:], rhs=stg[:, 0:512],
                     start=(dt == 0), stop=(dt == 15))
            return t.matmul(out=M.ps_B[0:1, 512:772], lhsT=M.ones_col_f[:], rhs=stg[:, 512:772],
                            start=(dt == 0), stop=(dt == 15))
        tv = P.emit("t", t_cs, waits=[("dlc", wv), ("sv", consts_v)] + (M.w_waits("B") if dt == 0 else []), inc="st")
    M.set_w("B", tv)
    vv = P.emit("v", lambda v: v.tensor_copy(out=csw[:], in_=M.ps_B[0:1, 0:772]),
                waits=M.r_waits("B"), inc="sv")
    M.set_r("B", "sv", vv)

    def s_bb(s):
        ins = s.dma_start(out=babb_r[:], in_=babb)
        ins.then_inc(P.sems["dbb"], 16)
        return ins
    P.emit("s", s_bb)
    bbv = P.bump("dbb", 16)
    tv = P.emit("t", lambda t: t.matmul(out=M.ps_T[:, 0:4], lhsT=M.ones_row_f[:], rhs=babb_r[:],
                                        start=True, stop=True),
                waits=[("dbb", bbv), ("sv", consts_v)] + M.w_waits("T"), inc="st")
    M.set_w("T", tv)
    vv = P.emit("v", lambda v: v.tensor_copy(out=bias_bc[:], in_=M.ps_T[:, 0:4]),
                waits=M.r_waits("T"), inc="sv")
    M.set_r("T", "sv", vv)

    xTr3 = xTr.rearrange("(dt p) c -> p dt c", p=128)
    sbw, sqw = sbx[0], sbx[1]
    for rb in range(4):
        for qb in range(4):
            def s_x(s, rb=rb, qb=qb):
                ins = s.dma_start(out=sbw[:].rearrange("p (dt c) -> p dt c", dt=4),
                                  in_=xTr3[:, qb * 4:(qb + 1) * 4, rb * 512:(rb + 1) * 512])
                ins.then_inc(P.sems["dsx"], 16)
                return ins
            P.emit("s", s_x, waits=[("sa", P.val("sa")), ("st", P.val("st")),
                                    ("sv", P.val("sv"))])
            xv = P.bump("dsx", 16)
            av = P.emit("a", lambda a: a.activation(out=sqw[:], in_=sbw[:], func=AF.Square),
                        waits=[("dsx", xv)], inc="sa")
            def t_st(t, rb=rb, qb=qb):
                for k in range(4):
                    t.matmul(out=M.ps_S[0:1, 0:512], lhsT=M.ones_col_f[:],
                             rhs=sbw[:, k * 512:(k + 1) * 512],
                             start=(qb == 0 and k == 0), stop=(qb == 3 and k == 3))
                    ins = t.matmul(out=M.ps_B[0:1, 0:512], lhsT=M.ones_col_f[:],
                                   rhs=sqw[:, k * 512:(k + 1) * 512],
                                   start=(qb == 0 and k == 0), stop=(qb == 3 and k == 3))
                return ins
            tv = P.emit("t", t_st, waits=[("dsx", xv), ("sa", av)]
                        + (M.w_waits("S", "B") if qb == 0 else []), inc="st")
        M.set_w("S", tv)
        M.set_w("B", tv)

        def v_fin(v, rb=rb):
            v.tensor_scalar(out=strow[0:1, rb * 512:(rb + 1) * 512], in0=M.ps_S[0:1, 0:512],
                            scalar1=1.0 / HID, scalar2=None, op0=MU)
            v.tensor_scalar(out=sbig[0:1, 0:512], in0=M.ps_B[0:1, 0:512],
                            scalar1=1.0 / HID, scalar2=None, op0=MU)
            v.tensor_tensor(out=sbig2[0:1, 0:512], in0=strow[0:1, rb * 512:(rb + 1) * 512],
                            in1=strow[0:1, rb * 512:(rb + 1) * 512], op=MU)
            v.tensor_tensor(out=sbig[0:1, 0:512], in0=sbig[0:1, 0:512], in1=sbig2[0:1, 0:512], op=SU)
            return v.tensor_scalar(out=sbig[0:1, 0:512], in0=sbig[0:1, 0:512],
                                   scalar1=1e-5, scalar2=None, op0=AD)
        vv = P.emit("v", v_fin, waits=M.r_waits("S", "B"), inc="sv")
        M.set_r("S", "sv", vv)
        M.set_r("B", "sv", vv)
        av = P.emit("a", lambda a, rb=rb: a.activation(out=sbig[0:1, 0:512], in_=sbig[0:1, 0:512],
                                                       func=AF.Sqrt),
                    waits=[("sv", vv)], inc="sa")
        vv = P.emit("v", lambda v, rb=rb: v.reciprocal(out=strow[0:1, RPC + rb * 512:RPC + (rb + 1) * 512],
                                                       in_=sbig[0:1, 0:512]),
                    waits=[("sa", av)], inc="sv")
    stats_v = vv
    stats_sa, stats_st = P.val("sa"), P.val("st")

    def g_st(g):
        ins = g.dma_start(out=st_in[:, :], in_=strow[0:1, :])
        ins.then_inc(P.sems["do"], 16)
        return ins
    P.emit("g", g_st, waits=[("sv", stats_v)])
    sv1 = P.bump("do", 16)
    def g_ag(g):
        return g.collective_compute("AllGather", mybir.AluOpType.bypass,
                                    replica_groups=[list(range(NCORES))],
                                    ins=[st_in[:, :].opt()], outs=[st_all[:, :].opt()])
    agv = P.emit("g", g_ag, waits=[("do", sv1)], inc="cc")
    tv = None
    for cb in range(8):
        def g_rd(g, cb=cb):
            ins = g.dma_start(out=rs_stg[:], in_=st_all[2 * cb + 1:2 * cb + 2, :])
            ins.then_inc(P.sems["dst"], 16)
            return ins
        P.emit("g", g_rd, waits=[("cc", agv), ("st", tv or 0)])
        stv = P.bump("dst", 16)
        def t_rsc(t, cb=cb):
            ins = None
            for j in range(16):
                ins = t.transpose(out=M.ps_T[:, cb * 16 + j:cb * 16 + j + 1],
                                  in_=rs_stg[0:1, j * 128:(j + 1) * 128],
                                  identity=M.ones_row_f[0:1, 0:1])
            return ins
        tv = P.emit("t", t_rsc, waits=[("dst", stv)] + (M.w_waits("T") if cb == 0 else []), inc="st")
    M.set_w("T", tv)
    vv = P.emit("v", lambda v: v.tensor_copy(out=rs_cols[:], in_=M.ps_T[:, 0:128]),
                waits=M.r_waits("T"), inc="sv")
    M.set_r("T", "sv", vv)
    rsc_v = vv

    wcast_v = P.val("sv")
    negmu_v = 0
    banks = ["Z", "B"]
    psq = [M.ps_Z, M.ps_B]
    cast_prev = [0, 0]
    xv_pre = [0, 0]
    tv_prev = [0, 0]
    sil_prev = [0, 0]
    ag_prev = [0, 0]
    g2_prev = [0, 0]
    do_prev = [0, 0]
    for rt in range(128):
        z = rt % 2
        dl_sem = f"dl{z}"
        bank = banks[z]
        ps = psq[z]
        if rt % 16 == 0:
            cb = rt // 16
            def g_mu(g, cb=cb):
                ins = g.dma_start(out=rs_stg[:], in_=st_all[2 * cb:2 * cb + 1, :])
                ins.then_inc(P.sems["dst"], 16)
                return ins
            P.emit("g", g_mu, waits=[("cc", agv), ("st", P.val("st"))])
            muv = P.bump("dst", 16)
            negmu_v = P.emit("v", lambda v: v.tensor_scalar(out=negmu_stg[:], in0=rs_stg[:],
                                                            scalar1=-1.0, scalar2=None, op0=MU),
                             waits=[("dst", muv), ("sv", rsc_v)], inc="sv")
        def s_xt(s, rt=rt, z=z, dl_sem=dl_sem):
            ins = None
            for dt in range(16):
                ins = s.dma_start(out=sbx[z][:, dt * 128:(dt + 1) * 128],
                                  in_=xT[dt * 128:(dt + 1) * 128, rt * 128:(rt + 1) * 128])
                ins.then_inc(P.sems[dl_sem], 16)
            return ins
        exw = [("sa", stats_sa), ("st", stats_st)] if rt < 2 else []
        P.emit("s", s_xt, waits=[("sv", cast_prev[z])] + exw)
        xv_pre[z] = P.bump(dl_sem, 16 * 16)
        cast_prev[z] = P.emit("v", lambda v, z=z: v.tensor_copy(out=sbxb[z][:], in_=sbx[z][:]),
                              waits=[(dl_sem, xv_pre[z]), ("sv", wcast_v), ("st", tv_prev[z])],
                              inc="sv")
        def t_pj(t, rt=rt, z=z, ps=ps):
            for dt in range(16):
                t.matmul(out=ps[:, 0:512], lhsT=sbxb[z][:, dt * 128:(dt + 1) * 128],
                         rhs=wcb[:, dt * 772:dt * 772 + 512], start=(dt == 0), stop=False)
                t.matmul(out=ps[:, 512:772], lhsT=sbxb[z][:, dt * 128:(dt + 1) * 128],
                         rhs=wcb[:, dt * 772 + 512:(dt + 1) * 772], start=(dt == 0), stop=False)
            rtl = rt % 16
            t.matmul(out=ps[:, 0:512], lhsT=negmu_stg[0:1, rtl * 128:(rtl + 1) * 128],
                     rhs=csw[0:1, 0:512], start=False, stop=True)
            return t.matmul(out=ps[:, 512:772], lhsT=negmu_stg[0:1, rtl * 128:(rtl + 1) * 128],
                            rhs=csw[0:1, 512:772], start=False, stop=True)
        tv = P.emit("t", t_pj, waits=[("sv", cast_prev[z]), ("sv", negmu_v), ("sv", rsc_v),
                                      ("sa", sil_prev[z])] + M.w_waits(bank),
                    inc="st")
        M.set_w(bank, tv)
        tv_prev[z] = tv

        def a_sil(a, rt=rt, z=z, ps=ps):
            a.activation(out=qsil[z][:], in_=ps[:, 0:512], func=AF.Silu,
                         scale=rs_cols[:, rt:rt + 1])
            a.activation(out=vout[z][:], in_=ps[:, 512:768], func=AF.Silu,
                         scale=rs_cols[:, rt:rt + 1])
            return a.activation(out=sq[z][:], in_=qsil[z][:], func=AF.Square)
        av = P.emit("a", a_sil, waits=M.r_waits(bank) + [(f"do{z}", do_prev[z]), ("sv", g2_prev[z])],
                    inc="sa")
        sil_prev[z] = av
        def v_ab(v, rt=rt, z=z, ps=ps):
            return v.tensor_scalar(out=abt[z][:], in0=ps[:, 768:772],
                                   scalar1=rs_cols[:, rt:rt + 1], scalar2=None, op0=MU)
        abv = P.emit("v", v_ab, waits=M.r_waits(bank) + [("sa", ag_prev[z]), ("sv", g2_prev[z])],
                     inc="sv")
        M.set_r(bank, "sv", abv)

        def v_l2(v, z=z):
            return v.tensor_reduce(out=rn4[z][:], in_=sq[z][:].rearrange("p (s d) -> p s d", s=4),
                                   axis=AX, op=AD)
        lv = P.emit("v", v_l2, waits=[("sa", av)], inc="sv")
        av2 = P.emit("a", lambda a, z=z: a.activation(out=rn4[z][:], in_=rn4[z][:], func=AF.Sqrt),
                     waits=[("sv", lv)], inc="sa")
        def v_rn(v, z=z):
            v.tensor_scalar(out=rn4[z][:], in0=rn4[z][:], scalar1=1e-12, scalar2=None, op0=MX)
            return v.reciprocal(out=rn4[z][:], in_=rn4[z][:])
        rv = P.emit("v", v_rn, waits=[("sa", av2)], inc="sv")
        def v_sc(v, z=z):
            for sgi in range(2):
                v.tensor_scalar(out=qout[z][:, sgi * 128:(sgi + 1) * 128],
                                in0=qsil[z][:, sgi * 128:(sgi + 1) * 128],
                                scalar1=rn4[z][:, sgi:sgi + 1], scalar2=None, op0=MU)
            for sgi in range(2):
                v.tensor_scalar(out=kout[z][:, sgi * 128:(sgi + 1) * 128],
                                in0=qsil[z][:, 256 + sgi * 128:256 + (sgi + 1) * 128],
                                scalar1=rn4[z][:, 2 + sgi:3 + sgi], scalar2=None, op0=MU)
            v.tensor_tensor(out=abt[z][:], in0=abt[z][:], in1=bias_bc[:], op=AD)
            v.tensor_scalar(out=gout[z][:], in0=abt[z][:, 0:2], scalar1=0.0, scalar2=None, op0=MI)
            v.tensor_scalar(out=sq[z][0:128, 0:2], in0=abt[z][:, 0:2], scalar1=-1.0, scalar2=None, op0=MU)
            return v.tensor_tensor(out=sq[z][0:128, 2:4], in0=abt[z][:, 0:2], in1=sq[z][0:128, 0:2], op=MI)
        scv = P.emit("v", v_sc, waits=[("sv", rv), ("sa", ag_prev[z]), (f"do{z}", do_prev[z])], inc="sv")
        def a_g(a, z=z):
            a.activation(out=sq[z][0:128, 4:6], in_=sq[z][0:128, 2:4], func=AF.Exp)
            a.activation(out=sq[z][0:128, 4:6], in_=sq[z][0:128, 4:6], func=AF.Ln,
                         bias=M.ones_col_f[:])
            return a.activation(out=btout[z][:], in_=abt[z][:, 2:4], func=AF.Sigmoid)
        agv2 = P.emit("a", a_g, waits=[("sv", scv)], inc="sa")
        ag_prev[z] = agv2
        def v_g2(v, z=z):
            return v.tensor_tensor(out=gout[z][:], in0=gout[z][:], in1=sq[z][0:128, 4:6], op=SU)
        gv2 = P.emit("v", v_g2, waits=[("sa", agv2)], inc="sv")
        g2_prev[z] = gv2

        def g_out(g, rt=rt, z=z):
            ins = None
            for (dst, srct) in ((q_d, qout[z]), (k_d, kout[z]), (v_d, vout[z]),
                                (g_d, gout[z]), (bt_d, btout[z])):
                ins = g.dma_start(out=dst[rt * 128:(rt + 1) * 128, :], in_=srct[:])
                ins.then_inc(P.sems[f"do{z}"], 16)
            return ins
        P.emit("g", g_out, waits=[("sv", gv2), ("sa", agv2)])
        P.bump(f"do{z}", 16 * 5)
        do_prev[z] = P.val(f"do{z}")
    return (P.val("do0"), P.val("do1"))


def emit_p4a(nc, prog, ctx4, xTr, wo, o_all, x1_d, x1b_d, ln2st_d, bw):
    P = prog
    S = lambda nm, sh, dt=F32: ctx4.enter_context(nc.sbuf_tensor(nm, sh, dt))
    P_ = lambda nm, sh, dt=F32: ctx4.enter_context(nc.psum_tensor(nm, sh, dt))
    wob = S("p4_wob", [128, 16 * 2048], BF16)
    stg = S("p4_stg", [128, 2048])
    oT = S("p4_oT", [128, 16 * 512], BF16)
    xa = S("p4_xa", [128, 16 * 512])
    atn = S("p4_atn", [128, 16 * 512], BF16)
    sqt = S("p4_sqt", [128, 512], BF16)
    mu_bc = S("p4_mubc", [128, 512])
    rs_bc = S("p4_rsbc", [128, 512])
    srow = S("p4_srow", [1, 1024])
    t1 = S("p4_t1", [128, 512])
    x1t = S("p4_x1t", [128, 512])
    x1bt = S("p4_x1bt", [128, 512], BF16)
    ones_col_f = S("p4_onescf", [128, 1])
    ones_col_b = S("p4_onescb", [128, 1], BF16)
    ones_row_f = S("p4_onesrf", [1, 128])
    ps1 = P_("p4_ps1", [128, 512])
    ps2 = P_("p4_ps2", [128, 512])
    pss = P_("p4_pss", [1, 2048])
    rd = {"1": ("sv", 0), "2": ("sv", 0), "s": ("sv", 0)}
    wr = {"1": 0, "2": 0, "s": 0}
    bar = bw

    gv = P.emit("g", lambda g: (g.memset(ones_col_f[:], 1.0), g.memset(ones_col_b[:], 1.0),
                                g.memset(ones_row_f[:], 1.0))[-1], waits=bar, inc="sg")

    cast_v = 0
    for dt in range(16):
        def s_w(s, dt=dt):
            ins = s.dma_start(out=stg[:], in_=wo[dt * 128:(dt + 1) * 128, :])
            ins.then_inc(P.sems["dlc"], 16)
            return ins
        P.emit("s", s_w, waits=bar + [("sv", cast_v)])
        wv = P.bump("dlc", 16)
        cast_v = P.emit("v", lambda v, dt=dt: v.tensor_copy(out=wob[:, dt * 2048:(dt + 1) * 2048],
                                                            in_=stg[:]),
                        waits=bar + [("dlc", wv)], inc="sv")

    ln2_parts = []
    ep_v = 0
    prev_l2do = 0
    for r in range(4):
        def s_o(g, r=r):
            pid = g.partition_id()
            src3 = o_all.rearrange("(dt p) r -> p dt r", p=128)[:, :, bass.ds(pid * RPC + r * 512, 512)]
            ins = g.dma_start(out=oT[:].rearrange("p (dt c) -> p dt c", dt=16), in_=src3)
            ins.then_inc(P.sems["ddy"], 16)
            return ins
        P.emit("g", s_o, waits=bar + [("sv", ep_v)])
        ov = P.bump("ddy", 16)
        def s_xa(s, r=r):
            ins = None
            for dt in range(16):
                ins = s.dma_start(out=xa[:, dt * 512:(dt + 1) * 512],
                                  in_=xTr[dt * 128:(dt + 1) * 128, r * 512:(r + 1) * 512])
                ins.then_inc(P.sems["dl1"], 16)
            return ins
        P.emit("s", s_xa, waits=bar + [("sv", ep_v)])
        xv = P.bump("dl1", 16 * 16)

        st_tv = None
        for cot in range(16):
            key = "1" if cot % 2 == 0 else "2"
            psx = ps1 if cot % 2 == 0 else ps2
            def t_wo2(t, cot=cot, psx=psx):
                ins = None
                for dt in range(16):
                    ins = t.matmul(out=psx[:], lhsT=wob[:, dt * 2048 + cot * 128:dt * 2048 + (cot + 1) * 128],
                                   rhs=oT[:, dt * 512:(dt + 1) * 512], start=(dt == 0), stop=(dt == 15))
                return ins
            tv = P.emit("t", t_wo2, waits=[("ddy", ov), ("sv", cast_v)] + [rd[key]], inc="st")
            wr[key] = tv
            vv = P.emit("v", lambda v, cot=cot, psx=psx: v.tensor_copy(out=atn[:, cot * 512:(cot + 1) * 512],
                                                                       in_=psx[:]),
                        waits=[("st", wr[key])], inc="sv")
            rd[key] = ("sv", vv)
            av = P.emit("a", lambda a, cot=cot: a.activation(out=sqt[:], in_=atn[:, cot * 512:(cot + 1) * 512],
                                                             func=AF.Square),
                        waits=[("sv", vv), ("sa", P.val("sa")), ("st", st_tv or wr["s"])], inc="sa")
            def t_st(t, cot=cot):
                t.matmul(out=pss[0:1, 0:512], lhsT=ones_col_b[:], rhs=atn[:, cot * 512:(cot + 1) * 512],
                         start=(cot == 0), stop=(cot == 15))
                return t.matmul(out=pss[0:1, 512:1024], lhsT=ones_col_b[:], rhs=sqt[:],
                                start=(cot == 0), stop=(cot == 15))
            st_tv = P.emit("t", t_st, waits=[("sv", vv), ("sa", av)] + ([rd["s"]] if cot == 0 else []),
                           inc="st")
        wr["s"] = st_tv

        def v_fin(v):
            v.tensor_scalar(out=srow[0:1, 0:512], in0=pss[0:1, 0:512], scalar1=1.0 / HID, scalar2=None, op0=MU)
            v.tensor_scalar(out=srow[0:1, 512:1024], in0=pss[0:1, 512:1024], scalar1=1.0 / HID, scalar2=None, op0=MU)
            v.tensor_tensor(out=t1[0:1, :], in0=srow[0:1, 0:512], in1=srow[0:1, 0:512], op=MU)
            v.tensor_tensor(out=srow[0:1, 512:1024], in0=srow[0:1, 512:1024], in1=t1[0:1, :], op=SU)
            return v.tensor_scalar(out=srow[0:1, 512:1024], in0=srow[0:1, 512:1024], scalar1=1e-5, scalar2=None, op0=AD)
        vv = P.emit("v", v_fin, waits=[("st", wr["s"]), ("do", prev_l2do)], inc="sv")
        rd["s"] = ("sv", vv)
        av = P.emit("a", lambda a: a.activation(out=srow[0:1, 512:1024], in_=srow[0:1, 512:1024], func=AF.Sqrt),
                    waits=[("sv", vv)], inc="sa")
        vv = P.emit("v", lambda v: v.reciprocal(out=srow[0:1, 512:1024], in_=srow[0:1, 512:1024]),
                    waits=[("sa", av)], inc="sv")
        def t_bc(t):
            t.matmul(out=ps1[:], lhsT=ones_row_f[:], rhs=srow[0:1, 0:512], start=True, stop=True)
            return t.matmul(out=ps2[:], lhsT=ones_row_f[:], rhs=srow[0:1, 512:1024], start=True, stop=True)
        tv = P.emit("t", t_bc, waits=[("sv", vv)] + [rd["1"], rd["2"]], inc="st")
        wr["1"] = wr["2"] = tv
        def v_bc(v):
            v.tensor_copy(out=mu_bc[:], in_=ps1[:])
            return v.tensor_copy(out=rs_bc[:], in_=ps2[:])
        vv = P.emit("v", v_bc, waits=[("st", tv)], inc="sv")
        rd["1"] = rd["2"] = ("sv", vv)
        bc_v = vv

        ln2_tv = None
        for cot in range(16):
            def v_x1(v, cot=cot):
                v.tensor_tensor(out=t1[:], in0=atn[:, cot * 512:(cot + 1) * 512], in1=mu_bc[:], op=SU)
                v.tensor_tensor(out=t1[:], in0=t1[:], in1=rs_bc[:], op=MU)
                v.tensor_tensor(out=x1t[:], in0=t1[:], in1=xa[:, cot * 512:(cot + 1) * 512], op=AD)
                return v.tensor_copy(out=x1bt[:], in_=x1t[:])
            vv = P.emit("v", v_x1, waits=[("sv", bc_v), ("dl1", xv), ("do", P.val("do"))], inc="sv")
            av = P.emit("a", lambda a: a.activation(out=sqt[:], in_=x1bt[:], func=AF.Square),
                        waits=[("sv", vv), ("st", ln2_tv or st_tv)], inc="sa")
            def t_l2(t, cot=cot):
                t.matmul(out=pss[0:1, 1024:1536], lhsT=ones_col_b[:], rhs=x1bt[:],
                         start=(cot == 0), stop=(cot == 15))
                return t.matmul(out=pss[0:1, 1536:2048], lhsT=ones_col_b[:], rhs=sqt[:],
                                start=(cot == 0), stop=(cot == 15))
            ln2_tv = P.emit("t", t_l2, waits=[("sv", vv), ("sa", av)], inc="st")
            def g_x1(g, cot=cot, r=r):
                ins = g.dma_start(out=x1_d[cot * 128:(cot + 1) * 128, r * 512:(r + 1) * 512], in_=x1t[:])
                ins.then_inc(P.sems["do"], 16)
                ins = g.dma_start(out=x1b_d[cot * 128:(cot + 1) * 128, r * 512:(r + 1) * 512], in_=x1bt[:])
                ins.then_inc(P.sems["do"], 16)
                return ins
            P.emit("g", g_x1, waits=[("st", ln2_tv)])
            P.bump("do", 32)

        def v_f2(v, r=r):
            v.tensor_scalar(out=srow[0:1, 0:512], in0=pss[0:1, 1024:1536], scalar1=1.0 / HID, scalar2=None, op0=MU)
            v.tensor_scalar(out=srow[0:1, 512:1024], in0=pss[0:1, 1536:2048], scalar1=1.0 / HID, scalar2=None, op0=MU)
            v.tensor_tensor(out=t1[0:1, :], in0=srow[0:1, 0:512], in1=srow[0:1, 0:512], op=MU)
            v.tensor_tensor(out=srow[0:1, 512:1024], in0=srow[0:1, 512:1024], in1=t1[0:1, :], op=SU)
            return v.tensor_scalar(out=srow[0:1, 512:1024], in0=srow[0:1, 512:1024], scalar1=1e-5, scalar2=None, op0=AD)
        vv = P.emit("v", v_f2, waits=[("st", ln2_tv)], inc="sv")
        av = P.emit("a", lambda a: a.activation(out=srow[0:1, 512:1024], in_=srow[0:1, 512:1024], func=AF.Sqrt),
                    waits=[("sv", vv)], inc="sa")
        vv = P.emit("v", lambda v: v.reciprocal(out=srow[0:1, 512:1024], in_=srow[0:1, 512:1024]),
                    waits=[("sa", av)], inc="sv")
        def g_l2(g, r=r):
            ins = g.dma_start(out=ln2st_d[0:1, r * 512:(r + 1) * 512], in_=srow[0:1, 0:512])
            ins.then_inc(P.sems["do"], 16)
            ins = g.dma_start(out=ln2st_d[0:1, 2048 + r * 512:2048 + (r + 1) * 512], in_=srow[0:1, 512:1024])
            ins.then_inc(P.sems["do"], 16)
            return ins
        P.emit("g", g_l2, waits=[("sv", vv)])
        P.bump("do", 32)
        prev_l2do = P.val("do")
        ep_v = vv
    return P.val("do")


def emit_p4b(nc, prog, ctx4, x1_d, x1b_d, ln2st_d, w1, b1, w2, b2, ucat, vcatT, y, bw):
    P = prog
    S = lambda nm, sh, dt=F32: ctx4.enter_context(nc.sbuf_tensor(nm, sh, dt))
    P_ = lambda nm, sh, dt=F32: ctx4.enter_context(nc.psum_tensor(nm, sh, dt))
    w1b = S("pb_w1b", [128, 16 * 2048], BF16)
    ucb = S("pb_ucb", [128, 4 * 2048], BF16)
    vcb = S("pb_vcb", [128, 16 * 512], BF16)
    w2b = S("pb_w2b", [128, 16 * 8], BF16)
    stg = S("pb_stg", [128, 2048])
    x1b = S("pb_x1b", [128, 16 * 512], BF16)
    h_bf = S("pb_hbf", [128, 16 * 512], BF16)
    vxt = S("pb_vxt", [128, 4 * 512], BF16)
    ffn_bf = S("pb_ffnbf", [128, 16 * 512], BF16)
    sqt = S("pb_sqt", [128, 512], BF16)
    t1 = S("pb_t1", [128, 512])
    x1t = S("pb_x1t", [128, 512])
    x2t = S("pb_x2t", [128, 512])
    mu_bc = S("pb_mubc", [128, 512])
    rs_bc = S("pb_rsbc", [128, 512])
    rs2_bc = S("pb_rs2bc", [128, 512])
    srow = S("pb_srow", [1, 1024])
    negmu2 = S("pb_negmu2", [1, 4 * 512])
    rs2stg = S("pb_rs2stg", [1, 512])
    csw1 = S("pb_csw1", [1, 2048])
    csv = S("pb_csv", [1, 512])
    b2r = S("pb_b2r", [1, 8])
    b1c = S("pb_b1c", [128, 16])
    E8 = S("pb_E8", [8, 512], BF16)
    alph = S("pb_alph", [128, 8])
    mx1 = S("pb_mx1", [128, 1])
    alT = S("pb_alT", [8, 512], BF16)
    ones_col_f = S("pb_onescf", [128, 1])
    ones_col_b = S("pb_onescb", [128, 1], BF16)
    ones_row_f = S("pb_onesrf", [1, 128])
    identf = S("pb_identf", [128, 128])
    ps1 = P_("pb_ps1", [128, 512])
    ps2 = P_("pb_ps2", [128, 512])
    pss = P_("pb_pss", [1, 2048])
    psa = P_("pb_psa", [128, 8])
    psT4 = P_("pb_psT4", [8, 512])
    rd = {"1": ("sv", 0), "2": ("sv", 0), "s": ("sv", 0), "a": ("sv", 0), "t4": ("sv", 0)}
    wr = {"1": 0, "2": 0, "s": 0, "a": 0, "t4": 0}
    bar = bw

    def g_c(g):
        g.memset(ones_col_f[:], 1.0)
        g.memset(ones_col_b[:], 1.0)
        g.memset(ones_row_f[:], 1.0)
        g.memset(E8[:], 1.0)
        g.memset(identf[:], 0.0)
        g.affine_select(out=identf[:], in_=identf[:], compare_op=mybir.AluOpType.not_equal,
                        fill=1.0, base=0, pattern=[[-1, 128]], channel_multiplier=1)
        g.affine_select(out=E8[:], in_=E8[:], compare_op=mybir.AluOpType.is_ge, fill=0.0,
                        base=0, pattern=[[1, 512]], channel_multiplier=-64)
        return g.affine_select(out=E8[:], in_=E8[:], compare_op=mybir.AluOpType.is_ge, fill=0.0,
                               base=63, pattern=[[-1, 512]], channel_multiplier=64)
    gv = P.emit("g", g_c, waits=bar, inc="sg")

    cast_v = 0
    csw_tv = None
    for dt in range(16):
        def s_w(s, dt=dt):
            ins = s.dma_start(out=stg[:], in_=w1[dt * 128:(dt + 1) * 128, :])
            ins.then_inc(P.sems["dlc"], 16)
            return ins
        P.emit("s", s_w, waits=bar + [("sv", cast_v), ("st", csw_tv or 0)])
        wv = P.bump("dlc", 16)
        cast_v = P.emit("v", lambda v, dt=dt: v.tensor_copy(out=w1b[:, dt * 2048:(dt + 1) * 2048],
                                                            in_=stg[:]),
                        waits=bar + [("dlc", wv)], inc="sv")
        def t_cs2(t, dt=dt):
            ins = None
            for nb in range(4):
                ins = t.matmul(out=pss[0:1, nb * 512:(nb + 1) * 512], lhsT=ones_col_f[:],
                               rhs=stg[:, nb * 512:(nb + 1) * 512],
                               start=(dt == 0), stop=(dt == 15))
            return ins
        csw_tv = P.emit("t", t_cs2, waits=[("dlc", wv)] + ([rd["s"], ("sg", gv)] if dt == 0 else []), inc="st")
    wr["s"] = csw_tv
    def v_cs(v):
        ins = None
        for nb in range(4):
            ins = v.tensor_copy(out=csw1[0:1, nb * 512:(nb + 1) * 512], in_=pss[0:1, nb * 512:(nb + 1) * 512])
        return ins
    vv = P.emit("v", v_cs, waits=[("st", csw_tv)], inc="sv")
    rd["s"] = ("sv", vv)

    csv_tv = csw_tv
    cast2_v = cast_v
    for dt in range(16):
        def s_w(s, dt=dt):
            ins = s.dma_start(out=stg[:, 0:512], in_=vcatT[dt * 128:(dt + 1) * 128, :])
            ins.then_inc(P.sems["dlc"], 16)
            return ins
        P.emit("s", s_w, waits=[("sv", cast2_v), ("st", csv_tv or 0)])
        wv = P.bump("dlc", 16)
        cast2_v = P.emit("v", lambda v, dt=dt: v.tensor_copy(out=vcb[:, dt * 512:(dt + 1) * 512],
                                                             in_=stg[:, 0:512]),
                         waits=[("dlc", wv)], inc="sv")
        csv_tv = P.emit("t", lambda t, dt=dt: t.matmul(out=ps2[0:1, 0:512],
                                                       lhsT=ones_col_f[:], rhs=stg[:, 0:512],
                                                       start=(dt == 0), stop=(dt == 15)),
                        waits=[("dlc", wv)] + ([rd["2"], ("sg", gv)] if dt == 0 else []), inc="st")
    wr["2"] = csv_tv
    vv = P.emit("v", lambda v: v.tensor_copy(out=csv[:], in_=ps2[0:1, 0:512]),
                waits=[("st", csv_tv)], inc="sv")
    rd["2"] = ("sv", vv)

    for kb in range(4):
        def s_w(s, kb=kb):
            ins = s.dma_start(out=stg[:], in_=ucat[kb * 128:(kb + 1) * 128, :])
            ins.then_inc(P.sems["dlc"], 16)
            return ins
        P.emit("s", s_w, waits=[("sv", P.val("sv")), ("st", csv_tv)])
        wv = P.bump("dlc", 16)
        P.emit("v", lambda v, kb=kb: v.tensor_copy(out=ucb[:, kb * 2048:(kb + 1) * 2048], in_=stg[:]),
               waits=[("dlc", wv)], inc="sv")
    def s_w2(s):
        ins = None
        for dt in range(16):
            ins = s.dma_start(out=stg[:, dt * 8:(dt + 1) * 8], in_=w2[dt * 128:(dt + 1) * 128, :])
            ins.then_inc(P.sems["dlc"], 16)
        ins = s.dma_start(out=b2r[:], in_=b2)
        ins.then_inc(P.sems["dlc"], 16)
        ins = s.dma_start(out=b1c[:], in_=b1.rearrange("(dt p) one -> p (dt one)", p=128), allow_slow_non_contiguous=True)
        ins.then_inc(P.sems["dlc"], 16)
        ins = s.dma_start(out=negmu2[:], in_=ln2st_d[0:1, 0:2048])
        ins.then_inc(P.sems["dlc"], 16)
        return ins
    P.emit("s", s_w2, waits=[("sv", P.val("sv")), ("do", P.val("do"))])
    mv_ = P.bump("dlc", 16 * 19)
    w2c_v = P.emit("v", lambda v: v.tensor_copy(out=w2b[:], in_=stg[:, 0:128]),
                   waits=[("dlc", mv_)], inc="sv")
    neg_v = P.emit("v", lambda v: v.tensor_scalar(out=negmu2[:], in0=negmu2[:],
                                                  scalar1=-1.0, scalar2=None, op0=MU),
                   waits=[("dlc", mv_)], inc="sv")

    ep_v = 0
    ldx_prev = 0
    for r in range(4):
        def s_x(s, r=r):
            ins = None
            for dt in range(16):
                ins = s.dma_start(out=x1b[:, dt * 512:(dt + 1) * 512],
                                  in_=x1b_d[dt * 128:(dt + 1) * 128, r * 512:(r + 1) * 512])
                ins.then_inc(P.sems["dl0"], 16)
            return ins
        P.emit("s", s_x, waits=[("sv", ep_v), ("do", P.val("do"))])
        xv = P.bump("dl0", 16 * 16)

        def s_r2(s, r=r):
            ins = s.dma_start(out=rs2stg[:], in_=ln2st_d[0:1, 2048 + r * 512:2048 + (r + 1) * 512])
            ins.then_inc(P.sems["dr2"], 16)
            return ins
        P.emit("s", s_r2, waits=[("st", P.val("st"))])
        r2v = P.bump("dr2", 16)
        def t_r2(t, r=r):
            return t.matmul(out=ps2[:], lhsT=ones_row_f[:], rhs=rs2stg[:],
                            start=True, stop=True)
        tv = P.emit("t", t_r2, waits=[("dr2", r2v)] + [rd["2"]], inc="st")
        wr["2"] = tv
        vv = P.emit("v", lambda v: v.tensor_copy(out=rs2_bc[:], in_=ps2[:]),
                    waits=[("st", tv)], inc="sv")
        rd["2"] = ("sv", vv)
        rs2_v = vv

        prev_hav = 0
        for et in range(16):
            key = "1" if et % 2 == 0 else "2"
            psx = ps1 if et % 2 == 0 else ps2
            def t_h(t, et=et, r=r, psx=psx):
                ins = None
                for dt in range(16):
                    ins = t.matmul(out=psx[:], lhsT=w1b[:, dt * 2048 + et * 128:dt * 2048 + (et + 1) * 128],
                                   rhs=x1b[:, dt * 512:(dt + 1) * 512], start=(dt == 0), stop=False)
                return t.matmul(out=psx[:], lhsT=csw1[0:1, et * 128:(et + 1) * 128],
                                rhs=negmu2[0:1, r * 512:(r + 1) * 512], start=False, stop=True)
            tv = P.emit("t", t_h, waits=[("dl0", xv), ("sv", rs2_v), ("sv", neg_v),
                                         ("sv", w2c_v)] + [rd[key]], inc="st")
            wr[key] = tv
            vv = P.emit("v", lambda v, psx=psx: v.tensor_tensor(out=t1[:], in0=psx[:], in1=rs2_bc[:], op=MU),
                        waits=[("st", tv), ("sa", prev_hav)], inc="sv")
            rd[key] = ("sv", vv)
            av = P.emit("a", lambda a, et=et: a.activation(out=h_bf[:, et * 512:(et + 1) * 512],
                                                           in_=t1[:], func=AF.Relu,
                                                           bias=b1c[:, et:et + 1]),
                        waits=[("sv", vv)], inc="sa")
            prev_hav = av
        h_v = av

        for rsub in range(4):
            def t_a(t, rsub=rsub):
                ins = None
                for et in range(16):
                    ins = t.matmul(out=psa[:], lhsT=h_bf[:, et * 512 + rsub * 128:et * 512 + (rsub + 1) * 128],
                                   rhs=w2b[:, et * 8:(et + 1) * 8], start=(et == 0), stop=False)
                return t.matmul(out=psa[:], lhsT=ones_row_f[:], rhs=b2r[:], start=False, stop=True)
            tv = P.emit("t", t_a, waits=[("sa", h_v)] + [rd["a"]], inc="st")
            wr["a"] = tv
            def v_sm(v):
                v.tensor_copy(out=alph[:], in_=psa[:])
                v.tensor_reduce(out=mx1[:], in_=alph[:], axis=AX, op=MX)
                return v.tensor_scalar(out=alph[:], in0=alph[:], scalar1=mx1[:, 0:1],
                                       scalar2=None, op0=SU)
            vv = P.emit("v", v_sm, waits=[("st", tv)], inc="sv")
            rd["a"] = ("sv", vv)
            av = P.emit("a", lambda a: a.activation(out=alph[:], in_=alph[:], func=AF.Exp),
                        waits=[("sv", vv)], inc="sa")
            def v_sm2(v):
                v.tensor_reduce(out=mx1[:], in_=alph[:], axis=AX, op=AD)
                v.reciprocal(out=mx1[:], in_=mx1[:])
                return v.tensor_scalar(out=alph[:], in0=alph[:], scalar1=mx1[:, 0:1],
                                       scalar2=None, op0=MU)
            vv = P.emit("v", v_sm2, waits=[("sa", av)], inc="sv")
            def t_tr(t, rsub=rsub):
                return t.transpose(out=psT4[0:8, rsub * 128:(rsub + 1) * 128], in_=alph[:],
                                   identity=identf[:])
            tv = P.emit("t", t_tr, waits=[("sv", vv)] + ([rd["t4"]] if rsub == 0 else []), inc="st")
        wr["t4"] = tv
        vv = P.emit("v", lambda v: v.tensor_copy(out=alT[:], in_=psT4[0:8, :]),
                    waits=[("st", tv)], inc="sv")
        rd["t4"] = ("sv", vv)
        al_v = vv

        for krt in range(4):
            def t_vx(t, krt=krt, r=r):
                ins = None
                for dt in range(16):
                    ins = t.matmul(out=ps1[:], lhsT=vcb[:, dt * 512 + krt * 128:dt * 512 + (krt + 1) * 128],
                                   rhs=x1b[:, dt * 512:(dt + 1) * 512], start=(dt == 0), stop=False)
                return t.matmul(out=ps1[:], lhsT=csv[0:1, krt * 128:(krt + 1) * 128],
                                rhs=negmu2[0:1, r * 512:(r + 1) * 512], start=False, stop=True)
            tv1 = P.emit("t", t_vx, waits=[("sa", h_v)] + [rd["1"]], inc="st")
            wr["1"] = tv1
            def t_ae(t, krt=krt):
                return t.matmul(out=ps2[:], lhsT=E8[0:8, krt * 128:(krt + 1) * 128], rhs=alT[:],
                                start=True, stop=True)
            tv2 = P.emit("t", t_ae, waits=[("sv", al_v)] + [rd["2"]], inc="st")
            wr["2"] = tv2
            def v_ax(v, krt=krt):
                v.tensor_tensor(out=t1[:], in0=ps2[:], in1=rs2_bc[:], op=MU)
                return v.tensor_tensor(out=x1t[:], in0=ps1[:], in1=t1[:], op=MU)
            vv = P.emit("v", v_ax, waits=[("st", tv1), ("st", tv2)], inc="sv")
            rd["1"] = rd["2"] = ("sv", vv)
            vv = P.emit("v", lambda v, krt=krt: v.tensor_copy(out=vxt[:, krt * 512:(krt + 1) * 512],
                                                              in_=x1t[:]),
                        waits=[("sv", vv)], inc="sv")
        vx_v = vv

        st_tv = None
        for dt in range(16):
            key = "1" if dt % 2 == 0 else "2"
            psx = ps1 if dt % 2 == 0 else ps2
            def t_f(t, dt=dt, psx=psx):
                ins = None
                for krt in range(4):
                    ins = t.matmul(out=psx[:], lhsT=ucb[:, krt * 2048 + dt * 128:krt * 2048 + (dt + 1) * 128],
                                   rhs=vxt[:, krt * 512:(krt + 1) * 512], start=(krt == 0), stop=(krt == 3))
                return ins
            tv = P.emit("t", t_f, waits=[("sv", vx_v)] + [rd[key]], inc="st")
            wr[key] = tv
            vv = P.emit("v", lambda v, dt=dt, psx=psx: v.tensor_copy(out=ffn_bf[:, dt * 512:(dt + 1) * 512],
                                                                     in_=psx[:]),
                        waits=[("st", tv)], inc="sv")
            rd[key] = ("sv", vv)
            av = P.emit("a", lambda a, dt=dt: a.activation(out=sqt[:], in_=ffn_bf[:, dt * 512:(dt + 1) * 512],
                                                           func=AF.Square),
                        waits=[("sv", vv), ("st", st_tv or wr["s"])], inc="sa")
            def t_s(t, dt=dt):
                t.matmul(out=pss[0:1, 0:512], lhsT=ones_col_b[:], rhs=ffn_bf[:, dt * 512:(dt + 1) * 512],
                         start=(dt == 0), stop=(dt == 15))
                return t.matmul(out=pss[0:1, 512:1024], lhsT=ones_col_b[:], rhs=sqt[:],
                                start=(dt == 0), stop=(dt == 15))
            st_tv = P.emit("t", t_s, waits=[("sv", vv), ("sa", av)] + ([rd["s"]] if dt == 0 else []),
                           inc="st")
        wr["s"] = st_tv

        def v_fin(v):
            v.tensor_scalar(out=srow[0:1, 0:512], in0=pss[0:1, 0:512], scalar1=1.0 / HID, scalar2=None, op0=MU)
            v.tensor_scalar(out=srow[0:1, 512:1024], in0=pss[0:1, 512:1024], scalar1=1.0 / HID, scalar2=None, op0=MU)
            v.tensor_tensor(out=t1[0:1, :], in0=srow[0:1, 0:512], in1=srow[0:1, 0:512], op=MU)
            v.tensor_tensor(out=srow[0:1, 512:1024], in0=srow[0:1, 512:1024], in1=t1[0:1, :], op=SU)
            return v.tensor_scalar(out=srow[0:1, 512:1024], in0=srow[0:1, 512:1024], scalar1=1e-5, scalar2=None, op0=AD)
        vv = P.emit("v", v_fin, waits=[("st", st_tv)], inc="sv")
        rd["s"] = ("sv", vv)
        av = P.emit("a", lambda a: a.activation(out=srow[0:1, 512:1024], in_=srow[0:1, 512:1024], func=AF.Sqrt),
                    waits=[("sv", vv)], inc="sa")
        vv = P.emit("v", lambda v: v.reciprocal(out=srow[0:1, 512:1024], in_=srow[0:1, 512:1024]),
                    waits=[("sa", av)], inc="sv")
        def t_bc(t):
            t.matmul(out=ps1[:], lhsT=ones_row_f[:], rhs=srow[0:1, 0:512], start=True, stop=True)
            return t.matmul(out=ps2[:], lhsT=ones_row_f[:], rhs=srow[0:1, 512:1024], start=True, stop=True)
        tv = P.emit("t", t_bc, waits=[("sv", vv)] + [rd["1"], rd["2"]], inc="st")
        wr["1"] = wr["2"] = tv
        def v_bc(v):
            v.tensor_copy(out=mu_bc[:], in_=ps1[:])
            return v.tensor_copy(out=rs_bc[:], in_=ps2[:])
        bc_v = P.emit("v", v_bc, waits=[("st", tv)], inc="sv")
        rd["1"] = rd["2"] = ("sv", bc_v)

        for dt in range(16):
            def s_x1(s, dt=dt, r=r):
                ins = s.dma_start(out=x1t[:], in_=x1_d[dt * 128:(dt + 1) * 128, r * 512:(r + 1) * 512])
                ins.then_inc(P.sems["dl1"], 16)
                return ins
            P.emit("s", s_x1, waits=[("sv", P.val("sv"))])
            x1v = P.bump("dl1", 16)
            def v_x2(v, dt=dt):
                v.tensor_tensor(out=t1[:], in0=ffn_bf[:, dt * 512:(dt + 1) * 512], in1=mu_bc[:], op=SU)
                v.tensor_tensor(out=t1[:], in0=t1[:], in1=rs_bc[:], op=MU)
                return v.tensor_tensor(out=x2t[:], in0=t1[:], in1=x1t[:], op=AD)
            vv = P.emit("v", v_x2, waits=[("sv", bc_v), ("dl1", x1v), ("do", P.val("do"))], inc="sv")
            def g_y(g, dt=dt, r=r):
                ins = g.dma_start(out=y[dt * 128:(dt + 1) * 128, r * 512:(r + 1) * 512], in_=x2t[:])
                ins.then_inc(P.sems["do"], 16)
                return ins
            P.emit("g", g_y, waits=[("sv", vv)])
            P.bump("do", 16)
        ep_v = P.val("sv")


def barrier_waits(prog):
    return [(s, prog.val(s)) for s in ("st", "sv", "sa", "sg", "dlc", "dl0", "dl1", "do", "cc", "dsx", "dbb", "dst", "dr2", "ddy", "do0", "do1")]


def build_full():
    PH = int(os.environ.get("FULL_PH", "9"))
    nc = bass.Bass("TRN2", target_bir_lowering=False, debug=False,
                   detect_race_conditions=bool(os.environ.get("KK_RACE")))
    DT = lambda nm, sh, dt=F32, kind="ExternalInput": nc.dram_tensor(nm, sh, dt, kind=kind).ap()
    xT = DT("xT", [HID, ROWS])
    xTr = DT("xTr", [HID, RPC])
    wcat = DT("wcat", [HID, 772])
    babb = DT("babb", [1, 4])
    lamp = DT("lamp", [1, 256])
    onw = DT("onw", [1, 128])
    wo = DT("wo", [HID, HID])
    w1 = DT("w1", [HID, HID])
    b1 = DT("b1", [HID, 1])
    w2 = DT("w2", [HID, 8])
    b2 = DT("b2", [1, 8])
    ucat = DT("ucat", [512, HID])
    vcatT = DT("vcatT", [HID, 512])
    y = DT("y", [HID, RPC], kind="ExternalOutput")

    _dbgk = "ExternalOutput" if os.environ.get("KK_DBG") else "Internal"
    q_d = DT("q_d", [ROWS, 256], kind=_dbgk)
    k_d = DT("k_d", [ROWS, 256], BF16, kind=_dbgk)
    v_d = DT("v_d", [ROWS, 256], BF16, kind=_dbgk)
    g_d = DT("g_d", [ROWS, HPC], kind=_dbgk)
    bt_d = DT("bt_d", [ROWS, HPC], kind=_dbgk)
    o_d = DT("o_d", [256, ROWS], BF16, kind="Internal")
    o_all = nc.dram_tensor("o_all", [HID, ROWS], BF16, kind="Internal",
                           addr_space="Shared").ap()
    st_in = DT("st_in", [2, RPC], kind="Internal")
    st_all = DT("st_all", [16, RPC], kind="Internal")
    x1_d = DT("x1_d", [HID, RPC], kind=_dbgk)
    x1b_d = DT("x1b_d", [HID, RPC], BF16, kind="Internal")
    ln2st_d = DT("ln2st_d", [1, 2 * RPC], kind="Internal")

    prog = Prog()
    sem_names = ["st", "sv", "sa", "sg", "dlc", "dl0", "dl1", "do", "cc", "dsx", "dbb", "dst", "dr2", "ddy", "do0", "do1"]
    with ExitStack() as ctx:
        sems = {n: ctx.enter_context(nc.semaphore(n)) for n in sem_names}
        prog.sems = sems

        def g_pid(g):
            prog.pid_g = g.partition_id()
            return g.nop()
        prog.emit("g", g_pid)

        with ExitStack() as c_mesa:
            M = Mesa(nc, c_mesa, prog, T_FULL)
            with ExitStack() as c_p2:
                M.emit_consts(lamp, onw)
                p2_d0, p2_d1 = emit_p12(nc, prog, M, c_p2, xT, xTr, wcat, babb, st_in, st_all,
                                        q_d, k_d, v_d, g_d, bt_d)

                _nck = min(int(os.environ.get("KK_NCHUNK", "32")), T_FULL // C)
                for cidx in range(0 if PH < 3 else _nck):
                    z = cidx % 2
                    dl_sem = f"dl{z}"
                    def s_ld(s, z=z, cidx=cidx, dl_sem=dl_sem):
                        ins = None
                        for b in range(4):
                            r0 = b * T_FULL + cidx * C
                            for (dst, srcd) in ((M.aq[z], q_d), (M.ak[z], k_d), (M.av[z], v_d)):
                                ins = s.dma_start(out=dst[:, b * 256:(b + 1) * 256],
                                                  in_=srcd[r0:r0 + C, :])
                                ins.then_inc(prog.sems[dl_sem], 16)
                            ins = s.dma_start(out=M.ag[z][:, b * HPC:(b + 1) * HPC],
                                              in_=g_d[r0:r0 + C, :])
                            ins.then_inc(prog.sems[dl_sem], 16)
                            ins = s.dma_start(out=M.ab[z][:, b * HPC:(b + 1) * HPC],
                                              in_=bt_d[r0:r0 + C, :])
                            ins.then_inc(prog.sems[dl_sem], 16)
                        return ins
                    w = ([("sv", M.hstate_v)] if cidx >= 2
                         else [("do0", p2_d0), ("do1", p2_d1)])
                    prog.emit("s", s_ld, waits=w)
                    dl_val = prog.bump(dl_sem, 16 * 20)
                    M.emit_chunk(z, cidx, o_d, dl_sem, dl_val, T_FULL)

                if PH >= 3:
                    def g_ago(g):
                        return g.collective_compute("AllGather", mybir.AluOpType.bypass,
                                                    replica_groups=[list(range(NCORES))],
                                                    ins=[o_d[:, :].opt()], outs=[o_all[:, :].opt()])
                    prog.emit("g", g_ago, waits=[("do", prog.val("do")), ("sv", M.hstate_v)], inc="cc")

        bw = barrier_waits(prog)
        if PH >= 4:
            with ExitStack() as c4a:
                emit_p4a(nc, prog, c4a, xTr, wo, o_all, x1_d, x1b_d, ln2st_d, bw)
        bw2 = barrier_waits(prog)
        if PH >= 5:
            with ExitStack() as c4b:
                emit_p4b(nc, prog, c4b, x1_d, x1b_d, ln2st_d, w1, b1, w2, b2, ucat, vcatT, y, bw2)

        prog.emit("g", lambda g: g.nop(), waits=[("do", prog.val("do"))])

        block = ctx.enter_context(nc.Block())
        prog.realize(block, sems)
    return nc


_NC_CACHE = {}


def kernel(L0_attn_x, L0_attn_Wq, L0_attn_Wk, L0_attn_Wv, L0_attn_Wa, L0_attn_ba,
           L0_attn_Wb, L0_attn_bb, L0_attn_lambda_params, L0_attn_o_norm_weight,
           L0_attn_Wo, L0_ffn_U, L0_ffn_V, L0_ffn_W1, L0_ffn_b1, L0_ffn_W2, L0_ffn_b2):
    f32 = np.float32
    x = np.ascontiguousarray(np.asarray(L0_attn_x, dtype=f32).reshape(ROWS, HID))
    xT = np.ascontiguousarray(x.T)
    Wq = np.asarray(L0_attn_Wq, f32)
    Wk = np.asarray(L0_attn_Wk, f32)
    Wv = np.asarray(L0_attn_Wv, f32)
    Wa = np.asarray(L0_attn_Wa, f32)
    Wb = np.asarray(L0_attn_Wb, f32)
    ba = np.asarray(L0_attn_ba, f32)
    bb = np.asarray(L0_attn_bb, f32)
    lam = np.asarray(L0_attn_lambda_params, f32).reshape(NH, HD)
    onw = np.asarray(L0_attn_o_norm_weight, f32).reshape(1, HD)
    wo = np.ascontiguousarray(np.asarray(L0_attn_Wo, f32))
    w1 = np.ascontiguousarray(np.asarray(L0_ffn_W1, f32))
    b1 = np.asarray(L0_ffn_b1, f32).reshape(HID, 1)
    w2 = np.ascontiguousarray(np.asarray(L0_ffn_W2, f32))
    b2 = np.asarray(L0_ffn_b2, f32).reshape(1, 8)
    U = np.asarray(L0_ffn_U, f32)
    V = np.asarray(L0_ffn_V, f32)
    ucat = np.ascontiguousarray(U.transpose(0, 2, 1).reshape(512, HID))
    vcatT = np.ascontiguousarray(V.reshape(512, HID).T)

    if "nc" not in _NC_CACHE:
        _NC_CACHE["nc"] = build_full()
    nc = _NC_CACHE["nc"]

    in_maps = []
    for c in range(NCORES):
        hs = slice(c * HPC * HD, (c + 1) * HPC * HD)
        hh = slice(c * HPC, (c + 1) * HPC)
        wcat_c = np.ascontiguousarray(np.concatenate(
            [Wq[:, hs], Wk[:, hs], Wv[:, hs], Wa[:, hh], Wb[:, hh]], axis=1))
        babb_c = np.ascontiguousarray(
            np.concatenate([ba[hh], bb[hh]]).reshape(1, 4))
        lamp_c = np.ascontiguousarray(lam[hh].reshape(1, HPC * HD))
        xTr_c = np.ascontiguousarray(xT[:, c * RPC:(c + 1) * RPC])
        in_maps.append(dict(xT=xT, xTr=xTr_c, wcat=wcat_c, babb=babb_c, lamp=lamp_c, onw=onw,
                            wo=wo, w1=w1, b1=b1, w2=w2, b2=b2, ucat=ucat, vcatT=vcatT))

    res = run_bass_kernel_spmd(nc, in_maps, core_ids=list(range(NCORES)))
    yT = np.concatenate([np.asarray(r["y"], dtype=f32) for r in res.results], axis=1)
    out = np.ascontiguousarray(yT.T).reshape(4, T_FULL, HID)
    return out
